# version 03
# baseline (speedup 1.0000x reference)
import numpy as np
from contextlib import ExitStack
import concourse.bass as bass
import concourse.mybir as mybir
from concourse.bass_utils import run_bass_kernel_spmd

F32 = mybir.dt.float32
BF16 = mybir.dt.bfloat16
I32 = mybir.dt.int32
AF = mybir.ActivationFunctionType
ALU = mybir.AluOpType
AX = mybir.AxisListType
AP = bass.AP

NCORES = 8
S = 2048
D = 1024
H = 16
NCH = 8
NT = 16
EPS = 1e-5

ENGS = ("sync", "gpsimd", "vector", "scalar", "tensor")
OWN = {"tensor": "s_mm", "vector": "s_ve", "scalar": "s_se", "gpsimd": "s_ge"}


class Sem:
    __slots__ = ("h", "n", "name")

    def __init__(self, h, name):
        self.h = h
        self.n = 0
        self.name = name


class Prog:
    def __init__(self, nc, ctx):
        self.nc = nc
        self.ctx = ctx
        self.ops = {e: [] for e in ENGS}
        self.hw = {e: {} for e in ENGS}

    def sem(self, name):
        return Sem(self.ctx.enter_context(self.nc.semaphore(name)), name)

    def add(self, eng, fn, waits=(), inc=None):
        ws = []
        best = {}
        for w in waits:
            if w is None:
                continue
            s, v = w
            if s.name == OWN.get(eng):
                continue
            if best.get(s.name, (None, -1))[1] < v:
                best[s.name] = (s, v)
        for s, v in best.values():
            if self.hw[eng].get(s.name, -1) >= v:
                continue
            self.hw[eng][s.name] = v
            ws.append((s, v))
        rec = None
        if inc is not None:
            sem, k = inc
            sem.n += k
            rec = (sem, k)
        self.ops[eng].append((tuple(ws), fn, rec))
        if rec is not None:
            return (rec[0], rec[0].n)
        return None

    def dma(self, eng, out, in_, waits=(), inc=None, slow=False):
        return self.add(eng,
                        lambda e, o=out, i=in_, sl=slow: e.dma_start(
                            out=o, in_=i, allow_slow_non_contiguous=sl),
                        waits=waits, inc=inc)

    def emit(self, block):
        for ename in ENGS:
            oplist = self.ops[ename]

            def runner(eng, oplist=oplist):
                for ws, fn, rec in oplist:
                    for (s, v) in ws:
                        eng.wait_ge(s.h, v)
                    if fn is None:
                        continue
                    ins = fn(eng)
                    if rec is not None:
                        ins.then_inc(rec[0].h, rec[1])

            getattr(block, ename)(runner)


def l0_chunks(t):
    i0 = 128 * t
    lo = max(0, i0 - 128)
    hi = min(S, i0 + 256)
    out = []
    j = 0
    while j < lo:
        w = min(512, lo - j)
        out.append(("L", j, w))
        j += w
    out.append(("B", lo, hi - lo))
    j = hi
    while j < S:
        w = min(512, S - j)
        out.append(("R", j, w))
        j += w
    assert len(out) <= 8
    return out


def l1_chunks(t):
    i0 = 128 * t
    out = []
    j = 0
    while j < i0:
        w = min(512, i0 - j)
        out.append(("F", j, w))
        j += w
    out.append(("D", i0, 128))
    assert len(out) <= 8
    return out


_LN_N = [0]


def build_program(stop="full", dbg=False):
    _LN_N[0] = 0
    nc = bass.Bass("TRN2", target_bir_lowering=False,
                   detect_race_conditions=False)
    ctx = ExitStack()
    e = ctx.enter_context
    dt = {}

    def din(name, shape, dtype=F32):
        dt[name] = nc.dram_tensor(name, list(shape), dtype, kind="ExternalInput").ap()

    din("x_fm", [NCH, 128, S])
    y = nc.dram_tensor("y", [NCH, 128, S], F32, kind="ExternalOutput").ap()
    din("l0_wqk", [8, 2, 8, 128, 128]); din("l0_wv", [8, 128, D])
    din("l0_bq", [128, 8]); din("l0_bk", [128, 8]); din("l0_bv", [1, D])
    din("l0_wrT", [64, 257]); din("l0_br", [1, 257])
    din("l0_w1", [32, 8, 128, 128]); din("l0_b1", [128, 32])
    din("l0_w2", [2, 32, 4, 128, 128]); din("l0_b2", [128, 8])
    din("l1_wqk", [8, 2, 8, 128, 128]); din("l1_wv", [8, 128, D])
    din("l1_bq", [128, 8]); din("l1_bk", [128, 8]); din("l1_bv", [1, D])
    din("l1_wo", [8, 8, 128, 128]); din("l1_bo", [128, 8])
    din("l1_wfc", [32, 8, 128, 128]); din("l1_bfc", [128, 32])
    din("l1_afc", [8, 128, 16]); din("l1_bfc_lo", [32, 16, 128])
    din("l1_wproj", [2, 32, 4, 128, 128]); din("l1_bproj", [128, 8])
    din("l1_aproj", [32, 128, 16]); din("l1_bproj_lo", [8, 16, 128])

    hbm_h = [nc.dram_tensor(f"hshear{i}", [128, 513], F32, kind="Internal")
             for i in range(2)]
    hbm_wq = nc.dram_tensor("wprojq", [2, 32, 4, 128, 128], BF16, kind="Internal")
    hbm_at = nc.dram_tensor("attnhbm", [8, 128, S], BF16, kind="Internal")
    hbm_q = nc.dram_tensor("hbmq", [1, 512], F32, kind="Internal")
    hbm_wfq = nc.dram_tensor("wfcq", [32, 128, 1024], BF16, kind="Internal")
    hbm_wpq = nc.dram_tensor("wprojq2", [2, 32, 128, 512], BF16, kind="Internal")

    xs = e(nc.sbuf_tensor("xs", [128, NCH * S], F32))
    xb = e(nc.sbuf_tensor("xb", [128, NCH * S], BF16))
    ident = e(nc.sbuf_tensor("ident", [128, 128], BF16))
    tri01 = e(nc.sbuf_tensor("tri01", [128, 128], BF16))
    ones_r = e(nc.sbuf_tensor("ones_r", [1, 128], F32))
    halfs_r = e(nc.sbuf_tensor("halfs_r", [1, 128], F32))
    ones_rb = e(nc.sbuf_tensor("ones_rb", [1, 128], BF16))
    ones_c = e(nc.sbuf_tensor("ones_c", [128, 1], BF16))
    ones_cf = e(nc.sbuf_tensor("ones_cf", [128, 1], F32))
    smalls = e(nc.sbuf_tensor("smalls", [128, 136], F32))
    brow = e(nc.sbuf_tensor("brow", [1, 257], F32))
    bvrow = e(nc.sbuf_tensor("bvrow", [1, D], BF16))
    scal = e(nc.sbuf_tensor("scal", [1, 32], F32))

    SM = dict(L0BQ=0, L0BK=8, L0B1=16, L0B2=48, L1BQ=56, L1BK=64, L1BO=72,
              L1BFC=80, L1BPJ=112, WR8L=120, WR8R=121, NC1=122, NBR=123,
              BR0=124, BR256=125, RSFC=126, SFC=127, RSPJ=128, SPJ=129, HB=130)

    p = Prog(nc, ctx)
    sems = {k: p.sem(k) for k in
            ("s_x", "s_mm", "s_ve", "s_se", "s_ge",
             "s_wA", "s_wB", "s_hw0", "s_hw1", "s_br0", "s_br1",
             "s_qn", "s_rro", "s_out", "s_wv", "s_wo",
             "s_fA", "s_fB", "s_gA", "s_gB", "s_tr", "s_ah", "s_qs", "s_qc")}

    st = dict(stop_=stop, dbg=dbg, dbgout={}, nc=nc, p=p, ctx=ctx, dt=dt, y=y, xs=xs, xb=xb, ident=ident,
              tri01=tri01, ones_r=ones_r, halfs_r=halfs_r, ones_rb=ones_rb, ones_c=ones_c, ones_cf=ones_cf,
              smalls=smalls, brow=brow, bvrow=bvrow, scal=scal,
              hbm_h=hbm_h, hbm_wq=hbm_wq, hbm_at=hbm_at, hbm_q=hbm_q,
              hbm_wfq=hbm_wfq, hbm_wpq=hbm_wpq, sems=sems, SM=SM,
              track=dict(wslot_free=[None, None], qn_chain=None, rro_chain=None,
                         bv_rd=None, tr_chain=None, ah_wr=[None] * 8))

    s_x, s_ge, s_ve = sems["s_x"], sems["s_ge"], sems["s_ve"]
    for c in range(0, NCH, 2):
        p.dma("sync",
              out=xs[:].rearrange("p (c t) -> p c t", t=S)[:, c:c + 2, :],
              in_=dt["x_fm"][c:c + 2].transpose([1, 0, 2]), inc=(s_x, 16))

    def lsm(col, n, name):
        p.dma("sync", out=smalls[:, col:col + n], in_=dt[name], inc=(s_x, 16))

    lsm(SM["L0BQ"], 8, "l0_bq"); lsm(SM["L0BK"], 8, "l0_bk")
    lsm(SM["L0B1"], 32, "l0_b1"); lsm(SM["L0B2"], 8, "l0_b2")
    lsm(SM["L1BQ"], 8, "l1_bq"); lsm(SM["L1BK"], 8, "l1_bk")
    lsm(SM["L1BO"], 8, "l1_bo"); lsm(SM["L1BFC"], 32, "l1_bfc")
    lsm(SM["L1BPJ"], 8, "l1_bproj")
    p.dma("sync", out=brow[0:1, :], in_=dt["l0_br"], inc=(s_x, 16))
    ev_sx = (s_x, s_x.n)

    p.add("gpsimd", lambda g: g.memset(ident[:], 1.0))
    p.add("gpsimd", lambda g: g.affine_select(
        out=ident[:], in_=ident[:], pattern=[[-1, 128]], base=0,
        channel_multiplier=1, compare_op=ALU.is_equal, fill=0.0))
    p.add("gpsimd", lambda g: g.memset(tri01[:], 1.0))
    p.add("gpsimd", lambda g: g.affine_select(
        out=tri01[:], in_=tri01[:], pattern=[[-1, 128]], base=0,
        channel_multiplier=1, compare_op=ALU.is_ge, fill=0.0))
    p.add("gpsimd", lambda g: g.memset(ones_r[:], 1.0))
    p.add("gpsimd", lambda g: g.memset(halfs_r[:], 0.5))
    p.add("gpsimd", lambda g: g.memset(ones_rb[:], 1.0))
    p.add("gpsimd", lambda g: g.memset(ones_cf[:], 1.0))
    ev_cn = p.add("gpsimd", lambda g: g.memset(ones_c[:], 1.0), inc=(s_ge, 1))
    st["marks"] = dict(ev_sx=ev_sx, ev_cn=ev_cn)

    evc = None
    for c in range(NCH):
        sl = slice(c * S, (c + 1) * S)
        evc = p.add("vector",
                    lambda v, sl=sl: v.tensor_copy(out=xb[:, sl], in_=xs[:, sl]),
                    waits=(ev_sx,), inc=(s_ve, 1))
    ev = [evc]

    ev = _attention(st, ev, 0)
    if stop != "a0res":
        ev = _layernorm(st, ev)
        if stop != "x1":
            ev = _ffn(st, ev, 0)
            if stop != "f0res":
                ev = _layernorm(st, ev)
                if stop != "x2":
                    ev = _attention(st, ev, 1)
                    if stop != "a1res":
                        ev = _layernorm(st, ev)
                        if stop != "x3":
                            ev = _ffn(st, ev, 1)
                            if stop != "f1res":
                                ev = _layernorm(st, ev)

    s_out = sems["s_out"]
    for c in range(NCH):
        p.dma("sync", out=y[c], in_=xs[:, c * S:(c + 1) * S],
              waits=tuple(ev), inc=(s_out, 16))
    tot = s_out.n
    p.add("sync", None, waits=((s_out, tot),))
    p.add("gpsimd", lambda g: g.nop(), waits=((s_out, tot),))
    with nc.Block() as block:
        p.emit(block)
    ctx.close()
    return nc


def _attention(st, ev_in, layer):
    nc, p = st["nc"], st["p"]
    SM, dt = st["SM"], st["dt"]
    xs, xb = st["xs"], st["xb"]
    smalls, scal, brow, bvrow = st["smalls"], st["scal"], st["brow"], st["bvrow"]
    ones_r, ones_rb, ones_c = st["ones_r"], st["ones_rb"], st["ones_c"]
    ones_cf = st["ones_cf"]
    halfs_r = st["halfs_r"]
    ident, tri01 = st["ident"], st["tri01"]
    sems = st["sems"]
    s_mm, s_ve, s_se, s_ge = sems["s_mm"], sems["s_ve"], sems["s_se"], sems["s_ge"]
    s_hw = [sems["s_hw0"], sems["s_hw1"]]
    s_brd = [sems["s_br0"], sems["s_br1"]]
    s_qn, s_rro, s_wv, s_ah = sems["s_qn"], sems["s_rro"], sems["s_wv"], sems["s_ah"]
    swslot = [sems["s_wA"], sems["s_wB"]]
    tr = st["track"]
    L0 = layer == 0
    wname = "l0" if L0 else "l1"
    chunks_fn = l0_chunks if L0 else l1_chunks
    ev_cn, ev_sx = st["marks"]["ev_cn"], st["marks"]["ev_sx"]

    ph = ExitStack()
    ee = ph.enter_context
    vts = ee(nc.sbuf_tensor(f"vts{layer}", [128, NT * 1024], BF16))
    pc = [ee(nc.psum_tensor(f"pc{layer}{i}", [128, 512], F32)) for i in range(2)]
    prp = ee(nc.psum_tensor(f"prp{layer}", [128, 512], F32))
    ptA = ee(nc.psum_tensor(f"ptA{layer}", [128, 1024], BF16))
    ptB = ee(nc.psum_tensor(f"ptB{layer}", [128, 1024], BF16))
    ppv = [ee(nc.psum_tensor(f"ppv{layer}{i}", [128, 512], F32)) for i in range(2)]
    pbc = ee(nc.psum_tensor(f"pbc{layer}", [128, 512], F32))

    bank_free = [None, None]
    pbc_busy = None
    prp_busy = None

    ev_bv = p.dma("gpsimd", out=bvrow[:], in_=dt[wname + "_bv"],
                  waits=(tr["bv_rd"],), inc=(s_wv, 16))

    with nc.sbuf_tensor(f"wvs{layer}", [128, 8 * 1024], BF16) as wvs:
        p.dma("gpsimd", out=wvs[:].rearrange("p (c n) -> p c n", n=1024),
              in_=dt[wname + "_wv"].transpose([1, 0, 2]), inc=(s_wv, 16))
        ev_wv = (s_wv, s_wv.n)
        ev_vts = None
        for tt in range(NT):
            for ng in range(2):
                bank = (tt * 2 + ng) % 2
                b = pc[bank]
                p.add("tensor", lambda t, b=b, ng=ng: t.matmul(
                    b[:, 0:512], ones_rb[0:1, :],
                    bvrow[0:1, ng * 512:(ng + 1) * 512], start=True, stop=False),
                    waits=(ev_wv, ev_cn, bank_free[bank]) + tuple(ev_in),
                    inc=(s_mm, 1))
                ev_mv = None
                for kc in range(8):
                    ev_mv = p.add("tensor", lambda t, b=b, kc=kc, tt=tt, ng=ng: t.matmul(
                        b[:, 0:512],
                        xb[:, kc * S + tt * 128: kc * S + tt * 128 + 128],
                        wvs[:, kc * 1024 + ng * 512: kc * 1024 + (ng + 1) * 512],
                        start=False, stop=(kc == 7)), inc=(s_mm, 1))
                ev_vts = p.add("vector", lambda v, b=b, tt=tt, ng=ng: v.tensor_copy(
                    out=vts[:, tt * 1024 + ng * 512: tt * 1024 + (ng + 1) * 512],
                    in_=b[:, 0:512]), waits=(ev_mv,), inc=(s_ve, 1))
                bank_free[bank] = ev_vts
        tr["bv_rd"] = (s_mm, s_mm.n)
        ev_vdone = (s_mm, s_mm.n)

    sub = ExitStack()
    se = sub.enter_context
    wqs = [se(nc.sbuf_tensor(f"wqs{layer}{i}", [128, 2048], BF16)) for i in range(2)]
    qat = se(nc.sbuf_tensor(f"qat{layer}", [128, S], BF16))
    kat = se(nc.sbuf_tensor(f"kat{layer}", [128, S], BF16))
    sq2 = se(nc.sbuf_tensor(f"sq2{layer}", [128, S], BF16))
    P = [se(nc.sbuf_tensor(f"P{layer}{i}", [128, S], BF16)) for i in range(2)]
    PTs = se(nc.sbuf_tensor(f"PTs{layer}", [128, 8192], BF16))
    sums = se(nc.sbuf_tensor(f"sums{layer}", [128, 128], F32))
    ssml = se(nc.sbuf_tensor(f"ssml{layer}", [128, 64], F32))
    qn16 = se(nc.sbuf_tensor(f"qn16{layer}", [128, 16], F32))
    qrow = se(nc.sbuf_tensor(f"qrow{layer}", [1, 512], F32))
    Pf = se(nc.sbuf_tensor(f"Pf{layer}", [128, 2 * S], F32))
    i8 = se(nc.sbuf_tensor(f"i8{layer}", [128, 32], F32))
    if L0:
        wr2m = se(nc.sbuf_tensor("wr2m", [128, 257], BF16))
        wrf32 = se(nc.sbuf_tensor("wrf32", [64, 257], F32))
        brb = se(nc.sbuf_tensor("brb", [128, 257], F32))
        rpe = [se(nc.sbuf_tensor(f"rpe{i}", [128, 513], F32)) for i in range(2)]
        bb = [se(nc.sbuf_tensor(f"bb{i}", [128, 384], F32)) for i in range(2)]
        sband0 = se(nc.sbuf_tensor("sband0", [128, 384], F32))
        sband = [sband0, sband0]
    else:
        Pd = se(nc.sbuf_tensor("Pd", [128, 128], F32))
        attn1 = se(nc.sbuf_tensor("attn1", [128, S], BF16))

    if L0:
        s_trx = sems["s_tr"]
        p.dma("gpsimd", out=wr2m[0:64, :], in_=dt["l0_wrT"], waits=(ev_vdone,),
              inc=(s_trx, 16))
        p.dma("gpsimd", out=wr2m[64:128, :], in_=dt["l0_wrT"], inc=(s_trx, 16))
        p.dma("gpsimd", out=wrf32[0:64, :], in_=dt["l0_wrT"], inc=(s_trx, 16))
        ev_wrl = (s_trx, s_trx.n)
        ev_m = p.add("tensor", lambda t: t.matmul(
            pbc[:, 0:257], ones_r[0:1, :], brow[0:1, :], start=True, stop=True),
            waits=(ev_sx, ev_cn), inc=(s_mm, 1))
        ev_brb = p.add("vector", lambda v: v.tensor_copy(out=brb[:], in_=pbc[:, 0:257]),
                       waits=(ev_m, ev_vdone), inc=(s_ve, 1))
        pbc_busy = ev_brb
        ev_q = p.add("scalar", lambda s: s.activation(
            out=wrf32[0:64, :], in_=wrf32[0:64, :], func=AF.Square),
            waits=(ev_wrl,), inc=(s_se, 1))
        ev_m2 = p.add("tensor", lambda t: t.matmul(
            pc[0][0:1, 0:257], ones_cf[0:64, 0:1], wrf32[0:64, :],
            start=True, stop=True), waits=(ev_q, ev_cn, bank_free[0]), inc=(s_mm, 1))
        ev_wm = p.add("vector", lambda v: v.reduce_max(
            out=scal[0:1, 0:1], in_=pc[0][0:1, 0:257], axis=AX.X),
            waits=(ev_m2,), inc=(s_ve, 1))
        bank_free[0] = ev_wm
        ev_ws = p.add("scalar", lambda s: s.activation(
            out=scal[0:1, 1:2], in_=scal[0:1, 0:1], func=AF.Sqrt),
            waits=(ev_wm,), inc=(s_se, 1))
        ev_nb = p.add("vector", lambda v: v.reduce_max(
            out=scal[0:1, 3:4], in_=brow[0:1, :], axis=AX.X,
            apply_absolute_value=True, negate=True), waits=(ev_sx,),
            inc=(s_ve, 1))
        ev_b1 = p.add("tensor", lambda t: t.matmul(
            pbc[:, 300:301], halfs_r[0:1, :], scal[0:1, 3:4], start=True, stop=True),
            waits=(ev_nb, ev_brb), inc=(s_mm, 1))
        ev_c3 = p.add("vector", lambda v: v.tensor_copy(
            out=smalls[:, SM["NBR"]:SM["NBR"] + 1], in_=pbc[:, 300:301]),
            waits=(ev_b1,), inc=(s_ve, 1))
        pbc_busy = ev_c3
        ev_consts = ev_c3
        scal_sqrt_rd = None
    else:
        ev_wrl = ev_brb = ev_ws = ev_consts = None
        scal_sqrt_rd = None

    ptA_free = None
    ptB_free = None
    p_free = [None, None]
    sums_free = None
    sq2_free = None
    rpe_hw = [None, None]
    bb_free = [None, None]
    sband_free = [None, None]
    hb_rd = [None, None]
    pt_region_pv = [None] * 4
    ppv_free = [None, None]
    pf_free = [None, None]
    i8_free = [None, None]
    qk_free = None
    qrow_rd = None
    scal_rd = None
    ev_last = None

    wslot_ev = [None, None]
    wslot_ev[0] = p.dma("gpsimd",
                        out=wqs[0][:].rearrange("p (a c m) -> p a c m", a=2, m=128),
                        in_=dt[wname + "_wqk"][0].transpose([2, 0, 1, 3]),
                        waits=(tr["wslot_free"][0], ev_vdone), inc=(swslot[0], 16))

    for pr in range(8):
        slot = pr % 2
        if pr + 1 < 8:
            ns = (pr + 1) % 2
            wslot_ev[ns] = p.dma(
                "gpsimd",
                out=wqs[ns][:].rearrange("p (a c m) -> p a c m", a=2, m=128),
                in_=dt[wname + "_wqk"][pr + 1].transpose([2, 0, 1, 3]),
                waits=(tr["wslot_free"][ns], ev_vdone), inc=(swslot[ns], 16))

        first = True
        ev_mv = None
        ev_ev = None
        for qk in range(2):
            dst = qat if qk == 0 else kat
            bcol = (SM["L0BQ" if L0 else "L1BQ"] if qk == 0
                    else SM["L0BK" if L0 else "L1BK"]) + pr
            for ng in range(4):
                bank = (qk * 4 + ng) % 2
                b = pc[bank]
                for kc in range(8):
                    wts = [bank_free[bank]] if kc == 0 else []
                    if first:
                        wts += [wslot_ev[slot], qk_free,
                                (s_se, s_se.n), (s_ve, s_ve.n)] + list(ev_in)
                        first = False
                    ev_mv = p.add("tensor", lambda t, b=b, slot=slot, qk=qk, kc=kc, ng=ng: t.matmul(
                        b[:, 0:512],
                        wqs[slot][:, qk * 1024 + kc * 128: qk * 1024 + (kc + 1) * 128],
                        xb[:, kc * S + ng * 512: kc * S + (ng + 1) * 512],
                        start=(kc == 0), stop=(kc == 7)),
                        waits=wts, inc=(s_mm, 1))
                ev_ev = p.add("scalar", lambda s, b=b, dst=dst, ng=ng, bcol=bcol: s.activation(
                    out=dst[:, ng * 512:(ng + 1) * 512], in_=b[:, 0:512],
                    func=AF.Identity, bias=smalls[:, bcol:bcol + 1], scale=1.0),
                    waits=(ev_sx, ev_mv), inc=(s_se, 1))
                bank_free[bank] = ev_ev
        ev_qk = ev_ev
        tr["wslot_free"][slot] = (s_mm, s_mm.n)

        ev_kvar = ev_qk

        for hh in range(2):
            h = 2 * pr + hh
            hb = hh * 64
            hsl = slice(hb, hb + 64)

            ev_k2 = p.add("scalar", lambda s, hsl=hsl: s.activation(
                out=sq2[hsl, :], in_=kat[hsl, :], func=AF.Square),
                waits=(ev_qk, sq2_free), inc=(s_se, 1))
            kn_ev = None
            for g in range(4):
                ev_mk = p.add("tensor", lambda t, g=g, hsl=hsl: t.matmul(
                    pc[0][0:1, 0:512], ones_c[hsl, 0:1],
                    sq2[hsl, g * 512:(g + 1) * 512], start=True, stop=True),
                    waits=(ev_k2, bank_free[0], kn_ev), inc=(s_mm, 1))
                kn_ev = p.add("vector", lambda v, g=g: v.reduce_max(
                    out=scal[0:1, 8 + g:9 + g], in_=pc[0][0:1, 0:512], axis=AX.X),
                    waits=(ev_mk, scal_rd), inc=(s_ve, 1))
            bank_free[0] = kn_ev
            ev_kb = p.add("gpsimd", lambda g: g.tensor_copy(
                out=scal[0:1, 16:20], in_=scal[0:1, 8:12]),
                waits=(kn_ev,), inc=(s_ge, 1))
            ev_km = p.add("vector", lambda v: v.reduce_max(
                out=scal[0:1, 12:13], in_=scal[0:1, 16:20], axis=AX.X),
                waits=(scal_sqrt_rd, ev_kb), inc=(s_ve, 1))
            ev_ks = p.add("scalar", lambda s: s.activation(
                out=scal[0:1, 13:14], in_=scal[0:1, 12:13], func=AF.Sqrt),
                waits=(ev_km,), inc=(s_se, 1))
            scal_sqrt_rd = ev_ks
            if L0:
                ev_n = p.add("vector", lambda v: v.scalar_tensor_tensor(
                    out=scal[0:1, 15:16], in0=scal[0:1, 13:14], scalar=-0.125,
                    in1=scal[0:1, 1:2], op0=ALU.mult, op1=ALU.subtract),
                    waits=(ev_ks, ev_ws), inc=(s_ve, 1))
            else:
                ev_n = p.add("vector", lambda v: v.tensor_scalar(
                    out=scal[0:1, 15:16], in0=scal[0:1, 13:14], scalar1=-0.125,
                    scalar2=None, op0=ALU.mult), waits=(ev_ks,), inc=(s_ve, 1))
            ev_bcm = p.add("tensor", lambda t: t.matmul(
                pbc[:, 0:1], halfs_r[0:1, :], scal[0:1, 15:16], start=True, stop=True),
                waits=(ev_n, pbc_busy), inc=(s_mm, 1))
            scal_rd = ev_bcm
            ev_nc1 = p.add("vector", lambda v: v.tensor_copy(
                out=smalls[:, SM["NC1"]:SM["NC1"] + 1], in_=pbc[:, 0:1]),
                waits=(ev_bcm,), inc=(s_ve, 1))
            pbc_busy = ev_nc1

            ev_q2 = p.add("scalar", lambda s, hsl=hsl: s.activation(
                out=sq2[hsl, :], in_=qat[hsl, :], func=AF.Square),
                waits=(kn_ev,), inc=(s_se, 1))
            qn_dma = None
            for g in range(4):
                ev_mq = p.add("tensor", lambda t, g=g, hsl=hsl: t.matmul(
                    pc[0][0:1, 0:512], ones_c[hsl, 0:1],
                    sq2[hsl, g * 512:(g + 1) * 512], start=True, stop=True),
                    waits=(ev_q2, bank_free[0]), inc=(s_mm, 1))
                ev_qc = p.add("vector", lambda v: v.tensor_copy(
                    out=qrow[0:1, :], in_=pc[0][0:1, 0:512]),
                    waits=(ev_mq, qrow_rd), inc=(s_ve, 1))
                bank_free[0] = ev_qc
                p.dma("sync", out=st["hbm_q"].ap(), in_=qrow[0:1, :],
                      waits=(ev_qc, tr["qn_chain"]), inc=(s_qn, 16))
                ev_qw = (s_qn, s_qn.n)
                hqap = st["hbm_q"].ap()
                qn_dma = p.dma("sync", out=qn16[:, 4 * g:4 * g + 4],
                               in_=AP(hqap.tensor, 0, [[1, 128], [128, 4]]),
                               waits=(ev_qw,), inc=(s_qn, 16), slow=True)
                tr["qn_chain"] = qn_dma
                qrow_rd = ev_qw
            sq2_free = (s_mm, s_mm.n)
            ev_sq16 = p.add("scalar", lambda s: s.activation(
                out=ssml[:, 48:64], in_=qn16[:, 0:16], func=AF.Sqrt),
                waits=(qn_dma,), inc=(s_se, 1))
            tr["qn_chain"] = ev_sq16
            ev_M = p.add("vector", lambda v: v.tensor_scalar(
                out=ssml[:, 0:16], in0=ssml[:, 48:64],
                scalar1=smalls[:, SM["NC1"]:SM["NC1"] + 1],
                scalar2=(smalls[:, SM["NBR"]:SM["NBR"] + 1] if L0 else 0.0),
                op0=ALU.mult, op1=ALU.add),
                waits=(ev_sq16, ev_nc1, ev_consts), inc=(s_ve, 1))
            ev_prep = ev_M
            ev_sz = p.add("gpsimd", lambda g: g.memset(sums[:, :], 0.0),
                          waits=(sums_free,), inc=(s_ge, 1))

            exp_last = {}
            ptev = {}
            pd_free = None
            ev_pv = None
            ev_am = None
            cyc = 0
            for t in range(NT + 1):
                if t < NT:
                    chl = chunks_fn(t)
                    tb = t % 2
                    pslot = t % 2
                    if L0:
                        ev_rp = p.add("tensor", lambda tn, t=t, hsl=hsl: tn.matmul(
                            prp[:, 0:257], qat[hsl, t * 128:(t + 1) * 128],
                            wr2m[hsl, :], start=True, stop=True),
                            waits=(ev_prep, prp_busy, ev_wrl), inc=(s_mm, 1))
                        ev_rpe = p.add("vector", lambda v, tb=tb: v.tensor_add(
                            out=rpe[tb][:, 128:385], in0=prp[:, 0:257],
                            in1=brb[:, :]),
                            waits=(ev_rp, ev_brb, rpe_hw[tb]), inc=(s_ve, 1))
                        prp_busy = ev_rpe
                        p.add("gpsimd", lambda g, tb=tb: g.tensor_copy(
                            out=rpe[tb][:, 0:128],
                            in_=rpe[tb][:, 128:129].broadcast_to([128, 128])),
                            waits=(ev_rpe,), inc=(s_ge, 1))
                        ev_fill = p.add("gpsimd", lambda g, tb=tb: g.tensor_copy(
                            out=rpe[tb][:, 385:513],
                            in_=rpe[tb][:, 384:385].broadcast_to([128, 128])),
                            inc=(s_ge, 1))
                        p.dma("sync", out=st["hbm_h"][tb].ap(), in_=rpe[tb][:],
                              waits=(ev_fill, hb_rd[tb]), inc=(s_hw[tb], 16))
                        ev_hwr = (s_hw[tb], s_hw[tb].n)
                        rpe_hw[tb] = ev_hwr
                        i0 = 128 * t
                        lo = max(0, i0 - 128)
                        wB = min(S, i0 + 256) - lo
                        off = lo - i0 + 256
                        hap = st["hbm_h"][tb].ap()
                        shear = AP(hap.tensor, off, [[512, 128], [1, wB]])
                        p.dma("sync", out=bb[tb][:, 0:wB], in_=shear,
                              waits=(ev_hwr, bb_free[tb]), inc=(s_brd[tb], 16))
                        hb_rd[tb] = (s_brd[tb], s_brd[tb].n)
                        p.add("gpsimd", lambda g, tb=tb, t=t: g.tensor_scalar(
                            out=ssml[:, 16 + t:17 + t], in0=rpe[tb][:, 128:129],
                            scalar1=0.5, scalar2=ssml[:, t:t + 1],
                            op0=ALU.mult, op1=ALU.add),
                            waits=(ev_rpe, ev_prep), inc=(s_ge, 1))
                        ev_bfar = p.add("gpsimd", lambda g, tb=tb, t=t: g.tensor_scalar(
                            out=ssml[:, 32 + t:33 + t], in0=rpe[tb][:, 384:385],
                            scalar1=0.5, scalar2=ssml[:, t:t + 1],
                            op0=ALU.mult, op1=ALU.add),
                            inc=(s_ge, 1))
                    ev_ex = None
                    pfo = (t % 2) * S
                    for ci, (kind, j0, w) in enumerate(chl):
                        bank = cyc % 2
                        cyc += 1
                        b = pc[bank]
                        ev_cm = p.add("tensor", lambda tn, b=b, j0=j0, w=w, t=t, hsl=hsl: tn.matmul(
                            b[:, 0:w], qat[hsl, t * 128:(t + 1) * 128],
                            kat[hsl, j0:j0 + w], start=True, stop=True),
                            waits=(bank_free[bank], ev_kvar, ev_prep), inc=(s_mm, 1))
                        slot_ap = sums[:, t * 8 + ci: t * 8 + ci + 1]
                        if L0 and kind == "B":
                            ev_bd = p.add("vector", lambda v, b=b, w=w, tb=tb: v.scalar_tensor_tensor(
                                out=sband[tb][:, 0:w], in0=b[:, 0:w],
                                scalar=0.125, in1=bb[tb][:, 0:w],
                                op0=ALU.mult, op1=ALU.add),
                                waits=(ev_cm, hb_rd[tb], sband_free[0],
                                       sband_free[1]),
                                inc=(s_ve, 1))
                            bank_free[bank] = ev_bd
                            bb_free[tb] = ev_bd
                            ev_ex = p.add("scalar", lambda s, w=w, j0=j0, t=t, tb=tb, pfo=pfo, slot_ap=slot_ap: s.activation(
                                out=Pf[:, pfo + j0:pfo + j0 + w], in_=sband[tb][:, 0:w],
                                func=AF.Exp, bias=ssml[:, t:t + 1], scale=0.5,
                                accum_out=slot_ap),
                                waits=(ev_bd, ev_sz, pf_free[t % 2]), inc=(s_se, 1))
                            sband_free[tb] = ev_ex
                        elif (not L0) and kind == "D":
                            ev_e0 = p.add("scalar", lambda s, b=b, t=t: s.activation(
                                out=Pd[:, :], in_=b[:, 0:128], func=AF.Exp,
                                bias=ssml[:, t:t + 1], scale=0.0625),
                                waits=(ev_cm, ev_sz, pd_free, pf_free[t % 2]), inc=(s_se, 1))
                            bank_free[bank] = ev_e0
                            ev_ex = p.add("vector", lambda v, j0=j0, pfo=pfo, slot_ap=slot_ap: v.scalar_tensor_tensor(
                                out=Pf[:, pfo + j0:pfo + j0 + 128], in0=Pd[:, :],
                                scalar=1.0, in1=tri01[:, :],
                                op0=ALU.mult, op1=ALU.mult, accum_out=slot_ap),
                                waits=(ev_e0, ev_sz, ev_cn),
                                inc=(s_ve, 1))
                            pd_free = ev_ex
                        else:
                            bcoli = {"L": 16, "B": 0, "R": 32, "F": 0}[kind] + t
                            wts_x = (ev_cm, ev_sz, pf_free[t % 2])
                            if L0 and kind in ("L", "R"):
                                wts_x += (ev_bfar,)
                            ev_ex = p.add("scalar", lambda s, b=b, w=w, j0=j0, pfo=pfo, bcoli=bcoli, slot_ap=slot_ap: s.activation(
                                out=Pf[:, pfo + j0:pfo + j0 + w], in_=b[:, 0:w],
                                func=AF.Exp, bias=ssml[:, bcoli:bcoli + 1],
                                scale=0.0625, accum_out=slot_ap),
                                waits=wts_x, inc=(s_se, 1))
                            bank_free[bank] = ev_ex
                    exp_last[t] = ev_ex

                    o8 = (t % 2) * 16
                    ext = S if L0 else 128 * (t + 1)
                    if L0:
                        ev_a = p.add("vector", lambda v, t=t, o8=o8: v.reduce_sum(
                            out=i8[:, o8:o8 + 1], in_=sums[:, t * 8:(t + 1) * 8],
                            axis=AX.X), waits=(ev_ex, i8_free[t % 2]), inc=(s_ve, 1))
                    else:
                        ev_a0 = p.add("gpsimd", lambda g, t=t, o8=o8: g.tensor_copy(
                            out=i8[:, o8 + 8:o8 + 16],
                            in_=sums[:, t * 8:(t + 1) * 8]),
                            waits=(ev_ex, i8_free[t % 2]), inc=(s_ge, 1))
                        ev_a = p.add("vector", lambda v, o8=o8: v.reduce_sum(
                            out=i8[:, o8:o8 + 1], in_=i8[:, o8 + 8:o8 + 16],
                            axis=AX.X), waits=(ev_a0,), inc=(s_ve, 1))
                    ev_b = p.add("gpsimd", lambda g, o8=o8: g.tensor_scalar(
                        out=i8[:, o8 + 1:o8 + 2], in0=i8[:, o8:o8 + 1],
                        scalar1=1e-30, scalar2=None, op0=ALU.add),
                        waits=(ev_a,), inc=(s_ge, 1))
                    ev_c = p.add("vector", lambda v, o8=o8: v.reciprocal(
                        out=i8[:, o8 + 2:o8 + 3], in_=i8[:, o8 + 1:o8 + 2]),
                        waits=(ev_b,), inc=(s_ve, 1))
                    ev_c2 = p.add("gpsimd", lambda g, o8=o8: g.tensor_copy(
                        out=i8[:, o8 + 3:o8 + 4], in_=i8[:, o8 + 2:o8 + 3]),
                        waits=(ev_c,), inc=(s_ge, 1))
                    ev_e1 = p.add("vector", lambda v, o8=o8, ext=ext, pfo=pfo, pslot=pslot: v.tensor_scalar(
                        out=P[pslot][:, 0:ext], in0=Pf[:, pfo:pfo + ext],
                        scalar1=i8[:, o8 + 3:o8 + 4], scalar2=None,
                        op0=ALU.mult),
                        waits=(ev_c2, p_free[pslot]), inc=(s_ve, 1))
                    ev_e = p.add("vector", lambda v, o8=o8, ext=ext, pfo=pfo, pslot=pslot: v.scalar_tensor_tensor(
                        out=P[pslot][:, 0:ext], in0=Pf[:, pfo:pfo + ext],
                        scalar=i8[:, o8 + 3:o8 + 4],
                        in1=P[pslot][:, 0:ext],
                        op0=ALU.mult, op1=ALU.mult,
                        accum_out=i8[:, o8 + 4:o8 + 5]),
                        waits=(ev_e1,), inc=(s_ve, 1))
                    pf_free[t % 2] = ev_e
                    ev_f = p.add("gpsimd", lambda g, o8=o8: g.tensor_scalar(
                        out=i8[:, o8 + 5:o8 + 6], in0=i8[:, o8 + 4:o8 + 5],
                        scalar1=1e-30, scalar2=None, op0=ALU.add),
                        waits=(ev_e,), inc=(s_ge, 1))
                    ev_g = p.add("vector", lambda v, o8=o8: v.reciprocal(
                        out=i8[:, o8 + 6:o8 + 7], in_=i8[:, o8 + 5:o8 + 6]),
                        waits=(ev_f,), inc=(s_ve, 1))
                    ev_h2 = p.add("gpsimd", lambda g, o8=o8: g.tensor_copy(
                        out=i8[:, o8 + 7:o8 + 8], in_=i8[:, o8 + 6:o8 + 7]),
                        waits=(ev_g,), inc=(s_ge, 1))
                    ev_i = p.add("vector", lambda v, o8=o8, ext=ext, pslot=pslot: v.tensor_scalar(
                        out=P[pslot][:, 0:ext], in0=P[pslot][:, 0:ext],
                        scalar1=i8[:, o8 + 7:o8 + 8], scalar2=None,
                        op0=ALU.mult), waits=(ev_h2, ev_e), inc=(s_ve, 1))
                    i8_free[t % 2] = ev_i
                    exp_last[t] = ev_i

                tp = t - 1
                if tp >= 0:
                    nkc = NT if L0 else (tp + 1)
                    pslot2 = tp % 2
                    evA = evB = None
                    for a in range(nkc):
                        dst2 = ptA if a < 8 else ptB
                        dsl = slice((a % 8) * 128, (a % 8) * 128 + 128)
                        wts = []
                        if a == 0:
                            wts = [exp_last[tp], ptA_free]
                        if a == 8:
                            wts = [ptB_free]
                        ev_t2 = p.add("tensor", lambda tn, dst2=dst2, dsl=dsl, a=a, pslot2=pslot2: tn.transpose(
                            dst2[:, dsl], P[pslot2][:, a * 128:(a + 1) * 128],
                            ident[:]), waits=wts, inc=(s_mm, 1))
                        if a == min(nkc, 8) - 1:
                            evA = ev_t2
                        if a == nkc - 1:
                            evB = ev_t2
                    p_free[pslot2] = evB
                    region = tp % 4
                    nA = min(nkc, 8)
                    ev_eA = p.add("vector", lambda v, region=region, nA=nA: v.tensor_copy(
                        out=PTs[:, region * 2048: region * 2048 + nA * 128],
                        in_=ptA[:, 0:nA * 128]),
                        waits=(evA, pt_region_pv[region]), inc=(s_ve, 1))
                    ptA_free = ev_eA
                    ev_eB = None
                    if nkc > 8:
                        nB = nkc - 8
                        ev_eB = p.add("scalar", lambda s, region=region, nB=nB: s.activation(
                            out=PTs[:, region * 2048 + 1024: region * 2048 + 1024 + nB * 128],
                            in_=ptB[:, 0:nB * 128], func=AF.Copy),
                            waits=(evB, pt_region_pv[region]), inc=(s_se, 1))
                        ptB_free = ev_eB
                    ptev[tp] = (ev_eA, ev_eB)

                if tp >= 3 and tp % 4 == 3:
                    g = tp // 4
                    tpos = None if hb == 0 else (0, 64)
                    pvb = ppv[hh]
                    nk = NT if L0 else (4 * g + 4)
                    ptsap = PTs[:]
                    for a in range(nk):
                        if L0 or a <= 4 * g:
                            qoff, qn = 0, 512
                        else:
                            qoff = (a - 4 * g) * 128
                            qn = 512 - qoff
                        rhs = AP(ptsap.tensor,
                                 ptsap.offset + a * 128 + (qoff // 128) * 2048,
                                 [[8192, 128], [2048, qn // 128], [1, 128]])
                        wts = []
                        if a == 0:
                            wts = [x for tpi in range(4 * g, 4 * g + 4)
                                   for x in ptev[tpi] if x]
                            wts += [ev_vts, ppv_free[hh]]
                        ev_pv = p.add("tensor", lambda tn, pvb=pvb, a=a, h=h, rhs=rhs, qoff=qoff, qn=qn, hsl=hsl, tpos=tpos, nk=nk: tn.matmul(
                            pvb[hsl, qoff:qoff + qn],
                            vts[:, a * 1024 + 64 * h: a * 1024 + 64 * h + 64],
                            rhs, start=(a == 0), stop=(a == nk - 1),
                            tile_position=tpos),
                            waits=wts, inc=(s_mm, 1))
                    for r4 in range(4):
                        pt_region_pv[r4] = ev_pv
                    if L0:
                        ev_am = p.add("vector", lambda v, g=g, hsl=hsl, pr=pr, pvb=pvb: v.scalar_tensor_tensor(
                            out=xs[:, pr * S + g * 512: pr * S + (g + 1) * 512][hsl, :],
                            in0=pvb[hsl, 0:512], scalar=0.0,
                            in1=xs[:, pr * S + g * 512: pr * S + (g + 1) * 512][hsl, :],
                            op0=ALU.add, op1=ALU.add),
                            waits=(ev_sx, ev_pv), inc=(s_ve, 1))
                    else:
                        ev_am = p.add("vector", lambda v, g=g, hsl=hsl, pvb=pvb: v.tensor_copy(
                            out=attn1[hsl, g * 512:(g + 1) * 512],
                            in_=pvb[hsl, 0:512]),
                            waits=(ev_pv,), inc=(s_ve, 1))
                    ppv_free[hh] = ev_am
                    ev_last = ev_am
                    if st["dbg"] and pr == 0 and hh == 0 and (not L0) and g == 0 \
                            and st.get("stop_") == "a1res":
                        ev_all2 = [(s_mm, s_mm.n), (s_ve, s_ve.n),
                                   (s_se, s_se.n), (s_ge, s_ge.n)]
                        for nm2, t2 in [("e_PTs", PTs), ("e_P0", P[0]),
                                        ("e_P1", P[1]), ("e_sums", sums),
                                        ("e_at1", attn1), ("e_Pf", Pf),
                                        ("e_vts", vts), ("e_i8", i8)]:
                            ap2 = t2[:]
                            od2 = nc.dram_tensor(nm2, list(ap2.shape), ap2.dtype,
                                                 kind="ExternalOutput").ap()
                            dump_ev2 = p.dma("gpsimd", out=od2, in_=ap2,
                                             waits=tuple(ev_all2),
                                             inc=(sems["s_tr"], 16))
                        for k2 in range(2):
                            p_free[k2] = dump_ev2
                            i8_free[k2] = dump_ev2
                        pf_free = [dump_ev2, dump_ev2]
                        for r42 in range(4):
                            pt_region_pv[r42] = dump_ev2
                        ppv_free[hh] = dump_ev2
            if st["dbg"] and pr == 0 and hh == 0 and (L0 == (st.get("stop_", "a0res") == "a0res")):
                ev_all = [(s_mm, s_mm.n), (s_ve, s_ve.n), (s_se, s_se.n),
                          (s_ge, s_ge.n)]
                s_dbg = sems["s_tr"]
                dump_list = [("d_qat", qat), ("d_kat", kat),
                             ("d_P0", P[0]), ("d_P1", P[1]),
                             ("d_sums", sums), ("d_ssml", ssml),
                             ("d_qn16", qn16), ("d_scal", scal),
                             ("d_smalls", smalls), ("d_PTs", PTs),
                             ("d_vts", vts), ("d_Pf", Pf),
                             ("d_i8", i8), ("d_xs0", xs[:, 0:S])]
                if L0:
                    dump_list += [("d_rpe0", rpe[0]), ("d_rpe1", rpe[1]),
                                  ("d_bb0", bb[0]), ("d_bb1", bb[1]),
                                  ("d_brb", brb)]
                else:
                    dump_list += [("d_Pd", Pd), ("d_at1", attn1)]
                dump_ev = None
                for nm, tsr in dump_list:
                    ap = tsr if isinstance(tsr, bass.AP) else tsr[:]
                    od = nc.dram_tensor(nm, list(ap.shape), ap.dtype,
                                        kind="ExternalOutput").ap()
                    st["dbgout"][nm] = True
                    dump_ev = p.dma("gpsimd", out=od, in_=ap,
                                    waits=tuple(ev_all), inc=(s_dbg, 16))
                for k in range(2):
                    p_free[k] = dump_ev
                    rpe_hw[k] = dump_ev
                    bb_free[k] = dump_ev
                    sband_free[k] = dump_ev
                sums_free = dump_ev
                sq2_free = dump_ev
            sums_free = (s_ve, s_ve.n)
            qk_free = (s_mm, s_mm.n)

        if not L0:
            p.dma("sync", out=st["hbm_at"][pr], in_=attn1[:, :],
                  waits=(ev_last, tr["ah_wr"][pr]), inc=(s_ah, 16))
            tr["ah_wr"][pr] = (s_ah, s_ah.n)
            ppv_free = [tr["ah_wr"][pr], tr["ah_wr"][pr]]

    if L0:
        res = [(s_ve, s_ve.n)]
        sub.close()
        ph.close()
        return res

    ev_attn_done = [(s_mm, s_mm.n), (s_ve, s_ve.n), (s_se, s_se.n)]
    sub.close()
    ob = ExitStack()
    oe = ob.enter_context
    wob = oe(nc.sbuf_tensor("wob", [128, 8 * 8 * 128], BF16))
    ats = oe(nc.sbuf_tensor("ats", [128, NCH * S], BF16))
    s_wo = sems["s_wo"]
    p.dma("gpsimd",
          out=wob[:].rearrange("p (a c m) -> p a c m", a=8, m=128),
          in_=dt["l1_wo"].transpose([2, 0, 1, 3]),
          waits=tuple(ev_attn_done), inc=(s_wo, 16))
    for kc in range(8):
        p.dma("sync", out=ats[:, kc * S:(kc + 1) * S], in_=st["hbm_at"][kc],
              waits=[(s_ah, s_ah.n)] + list(ev_attn_done), inc=(s_wo, 16))
    ev_wo = (s_wo, s_wo.n)
    bank_free = [None, None]
    ev_o = None
    for oc in range(8):
        for tg in range(4):
            bank = (oc * 4 + tg) % 2
            b = pc[bank]
            ev_m3 = None
            for kc in range(8):
                wts = [bank_free[bank], ev_wo] if kc == 0 else []
                ev_m3 = p.add("tensor", lambda t, b=b, oc=oc, tg=tg, kc=kc: t.matmul(
                    b[:, 0:512],
                    wob[:, (oc * 8 + kc) * 128:(oc * 8 + kc) * 128 + 128],
                    ats[:, kc * S + tg * 512: kc * S + (tg + 1) * 512],
                    start=(kc == 0), stop=(kc == 7)), waits=wts, inc=(s_mm, 1))
            ev_o = p.add("vector", lambda v, b=b, oc=oc, tg=tg: v.scalar_tensor_tensor(
                out=xs[:, oc * S + tg * 512: oc * S + (tg + 1) * 512],
                in0=b[:, 0:512],
                scalar=smalls[:, SM["L1BO"] + oc:SM["L1BO"] + oc + 1],
                in1=xs[:, oc * S + tg * 512: oc * S + (tg + 1) * 512],
                op0=ALU.add, op1=ALU.add),
                waits=(ev_m3, ev_sx), inc=(s_ve, 1))
            bank_free[bank] = ev_o
    res = [(s_ve, s_ve.n)]
    ob.close()
    ph.close()
    return res


def _layernorm(st, ev_in):
    nc, p = st["nc"], st["p"]
    li = _LN_N[0]
    _LN_N[0] += 1
    xs, xb = st["xs"], st["xb"]
    ones_r, ones_c = st["ones_r"], st["ones_c"]
    sems = st["sems"]
    s_mm, s_ve, s_se = sems["s_mm"], sems["s_ve"], sems["s_se"]
    s_ge = sems["s_ge"]
    ev_cn = st["marks"]["ev_cn"]

    ph = ExitStack()
    ee = ph.enter_context
    lnsq = ee(nc.sbuf_tensor(f"lnsq{li}", [128, 8 * 512], BF16))
    lrow = ee(nc.sbuf_tensor(f"lrow{li}", [1, 6 * 512], F32))
    tmp = ee(nc.sbuf_tensor(f"lntmp{li}", [128, 512], F32))
    pS = [ee(nc.psum_tensor(f"lnS{li}_{i}", [1, 512], F32)) for i in range(2)]
    pQ = [ee(nc.psum_tensor(f"lnQ{li}_{i}", [1, 512], F32)) for i in range(2)]
    pA = [ee(nc.psum_tensor(f"lnA{li}_{i}", [128, 512], F32)) for i in range(2)]
    pB = [ee(nc.psum_tensor(f"lnB{li}_{i}", [128, 512], F32)) for i in range(2)]
    M0, T0, T1, T2, CA, CB = 0, 512, 1024, 1536, 2048, 2560

    ev_cast = []
    for c in range(NCH):
        sl = slice(c * S, (c + 1) * S)
        ev_cast.append(p.add("vector",
                             lambda v, sl=sl: v.tensor_copy(out=xb[:, sl], in_=xs[:, sl]),
                             waits=tuple(ev_in), inc=(s_ve, 1)))
    stat_free = [None, None]
    ab_free = [None, None]
    lrow_rd = None
    evs = [None, None]
    for tg in range(4):
        bk = tg % 2
        ev_sq = []
        for c in range(NCH):
            ev_sq.append(p.add("scalar", lambda s, c=c, tg=tg: s.activation(
                out=lnsq[:, c * 512:(c + 1) * 512],
                in_=xs[:, c * S + tg * 512: c * S + (tg + 1) * 512],
                func=AF.Square), waits=tuple(ev_in), inc=(s_se, 1)))
        ev_ms = ev_mq = None
        for c in range(NCH):
            ev_ms = p.add("tensor", lambda t, c=c, tg=tg, bk=bk: t.matmul(
                pS[bk][0:1, :], ones_c[:, 0:1],
                xb[:, c * S + tg * 512: c * S + (tg + 1) * 512],
                start=(c == 0), stop=(c == 7)),
                waits=[ev_cast[c]] + ([stat_free[bk], ev_cn] if c == 0 else []),
                inc=(s_mm, 1))
        for c in range(NCH):
            ev_mq = p.add("tensor", lambda t, c=c, bk=bk: t.matmul(
                pQ[bk][0:1, :], ones_c[:, 0:1], lnsq[:, c * 512:(c + 1) * 512],
                start=(c == 0), stop=(c == 7)),
                waits=[ev_sq[c]] + ([stat_free[bk]] if c == 0 else []),
                inc=(s_mm, 1))
        p.add("vector", lambda v, bk=bk: v.tensor_scalar(
            out=lrow[0:1, M0:M0 + 512], in0=pS[bk][0:1, :], scalar1=1.0 / D,
            scalar2=None, op0=ALU.mult), waits=(ev_ms, lrow_rd), inc=(s_ve, 1))
        p.add("vector", lambda v, bk=bk: v.tensor_scalar(
            out=lrow[0:1, T0:T0 + 512], in0=pQ[bk][0:1, :], scalar1=1.0 / D,
            scalar2=EPS, op0=ALU.mult, op1=ALU.add), waits=(ev_mq,), inc=(s_ve, 1))
        ev3 = p.add("vector", lambda v: v.tensor_mul(
            out=lrow[0:1, T1:T1 + 512], in0=lrow[0:1, M0:M0 + 512],
            in1=lrow[0:1, M0:M0 + 512]), inc=(s_ve, 1))
        ev3b = p.add("gpsimd", lambda g: g.tensor_copy(
            out=lrow[0:1, T2:T2 + 512], in_=lrow[0:1, T1:T1 + 512]),
            waits=(ev3,), inc=(s_ge, 1))
        ev4 = p.add("vector", lambda v: v.scalar_tensor_tensor(
            out=lrow[0:1, T1:T1 + 512], in0=lrow[0:1, T2:T2 + 512], scalar=-1.0,
            in1=lrow[0:1, T0:T0 + 512], op0=ALU.mult, op1=ALU.add),
            waits=(ev3b,), inc=(s_ve, 1))
        ev5 = p.add("scalar", lambda s: s.activation(
            out=lrow[0:1, T0:T0 + 512], in_=lrow[0:1, T1:T1 + 512], func=AF.Sqrt),
            waits=(ev4,), inc=(s_se, 1))
        ev6 = p.add("vector", lambda v: v.reciprocal(
            out=lrow[0:1, CA:CA + 512], in_=lrow[0:1, T0:T0 + 512]),
            waits=(ev5,), inc=(s_ve, 1))
        ev6b = p.add("gpsimd", lambda g: g.tensor_copy(
            out=lrow[0:1, T2:T2 + 512], in_=lrow[0:1, CA:CA + 512]),
            waits=(ev6,), inc=(s_ge, 1))
        ev7 = p.add("vector", lambda v: v.scalar_tensor_tensor(
            out=lrow[0:1, CB:CB + 512], in0=lrow[0:1, M0:M0 + 512], scalar=-1.0,
            in1=lrow[0:1, T2:T2 + 512], op0=ALU.mult, op1=ALU.mult),
            waits=(ev6b,), inc=(s_ve, 1))
        stat_free[bk] = ev7
        ev_mA = p.add("tensor", lambda t, bk=bk: t.matmul(
            pA[bk][:, :], ones_r[0:1, :], lrow[0:1, CA:CA + 512],
            start=True, stop=True), waits=(ev6, ev7, ab_free[bk]), inc=(s_mm, 1))
        ev_mB = p.add("tensor", lambda t, bk=bk: t.matmul(
            pB[bk][:, :], ones_r[0:1, :], lrow[0:1, CB:CB + 512],
            start=True, stop=True), inc=(s_mm, 1))
        lrow_rd = ev_mB
        ev_ap = ev_cb = None
        for c in range(NCH):
            xsl = slice(c * S + tg * 512, c * S + (tg + 1) * 512)
            p.add("vector", lambda v, xsl=xsl, bk=bk: v.tensor_mul(
                out=tmp[:, :], in0=xs[:, xsl], in1=pA[bk][:, :]),
                waits=(ev_mA,), inc=(s_ve, 1))
            ev_ap = p.add("vector", lambda v, xsl=xsl, bk=bk: v.tensor_add(
                out=xs[:, xsl], in0=tmp[:, :], in1=pB[bk][:, :]),
                waits=(ev_mB,), inc=(s_ve, 1))
            ev_cb = p.add("scalar", lambda s, xsl=xsl: s.activation(
                out=xb[:, xsl], in_=xs[:, xsl], func=AF.Copy),
                waits=(ev_ap,), inc=(s_se, 1))
        ab_free[bk] = ev_ap
        evs = [ev_ap, ev_cb]
    ph.close()
    return evs


def _ffn(st, ev_in, layer):
    nc, p = st["nc"], st["p"]
    dt, SM = st["dt"], st["SM"]
    xs, xb = st["xs"], st["xb"]
    smalls, scal, ones_r = st["smalls"], st["scal"], st["ones_r"]
    sems = st["sems"]
    s_mm, s_ve, s_se, s_ge = sems["s_mm"], sems["s_ve"], sems["s_se"], sems["s_ge"]
    sfs = [sems["s_fA"], sems["s_fB"]]
    sgs = [sems["s_gA"], sems["s_gB"]]
    s_qs, s_qc, s_tr = sems["s_qs"], sems["s_qc"], sems["s_tr"]
    L0 = layer == 0
    QAT = not L0

    ph = ExitStack()
    ee = ph.enter_context

    if QAT:
        with ExitStack() as am:
            ae = am.enter_context
            qsl = [ae(nc.sbuf_tensor(f"qsl{i}", [128, 2048], F32)) for i in range(2)]
            amax = ae(nc.sbuf_tensor("amax", [128, 32], F32))
            arow = ae(nc.sbuf_tensor("arow", [1, 128], F32))
            pk67 = [ae(nc.psum_tensor(f"pam{i}", [128, 512], F32)) for i in range(2)]
            slot_rd = [None, None]
            for wi, wname in enumerate(["l1_wfc", "l1_wproj"]):
                wap = dt[wname]
                flat = AP(wap.tensor, 0, [[2048, 128], [1, 2048]])
                red_evs = []
                for sl in range(16):
                    ss = sl % 2
                    src = AP(wap.tensor, sl * 128 * 2048, [[2048, 128], [1, 2048]])
                    ev_l = p.dma("sync", out=qsl[ss][:], in_=src,
                                 waits=(slot_rd[ss],) + (tuple(ev_in) if sl < 2 and wi == 0 else ()),
                                 inc=(s_qs, 16))
                    ev_r = p.add("vector", lambda v, ss=ss, sl=sl: v.reduce_max(
                        out=amax[:, sl:sl + 1], in_=qsl[ss][:], axis=AX.X,
                        apply_absolute_value=True), waits=(ev_l,), inc=(s_ve, 1))
                    slot_rd[ss] = ev_r
                    red_evs.append(ev_r)
                ev_cb0 = p.add("gpsimd", lambda g: g.tensor_copy(
                    out=amax[:, 16:32], in_=amax[:, 0:16]),
                    waits=(red_evs[-1],), inc=(s_ge, 1))
                ev_c = p.add("vector", lambda v: v.reduce_max(
                    out=amax[:, 0:1], in_=amax[:, 16:32], axis=AX.X),
                    waits=(ev_cb0,), inc=(s_ve, 1))
                ev_t = p.dma("sync", out=arow[0:1, :], in_=amax[:, 0:1],
                             waits=(ev_c, st["track"]["tr_chain"]),
                             inc=(s_tr, 16), slow=True)
                st["track"]["tr_chain"] = ev_t
                ev_m = p.add("vector", lambda v, wi=wi: v.reduce_max(
                    out=scal[0:1, 20 + 4 * wi:21 + 4 * wi], in_=arow[0:1, :],
                    axis=AX.X), waits=(ev_t,), inc=(s_ve, 1))
                ev_s = p.add("gpsimd", lambda g, wi=wi: g.tensor_scalar(
                    out=scal[0:1, 21 + 4 * wi:22 + 4 * wi],
                    in0=scal[0:1, 20 + 4 * wi:21 + 4 * wi],
                    scalar1=1.0 / 127.0, scalar2=None, op0=ALU.mult),
                    waits=(ev_m,), inc=(s_ge, 1))
                ev_rs = p.add("vector", lambda v, wi=wi: v.reciprocal(
                    out=scal[0:1, 22 + 4 * wi:23 + 4 * wi],
                    in_=scal[0:1, 21 + 4 * wi:22 + 4 * wi]),
                    waits=(ev_s,), inc=(s_ve, 1))
                cS = SM["SFC"] if wi == 0 else SM["SPJ"]
                cR = SM["RSFC"] if wi == 0 else SM["RSPJ"]
                ev_b1 = p.add("tensor", lambda t, wi=wi: t.matmul(
                    pk67[0][:, 0:1], ones_r[0:1, :],
                    scal[0:1, 21 + 4 * wi:22 + 4 * wi], start=True, stop=True),
                    waits=(ev_s, ev_rs), inc=(s_mm, 1))
                ev_b2 = p.add("tensor", lambda t, wi=wi: t.matmul(
                    pk67[1][:, 0:1], ones_r[0:1, :],
                    scal[0:1, 22 + 4 * wi:23 + 4 * wi], start=True, stop=True),
                    inc=(s_mm, 1))
                p.add("vector", lambda v, cS=cS: v.tensor_copy(
                    out=smalls[:, cS:cS + 1], in_=pk67[0][:, 0:1]),
                    waits=(ev_b1,), inc=(s_ve, 1))
                ev_cp = p.add("vector", lambda v, cR=cR: v.tensor_copy(
                    out=smalls[:, cR:cR + 1], in_=pk67[1][:, 0:1]),
                    waits=(ev_b2,), inc=(s_ve, 1))
            cal = ae(nc.sbuf_tensor("cal", [1, 8], F32))
            cali = ae(nc.sbuf_tensor("cali", [1, 2], I32))
            ev_z0 = p.add("gpsimd", lambda g: g.memset(cal[0:1, 0:1], 1.5),
                          inc=(s_ge, 1))
            ev_z1 = p.add("vector", lambda v: v.tensor_copy(
                out=cali[0:1, 0:1], in_=cal[0:1, 0:1]), waits=(ev_z0,),
                inc=(s_ve, 1))
            ev_z2 = p.add("gpsimd", lambda g: g.tensor_copy(
                out=cali[0:1, 1:2], in_=cali[0:1, 0:1]), waits=(ev_z1,),
                inc=(s_ge, 1))
            ev_z3 = p.add("vector", lambda v: v.tensor_scalar(
                out=cal[0:1, 2:3], in0=cali[0:1, 1:2], scalar1=-0.5,
                scalar2=257.0, op0=ALU.mult, op1=ALU.add), waits=(ev_z2,),
                inc=(s_ve, 1))
            ev_z4 = p.add("tensor", lambda t: t.matmul(
                pk67[0][:, 4:5], ones_r[0:1, :], cal[0:1, 2:3],
                start=True, stop=True), waits=(ev_z3,), inc=(s_mm, 1))
            ev_hb = p.add("vector", lambda v: v.tensor_copy(
                out=smalls[:, SM["HB"]:SM["HB"] + 1], in_=pk67[0][:, 4:5]),
                waits=(ev_z4,), inc=(s_ve, 1))
            ev_scales = ev_hb
    else:
        ev_scales = None

    hbuf = ee(nc.sbuf_tensor(f"hbuf{layer}", [128, 32 * 1024], BF16))
    if QAT:
        w1s = [ee(nc.sbuf_tensor(f"w1f{i}", [128, 1024], F32)) for i in range(2)]
        wqs1 = [ee(nc.sbuf_tensor(f"wq1{i}", [128, 1024], BF16)) for i in range(2)]
        qtmp = ee(nc.sbuf_tensor("qtmp", [128, 1024], F32))
        qint = ee(nc.sbuf_tensor("qint", [128, 1024], I32))
        blo_all = ee(nc.sbuf_tensor("blo_all", [16, 32 * 128], BF16))
        afc = ee(nc.sbuf_tensor("afc", [128, 8 * 16], BF16))
        apj = ee(nc.sbuf_tensor("apj", [128, 32 * 16], BF16))
        bpl = ee(nc.sbuf_tensor("bpl", [16, 8 * 128], BF16))
        lrx = ee(nc.sbuf_tensor("lrx", [16, 1024], BF16))
        lr2x = ee(nc.sbuf_tensor("lr2x", [16, 1024], BF16))
        w2q = [ee(nc.sbuf_tensor(f"w2q{i}", [128, 512], F32)) for i in range(2)]
    else:
        w1s = [ee(nc.sbuf_tensor(f"w1b{i}", [128, 1024], BF16)) for i in range(2)]
    w2s = [ee(nc.sbuf_tensor(f"w2s{layer}_{i}", [128, 512], BF16)) for i in range(2)]
    pk = [ee(nc.psum_tensor(f"pk{layer}{i}", [128, 512], F32)) for i in range(8)]

    bank_free = [None] * 8
    w1_rd = [None, None]
    w2_rd = [None, None]
    w1_cw = [None, None]
    w2_cw = [None, None]
    hb_free = None
    lrx_rd = None
    lr2x_rd = None
    ev_out = list(ev_in)

    b1c = SM["L0B1"] if L0 else SM["L1BFC"]
    b2c = SM["L0B2"] if L0 else SM["L1BPJ"]

    if QAT:
        s_lo = sems["s_rro"]
        p.dma("gpsimd", out=afc[:].rearrange("p (c n) -> p c n", n=16),
              in_=dt["l1_afc"].transpose([1, 0, 2]),
              waits=tuple(ev_in), inc=(s_lo, 16))
        p.dma("gpsimd", out=apj[:].rearrange("p (c n) -> p c n", n=16),
              in_=dt["l1_aproj"].transpose([1, 0, 2]), inc=(s_lo, 16))
        p.dma("gpsimd", out=bpl[:].rearrange("p (c n) -> p c n", n=128),
              in_=dt["l1_bproj_lo"].transpose([1, 0, 2]), inc=(s_lo, 16))
        p.dma("gpsimd", out=blo_all[:].rearrange("p (c n) -> p c n", n=128),
              in_=dt["l1_bfc_lo"].transpose([1, 0, 2]), inc=(s_lo, 16))
        ev_lores = (s_lo, sems["s_rro"].n)

    for th in range(2):
        tok0 = th * 1024
        if QAT:
            for ng in range(2):
                ev_ml = None
                for kc in range(8):
                    wts = []
                    if kc == 0:
                        wts = [ev_lores, bank_free[2], lrx_rd] + list(ev_in)
                    ev_ml = p.add("tensor", lambda t, kc=kc, ng=ng, tok0=tok0: t.matmul(
                        pk[2][0:16, 0:512], afc[:, kc * 16:(kc + 1) * 16],
                        xb[:, kc * S + tok0 + ng * 512: kc * S + tok0 + (ng + 1) * 512],
                        start=(kc == 0), stop=(kc == 7)), waits=wts, inc=(s_mm, 1))
                ev_le = p.add("scalar", lambda sc, ng=ng: sc.activation(
                    out=lrx[0:16, ng * 512:(ng + 1) * 512], in_=pk[2][0:16, 0:512],
                    func=AF.Identity, scale=2.0, bias=0.0),
                    waits=(ev_ml,), inc=(s_se, 1))
                bank_free[2] = ev_le
            ev_lrx = ev_le

        for ic in range(32):
            sl = ic % 2
            if QAT and th == 0:
                ev_w = p.dma("gpsimd", out=w1s[sl][:].rearrange("p (c m) -> p c m", m=128),
                             in_=dt["l1_wfc"][ic].transpose([1, 0, 2]),
                             waits=(w1_rd[sl], ev_scales), inc=(sfs[sl], 16))
                ev_q1 = p.add("vector", lambda v, sl=sl: v.tensor_scalar(
                    out=qtmp[:], in0=w1s[sl][:],
                    scalar1=smalls[:, SM["RSFC"]:SM["RSFC"] + 1],
                    scalar2=smalls[:, SM["HB"]:SM["HB"] + 1],
                    op0=ALU.mult, op1=ALU.add), waits=(ev_w,), inc=(s_ve, 1))
                w1_rd[sl] = ev_q1
                p.add("vector", lambda v: v.tensor_copy(
                    out=qint[:], in_=qtmp[:]), inc=(s_ve, 1))
                ev_q4 = p.add("vector", lambda v, sl=sl: v.tensor_scalar(
                    out=wqs1[sl][:], in0=qint[:], scalar1=-256.0,
                    scalar2=smalls[:, SM["SFC"]:SM["SFC"] + 1],
                    op0=ALU.add, op1=ALU.mult),
                    waits=(w1_cw[sl],), inc=(s_ve, 1))
                ev_cw = p.dma("sync", out=st["hbm_wfq"].ap()[ic], in_=wqs1[sl][:],
                              waits=(ev_q4,), inc=(s_qc, 16))
                w1_cw[sl] = (sems["s_qc"], sems["s_qc"].n)
                wsrc = wqs1[sl]
                ev_wrdy = ev_q4
            elif QAT:
                ev_w = p.dma("sync", out=wqs1[sl][:], in_=st["hbm_wfq"].ap()[ic],
                             waits=(w1_rd[sl], (s_qc, sems["s_qc"].n)),
                             inc=(sfs[sl], 16))
                ev_wrdy = (sfs[sl], sfs[sl].n)
                wsrc = wqs1[sl]
            else:
                p.dma("gpsimd", out=w1s[sl][:].rearrange("p (c m) -> p c m", m=128),
                      in_=dt["l0_w1"][ic].transpose([1, 0, 2]),
                      waits=(w1_rd[sl],) + (tuple(ev_in) if ic < 2 and th == 0 else ()),
                      inc=(sfs[sl], 16))
                ev_wrdy = (sfs[sl], sfs[sl].n)
                wsrc = w1s[sl]
            for ng in range(2):
                bank = ng
                ev_m1 = None
                first = True
                if QAT:
                    ev_m1 = p.add("tensor", lambda t, ic=ic, ng=ng: t.matmul(
                        pk[ng][:, 0:512],
                        blo_all[0:16, ic * 128:(ic + 1) * 128],
                        lrx[0:16, ng * 512:(ng + 1) * 512],
                        start=True, stop=False),
                        waits=(ev_wrdy, ev_lrx, bank_free[ng], hb_free, ev_lores),
                        inc=(s_mm, 1))
                    first = False
                for kc in range(8):
                    wts = []
                    if first and kc == 0:
                        wts = [ev_wrdy, bank_free[ng], hb_free]
                    ev_m1 = p.add("tensor", lambda t, wsrc=wsrc, kc=kc, ng=ng, tok0=tok0: t.matmul(
                        pk[ng][:, 0:512], wsrc[:, kc * 128:(kc + 1) * 128],
                        xb[:, kc * S + tok0 + ng * 512: kc * S + tok0 + (ng + 1) * 512],
                        start=(kc == 0 and not QAT), stop=(kc == 7)),
                        waits=wts, inc=(s_mm, 1))
                w1_rd[sl] = ev_m1
                ev_h = p.add("scalar", lambda sc, ic=ic, ng=ng, b1c=b1c: sc.activation(
                    out=hbuf[:, ic * 1024 + ng * 512: ic * 1024 + (ng + 1) * 512],
                    in_=pk[ng][:, 0:512], func=(AF.Relu if L0 else AF.Gelu),
                    bias=smalls[:, b1c + ic:b1c + ic + 1], scale=1.0),
                    waits=(ev_m1,), inc=(s_se, 1))
                bank_free[ng] = ev_h
        ev_h_done = ev_h

        if QAT:
            for sg in range(2):
                ev_ml2 = None
                for kc in range(32):
                    wts = []
                    if kc == 0:
                        wts = [ev_h_done, bank_free[3], lr2x_rd, ev_lores]
                    ev_ml2 = p.add("tensor", lambda t, kc=kc, sg=sg: t.matmul(
                        pk[3][0:16, 0:512], apj[:, kc * 16:(kc + 1) * 16],
                        hbuf[:, kc * 1024 + sg * 512: kc * 1024 + (sg + 1) * 512],
                        start=(kc == 0), stop=(kc == 31)), waits=wts, inc=(s_mm, 1))
                ev_l2 = p.add("scalar", lambda sc, sg=sg: sc.activation(
                    out=lr2x[0:16, sg * 512:(sg + 1) * 512], in_=pk[3][0:16, 0:512],
                    func=AF.Identity, scale=2.0, bias=0.0),
                    waits=(ev_ml2,), inc=(s_se, 1))
                bank_free[3] = ev_l2
            ev_lr2 = ev_l2

        for half in range(2):
            if QAT:
                for oc in range(4):
                    for sg in range(2):
                        bank = oc * 2 + sg
                        p.add("tensor", lambda t, half=half, oc=oc, sg=sg, bank=bank: t.matmul(
                            pk[bank][:, 0:512],
                            bpl[0:16, (half * 4 + oc) * 128:(half * 4 + oc) * 128 + 128],
                            lr2x[0:16, sg * 512:(sg + 1) * 512],
                            start=True, stop=False),
                            waits=(ev_lr2, ev_lores, bank_free[bank]),
                            inc=(s_mm, 1))
            for kc in range(32):
                sl = kc % 2
                if QAT and th == 0:
                    ev_w2 = p.dma("sync", out=w2q[sl][:].rearrange("p (a m) -> p a m", m=128),
                                  in_=dt["l1_wproj"][half][kc].rearrange("a p m -> p a m"),
                                  waits=(w2_rd[sl], ev_scales), inc=(sgs[sl], 16))
                    ev_q1 = p.add("vector", lambda v, sl=sl: v.tensor_scalar(
                        out=qtmp[:, 0:512], in0=w2q[sl][:],
                        scalar1=smalls[:, SM["RSPJ"]:SM["RSPJ"] + 1],
                        scalar2=smalls[:, SM["HB"]:SM["HB"] + 1],
                        op0=ALU.mult, op1=ALU.add), waits=(ev_w2,), inc=(s_ve, 1))
                    w2_rd[sl] = ev_q1
                    p.add("vector", lambda v: v.tensor_copy(
                        out=qint[:, 0:512], in_=qtmp[:, 0:512]), inc=(s_ve, 1))
                    ev_q4 = p.add("vector", lambda v, sl=sl: v.tensor_scalar(
                        out=w2s[sl][:], in0=qint[:, 0:512], scalar1=-256.0,
                        scalar2=smalls[:, SM["SPJ"]:SM["SPJ"] + 1],
                        op0=ALU.add, op1=ALU.mult),
                        waits=(w2_cw[sl],), inc=(s_ve, 1))
                    p.dma("sync", out=st["hbm_wpq"].ap()[half][kc], in_=w2s[sl][:],
                          waits=(ev_q4,), inc=(s_qc, 16))
                    w2_cw[sl] = (sems["s_qc"], sems["s_qc"].n)
                    ev_w2r = ev_q4
                elif QAT:
                    ev_w2 = p.dma("sync", out=w2s[sl][:],
                                  in_=st["hbm_wpq"].ap()[half][kc],
                                  waits=(w2_rd[sl], (s_qc, sems["s_qc"].n)),
                                  inc=(sgs[sl], 16))
                    ev_w2r = (sgs[sl], sgs[sl].n)
                else:
                    p.dma("gpsimd", out=w2s[sl][:].rearrange("p (a m) -> p a m", m=128),
                          in_=dt["l0_w2"][half][kc].transpose([1, 0, 2]),
                          waits=(w2_rd[sl],), inc=(sgs[sl], 16))
                    ev_w2r = (sgs[sl], sgs[sl].n)
                ev_m2 = None
                for oc in range(4):
                    for sg in range(2):
                        bank = oc * 2 + sg
                        wts = []
                        if oc == 0 and sg == 0:
                            wts = [ev_w2r, ev_h_done]
                            if kc == 0 and not QAT:
                                wts += [bank_free[b] for b in range(8)]
                        ev_m2 = p.add("tensor", lambda t, oc=oc, sg=sg, kc=kc, sl=sl, bank=bank: t.matmul(
                            pk[bank][:, 0:512], w2s[sl][:, oc * 128:(oc + 1) * 128],
                            hbuf[:, kc * 1024 + sg * 512: kc * 1024 + (sg + 1) * 512],
                            start=(kc == 0 and not QAT), stop=(kc == 31)),
                            waits=wts, inc=(s_mm, 1))
                w2_rd[sl] = ev_m2
            for oc in range(4):
                for sg in range(2):
                    bank = oc * 2 + sg
                    ocabs = half * 4 + oc
                    xsl = slice(ocabs * S + tok0 + sg * 512,
                                ocabs * S + tok0 + (sg + 1) * 512)
                    ev_e2 = p.add("vector", lambda v, bank=bank, ocabs=ocabs, xsl=xsl, b2c=b2c: v.scalar_tensor_tensor(
                        out=xs[:, xsl], in0=pk[bank][:, 0:512],
                        scalar=smalls[:, b2c + ocabs:b2c + ocabs + 1],
                        in1=xs[:, xsl], op0=ALU.add, op1=ALU.add),
                        waits=(ev_m2, st["marks"]["ev_sx"]), inc=(s_ve, 1))
                    bank_free[bank] = ev_e2
                    ev_out = [ev_e2]
        hb_free = (s_mm, s_mm.n)
        if QAT:
            lrx_rd = (s_mm, s_mm.n)
            lr2x_rd = (s_mm, s_mm.n)
        if st["dbg"] and QAT and th == 1 and st.get("stop_") == "f1res":
            ev_all3 = [(s_mm, s_mm.n), (s_ve, s_ve.n), (s_se, s_se.n),
                       (s_ge, s_ge.n)]
            dump_ev3 = None
            for nm3, t3 in [("f_lrx", lrx), ("f_lr2x", lr2x), ("f_hbuf", hbuf),
                            ("f_afc", afc), ("f_apj", apj), ("f_bpl", bpl),
                            ("f_wq1a", wqs1[0]), ("f_wq1b", wqs1[1]),
                            ("f_smalls", smalls), ("f_scal", scal),
                            ("f_xs", xs), ("f_xb", xb)]:
                ap3 = t3[:]
                od3 = nc.dram_tensor(nm3, list(ap3.shape), ap3.dtype,
                                     kind="ExternalOutput").ap()
                dump_ev3 = p.dma("gpsimd", out=od3, in_=ap3, waits=tuple(ev_all3),
                                 inc=(sems["s_tr"], 16))
            hb_free = dump_ev3
            lrx_rd = dump_ev3
            lr2x_rd = dump_ev3
            for bb2 in range(8):
                bank_free[bb2] = dump_ev3
            w1_rd = [dump_ev3, dump_ev3]
            w2_rd = [dump_ev3, dump_ev3]
    ph.close()
    return ev_out


def _layout_weights(inputs):
    f = np.ascontiguousarray
    w = {}
    Wqkv = inputs["L0_attn_Wqkv"]
    qr = np.concatenate([np.arange(192 * h, 192 * h + 64) for h in range(H)])
    kr = qr + 64
    vr = qr + 128
    WqT = Wqkv[qr].T
    WkT = Wqkv[kr].T
    WvT = Wqkv[vr].T
    l0_wqk = np.empty((8, 2, 8, 128, 128), np.float32)
    for pr in range(8):
        cols = slice(128 * pr, 128 * pr + 128)
        l0_wqk[pr, 0] = WqT[:, cols].reshape(8, 128, 128)
        l0_wqk[pr, 1] = WkT[:, cols].reshape(8, 128, 128)
    w["l0_wqk"] = f(l0_wqk)
    w["l0_wv"] = f(WvT.reshape(8, 128, D))
    bqkv = inputs["L0_attn_bqkv"]
    w["l0_bq"] = f(bqkv[qr].reshape(8, 128).T)
    w["l0_bk"] = f(bqkv[kr].reshape(8, 128).T)
    w["l0_bv"] = f(bqkv[vr].reshape(1, D))
    w["l0_wrT"] = f(inputs["L0_attn_Wr"].T)
    w["l0_br"] = f(inputs["L0_attn_br"].reshape(1, 257))
    W1 = inputs["L0_ffn_W1"]
    w["l0_w1"] = f(W1.reshape(8, 128, 32, 128).transpose(2, 0, 1, 3))
    w["l0_b1"] = f(inputs["L0_ffn_b1"].reshape(32, 128).T)
    W2 = inputs["L0_ffn_W2"]
    w["l0_w2"] = f(W2.reshape(32, 128, 2, 4, 128).transpose(2, 0, 3, 1, 4))
    w["l0_b2"] = f(inputs["L0_ffn_b2"].reshape(8, 128).T)
    l1_wqk = np.empty((8, 2, 8, 128, 128), np.float32)
    Wq, Wk = inputs["L1_attn_Wq"], inputs["L1_attn_Wk"]
    for pr in range(8):
        cols = slice(128 * pr, 128 * pr + 128)
        l1_wqk[pr, 0] = Wq[:, cols].reshape(8, 128, 128)
        l1_wqk[pr, 1] = Wk[:, cols].reshape(8, 128, 128)
    w["l1_wqk"] = f(l1_wqk)
    w["l1_wv"] = f(inputs["L1_attn_Wv"].reshape(8, 128, D))
    w["l1_bq"] = f(inputs["L1_attn_bq"].reshape(8, 128).T)
    w["l1_bk"] = f(inputs["L1_attn_bk"].reshape(8, 128).T)
    w["l1_bv"] = f(inputs["L1_attn_bv"].reshape(1, D))
    w["l1_wo"] = f(inputs["L1_attn_Wo"].reshape(8, 128, 8, 128).transpose(2, 0, 1, 3))
    w["l1_bo"] = f(inputs["L1_attn_bo"].reshape(8, 128).T)
    WfcT = inputs["L1_ffn_W_fc"].T
    w["l1_wfc"] = f(WfcT.reshape(8, 128, 32, 128).transpose(2, 0, 1, 3))
    w["l1_bfc"] = f(inputs["L1_ffn_b_fc"].reshape(32, 128).T)
    w["l1_afc"] = f(inputs["L1_ffn_A_fc"].T.reshape(8, 128, 16))
    w["l1_bfc_lo"] = f(inputs["L1_ffn_B_fc"].T.reshape(16, 32, 128).transpose(1, 0, 2))
    WprojT = inputs["L1_ffn_W_proj"].T
    w["l1_wproj"] = f(WprojT.reshape(32, 128, 2, 4, 128).transpose(2, 0, 3, 1, 4))
    w["l1_bproj"] = f(inputs["L1_ffn_b_proj"].reshape(8, 128).T)
    w["l1_aproj"] = f(inputs["L1_ffn_A_proj"].T.reshape(32, 128, 16))
    w["l1_bproj_lo"] = f(inputs["L1_ffn_B_proj"].T.reshape(16, 8, 128).transpose(1, 0, 2))
    return w


def kernel(_stop="full", _dbg=False, **inputs):
    x = inputs["L0_attn_x"]
    w = _layout_weights(inputs)
    nc = build_program(stop=_stop, dbg=_dbg)
    in_maps = []
    for b in range(NCORES):
        m = dict(w)
        m["x_fm"] = np.ascontiguousarray(x[b].T).reshape(NCH, 128, S)
        in_maps.append(m)
    res = run_bass_kernel_spmd(nc, in_maps, core_ids=list(range(NCORES)))
    out = np.empty((NCORES, S, D), np.float32)
    for b in range(NCORES):
        out[b] = res.results[b]["y"].reshape(D, S).T
    if _dbg:
        return out, res.results
    return out


# version 04
# speedup vs baseline: 1.0542x; 1.0542x over previous
import numpy as np
from contextlib import ExitStack
import concourse.bass as bass
import concourse.mybir as mybir
from concourse.bass_utils import run_bass_kernel_spmd

F32 = mybir.dt.float32
BF16 = mybir.dt.bfloat16
I32 = mybir.dt.int32
AF = mybir.ActivationFunctionType
ALU = mybir.AluOpType
AX = mybir.AxisListType
AP = bass.AP

NCORES = 8
S = 2048
D = 1024
H = 16
NCH = 8
NT = 16
EPS = 1e-5

ENGS = ("sync", "gpsimd", "vector", "scalar", "tensor")
OWN = {"tensor": "s_mm", "vector": "s_ve", "scalar": "s_se", "gpsimd": "s_ge"}


class Sem:
    __slots__ = ("h", "n", "name")

    def __init__(self, h, name):
        self.h = h
        self.n = 0
        self.name = name


class Prog:
    def __init__(self, nc, ctx):
        self.nc = nc
        self.ctx = ctx
        self.ops = {e: [] for e in ENGS}
        self.hw = {e: {} for e in ENGS}

    def sem(self, name):
        return Sem(self.ctx.enter_context(self.nc.semaphore(name)), name)

    def add(self, eng, fn, waits=(), inc=None):
        ws = []
        best = {}
        for w in waits:
            if w is None:
                continue
            s, v = w
            if s.name == OWN.get(eng):
                continue
            if best.get(s.name, (None, -1))[1] < v:
                best[s.name] = (s, v)
        for s, v in best.values():
            if self.hw[eng].get(s.name, -1) >= v:
                continue
            self.hw[eng][s.name] = v
            ws.append((s, v))
        rec = None
        if inc is not None:
            sem, k = inc
            sem.n += k
            rec = (sem, k)
        self.ops[eng].append((tuple(ws), fn, rec))
        if rec is not None:
            return (rec[0], rec[0].n)
        return None

    def dma(self, eng, out, in_, waits=(), inc=None, slow=False):
        return self.add(eng,
                        lambda e, o=out, i=in_, sl=slow: e.dma_start(
                            out=o, in_=i, allow_slow_non_contiguous=sl),
                        waits=waits, inc=inc)

    def emit(self, block):
        for ename in ENGS:
            oplist = self.ops[ename]

            def runner(eng, oplist=oplist):
                for ws, fn, rec in oplist:
                    for (s, v) in ws:
                        eng.wait_ge(s.h, v)
                    if fn is None:
                        continue
                    ins = fn(eng)
                    if rec is not None:
                        ins.then_inc(rec[0].h, rec[1])

            getattr(block, ename)(runner)


def l0_chunks(t):
    i0 = 128 * t
    lo = max(0, i0 - 128)
    hi = min(S, i0 + 256)
    out = []
    j = 0
    while j < lo:
        w = min(512, lo - j)
        out.append(("L", j, w))
        j += w
    out.append(("B", lo, hi - lo))
    j = hi
    while j < S:
        w = min(512, S - j)
        out.append(("R", j, w))
        j += w
    assert len(out) <= 8
    return out


def l1_chunks(t):
    i0 = 128 * t
    out = []
    j = 0
    while j < i0:
        w = min(512, i0 - j)
        out.append(("F", j, w))
        j += w
    out.append(("D", i0, 128))
    assert len(out) <= 8
    return out


_LN_N = [0]


def build_program(stop="full", dbg=False):
    _LN_N[0] = 0
    nc = bass.Bass("TRN2", target_bir_lowering=False,
                   detect_race_conditions=False)
    ctx = ExitStack()
    e = ctx.enter_context
    dt = {}

    def din(name, shape, dtype=F32):
        dt[name] = nc.dram_tensor(name, list(shape), dtype, kind="ExternalInput").ap()

    din("x_fm", [NCH, 128, S])
    y = nc.dram_tensor("y", [NCH, 128, S], F32, kind="ExternalOutput").ap()
    din("l0_wqk", [8, 2, 8, 128, 128]); din("l0_wv", [8, 128, D])
    din("l0_bq", [128, 8]); din("l0_bk", [128, 8]); din("l0_bv", [1, D])
    din("l0_wrT", [64, 257]); din("l0_br", [1, 257])
    din("l0_w1", [32, 8, 128, 128]); din("l0_b1", [128, 32])
    din("l0_w2", [2, 32, 4, 128, 128]); din("l0_b2", [128, 8])
    din("l1_wqk", [8, 2, 8, 128, 128]); din("l1_wv", [8, 128, D])
    din("l1_bq", [128, 8]); din("l1_bk", [128, 8]); din("l1_bv", [1, D])
    din("l1_wo", [8, 8, 128, 128]); din("l1_bo", [128, 8])
    din("l1_wfc", [32, 8, 128, 128]); din("l1_bfc", [128, 32])
    din("l1_afc", [8, 128, 16]); din("l1_bfc_lo", [32, 16, 128])
    din("l1_wproj", [2, 32, 4, 128, 128]); din("l1_bproj", [128, 8])
    din("l1_aproj", [32, 128, 16]); din("l1_bproj_lo", [8, 16, 128])

    hbm_h = [nc.dram_tensor(f"hshear{i}", [128, 513], F32, kind="Internal")
             for i in range(2)]
    hbm_wq = nc.dram_tensor("wprojq", [2, 32, 4, 128, 128], BF16, kind="Internal")
    hbm_at = nc.dram_tensor("attnhbm", [8, 128, S], BF16, kind="Internal")
    hbm_q = nc.dram_tensor("hbmq", [1, 512], F32, kind="Internal")
    hbm_wfq = nc.dram_tensor("wfcq", [32, 128, 1024], BF16, kind="Internal")
    hbm_wpq = nc.dram_tensor("wprojq2", [2, 32, 128, 512], BF16, kind="Internal")

    xs = e(nc.sbuf_tensor("xs", [128, NCH * S], F32))
    xb = e(nc.sbuf_tensor("xb", [128, NCH * S], BF16))
    ident = e(nc.sbuf_tensor("ident", [128, 128], BF16))
    tri01 = e(nc.sbuf_tensor("tri01", [128, 128], BF16))
    ones_r = e(nc.sbuf_tensor("ones_r", [1, 128], F32))
    halfs_r = e(nc.sbuf_tensor("halfs_r", [1, 128], F32))
    ones_rb = e(nc.sbuf_tensor("ones_rb", [1, 128], BF16))
    ones_c = e(nc.sbuf_tensor("ones_c", [128, 1], BF16))
    ones_cf = e(nc.sbuf_tensor("ones_cf", [128, 1], F32))
    smalls = e(nc.sbuf_tensor("smalls", [128, 136], F32))
    brow = e(nc.sbuf_tensor("brow", [1, 257], F32))
    bvrow = e(nc.sbuf_tensor("bvrow", [1, D], BF16))
    scal = e(nc.sbuf_tensor("scal", [1, 32], F32))

    SM = dict(L0BQ=0, L0BK=8, L0B1=16, L0B2=48, L1BQ=56, L1BK=64, L1BO=72,
              L1BFC=80, L1BPJ=112, WR8L=120, WR8R=121, NC1=122, NBR=123,
              BR0=124, BR256=125, RSFC=126, SFC=127, RSPJ=128, SPJ=129, HB=130)

    p = Prog(nc, ctx)
    sems = {k: p.sem(k) for k in
            ("s_x", "s_mm", "s_ve", "s_se", "s_ge",
             "s_wA", "s_wB", "s_hw0", "s_hw1", "s_br0", "s_br1",
             "s_qn", "s_rro", "s_out", "s_wv", "s_wo",
             "s_fA", "s_fB", "s_gA", "s_gB", "s_tr", "s_ah", "s_qs", "s_qc")}

    st = dict(stop_=stop, dbg=dbg, dbgout={}, nc=nc, p=p, ctx=ctx, dt=dt, y=y, xs=xs, xb=xb, ident=ident,
              tri01=tri01, ones_r=ones_r, halfs_r=halfs_r, ones_rb=ones_rb, ones_c=ones_c, ones_cf=ones_cf,
              smalls=smalls, brow=brow, bvrow=bvrow, scal=scal,
              hbm_h=hbm_h, hbm_wq=hbm_wq, hbm_at=hbm_at, hbm_q=hbm_q,
              hbm_wfq=hbm_wfq, hbm_wpq=hbm_wpq, sems=sems, SM=SM,
              track=dict(wslot_free=[None, None], qn_chain=None, rro_chain=None,
                         bv_rd=None, tr_chain=None, ah_wr=[None] * 8))

    s_x, s_ge, s_ve = sems["s_x"], sems["s_ge"], sems["s_ve"]
    for c in range(0, NCH, 2):
        p.dma("sync",
              out=xs[:].rearrange("p (c t) -> p c t", t=S)[:, c:c + 2, :],
              in_=dt["x_fm"][c:c + 2].transpose([1, 0, 2]), inc=(s_x, 16))

    def lsm(col, n, name):
        p.dma("sync", out=smalls[:, col:col + n], in_=dt[name], inc=(s_x, 16))

    lsm(SM["L0BQ"], 8, "l0_bq"); lsm(SM["L0BK"], 8, "l0_bk")
    lsm(SM["L0B1"], 32, "l0_b1"); lsm(SM["L0B2"], 8, "l0_b2")
    lsm(SM["L1BQ"], 8, "l1_bq"); lsm(SM["L1BK"], 8, "l1_bk")
    lsm(SM["L1BO"], 8, "l1_bo"); lsm(SM["L1BFC"], 32, "l1_bfc")
    lsm(SM["L1BPJ"], 8, "l1_bproj")
    p.dma("sync", out=brow[0:1, :], in_=dt["l0_br"], inc=(s_x, 16))
    ev_sx = (s_x, s_x.n)

    p.add("gpsimd", lambda g: g.memset(ident[:], 1.0))
    p.add("gpsimd", lambda g: g.affine_select(
        out=ident[:], in_=ident[:], pattern=[[-1, 128]], base=0,
        channel_multiplier=1, compare_op=ALU.is_equal, fill=0.0))
    p.add("gpsimd", lambda g: g.memset(tri01[:], 1.0))
    p.add("gpsimd", lambda g: g.affine_select(
        out=tri01[:], in_=tri01[:], pattern=[[-1, 128]], base=0,
        channel_multiplier=1, compare_op=ALU.is_ge, fill=0.0))
    p.add("gpsimd", lambda g: g.memset(ones_r[:], 1.0))
    p.add("gpsimd", lambda g: g.memset(halfs_r[:], 0.5))
    p.add("gpsimd", lambda g: g.memset(ones_rb[:], 1.0))
    p.add("gpsimd", lambda g: g.memset(ones_cf[:], 1.0))
    ev_cn = p.add("gpsimd", lambda g: g.memset(ones_c[:], 1.0), inc=(s_ge, 1))
    st["marks"] = dict(ev_sx=ev_sx, ev_cn=ev_cn)

    evc = None
    for c in range(NCH):
        sl = slice(c * S, (c + 1) * S)
        evc = p.add("vector",
                    lambda v, sl=sl: v.tensor_copy(out=xb[:, sl], in_=xs[:, sl]),
                    waits=(ev_sx,), inc=(s_ve, 1))
    ev = [evc]

    ev = _attention(st, ev, 0)
    if stop != "a0res":
        ev = _layernorm(st, ev)
        if stop != "x1":
            ev = _ffn(st, ev, 0)
            if stop != "f0res":
                ev = _layernorm(st, ev)
                if stop != "x2":
                    ev = _attention(st, ev, 1)
                    if stop != "a1res":
                        ev = _layernorm(st, ev)
                        if stop != "x3":
                            ev = _ffn(st, ev, 1)
                            if stop != "f1res":
                                ev = _layernorm(st, ev)

    s_out = sems["s_out"]
    for c in range(NCH):
        p.dma("sync", out=y[c], in_=xs[:, c * S:(c + 1) * S],
              waits=tuple(ev), inc=(s_out, 16))
    tot = s_out.n
    p.add("sync", None, waits=((s_out, tot),))
    p.add("gpsimd", lambda g: g.nop(), waits=((s_out, tot),))
    with nc.Block() as block:
        p.emit(block)
    ctx.close()
    return nc


def _attention(st, ev_in, layer):
    nc, p = st["nc"], st["p"]
    SM, dt = st["SM"], st["dt"]
    xs, xb = st["xs"], st["xb"]
    smalls, scal, brow, bvrow = st["smalls"], st["scal"], st["brow"], st["bvrow"]
    ones_r, ones_rb, ones_c = st["ones_r"], st["ones_rb"], st["ones_c"]
    ones_cf = st["ones_cf"]
    halfs_r = st["halfs_r"]
    ident, tri01 = st["ident"], st["tri01"]
    sems = st["sems"]
    s_mm, s_ve, s_se, s_ge = sems["s_mm"], sems["s_ve"], sems["s_se"], sems["s_ge"]
    s_hw = [sems["s_hw0"], sems["s_hw1"]]
    s_brd = [sems["s_br0"], sems["s_br1"]]
    s_qn, s_rro, s_wv, s_ah = sems["s_qn"], sems["s_rro"], sems["s_wv"], sems["s_ah"]
    swslot = [sems["s_wA"], sems["s_wB"]]
    tr = st["track"]
    L0 = layer == 0
    wname = "l0" if L0 else "l1"
    chunks_fn = l0_chunks if L0 else l1_chunks
    ev_cn, ev_sx = st["marks"]["ev_cn"], st["marks"]["ev_sx"]

    ph = ExitStack()
    ee = ph.enter_context
    vts = ee(nc.sbuf_tensor(f"vts{layer}", [128, NT * 1024], BF16))
    pc = [ee(nc.psum_tensor(f"pc{layer}{i}", [128, 512], F32)) for i in range(2)]
    prp = ee(nc.psum_tensor(f"prp{layer}", [128, 512], F32))
    ptA = ee(nc.psum_tensor(f"ptA{layer}", [128, 1024], BF16))
    ptB = ee(nc.psum_tensor(f"ptB{layer}", [128, 1024], BF16))
    ppv = [ee(nc.psum_tensor(f"ppv{layer}{i}", [128, 512], F32)) for i in range(2)]
    pbc = ee(nc.psum_tensor(f"pbc{layer}", [128, 512], F32))

    bank_free = [None, None]
    pbc_busy = None
    prp_busy = None

    ev_bv = p.dma("gpsimd", out=bvrow[:], in_=dt[wname + "_bv"],
                  waits=(tr["bv_rd"],), inc=(s_wv, 16))

    with nc.sbuf_tensor(f"wvs{layer}", [128, 8 * 1024], BF16) as wvs:
        p.dma("gpsimd", out=wvs[:].rearrange("p (c n) -> p c n", n=1024),
              in_=dt[wname + "_wv"].transpose([1, 0, 2]), inc=(s_wv, 16))
        ev_wv = (s_wv, s_wv.n)
        ev_vts = None
        for tt in range(NT):
            for ng in range(2):
                bank = (tt * 2 + ng) % 2
                b = pc[bank]
                p.add("tensor", lambda t, b=b, ng=ng: t.matmul(
                    b[:, 0:512], ones_rb[0:1, :],
                    bvrow[0:1, ng * 512:(ng + 1) * 512], start=True, stop=False),
                    waits=(ev_wv, ev_cn, bank_free[bank]) + tuple(ev_in),
                    inc=(s_mm, 1))
                ev_mv = None
                for kc in range(8):
                    ev_mv = p.add("tensor", lambda t, b=b, kc=kc, tt=tt, ng=ng: t.matmul(
                        b[:, 0:512],
                        xb[:, kc * S + tt * 128: kc * S + tt * 128 + 128],
                        wvs[:, kc * 1024 + ng * 512: kc * 1024 + (ng + 1) * 512],
                        start=False, stop=(kc == 7)), inc=(s_mm, 1))
                ev_vts = p.add("vector", lambda v, b=b, tt=tt, ng=ng: v.tensor_copy(
                    out=vts[:, tt * 1024 + ng * 512: tt * 1024 + (ng + 1) * 512],
                    in_=b[:, 0:512]), waits=(ev_mv,), inc=(s_ve, 1))
                bank_free[bank] = ev_vts
        tr["bv_rd"] = (s_mm, s_mm.n)
        ev_vdone = (s_mm, s_mm.n)

    sub = ExitStack()
    se = sub.enter_context
    wqs = [se(nc.sbuf_tensor(f"wqs{layer}{i}", [128, 2048], BF16)) for i in range(2)]
    qat = se(nc.sbuf_tensor(f"qat{layer}", [128, S], BF16))
    kat = se(nc.sbuf_tensor(f"kat{layer}", [128, S], BF16))
    sq2 = se(nc.sbuf_tensor(f"sq2{layer}", [128, S], BF16))
    P = [se(nc.sbuf_tensor(f"P{layer}{i}", [128, S], BF16)) for i in range(2)]
    PTs = se(nc.sbuf_tensor(f"PTs{layer}", [128, 8192], BF16))
    sums = se(nc.sbuf_tensor(f"sums{layer}", [128, 128], F32))
    ssml = se(nc.sbuf_tensor(f"ssml{layer}", [128, 64], F32))
    qn16 = se(nc.sbuf_tensor(f"qn16{layer}", [128, 16], F32))
    qrow = se(nc.sbuf_tensor(f"qrow{layer}", [1, 512], F32))
    Pf = se(nc.sbuf_tensor(f"Pf{layer}", [128, 2 * S], F32))
    i8 = se(nc.sbuf_tensor(f"i8{layer}", [128, 32], F32))
    if L0:
        wr2m = se(nc.sbuf_tensor("wr2m", [128, 257], BF16))
        wrf32 = se(nc.sbuf_tensor("wrf32", [64, 257], F32))
        rpe = [se(nc.sbuf_tensor(f"rpe{i}", [128, 513], F32)) for i in range(2)]
        bb = [se(nc.sbuf_tensor(f"bb{i}", [128, 384], F32)) for i in range(2)]
        sband0 = se(nc.sbuf_tensor("sband0", [128, 384], F32))
        sband = [sband0, sband0]
    else:
        Pd = se(nc.sbuf_tensor("Pd", [128, 128], F32))
        attn1 = se(nc.sbuf_tensor("attn1", [128, S], BF16))

    if L0:
        s_trx = sems["s_tr"]
        p.dma("gpsimd", out=wr2m[0:64, :], in_=dt["l0_wrT"], waits=(ev_vdone,),
              inc=(s_trx, 16))
        p.dma("gpsimd", out=wr2m[64:128, :], in_=dt["l0_wrT"], inc=(s_trx, 16))
        p.dma("gpsimd", out=wrf32[0:64, :], in_=dt["l0_wrT"], inc=(s_trx, 16))
        ev_wrl = (s_trx, s_trx.n)
        ev_q = p.add("scalar", lambda s: s.activation(
            out=wrf32[0:64, :], in_=wrf32[0:64, :], func=AF.Square),
            waits=(ev_wrl,), inc=(s_se, 1))
        ev_m2 = p.add("tensor", lambda t: t.matmul(
            pc[0][0:1, 0:257], ones_cf[0:64, 0:1], wrf32[0:64, :],
            start=True, stop=True), waits=(ev_q, ev_cn, bank_free[0]), inc=(s_mm, 1))
        ev_wm = p.add("vector", lambda v: v.reduce_max(
            out=scal[0:1, 0:1], in_=pc[0][0:1, 0:257], axis=AX.X),
            waits=(ev_m2,), inc=(s_ve, 1))
        bank_free[0] = ev_wm
        ev_ws = p.add("scalar", lambda s: s.activation(
            out=scal[0:1, 1:2], in_=scal[0:1, 0:1], func=AF.Sqrt),
            waits=(ev_wm,), inc=(s_se, 1))
        ev_nb = p.add("vector", lambda v: v.reduce_max(
            out=scal[0:1, 3:4], in_=brow[0:1, :], axis=AX.X,
            apply_absolute_value=True, negate=True), waits=(ev_sx,),
            inc=(s_ve, 1))
        ev_b1 = p.add("tensor", lambda t: t.matmul(
            pbc[:, 300:301], halfs_r[0:1, :], scal[0:1, 3:4], start=True, stop=True),
            waits=(ev_nb,), inc=(s_mm, 1))
        ev_c3 = p.add("vector", lambda v: v.tensor_copy(
            out=smalls[:, SM["NBR"]:SM["NBR"] + 1], in_=pbc[:, 300:301]),
            waits=(ev_b1,), inc=(s_ve, 1))
        pbc_busy = ev_c3
        ev_consts = ev_c3
        scal_sqrt_rd = None
    else:
        ev_wrl = ev_ws = ev_consts = None
        scal_sqrt_rd = None

    ptA_free = None
    ptB_free = None
    p_free = [None, None]
    sums_free = None
    sq2_free = None
    rpe_hw = [None, None]
    bb_free = [None, None]
    sband_free = [None, None]
    hb_rd = [None, None]
    pt_region_pv = [None] * 4
    ppv_free = [None, None]
    pf_free = [None, None]
    i8_free = [None, None]
    qk_free = None
    qrow_rd = None
    scal_rd = None
    ev_last = None

    wslot_ev = [None, None]
    wslot_ev[0] = p.dma("gpsimd",
                        out=wqs[0][:].rearrange("p (a c m) -> p a c m", a=2, m=128),
                        in_=dt[wname + "_wqk"][0].transpose([2, 0, 1, 3]),
                        waits=(tr["wslot_free"][0], ev_vdone), inc=(swslot[0], 16))

    for pr in range(8):
        slot = pr % 2
        if pr + 1 < 8:
            ns = (pr + 1) % 2
            wslot_ev[ns] = p.dma(
                "gpsimd",
                out=wqs[ns][:].rearrange("p (a c m) -> p a c m", a=2, m=128),
                in_=dt[wname + "_wqk"][pr + 1].transpose([2, 0, 1, 3]),
                waits=(tr["wslot_free"][ns], ev_vdone), inc=(swslot[ns], 16))

        first = True
        ev_mv = None
        ev_ev = None
        for qk in range(2):
            dst = qat if qk == 0 else kat
            bcol = (SM["L0BQ" if L0 else "L1BQ"] if qk == 0
                    else SM["L0BK" if L0 else "L1BK"]) + pr
            for ng in range(4):
                bank = (qk * 4 + ng) % 2
                b = pc[bank]
                for kc in range(8):
                    wts = [bank_free[bank]] if kc == 0 else []
                    if first:
                        wts += [wslot_ev[slot], qk_free,
                                (s_se, s_se.n), (s_ve, s_ve.n)] + list(ev_in)
                        first = False
                    ev_mv = p.add("tensor", lambda t, b=b, slot=slot, qk=qk, kc=kc, ng=ng: t.matmul(
                        b[:, 0:512],
                        wqs[slot][:, qk * 1024 + kc * 128: qk * 1024 + (kc + 1) * 128],
                        xb[:, kc * S + ng * 512: kc * S + (ng + 1) * 512],
                        start=(kc == 0), stop=(kc == 7)),
                        waits=wts, inc=(s_mm, 1))
                ev_ev = p.add("scalar", lambda s, b=b, dst=dst, ng=ng, bcol=bcol: s.activation(
                    out=dst[:, ng * 512:(ng + 1) * 512], in_=b[:, 0:512],
                    func=AF.Identity, bias=smalls[:, bcol:bcol + 1], scale=1.0),
                    waits=(ev_sx, ev_mv), inc=(s_se, 1))
                bank_free[bank] = ev_ev
        ev_qk = ev_ev
        tr["wslot_free"][slot] = (s_mm, s_mm.n)

        ev_kvar = ev_qk

        for hh in range(2):
            h = 2 * pr + hh
            hb = hh * 64
            hsl = slice(hb, hb + 64)

            ev_k2 = p.add("scalar", lambda s, hsl=hsl: s.activation(
                out=sq2[hsl, :], in_=kat[hsl, :], func=AF.Square),
                waits=(ev_qk, sq2_free), inc=(s_se, 1))
            kn_ev = None
            for g in range(4):
                ev_mk = p.add("tensor", lambda t, g=g, hsl=hsl: t.matmul(
                    pc[0][0:1, 0:512], ones_c[hsl, 0:1],
                    sq2[hsl, g * 512:(g + 1) * 512], start=True, stop=True),
                    waits=(ev_k2, bank_free[0], kn_ev), inc=(s_mm, 1))
                kn_ev = p.add("vector", lambda v, g=g: v.reduce_max(
                    out=scal[0:1, 8 + g:9 + g], in_=pc[0][0:1, 0:512], axis=AX.X),
                    waits=(ev_mk, scal_rd), inc=(s_ve, 1))
            bank_free[0] = kn_ev
            ev_kb = p.add("gpsimd", lambda g: g.tensor_copy(
                out=scal[0:1, 16:20], in_=scal[0:1, 8:12]),
                waits=(kn_ev,), inc=(s_ge, 1))
            ev_km = p.add("vector", lambda v: v.reduce_max(
                out=scal[0:1, 12:13], in_=scal[0:1, 16:20], axis=AX.X),
                waits=(scal_sqrt_rd, ev_kb), inc=(s_ve, 1))
            ev_ks = p.add("scalar", lambda s: s.activation(
                out=scal[0:1, 13:14], in_=scal[0:1, 12:13], func=AF.Sqrt),
                waits=(ev_km,), inc=(s_se, 1))
            scal_sqrt_rd = ev_ks
            if L0:
                ev_n = p.add("vector", lambda v: v.scalar_tensor_tensor(
                    out=scal[0:1, 15:16], in0=scal[0:1, 13:14], scalar=-0.125,
                    in1=scal[0:1, 1:2], op0=ALU.mult, op1=ALU.subtract),
                    waits=(ev_ks, ev_ws), inc=(s_ve, 1))
            else:
                ev_n = p.add("vector", lambda v: v.tensor_scalar(
                    out=scal[0:1, 15:16], in0=scal[0:1, 13:14], scalar1=-0.125,
                    scalar2=None, op0=ALU.mult), waits=(ev_ks,), inc=(s_ve, 1))
            ev_bcm = p.add("tensor", lambda t: t.matmul(
                pbc[:, 0:1], halfs_r[0:1, :], scal[0:1, 15:16], start=True, stop=True),
                waits=(ev_n, pbc_busy), inc=(s_mm, 1))
            scal_rd = ev_bcm
            ev_nc1 = p.add("vector", lambda v: v.tensor_copy(
                out=smalls[:, SM["NC1"]:SM["NC1"] + 1], in_=pbc[:, 0:1]),
                waits=(ev_bcm,), inc=(s_ve, 1))
            pbc_busy = ev_nc1

            ev_q2 = p.add("scalar", lambda s, hsl=hsl: s.activation(
                out=sq2[hsl, :], in_=qat[hsl, :], func=AF.Square),
                waits=(kn_ev,), inc=(s_se, 1))
            qn_dma = None
            for g in range(4):
                ev_mq = p.add("tensor", lambda t, g=g, hsl=hsl: t.matmul(
                    pc[0][0:1, 0:512], ones_c[hsl, 0:1],
                    sq2[hsl, g * 512:(g + 1) * 512], start=True, stop=True),
                    waits=(ev_q2, bank_free[0]), inc=(s_mm, 1))
                ev_qc = p.add("vector", lambda v: v.tensor_copy(
                    out=qrow[0:1, :], in_=pc[0][0:1, 0:512]),
                    waits=(ev_mq, qrow_rd), inc=(s_ve, 1))
                bank_free[0] = ev_qc
                p.dma("sync", out=st["hbm_q"].ap(), in_=qrow[0:1, :],
                      waits=(ev_qc, tr["qn_chain"]), inc=(s_qn, 16))
                ev_qw = (s_qn, s_qn.n)
                hqap = st["hbm_q"].ap()
                qn_dma = p.dma("sync", out=qn16[:, 4 * g:4 * g + 4],
                               in_=AP(hqap.tensor, 0, [[1, 128], [128, 4]]),
                               waits=(ev_qw,), inc=(s_qn, 16), slow=True)
                tr["qn_chain"] = qn_dma
                qrow_rd = ev_qw
            sq2_free = (s_mm, s_mm.n)
            ev_sq16 = p.add("scalar", lambda s: s.activation(
                out=ssml[:, 48:64], in_=qn16[:, 0:16], func=AF.Sqrt),
                waits=(qn_dma,), inc=(s_se, 1))
            tr["qn_chain"] = ev_sq16
            ev_M = p.add("vector", lambda v: v.tensor_scalar(
                out=ssml[:, 0:16], in0=ssml[:, 48:64],
                scalar1=smalls[:, SM["NC1"]:SM["NC1"] + 1],
                scalar2=(smalls[:, SM["NBR"]:SM["NBR"] + 1] if L0 else 0.0),
                op0=ALU.mult, op1=ALU.add),
                waits=(ev_sq16, ev_nc1, ev_consts), inc=(s_ve, 1))
            ev_prep = ev_M
            ev_sz = p.add("gpsimd", lambda g: g.memset(sums[:, :], 0.0),
                          waits=(sums_free,), inc=(s_ge, 1))

            exp_last = {}
            ptev = {}
            pd_free = None
            ev_pv = None
            ev_am = None
            cyc = 0
            for t in range(NT + 1):
                if t < NT:
                    chl = chunks_fn(t)
                    tb = t % 2
                    pslot = t % 2
                    if L0:
                        p.add("tensor", lambda tn: tn.matmul(
                            prp[:, 0:257], ones_r[0:1, :], brow[0:1, 0:257],
                            start=True, stop=False),
                            waits=(ev_sx, ev_cn, prp_busy), inc=(s_mm, 1))
                        ev_rp = p.add("tensor", lambda tn, t=t, hsl=hsl: tn.matmul(
                            prp[:, 0:257], qat[hsl, t * 128:(t + 1) * 128],
                            wr2m[hsl, :], start=False, stop=True),
                            waits=(ev_prep, ev_wrl), inc=(s_mm, 1))
                        ev_rpe = p.add("scalar", lambda s, tb=tb: s.activation(
                            out=rpe[tb][:, 128:385], in_=prp[:, 0:257],
                            func=AF.Copy),
                            waits=(ev_rp, rpe_hw[tb]), inc=(s_se, 1))
                        prp_busy = ev_rpe
                        p.add("gpsimd", lambda g, tb=tb: g.tensor_copy(
                            out=rpe[tb][:, 0:128],
                            in_=rpe[tb][:, 128:129].broadcast_to([128, 128])),
                            waits=(ev_rpe,), inc=(s_ge, 1))
                        ev_fill = p.add("gpsimd", lambda g, tb=tb: g.tensor_copy(
                            out=rpe[tb][:, 385:513],
                            in_=rpe[tb][:, 384:385].broadcast_to([128, 128])),
                            inc=(s_ge, 1))
                        p.dma("sync", out=st["hbm_h"][tb].ap(), in_=rpe[tb][:],
                              waits=(ev_fill, hb_rd[tb]), inc=(s_hw[tb], 16))
                        ev_hwr = (s_hw[tb], s_hw[tb].n)
                        rpe_hw[tb] = ev_hwr
                        i0 = 128 * t
                        lo = max(0, i0 - 128)
                        wB = min(S, i0 + 256) - lo
                        off = lo - i0 + 256
                        hap = st["hbm_h"][tb].ap()
                        shear = AP(hap.tensor, off, [[512, 128], [1, wB]])
                        p.dma("sync", out=bb[tb][:, 0:wB], in_=shear,
                              waits=(ev_hwr, bb_free[tb]), inc=(s_brd[tb], 16))
                        hb_rd[tb] = (s_brd[tb], s_brd[tb].n)
                        p.add("gpsimd", lambda g, tb=tb, t=t: g.tensor_scalar(
                            out=ssml[:, 16 + t:17 + t], in0=rpe[tb][:, 128:129],
                            scalar1=0.5, scalar2=ssml[:, t:t + 1],
                            op0=ALU.mult, op1=ALU.add),
                            waits=(ev_rpe, ev_prep), inc=(s_ge, 1))
                        ev_bfar = p.add("gpsimd", lambda g, tb=tb, t=t: g.tensor_scalar(
                            out=ssml[:, 32 + t:33 + t], in0=rpe[tb][:, 384:385],
                            scalar1=0.5, scalar2=ssml[:, t:t + 1],
                            op0=ALU.mult, op1=ALU.add),
                            inc=(s_ge, 1))
                    ev_ex = None
                    pfo = (t % 2) * S
                    for ci, (kind, j0, w) in enumerate(chl):
                        bank = cyc % 2
                        cyc += 1
                        b = pc[bank]
                        ev_cm = p.add("tensor", lambda tn, b=b, j0=j0, w=w, t=t, hsl=hsl: tn.matmul(
                            b[:, 0:w], qat[hsl, t * 128:(t + 1) * 128],
                            kat[hsl, j0:j0 + w], start=True, stop=True),
                            waits=(bank_free[bank], ev_kvar, ev_prep), inc=(s_mm, 1))
                        slot_ap = sums[:, t * 8 + ci: t * 8 + ci + 1]
                        if L0 and kind == "B":
                            ev_bd = p.add("vector", lambda v, b=b, w=w, tb=tb: v.scalar_tensor_tensor(
                                out=sband[tb][:, 0:w], in0=b[:, 0:w],
                                scalar=0.125, in1=bb[tb][:, 0:w],
                                op0=ALU.mult, op1=ALU.add),
                                waits=(ev_cm, hb_rd[tb], sband_free[0],
                                       sband_free[1]),
                                inc=(s_ve, 1))
                            bank_free[bank] = ev_bd
                            bb_free[tb] = ev_bd
                            ev_ex = p.add("scalar", lambda s, w=w, j0=j0, t=t, tb=tb, pfo=pfo, slot_ap=slot_ap: s.activation(
                                out=Pf[:, pfo + j0:pfo + j0 + w], in_=sband[tb][:, 0:w],
                                func=AF.Exp, bias=ssml[:, t:t + 1], scale=0.5,
                                accum_out=slot_ap),
                                waits=(ev_bd, ev_sz, pf_free[t % 2]), inc=(s_se, 1))
                            sband_free[tb] = ev_ex
                        elif (not L0) and kind == "D":
                            ev_e0 = p.add("scalar", lambda s, b=b, t=t: s.activation(
                                out=Pd[:, :], in_=b[:, 0:128], func=AF.Exp,
                                bias=ssml[:, t:t + 1], scale=0.0625),
                                waits=(ev_cm, ev_sz, pd_free, pf_free[t % 2]), inc=(s_se, 1))
                            bank_free[bank] = ev_e0
                            ev_ex = p.add("vector", lambda v, j0=j0, pfo=pfo, slot_ap=slot_ap: v.scalar_tensor_tensor(
                                out=Pf[:, pfo + j0:pfo + j0 + 128], in0=Pd[:, :],
                                scalar=1.0, in1=tri01[:, :],
                                op0=ALU.mult, op1=ALU.mult, accum_out=slot_ap),
                                waits=(ev_e0, ev_sz, ev_cn),
                                inc=(s_ve, 1))
                            pd_free = ev_ex
                        else:
                            bcoli = {"L": 16, "B": 0, "R": 32, "F": 0}[kind] + t
                            wts_x = (ev_cm, ev_sz, pf_free[t % 2])
                            if L0 and kind in ("L", "R"):
                                wts_x += (ev_bfar,)
                            ev_ex = p.add("scalar", lambda s, b=b, w=w, j0=j0, pfo=pfo, bcoli=bcoli, slot_ap=slot_ap: s.activation(
                                out=Pf[:, pfo + j0:pfo + j0 + w], in_=b[:, 0:w],
                                func=AF.Exp, bias=ssml[:, bcoli:bcoli + 1],
                                scale=0.0625, accum_out=slot_ap),
                                waits=wts_x, inc=(s_se, 1))
                            bank_free[bank] = ev_ex
                    exp_last[t] = ev_ex

                    o8 = (t % 2) * 16
                    ext = S if L0 else 128 * (t + 1)
                    if L0:
                        ev_a = p.add("vector", lambda v, t=t, o8=o8: v.reduce_sum(
                            out=i8[:, o8:o8 + 1], in_=sums[:, t * 8:(t + 1) * 8],
                            axis=AX.X), waits=(ev_ex, i8_free[t % 2]), inc=(s_ve, 1))
                    else:
                        ev_a0 = p.add("gpsimd", lambda g, t=t, o8=o8: g.tensor_copy(
                            out=i8[:, o8 + 8:o8 + 16],
                            in_=sums[:, t * 8:(t + 1) * 8]),
                            waits=(ev_ex, i8_free[t % 2]), inc=(s_ge, 1))
                        ev_a = p.add("vector", lambda v, o8=o8: v.reduce_sum(
                            out=i8[:, o8:o8 + 1], in_=i8[:, o8 + 8:o8 + 16],
                            axis=AX.X), waits=(ev_a0,), inc=(s_ve, 1))
                    ev_b = p.add("gpsimd", lambda g, o8=o8: g.tensor_scalar(
                        out=i8[:, o8 + 1:o8 + 2], in0=i8[:, o8:o8 + 1],
                        scalar1=1e-30, scalar2=None, op0=ALU.add),
                        waits=(ev_a,), inc=(s_ge, 1))
                    ev_c = p.add("vector", lambda v, o8=o8: v.reciprocal(
                        out=i8[:, o8 + 2:o8 + 3], in_=i8[:, o8 + 1:o8 + 2]),
                        waits=(ev_b,), inc=(s_ve, 1))
                    ev_c2 = p.add("gpsimd", lambda g, o8=o8: g.tensor_scalar(
                        out=i8[:, o8 + 3:o8 + 4], in0=i8[:, o8 + 2:o8 + 3],
                        scalar1=i8[:, o8 + 2:o8 + 3], scalar2=1e37,
                        op0=ALU.mult, op1=ALU.min),
                        waits=(ev_c,), inc=(s_ge, 1))
                    ev_e = p.add("vector", lambda v, o8=o8, ext=ext, pfo=pfo, pslot=pslot: v.scalar_tensor_tensor(
                        out=P[pslot][:, 0:ext], in0=Pf[:, pfo:pfo + ext],
                        scalar=i8[:, o8 + 3:o8 + 4],
                        in1=Pf[:, pfo:pfo + ext],
                        op0=ALU.mult, op1=ALU.mult,
                        accum_out=i8[:, o8 + 4:o8 + 5]),
                        waits=(ev_c2, p_free[pslot]), inc=(s_ve, 1))
                    pf_free[t % 2] = ev_e
                    ev_f = p.add("gpsimd", lambda g, o8=o8: g.tensor_scalar(
                        out=i8[:, o8 + 5:o8 + 6], in0=i8[:, o8 + 4:o8 + 5],
                        scalar1=1e-30, scalar2=None, op0=ALU.add),
                        waits=(ev_e,), inc=(s_ge, 1))
                    ev_g = p.add("vector", lambda v, o8=o8: v.reciprocal(
                        out=i8[:, o8 + 6:o8 + 7], in_=i8[:, o8 + 5:o8 + 6]),
                        waits=(ev_f,), inc=(s_ve, 1))
                    ev_h2 = p.add("gpsimd", lambda g, o8=o8: g.tensor_copy(
                        out=i8[:, o8 + 7:o8 + 8], in_=i8[:, o8 + 6:o8 + 7]),
                        waits=(ev_g,), inc=(s_ge, 1))
                    ev_i = p.add("vector", lambda v, o8=o8, ext=ext, pslot=pslot: v.tensor_scalar(
                        out=P[pslot][:, 0:ext], in0=P[pslot][:, 0:ext],
                        scalar1=i8[:, o8 + 7:o8 + 8], scalar2=None,
                        op0=ALU.mult), waits=(ev_h2, ev_e), inc=(s_ve, 1))
                    i8_free[t % 2] = ev_i
                    exp_last[t] = ev_i

                tp = t - 1
                if tp >= 0:
                    nkc = NT if L0 else (tp + 1)
                    pslot2 = tp % 2
                    evA = evB = None
                    for a in range(nkc):
                        dst2 = ptA if a < 8 else ptB
                        dsl = slice((a % 8) * 128, (a % 8) * 128 + 128)
                        wts = []
                        if a == 0:
                            wts = [exp_last[tp], ptA_free]
                        if a == 8:
                            wts = [ptB_free]
                        ev_t2 = p.add("tensor", lambda tn, dst2=dst2, dsl=dsl, a=a, pslot2=pslot2: tn.transpose(
                            dst2[:, dsl], P[pslot2][:, a * 128:(a + 1) * 128],
                            ident[:]), waits=wts, inc=(s_mm, 1))
                        if a == min(nkc, 8) - 1:
                            evA = ev_t2
                        if a == nkc - 1:
                            evB = ev_t2
                    p_free[pslot2] = evB
                    region = tp % 4
                    nA = min(nkc, 8)
                    ev_eA = p.add("vector", lambda v, region=region, nA=nA: v.tensor_copy(
                        out=PTs[:, region * 2048: region * 2048 + nA * 128],
                        in_=ptA[:, 0:nA * 128]),
                        waits=(evA, pt_region_pv[region]), inc=(s_ve, 1))
                    ptA_free = ev_eA
                    ev_eB = None
                    if nkc > 8:
                        nB = nkc - 8
                        ev_eB = p.add("scalar", lambda s, region=region, nB=nB: s.activation(
                            out=PTs[:, region * 2048 + 1024: region * 2048 + 1024 + nB * 128],
                            in_=ptB[:, 0:nB * 128], func=AF.Copy),
                            waits=(evB, pt_region_pv[region]), inc=(s_se, 1))
                        ptB_free = ev_eB
                    ptev[tp] = (ev_eA, ev_eB)

                if tp >= 3 and tp % 4 == 3:
                    g = tp // 4
                    tpos = None if hb == 0 else (0, 64)
                    pvb = ppv[hh]
                    nk = NT if L0 else (4 * g + 4)
                    ptsap = PTs[:]
                    for a in range(nk):
                        if L0 or a <= 4 * g:
                            qoff, qn = 0, 512
                        else:
                            qoff = (a - 4 * g) * 128
                            qn = 512 - qoff
                        rhs = AP(ptsap.tensor,
                                 ptsap.offset + a * 128 + (qoff // 128) * 2048,
                                 [[8192, 128], [2048, qn // 128], [1, 128]])
                        wts = []
                        if a == 0:
                            wts = [x for tpi in range(4 * g, 4 * g + 4)
                                   for x in ptev[tpi] if x]
                            wts += [ev_vts, ppv_free[hh]]
                        ev_pv = p.add("tensor", lambda tn, pvb=pvb, a=a, h=h, rhs=rhs, qoff=qoff, qn=qn, hsl=hsl, tpos=tpos, nk=nk: tn.matmul(
                            pvb[hsl, qoff:qoff + qn],
                            vts[:, a * 1024 + 64 * h: a * 1024 + 64 * h + 64],
                            rhs, start=(a == 0), stop=(a == nk - 1),
                            tile_position=tpos),
                            waits=wts, inc=(s_mm, 1))
                    for r4 in range(4):
                        pt_region_pv[r4] = ev_pv
                    if L0:
                        ev_am = p.add("vector", lambda v, g=g, hsl=hsl, pr=pr, pvb=pvb: v.scalar_tensor_tensor(
                            out=xs[:, pr * S + g * 512: pr * S + (g + 1) * 512][hsl, :],
                            in0=pvb[hsl, 0:512], scalar=0.0,
                            in1=xs[:, pr * S + g * 512: pr * S + (g + 1) * 512][hsl, :],
                            op0=ALU.add, op1=ALU.add),
                            waits=(ev_sx, ev_pv), inc=(s_ve, 1))
                    else:
                        ev_am = p.add("vector", lambda v, g=g, hsl=hsl, pvb=pvb: v.tensor_copy(
                            out=attn1[hsl, g * 512:(g + 1) * 512],
                            in_=pvb[hsl, 0:512]),
                            waits=(ev_pv,), inc=(s_ve, 1))
                    ppv_free[hh] = ev_am
                    ev_last = ev_am
                    if st["dbg"] and pr == 0 and hh == 0 and (not L0) and g == 0 \
                            and st.get("stop_") == "a1res":
                        ev_all2 = [(s_mm, s_mm.n), (s_ve, s_ve.n),
                                   (s_se, s_se.n), (s_ge, s_ge.n)]
                        for nm2, t2 in [("e_PTs", PTs), ("e_P0", P[0]),
                                        ("e_P1", P[1]), ("e_sums", sums),
                                        ("e_at1", attn1), ("e_Pf", Pf),
                                        ("e_vts", vts), ("e_i8", i8)]:
                            ap2 = t2[:]
                            od2 = nc.dram_tensor(nm2, list(ap2.shape), ap2.dtype,
                                                 kind="ExternalOutput").ap()
                            dump_ev2 = p.dma("gpsimd", out=od2, in_=ap2,
                                             waits=tuple(ev_all2),
                                             inc=(sems["s_tr"], 16))
                        for k2 in range(2):
                            p_free[k2] = dump_ev2
                            i8_free[k2] = dump_ev2
                        pf_free = [dump_ev2, dump_ev2]
                        for r42 in range(4):
                            pt_region_pv[r42] = dump_ev2
                        ppv_free[hh] = dump_ev2
            if st["dbg"] and pr == 0 and hh == 0 and (L0 == (st.get("stop_", "a0res") == "a0res")):
                ev_all = [(s_mm, s_mm.n), (s_ve, s_ve.n), (s_se, s_se.n),
                          (s_ge, s_ge.n)]
                s_dbg = sems["s_tr"]
                dump_list = [("d_qat", qat), ("d_kat", kat),
                             ("d_P0", P[0]), ("d_P1", P[1]),
                             ("d_sums", sums), ("d_ssml", ssml),
                             ("d_qn16", qn16), ("d_scal", scal),
                             ("d_smalls", smalls), ("d_PTs", PTs),
                             ("d_vts", vts), ("d_Pf", Pf),
                             ("d_i8", i8), ("d_xs0", xs[:, 0:S])]
                if L0:
                    dump_list += [("d_rpe0", rpe[0]), ("d_rpe1", rpe[1]),
                                  ("d_bb0", bb[0]), ("d_bb1", bb[1])]
                else:
                    dump_list += [("d_Pd", Pd), ("d_at1", attn1)]
                dump_ev = None
                for nm, tsr in dump_list:
                    ap = tsr if isinstance(tsr, bass.AP) else tsr[:]
                    od = nc.dram_tensor(nm, list(ap.shape), ap.dtype,
                                        kind="ExternalOutput").ap()
                    st["dbgout"][nm] = True
                    dump_ev = p.dma("gpsimd", out=od, in_=ap,
                                    waits=tuple(ev_all), inc=(s_dbg, 16))
                for k in range(2):
                    p_free[k] = dump_ev
                    rpe_hw[k] = dump_ev
                    bb_free[k] = dump_ev
                    sband_free[k] = dump_ev
                sums_free = dump_ev
                sq2_free = dump_ev
            sums_free = (s_ve, s_ve.n)
            qk_free = (s_mm, s_mm.n)

        if not L0:
            p.dma("sync", out=st["hbm_at"][pr], in_=attn1[:, :],
                  waits=(ev_last, tr["ah_wr"][pr]), inc=(s_ah, 16))
            tr["ah_wr"][pr] = (s_ah, s_ah.n)
            ppv_free = [tr["ah_wr"][pr], tr["ah_wr"][pr]]

    if L0:
        res = [(s_ve, s_ve.n)]
        sub.close()
        ph.close()
        return res

    ev_attn_done = [(s_mm, s_mm.n), (s_ve, s_ve.n), (s_se, s_se.n)]
    sub.close()
    ob = ExitStack()
    oe = ob.enter_context
    wob = oe(nc.sbuf_tensor("wob", [128, 8 * 8 * 128], BF16))
    ats = oe(nc.sbuf_tensor("ats", [128, NCH * S], BF16))
    s_wo = sems["s_wo"]
    p.dma("gpsimd",
          out=wob[:].rearrange("p (a c m) -> p a c m", a=8, m=128),
          in_=dt["l1_wo"].transpose([2, 0, 1, 3]),
          waits=tuple(ev_attn_done), inc=(s_wo, 16))
    for kc in range(8):
        p.dma("sync", out=ats[:, kc * S:(kc + 1) * S], in_=st["hbm_at"][kc],
              waits=[(s_ah, s_ah.n)] + list(ev_attn_done), inc=(s_wo, 16))
    ev_wo = (s_wo, s_wo.n)
    bank_free = [None, None]
    ev_o = None
    for oc in range(8):
        for tg in range(4):
            bank = (oc * 4 + tg) % 2
            b = pc[bank]
            ev_m3 = None
            for kc in range(8):
                wts = [bank_free[bank], ev_wo] if kc == 0 else []
                ev_m3 = p.add("tensor", lambda t, b=b, oc=oc, tg=tg, kc=kc: t.matmul(
                    b[:, 0:512],
                    wob[:, (oc * 8 + kc) * 128:(oc * 8 + kc) * 128 + 128],
                    ats[:, kc * S + tg * 512: kc * S + (tg + 1) * 512],
                    start=(kc == 0), stop=(kc == 7)), waits=wts, inc=(s_mm, 1))
            ev_o = p.add("vector", lambda v, b=b, oc=oc, tg=tg: v.scalar_tensor_tensor(
                out=xs[:, oc * S + tg * 512: oc * S + (tg + 1) * 512],
                in0=b[:, 0:512],
                scalar=smalls[:, SM["L1BO"] + oc:SM["L1BO"] + oc + 1],
                in1=xs[:, oc * S + tg * 512: oc * S + (tg + 1) * 512],
                op0=ALU.add, op1=ALU.add),
                waits=(ev_m3, ev_sx), inc=(s_ve, 1))
            bank_free[bank] = ev_o
    res = [(s_ve, s_ve.n)]
    ob.close()
    ph.close()
    return res


def _layernorm(st, ev_in):
    nc, p = st["nc"], st["p"]
    li = _LN_N[0]
    _LN_N[0] += 1
    xs, xb = st["xs"], st["xb"]
    ones_r, ones_c = st["ones_r"], st["ones_c"]
    sems = st["sems"]
    s_mm, s_ve, s_se = sems["s_mm"], sems["s_ve"], sems["s_se"]
    s_ge = sems["s_ge"]
    ev_cn = st["marks"]["ev_cn"]

    ph = ExitStack()
    ee = ph.enter_context
    lnsq = ee(nc.sbuf_tensor(f"lnsq{li}", [128, 8 * 512], BF16))
    lrow = ee(nc.sbuf_tensor(f"lrow{li}", [1, 6 * 512], F32))
    tmp = ee(nc.sbuf_tensor(f"lntmp{li}", [128, 512], F32))
    pS = [ee(nc.psum_tensor(f"lnS{li}_{i}", [1, 512], F32)) for i in range(2)]
    pQ = [ee(nc.psum_tensor(f"lnQ{li}_{i}", [1, 512], F32)) for i in range(2)]
    pA = [ee(nc.psum_tensor(f"lnA{li}_{i}", [128, 512], F32)) for i in range(2)]
    pB = [ee(nc.psum_tensor(f"lnB{li}_{i}", [128, 512], F32)) for i in range(2)]
    M0, T0, T1, T2, CA, CB = 0, 512, 1024, 1536, 2048, 2560

    ev_cast = []
    for c in range(NCH):
        sl = slice(c * S, (c + 1) * S)
        ev_cast.append(p.add("vector",
                             lambda v, sl=sl: v.tensor_copy(out=xb[:, sl], in_=xs[:, sl]),
                             waits=tuple(ev_in), inc=(s_ve, 1)))
    stat_free = [None, None]
    ab_free = [None, None]
    lrow_rd = None
    evs = [None, None]
    for tg in range(4):
        bk = tg % 2
        ev_sq = []
        for c in range(NCH):
            ev_sq.append(p.add("scalar", lambda s, c=c, tg=tg: s.activation(
                out=lnsq[:, c * 512:(c + 1) * 512],
                in_=xs[:, c * S + tg * 512: c * S + (tg + 1) * 512],
                func=AF.Square), waits=tuple(ev_in), inc=(s_se, 1)))
        ev_ms = ev_mq = None
        for c in range(NCH):
            ev_ms = p.add("tensor", lambda t, c=c, tg=tg, bk=bk: t.matmul(
                pS[bk][0:1, :], ones_c[:, 0:1],
                xb[:, c * S + tg * 512: c * S + (tg + 1) * 512],
                start=(c == 0), stop=(c == 7)),
                waits=[ev_cast[c]] + ([stat_free[bk], ev_cn] if c == 0 else []),
                inc=(s_mm, 1))
        for c in range(NCH):
            ev_mq = p.add("tensor", lambda t, c=c, bk=bk: t.matmul(
                pQ[bk][0:1, :], ones_c[:, 0:1], lnsq[:, c * 512:(c + 1) * 512],
                start=(c == 0), stop=(c == 7)),
                waits=[ev_sq[c]] + ([stat_free[bk]] if c == 0 else []),
                inc=(s_mm, 1))
        p.add("vector", lambda v, bk=bk: v.tensor_scalar(
            out=lrow[0:1, M0:M0 + 512], in0=pS[bk][0:1, :], scalar1=1.0 / D,
            scalar2=None, op0=ALU.mult), waits=(ev_ms, lrow_rd), inc=(s_ve, 1))
        p.add("vector", lambda v, bk=bk: v.tensor_scalar(
            out=lrow[0:1, T0:T0 + 512], in0=pQ[bk][0:1, :], scalar1=1.0 / D,
            scalar2=EPS, op0=ALU.mult, op1=ALU.add), waits=(ev_mq,), inc=(s_ve, 1))
        ev3 = p.add("vector", lambda v: v.tensor_mul(
            out=lrow[0:1, T1:T1 + 512], in0=lrow[0:1, M0:M0 + 512],
            in1=lrow[0:1, M0:M0 + 512]), inc=(s_ve, 1))
        ev3b = p.add("gpsimd", lambda g: g.tensor_copy(
            out=lrow[0:1, T2:T2 + 512], in_=lrow[0:1, T1:T1 + 512]),
            waits=(ev3,), inc=(s_ge, 1))
        ev4 = p.add("vector", lambda v: v.scalar_tensor_tensor(
            out=lrow[0:1, T1:T1 + 512], in0=lrow[0:1, T2:T2 + 512], scalar=-1.0,
            in1=lrow[0:1, T0:T0 + 512], op0=ALU.mult, op1=ALU.add),
            waits=(ev3b,), inc=(s_ve, 1))
        ev5 = p.add("scalar", lambda s: s.activation(
            out=lrow[0:1, T0:T0 + 512], in_=lrow[0:1, T1:T1 + 512], func=AF.Sqrt),
            waits=(ev4,), inc=(s_se, 1))
        ev6 = p.add("vector", lambda v: v.reciprocal(
            out=lrow[0:1, CA:CA + 512], in_=lrow[0:1, T0:T0 + 512]),
            waits=(ev5,), inc=(s_ve, 1))
        ev6b = p.add("gpsimd", lambda g: g.tensor_copy(
            out=lrow[0:1, T2:T2 + 512], in_=lrow[0:1, CA:CA + 512]),
            waits=(ev6,), inc=(s_ge, 1))
        ev7 = p.add("vector", lambda v: v.scalar_tensor_tensor(
            out=lrow[0:1, CB:CB + 512], in0=lrow[0:1, M0:M0 + 512], scalar=-1.0,
            in1=lrow[0:1, T2:T2 + 512], op0=ALU.mult, op1=ALU.mult),
            waits=(ev6b,), inc=(s_ve, 1))
        stat_free[bk] = ev7
        ev_mA = p.add("tensor", lambda t, bk=bk: t.matmul(
            pA[bk][:, :], ones_r[0:1, :], lrow[0:1, CA:CA + 512],
            start=True, stop=True), waits=(ev6, ev7, ab_free[bk]), inc=(s_mm, 1))
        ev_mB = p.add("tensor", lambda t, bk=bk: t.matmul(
            pB[bk][:, :], ones_r[0:1, :], lrow[0:1, CB:CB + 512],
            start=True, stop=True), inc=(s_mm, 1))
        lrow_rd = ev_mB
        ev_ap = ev_cb = None
        for c in range(NCH):
            xsl = slice(c * S + tg * 512, c * S + (tg + 1) * 512)
            p.add("vector", lambda v, xsl=xsl, bk=bk: v.tensor_mul(
                out=tmp[:, :], in0=xs[:, xsl], in1=pA[bk][:, :]),
                waits=(ev_mA,), inc=(s_ve, 1))
            ev_ap = p.add("vector", lambda v, xsl=xsl, bk=bk: v.tensor_add(
                out=xs[:, xsl], in0=tmp[:, :], in1=pB[bk][:, :]),
                waits=(ev_mB,), inc=(s_ve, 1))
            ev_cb = p.add("scalar", lambda s, xsl=xsl: s.activation(
                out=xb[:, xsl], in_=xs[:, xsl], func=AF.Copy),
                waits=(ev_ap,), inc=(s_se, 1))
        ab_free[bk] = ev_ap
        evs = [ev_ap, ev_cb]
    ph.close()
    return evs


def _ffn(st, ev_in, layer):
    nc, p = st["nc"], st["p"]
    dt, SM = st["dt"], st["SM"]
    xs, xb = st["xs"], st["xb"]
    smalls, scal, ones_r = st["smalls"], st["scal"], st["ones_r"]
    sems = st["sems"]
    s_mm, s_ve, s_se, s_ge = sems["s_mm"], sems["s_ve"], sems["s_se"], sems["s_ge"]
    sfs = [sems["s_fA"], sems["s_fB"]]
    sgs = [sems["s_gA"], sems["s_gB"]]
    s_qs, s_qc, s_tr = sems["s_qs"], sems["s_qc"], sems["s_tr"]
    L0 = layer == 0
    QAT = not L0

    ph = ExitStack()
    ee = ph.enter_context

    if QAT:
        with ExitStack() as am:
            ae = am.enter_context
            qsl = [ae(nc.sbuf_tensor(f"qsl{i}", [128, 2048], F32)) for i in range(2)]
            amax = ae(nc.sbuf_tensor("amax", [128, 32], F32))
            arow = ae(nc.sbuf_tensor("arow", [1, 128], F32))
            pk67 = [ae(nc.psum_tensor(f"pam{i}", [128, 512], F32)) for i in range(2)]
            slot_rd = [None, None]
            for wi, wname in enumerate(["l1_wfc", "l1_wproj"]):
                wap = dt[wname]
                flat = AP(wap.tensor, 0, [[2048, 128], [1, 2048]])
                red_evs = []
                for sl in range(16):
                    ss = sl % 2
                    src = AP(wap.tensor, sl * 128 * 2048, [[2048, 128], [1, 2048]])
                    ev_l = p.dma("sync", out=qsl[ss][:], in_=src,
                                 waits=(slot_rd[ss],) + (tuple(ev_in) if sl < 2 and wi == 0 else ()),
                                 inc=(s_qs, 16))
                    ev_r = p.add("vector", lambda v, ss=ss, sl=sl: v.reduce_max(
                        out=amax[:, sl:sl + 1], in_=qsl[ss][:], axis=AX.X,
                        apply_absolute_value=True), waits=(ev_l,), inc=(s_ve, 1))
                    slot_rd[ss] = ev_r
                    red_evs.append(ev_r)
                ev_cb0 = p.add("gpsimd", lambda g: g.tensor_copy(
                    out=amax[:, 16:32], in_=amax[:, 0:16]),
                    waits=(red_evs[-1],), inc=(s_ge, 1))
                ev_c = p.add("vector", lambda v: v.reduce_max(
                    out=amax[:, 0:1], in_=amax[:, 16:32], axis=AX.X),
                    waits=(ev_cb0,), inc=(s_ve, 1))
                ev_t = p.dma("sync", out=arow[0:1, :], in_=amax[:, 0:1],
                             waits=(ev_c, st["track"]["tr_chain"]),
                             inc=(s_tr, 16), slow=True)
                st["track"]["tr_chain"] = ev_t
                ev_m = p.add("vector", lambda v, wi=wi: v.reduce_max(
                    out=scal[0:1, 20 + 4 * wi:21 + 4 * wi], in_=arow[0:1, :],
                    axis=AX.X), waits=(ev_t,), inc=(s_ve, 1))
                ev_s = p.add("gpsimd", lambda g, wi=wi: g.tensor_scalar(
                    out=scal[0:1, 21 + 4 * wi:22 + 4 * wi],
                    in0=scal[0:1, 20 + 4 * wi:21 + 4 * wi],
                    scalar1=1.0 / 127.0, scalar2=None, op0=ALU.mult),
                    waits=(ev_m,), inc=(s_ge, 1))
                ev_rs = p.add("vector", lambda v, wi=wi: v.reciprocal(
                    out=scal[0:1, 22 + 4 * wi:23 + 4 * wi],
                    in_=scal[0:1, 21 + 4 * wi:22 + 4 * wi]),
                    waits=(ev_s,), inc=(s_ve, 1))
                cS = SM["SFC"] if wi == 0 else SM["SPJ"]
                cR = SM["RSFC"] if wi == 0 else SM["RSPJ"]
                ev_b1 = p.add("tensor", lambda t, wi=wi: t.matmul(
                    pk67[0][:, 0:1], ones_r[0:1, :],
                    scal[0:1, 21 + 4 * wi:22 + 4 * wi], start=True, stop=True),
                    waits=(ev_s, ev_rs), inc=(s_mm, 1))
                ev_b2 = p.add("tensor", lambda t, wi=wi: t.matmul(
                    pk67[1][:, 0:1], ones_r[0:1, :],
                    scal[0:1, 22 + 4 * wi:23 + 4 * wi], start=True, stop=True),
                    inc=(s_mm, 1))
                p.add("vector", lambda v, cS=cS: v.tensor_copy(
                    out=smalls[:, cS:cS + 1], in_=pk67[0][:, 0:1]),
                    waits=(ev_b1,), inc=(s_ve, 1))
                ev_cp = p.add("vector", lambda v, cR=cR: v.tensor_copy(
                    out=smalls[:, cR:cR + 1], in_=pk67[1][:, 0:1]),
                    waits=(ev_b2,), inc=(s_ve, 1))
            cal = ae(nc.sbuf_tensor("cal", [1, 8], F32))
            cali = ae(nc.sbuf_tensor("cali", [1, 2], I32))
            ev_z0 = p.add("gpsimd", lambda g: g.memset(cal[0:1, 0:1], 1.5),
                          inc=(s_ge, 1))
            ev_z1 = p.add("vector", lambda v: v.tensor_copy(
                out=cali[0:1, 0:1], in_=cal[0:1, 0:1]), waits=(ev_z0,),
                inc=(s_ve, 1))
            ev_z2 = p.add("gpsimd", lambda g: g.tensor_copy(
                out=cali[0:1, 1:2], in_=cali[0:1, 0:1]), waits=(ev_z1,),
                inc=(s_ge, 1))
            ev_z3 = p.add("vector", lambda v: v.tensor_scalar(
                out=cal[0:1, 2:3], in0=cali[0:1, 1:2], scalar1=-0.5,
                scalar2=257.0, op0=ALU.mult, op1=ALU.add), waits=(ev_z2,),
                inc=(s_ve, 1))
            ev_z4 = p.add("tensor", lambda t: t.matmul(
                pk67[0][:, 4:5], ones_r[0:1, :], cal[0:1, 2:3],
                start=True, stop=True), waits=(ev_z3,), inc=(s_mm, 1))
            ev_hb = p.add("vector", lambda v: v.tensor_copy(
                out=smalls[:, SM["HB"]:SM["HB"] + 1], in_=pk67[0][:, 4:5]),
                waits=(ev_z4,), inc=(s_ve, 1))
            ev_scales = ev_hb
    else:
        ev_scales = None

    hbuf = ee(nc.sbuf_tensor(f"hbuf{layer}", [128, 32 * 1024], BF16))
    if QAT:
        w1s = [ee(nc.sbuf_tensor(f"w1f{i}", [128, 1024], F32)) for i in range(2)]
        wqs1 = [ee(nc.sbuf_tensor(f"wq1{i}", [128, 1024], BF16)) for i in range(2)]
        qtmp = ee(nc.sbuf_tensor("qtmp", [128, 1024], F32))
        qint = ee(nc.sbuf_tensor("qint", [128, 1024], I32))
        blo_all = ee(nc.sbuf_tensor("blo_all", [16, 32 * 128], BF16))
        afc = ee(nc.sbuf_tensor("afc", [128, 8 * 16], BF16))
        apj = ee(nc.sbuf_tensor("apj", [128, 32 * 16], BF16))
        bpl = ee(nc.sbuf_tensor("bpl", [16, 8 * 128], BF16))
        lrx = ee(nc.sbuf_tensor("lrx", [16, 1024], BF16))
        lr2x = ee(nc.sbuf_tensor("lr2x", [16, 1024], BF16))
        w2q = [ee(nc.sbuf_tensor(f"w2q{i}", [128, 512], F32)) for i in range(2)]
    else:
        w1s = [ee(nc.sbuf_tensor(f"w1b{i}", [128, 1024], BF16)) for i in range(2)]
    w2s = [ee(nc.sbuf_tensor(f"w2s{layer}_{i}", [128, 512], BF16)) for i in range(2)]
    pk = [ee(nc.psum_tensor(f"pk{layer}{i}", [128, 512], F32)) for i in range(8)]

    bank_free = [None] * 8
    w1_rd = [None, None]
    w2_rd = [None, None]
    w1_cw = [None, None]
    w2_cw = [None, None]
    hb_free = None
    lrx_rd = None
    lr2x_rd = None
    ev_out = list(ev_in)

    b1c = SM["L0B1"] if L0 else SM["L1BFC"]
    b2c = SM["L0B2"] if L0 else SM["L1BPJ"]

    if QAT:
        s_lo = sems["s_rro"]
        p.dma("gpsimd", out=afc[:].rearrange("p (c n) -> p c n", n=16),
              in_=dt["l1_afc"].transpose([1, 0, 2]),
              waits=tuple(ev_in), inc=(s_lo, 16))
        p.dma("gpsimd", out=apj[:].rearrange("p (c n) -> p c n", n=16),
              in_=dt["l1_aproj"].transpose([1, 0, 2]), inc=(s_lo, 16))
        p.dma("gpsimd", out=bpl[:].rearrange("p (c n) -> p c n", n=128),
              in_=dt["l1_bproj_lo"].transpose([1, 0, 2]), inc=(s_lo, 16))
        p.dma("gpsimd", out=blo_all[:].rearrange("p (c n) -> p c n", n=128),
              in_=dt["l1_bfc_lo"].transpose([1, 0, 2]), inc=(s_lo, 16))
        ev_lores = (s_lo, sems["s_rro"].n)

    for th in range(2):
        tok0 = th * 1024
        if QAT:
            for ng in range(2):
                ev_ml = None
                for kc in range(8):
                    wts = []
                    if kc == 0:
                        wts = [ev_lores, bank_free[2], lrx_rd] + list(ev_in)
                    ev_ml = p.add("tensor", lambda t, kc=kc, ng=ng, tok0=tok0: t.matmul(
                        pk[2][0:16, 0:512], afc[:, kc * 16:(kc + 1) * 16],
                        xb[:, kc * S + tok0 + ng * 512: kc * S + tok0 + (ng + 1) * 512],
                        start=(kc == 0), stop=(kc == 7)), waits=wts, inc=(s_mm, 1))
                ev_le = p.add("scalar", lambda sc, ng=ng: sc.activation(
                    out=lrx[0:16, ng * 512:(ng + 1) * 512], in_=pk[2][0:16, 0:512],
                    func=AF.Identity, scale=2.0, bias=0.0),
                    waits=(ev_ml,), inc=(s_se, 1))
                bank_free[2] = ev_le
            ev_lrx = ev_le

        for ic in range(32):
            sl = ic % 2
            if QAT and th == 0:
                ev_w = p.dma("gpsimd", out=w1s[sl][:].rearrange("p (c m) -> p c m", m=128),
                             in_=dt["l1_wfc"][ic].transpose([1, 0, 2]),
                             waits=(w1_rd[sl], ev_scales), inc=(sfs[sl], 16))
                ev_q1 = p.add("vector", lambda v, sl=sl: v.tensor_scalar(
                    out=qtmp[:], in0=w1s[sl][:],
                    scalar1=smalls[:, SM["RSFC"]:SM["RSFC"] + 1],
                    scalar2=smalls[:, SM["HB"]:SM["HB"] + 1],
                    op0=ALU.mult, op1=ALU.add), waits=(ev_w,), inc=(s_ve, 1))
                w1_rd[sl] = ev_q1
                p.add("vector", lambda v: v.tensor_copy(
                    out=qint[:], in_=qtmp[:]), inc=(s_ve, 1))
                ev_q4 = p.add("vector", lambda v, sl=sl: v.tensor_scalar(
                    out=wqs1[sl][:], in0=qint[:], scalar1=-256.0,
                    scalar2=smalls[:, SM["SFC"]:SM["SFC"] + 1],
                    op0=ALU.add, op1=ALU.mult),
                    waits=(w1_cw[sl],), inc=(s_ve, 1))
                ev_cw = p.dma("sync", out=st["hbm_wfq"].ap()[ic], in_=wqs1[sl][:],
                              waits=(ev_q4,), inc=(s_qc, 16))
                w1_cw[sl] = (sems["s_qc"], sems["s_qc"].n)
                wsrc = wqs1[sl]
                ev_wrdy = ev_q4
            elif QAT:
                ev_w = p.dma("sync", out=wqs1[sl][:], in_=st["hbm_wfq"].ap()[ic],
                             waits=(w1_rd[sl], (s_qc, sems["s_qc"].n)),
                             inc=(sfs[sl], 16))
                ev_wrdy = (sfs[sl], sfs[sl].n)
                wsrc = wqs1[sl]
            else:
                p.dma("gpsimd", out=w1s[sl][:].rearrange("p (c m) -> p c m", m=128),
                      in_=dt["l0_w1"][ic].transpose([1, 0, 2]),
                      waits=(w1_rd[sl],) + (tuple(ev_in) if ic < 2 and th == 0 else ()),
                      inc=(sfs[sl], 16))
                ev_wrdy = (sfs[sl], sfs[sl].n)
                wsrc = w1s[sl]
            for ng in range(2):
                bank = ng
                ev_m1 = None
                first = True
                if QAT:
                    ev_m1 = p.add("tensor", lambda t, ic=ic, ng=ng: t.matmul(
                        pk[ng][:, 0:512],
                        blo_all[0:16, ic * 128:(ic + 1) * 128],
                        lrx[0:16, ng * 512:(ng + 1) * 512],
                        start=True, stop=False),
                        waits=(ev_wrdy, ev_lrx, bank_free[ng], hb_free, ev_lores),
                        inc=(s_mm, 1))
                    first = False
                for kc in range(8):
                    wts = []
                    if first and kc == 0:
                        wts = [ev_wrdy, bank_free[ng], hb_free]
                    ev_m1 = p.add("tensor", lambda t, wsrc=wsrc, kc=kc, ng=ng, tok0=tok0: t.matmul(
                        pk[ng][:, 0:512], wsrc[:, kc * 128:(kc + 1) * 128],
                        xb[:, kc * S + tok0 + ng * 512: kc * S + tok0 + (ng + 1) * 512],
                        start=(kc == 0 and not QAT), stop=(kc == 7)),
                        waits=wts, inc=(s_mm, 1))
                w1_rd[sl] = ev_m1
                ev_h = p.add("scalar", lambda sc, ic=ic, ng=ng, b1c=b1c: sc.activation(
                    out=hbuf[:, ic * 1024 + ng * 512: ic * 1024 + (ng + 1) * 512],
                    in_=pk[ng][:, 0:512], func=(AF.Relu if L0 else AF.Gelu),
                    bias=smalls[:, b1c + ic:b1c + ic + 1], scale=1.0),
                    waits=(ev_m1,), inc=(s_se, 1))
                bank_free[ng] = ev_h
        ev_h_done = ev_h

        if QAT:
            for sg in range(2):
                ev_ml2 = None
                for kc in range(32):
                    wts = []
                    if kc == 0:
                        wts = [ev_h_done, bank_free[3], lr2x_rd, ev_lores]
                    ev_ml2 = p.add("tensor", lambda t, kc=kc, sg=sg: t.matmul(
                        pk[3][0:16, 0:512], apj[:, kc * 16:(kc + 1) * 16],
                        hbuf[:, kc * 1024 + sg * 512: kc * 1024 + (sg + 1) * 512],
                        start=(kc == 0), stop=(kc == 31)), waits=wts, inc=(s_mm, 1))
                ev_l2 = p.add("scalar", lambda sc, sg=sg: sc.activation(
                    out=lr2x[0:16, sg * 512:(sg + 1) * 512], in_=pk[3][0:16, 0:512],
                    func=AF.Identity, scale=2.0, bias=0.0),
                    waits=(ev_ml2,), inc=(s_se, 1))
                bank_free[3] = ev_l2
            ev_lr2 = ev_l2

        for half in range(2):
            if QAT:
                for oc in range(4):
                    for sg in range(2):
                        bank = oc * 2 + sg
                        p.add("tensor", lambda t, half=half, oc=oc, sg=sg, bank=bank: t.matmul(
                            pk[bank][:, 0:512],
                            bpl[0:16, (half * 4 + oc) * 128:(half * 4 + oc) * 128 + 128],
                            lr2x[0:16, sg * 512:(sg + 1) * 512],
                            start=True, stop=False),
                            waits=(ev_lr2, ev_lores, bank_free[bank]),
                            inc=(s_mm, 1))
            for kc in range(32):
                sl = kc % 2
                if QAT and th == 0:
                    ev_w2 = p.dma("sync", out=w2q[sl][:].rearrange("p (a m) -> p a m", m=128),
                                  in_=dt["l1_wproj"][half][kc].rearrange("a p m -> p a m"),
                                  waits=(w2_rd[sl], ev_scales), inc=(sgs[sl], 16))
                    ev_q1 = p.add("vector", lambda v, sl=sl: v.tensor_scalar(
                        out=qtmp[:, 0:512], in0=w2q[sl][:],
                        scalar1=smalls[:, SM["RSPJ"]:SM["RSPJ"] + 1],
                        scalar2=smalls[:, SM["HB"]:SM["HB"] + 1],
                        op0=ALU.mult, op1=ALU.add), waits=(ev_w2,), inc=(s_ve, 1))
                    w2_rd[sl] = ev_q1
                    p.add("vector", lambda v: v.tensor_copy(
                        out=qint[:, 0:512], in_=qtmp[:, 0:512]), inc=(s_ve, 1))
                    ev_q4 = p.add("vector", lambda v, sl=sl: v.tensor_scalar(
                        out=w2s[sl][:], in0=qint[:, 0:512], scalar1=-256.0,
                        scalar2=smalls[:, SM["SPJ"]:SM["SPJ"] + 1],
                        op0=ALU.add, op1=ALU.mult),
                        waits=(w2_cw[sl],), inc=(s_ve, 1))
                    p.dma("sync", out=st["hbm_wpq"].ap()[half][kc], in_=w2s[sl][:],
                          waits=(ev_q4,), inc=(s_qc, 16))
                    w2_cw[sl] = (sems["s_qc"], sems["s_qc"].n)
                    ev_w2r = ev_q4
                elif QAT:
                    ev_w2 = p.dma("sync", out=w2s[sl][:],
                                  in_=st["hbm_wpq"].ap()[half][kc],
                                  waits=(w2_rd[sl], (s_qc, sems["s_qc"].n)),
                                  inc=(sgs[sl], 16))
                    ev_w2r = (sgs[sl], sgs[sl].n)
                else:
                    p.dma("gpsimd", out=w2s[sl][:].rearrange("p (a m) -> p a m", m=128),
                          in_=dt["l0_w2"][half][kc].transpose([1, 0, 2]),
                          waits=(w2_rd[sl],), inc=(sgs[sl], 16))
                    ev_w2r = (sgs[sl], sgs[sl].n)
                ev_m2 = None
                for oc in range(4):
                    for sg in range(2):
                        bank = oc * 2 + sg
                        wts = []
                        if oc == 0 and sg == 0:
                            wts = [ev_w2r, ev_h_done]
                            if kc == 0 and not QAT:
                                wts += [bank_free[b] for b in range(8)]
                        ev_m2 = p.add("tensor", lambda t, oc=oc, sg=sg, kc=kc, sl=sl, bank=bank: t.matmul(
                            pk[bank][:, 0:512], w2s[sl][:, oc * 128:(oc + 1) * 128],
                            hbuf[:, kc * 1024 + sg * 512: kc * 1024 + (sg + 1) * 512],
                            start=(kc == 0 and not QAT), stop=(kc == 31)),
                            waits=wts, inc=(s_mm, 1))
                w2_rd[sl] = ev_m2
            for oc in range(4):
                for sg in range(2):
                    bank = oc * 2 + sg
                    ocabs = half * 4 + oc
                    xsl = slice(ocabs * S + tok0 + sg * 512,
                                ocabs * S + tok0 + (sg + 1) * 512)
                    ev_e2 = p.add("vector", lambda v, bank=bank, ocabs=ocabs, xsl=xsl, b2c=b2c: v.scalar_tensor_tensor(
                        out=xs[:, xsl], in0=pk[bank][:, 0:512],
                        scalar=smalls[:, b2c + ocabs:b2c + ocabs + 1],
                        in1=xs[:, xsl], op0=ALU.add, op1=ALU.add),
                        waits=(ev_m2, st["marks"]["ev_sx"]), inc=(s_ve, 1))
                    bank_free[bank] = ev_e2
                    ev_out = [ev_e2]
        hb_free = (s_mm, s_mm.n)
        if QAT:
            lrx_rd = (s_mm, s_mm.n)
            lr2x_rd = (s_mm, s_mm.n)
        if st["dbg"] and QAT and th == 1 and st.get("stop_") == "f1res":
            ev_all3 = [(s_mm, s_mm.n), (s_ve, s_ve.n), (s_se, s_se.n),
                       (s_ge, s_ge.n)]
            dump_ev3 = None
            for nm3, t3 in [("f_lrx", lrx), ("f_lr2x", lr2x), ("f_hbuf", hbuf),
                            ("f_afc", afc), ("f_apj", apj), ("f_bpl", bpl),
                            ("f_wq1a", wqs1[0]), ("f_wq1b", wqs1[1]),
                            ("f_smalls", smalls), ("f_scal", scal),
                            ("f_xs", xs), ("f_xb", xb)]:
                ap3 = t3[:]
                od3 = nc.dram_tensor(nm3, list(ap3.shape), ap3.dtype,
                                     kind="ExternalOutput").ap()
                dump_ev3 = p.dma("gpsimd", out=od3, in_=ap3, waits=tuple(ev_all3),
                                 inc=(sems["s_tr"], 16))
            hb_free = dump_ev3
            lrx_rd = dump_ev3
            lr2x_rd = dump_ev3
            for bb2 in range(8):
                bank_free[bb2] = dump_ev3
            w1_rd = [dump_ev3, dump_ev3]
            w2_rd = [dump_ev3, dump_ev3]
    ph.close()
    return ev_out


def _layout_weights(inputs):
    f = np.ascontiguousarray
    w = {}
    Wqkv = inputs["L0_attn_Wqkv"]
    qr = np.concatenate([np.arange(192 * h, 192 * h + 64) for h in range(H)])
    kr = qr + 64
    vr = qr + 128
    WqT = Wqkv[qr].T
    WkT = Wqkv[kr].T
    WvT = Wqkv[vr].T
    l0_wqk = np.empty((8, 2, 8, 128, 128), np.float32)
    for pr in range(8):
        cols = slice(128 * pr, 128 * pr + 128)
        l0_wqk[pr, 0] = WqT[:, cols].reshape(8, 128, 128)
        l0_wqk[pr, 1] = WkT[:, cols].reshape(8, 128, 128)
    w["l0_wqk"] = f(l0_wqk)
    w["l0_wv"] = f(WvT.reshape(8, 128, D))
    bqkv = inputs["L0_attn_bqkv"]
    w["l0_bq"] = f(bqkv[qr].reshape(8, 128).T)
    w["l0_bk"] = f(bqkv[kr].reshape(8, 128).T)
    w["l0_bv"] = f(bqkv[vr].reshape(1, D))
    w["l0_wrT"] = f(inputs["L0_attn_Wr"].T)
    w["l0_br"] = f(inputs["L0_attn_br"].reshape(1, 257))
    W1 = inputs["L0_ffn_W1"]
    w["l0_w1"] = f(W1.reshape(8, 128, 32, 128).transpose(2, 0, 1, 3))
    w["l0_b1"] = f(inputs["L0_ffn_b1"].reshape(32, 128).T)
    W2 = inputs["L0_ffn_W2"]
    w["l0_w2"] = f(W2.reshape(32, 128, 2, 4, 128).transpose(2, 0, 3, 1, 4))
    w["l0_b2"] = f(inputs["L0_ffn_b2"].reshape(8, 128).T)
    l1_wqk = np.empty((8, 2, 8, 128, 128), np.float32)
    Wq, Wk = inputs["L1_attn_Wq"], inputs["L1_attn_Wk"]
    for pr in range(8):
        cols = slice(128 * pr, 128 * pr + 128)
        l1_wqk[pr, 0] = Wq[:, cols].reshape(8, 128, 128)
        l1_wqk[pr, 1] = Wk[:, cols].reshape(8, 128, 128)
    w["l1_wqk"] = f(l1_wqk)
    w["l1_wv"] = f(inputs["L1_attn_Wv"].reshape(8, 128, D))
    w["l1_bq"] = f(inputs["L1_attn_bq"].reshape(8, 128).T)
    w["l1_bk"] = f(inputs["L1_attn_bk"].reshape(8, 128).T)
    w["l1_bv"] = f(inputs["L1_attn_bv"].reshape(1, D))
    w["l1_wo"] = f(inputs["L1_attn_Wo"].reshape(8, 128, 8, 128).transpose(2, 0, 1, 3))
    w["l1_bo"] = f(inputs["L1_attn_bo"].reshape(8, 128).T)
    WfcT = inputs["L1_ffn_W_fc"].T
    w["l1_wfc"] = f(WfcT.reshape(8, 128, 32, 128).transpose(2, 0, 1, 3))
    w["l1_bfc"] = f(inputs["L1_ffn_b_fc"].reshape(32, 128).T)
    w["l1_afc"] = f(inputs["L1_ffn_A_fc"].T.reshape(8, 128, 16))
    w["l1_bfc_lo"] = f(inputs["L1_ffn_B_fc"].T.reshape(16, 32, 128).transpose(1, 0, 2))
    WprojT = inputs["L1_ffn_W_proj"].T
    w["l1_wproj"] = f(WprojT.reshape(32, 128, 2, 4, 128).transpose(2, 0, 3, 1, 4))
    w["l1_bproj"] = f(inputs["L1_ffn_b_proj"].reshape(8, 128).T)
    w["l1_aproj"] = f(inputs["L1_ffn_A_proj"].T.reshape(32, 128, 16))
    w["l1_bproj_lo"] = f(inputs["L1_ffn_B_proj"].T.reshape(16, 8, 128).transpose(1, 0, 2))
    return w


def kernel(_stop="full", _dbg=False, **inputs):
    x = inputs["L0_attn_x"]
    w = _layout_weights(inputs)
    nc = build_program(stop=_stop, dbg=_dbg)
    in_maps = []
    for b in range(NCORES):
        m = dict(w)
        m["x_fm"] = np.ascontiguousarray(x[b].T).reshape(NCH, 128, S)
        in_maps.append(m)
    res = run_bass_kernel_spmd(nc, in_maps, core_ids=list(range(NCORES)))
    out = np.empty((NCORES, S, D), np.float32)
    for b in range(NCORES):
        out[b] = res.results[b]["y"].reshape(D, S).T
    if _dbg:
        return out, res.results
    return out


# version 05
# speedup vs baseline: 1.0959x; 1.0395x over previous
import numpy as np
from contextlib import ExitStack
import concourse.bass as bass
import concourse.mybir as mybir
from concourse.bass_utils import run_bass_kernel_spmd

F32 = mybir.dt.float32
BF16 = mybir.dt.bfloat16
I32 = mybir.dt.int32
AF = mybir.ActivationFunctionType
ALU = mybir.AluOpType
AX = mybir.AxisListType
AP = bass.AP

NCORES = 8
S = 2048
D = 1024
H = 16
NCH = 8
NT = 16
EPS = 1e-5

ENGS = ("sync", "gpsimd", "vector", "scalar", "tensor")
OWN = {"tensor": "s_mm", "vector": "s_ve", "scalar": "s_se", "gpsimd": "s_ge"}


class Sem:
    __slots__ = ("h", "n", "name")

    def __init__(self, h, name):
        self.h = h
        self.n = 0
        self.name = name


class Prog:
    def __init__(self, nc, ctx):
        self.nc = nc
        self.ctx = ctx
        self.ops = {e: [] for e in ENGS}
        self.hw = {e: {} for e in ENGS}

    def sem(self, name):
        return Sem(self.ctx.enter_context(self.nc.semaphore(name)), name)

    def add(self, eng, fn, waits=(), inc=None):
        ws = []
        best = {}
        for w in waits:
            if w is None:
                continue
            s, v = w
            if s.name == OWN.get(eng):
                continue
            if best.get(s.name, (None, -1))[1] < v:
                best[s.name] = (s, v)
        for s, v in best.values():
            if self.hw[eng].get(s.name, -1) >= v:
                continue
            self.hw[eng][s.name] = v
            ws.append((s, v))
        rec = None
        if inc is not None:
            sem, k = inc
            sem.n += k
            rec = (sem, k)
        self.ops[eng].append((tuple(ws), fn, rec))
        if rec is not None:
            return (rec[0], rec[0].n)
        return None

    def dma(self, eng, out, in_, waits=(), inc=None, slow=False):
        return self.add(eng,
                        lambda e, o=out, i=in_, sl=slow: e.dma_start(
                            out=o, in_=i, allow_slow_non_contiguous=sl),
                        waits=waits, inc=inc)

    def emit(self, block):
        for ename in ENGS:
            oplist = self.ops[ename]

            def runner(eng, oplist=oplist):
                for ws, fn, rec in oplist:
                    for (s, v) in ws:
                        eng.wait_ge(s.h, v)
                    if fn is None:
                        continue
                    ins = fn(eng)
                    if rec is not None:
                        ins.then_inc(rec[0].h, rec[1])

            getattr(block, ename)(runner)


def l0_chunks(t):
    i0 = 128 * t
    lo = max(0, i0 - 128)
    hi = min(S, i0 + 256)
    out = []
    j = 0
    while j < lo:
        w = min(512, lo - j)
        out.append(("L", j, w))
        j += w
    out.append(("B", lo, hi - lo))
    j = hi
    while j < S:
        w = min(512, S - j)
        out.append(("R", j, w))
        j += w
    assert len(out) <= 8
    return out


def l1_chunks(t):
    i0 = 128 * t
    out = []
    j = 0
    while j < i0:
        w = min(512, i0 - j)
        out.append(("F", j, w))
        j += w
    out.append(("D", i0, 128))
    assert len(out) <= 8
    return out


_LN_N = [0]


def build_program(stop="full", dbg=False):
    _LN_N[0] = 0
    nc = bass.Bass("TRN2", target_bir_lowering=False,
                   detect_race_conditions=False)
    ctx = ExitStack()
    e = ctx.enter_context
    dt = {}

    def din(name, shape, dtype=F32):
        dt[name] = nc.dram_tensor(name, list(shape), dtype, kind="ExternalInput").ap()

    din("x_fm", [NCH, 128, S])
    y = nc.dram_tensor("y", [NCH, 128, S], F32, kind="ExternalOutput").ap()
    din("l0_wqk", [8, 2, 8, 128, 128]); din("l0_wv", [8, 128, D])
    din("l0_bq", [128, 8]); din("l0_bk", [128, 8]); din("l0_bv", [1, D])
    din("l0_wrT", [64, 257]); din("l0_br", [1, 257])
    din("l0_w1", [32, 8, 128, 128]); din("l0_b1", [128, 32])
    din("l0_w2", [2, 32, 4, 128, 128]); din("l0_b2", [128, 8])
    din("l1_wqk", [8, 2, 8, 128, 128]); din("l1_wv", [8, 128, D])
    din("l1_bq", [128, 8]); din("l1_bk", [128, 8]); din("l1_bv", [1, D])
    din("l1_wo", [8, 8, 128, 128]); din("l1_bo", [128, 8])
    din("l1_wfc", [32, 8, 128, 128]); din("l1_bfc", [128, 32])
    din("l1_afc", [8, 128, 16]); din("l1_bfc_lo", [32, 16, 128])
    din("l1_wproj", [2, 32, 4, 128, 128]); din("l1_bproj", [128, 8])
    din("l1_aproj", [32, 128, 16]); din("l1_bproj_lo", [8, 16, 128])

    hbm_h = [nc.dram_tensor(f"hshear{i}", [128, 513], F32, kind="Internal")
             for i in range(2)]
    hbm_wq = nc.dram_tensor("wprojq", [2, 32, 4, 128, 128], BF16, kind="Internal")
    hbm_at = nc.dram_tensor("attnhbm", [8, 128, S], BF16, kind="Internal")
    hbm_q = nc.dram_tensor("hbmq", [1, 512], F32, kind="Internal")
    hbm_wfq = nc.dram_tensor("wfcq", [32, 128, 1024], BF16, kind="Internal")
    hbm_wpq = nc.dram_tensor("wprojq2", [2, 32, 128, 512], BF16, kind="Internal")

    xs = e(nc.sbuf_tensor("xs", [128, NCH * S], F32))
    xb = e(nc.sbuf_tensor("xb", [128, NCH * S], BF16))
    ident = e(nc.sbuf_tensor("ident", [128, 128], BF16))
    tri01 = e(nc.sbuf_tensor("tri01", [128, 128], BF16))
    ones_r = e(nc.sbuf_tensor("ones_r", [1, 128], F32))
    halfs_r = e(nc.sbuf_tensor("halfs_r", [1, 128], F32))
    ones_rb = e(nc.sbuf_tensor("ones_rb", [1, 128], BF16))
    ones_c = e(nc.sbuf_tensor("ones_c", [128, 1], BF16))
    ones_cf = e(nc.sbuf_tensor("ones_cf", [128, 1], F32))
    smalls = e(nc.sbuf_tensor("smalls", [128, 136], F32))
    brow = e(nc.sbuf_tensor("brow", [1, 257], F32))
    bvrow = e(nc.sbuf_tensor("bvrow", [1, D], BF16))
    scal = e(nc.sbuf_tensor("scal", [1, 32], F32))

    SM = dict(L0BQ=0, L0BK=8, L0B1=16, L0B2=48, L1BQ=56, L1BK=64, L1BO=72,
              L1BFC=80, L1BPJ=112, WR8L=120, WR8R=121, NC1=122, NBR=123,
              BR0=124, BR256=125, RSFC=126, SFC=127, RSPJ=128, SPJ=129, HB=130)

    p = Prog(nc, ctx)
    sems = {k: p.sem(k) for k in
            ("s_x", "s_mm", "s_ve", "s_se", "s_ge",
             "s_wA", "s_wB", "s_hw0", "s_hw1", "s_br0", "s_br1",
             "s_qn", "s_rro", "s_out", "s_wv", "s_wo",
             "s_fA", "s_fB", "s_gA", "s_gB", "s_tr", "s_ah", "s_qs", "s_qc")}

    st = dict(stop_=stop, dbg=dbg, dbgout={}, nc=nc, p=p, ctx=ctx, dt=dt, y=y, xs=xs, xb=xb, ident=ident,
              tri01=tri01, ones_r=ones_r, halfs_r=halfs_r, ones_rb=ones_rb, ones_c=ones_c, ones_cf=ones_cf,
              smalls=smalls, brow=brow, bvrow=bvrow, scal=scal,
              hbm_h=hbm_h, hbm_wq=hbm_wq, hbm_at=hbm_at, hbm_q=hbm_q,
              hbm_wfq=hbm_wfq, hbm_wpq=hbm_wpq, sems=sems, SM=SM,
              track=dict(wslot_free=[None, None], qn_chain=None, rro_chain=None,
                         bv_rd=None, tr_chain=None, ah_wr=[None] * 8))

    s_x, s_ge, s_ve = sems["s_x"], sems["s_ge"], sems["s_ve"]
    for c in range(0, NCH, 2):
        p.dma("sync",
              out=xs[:].rearrange("p (c t) -> p c t", t=S)[:, c:c + 2, :],
              in_=dt["x_fm"][c:c + 2].transpose([1, 0, 2]), inc=(s_x, 16))

    def lsm(col, n, name):
        p.dma("sync", out=smalls[:, col:col + n], in_=dt[name], inc=(s_x, 16))

    lsm(SM["L0BQ"], 8, "l0_bq"); lsm(SM["L0BK"], 8, "l0_bk")
    lsm(SM["L0B1"], 32, "l0_b1"); lsm(SM["L0B2"], 8, "l0_b2")
    lsm(SM["L1BQ"], 8, "l1_bq"); lsm(SM["L1BK"], 8, "l1_bk")
    lsm(SM["L1BO"], 8, "l1_bo"); lsm(SM["L1BFC"], 32, "l1_bfc")
    lsm(SM["L1BPJ"], 8, "l1_bproj")
    p.dma("sync", out=brow[0:1, :], in_=dt["l0_br"], inc=(s_x, 16))
    ev_sx = (s_x, s_x.n)

    p.add("gpsimd", lambda g: g.memset(ident[:], 1.0))
    p.add("gpsimd", lambda g: g.affine_select(
        out=ident[:], in_=ident[:], pattern=[[-1, 128]], base=0,
        channel_multiplier=1, compare_op=ALU.is_equal, fill=0.0))
    p.add("gpsimd", lambda g: g.memset(tri01[:], 1.0))
    p.add("gpsimd", lambda g: g.affine_select(
        out=tri01[:], in_=tri01[:], pattern=[[-1, 128]], base=0,
        channel_multiplier=1, compare_op=ALU.is_ge, fill=0.0))
    p.add("gpsimd", lambda g: g.memset(ones_r[:], 1.0))
    p.add("gpsimd", lambda g: g.memset(halfs_r[:], 0.5))
    p.add("gpsimd", lambda g: g.memset(ones_rb[:], 1.0))
    p.add("gpsimd", lambda g: g.memset(ones_cf[:], 1.0))
    ev_cn = p.add("gpsimd", lambda g: g.memset(ones_c[:], 1.0), inc=(s_ge, 1))
    st["marks"] = dict(ev_sx=ev_sx, ev_cn=ev_cn)

    evc = None
    for c in range(NCH):
        sl = slice(c * S, (c + 1) * S)
        evc = p.add("vector",
                    lambda v, sl=sl: v.tensor_copy(out=xb[:, sl], in_=xs[:, sl]),
                    waits=(ev_sx,), inc=(s_ve, 1))
    ev = [evc]

    ev = _attention(st, ev, 0)
    if stop != "a0res":
        ev = _layernorm(st, ev)
        if stop != "x1":
            ev = _ffn(st, ev, 0)
            if stop != "f0res":
                ev = _layernorm(st, ev)
                if stop != "x2":
                    ev = _attention(st, ev, 1)
                    if stop != "a1res":
                        ev = _layernorm(st, ev)
                        if stop != "x3":
                            ev = _ffn(st, ev, 1)
                            if stop != "f1res":
                                ev = _layernorm(st, ev)

    s_out = sems["s_out"]
    for c in range(NCH):
        p.dma("sync", out=y[c], in_=xs[:, c * S:(c + 1) * S],
              waits=tuple(ev), inc=(s_out, 16))
    tot = s_out.n
    p.add("sync", None, waits=((s_out, tot),))
    p.add("gpsimd", lambda g: g.nop(), waits=((s_out, tot),))
    with nc.Block() as block:
        p.emit(block)
    ctx.close()
    return nc


def _attention(st, ev_in, layer):
    nc, p = st["nc"], st["p"]
    SM, dt = st["SM"], st["dt"]
    xs, xb = st["xs"], st["xb"]
    smalls, scal, brow, bvrow = st["smalls"], st["scal"], st["brow"], st["bvrow"]
    ones_r, ones_rb, ones_c = st["ones_r"], st["ones_rb"], st["ones_c"]
    ones_cf = st["ones_cf"]
    halfs_r = st["halfs_r"]
    ident, tri01 = st["ident"], st["tri01"]
    sems = st["sems"]
    s_mm, s_ve, s_se, s_ge = sems["s_mm"], sems["s_ve"], sems["s_se"], sems["s_ge"]
    s_hw = [sems["s_hw0"], sems["s_hw1"]]
    s_brd = [sems["s_br0"], sems["s_br1"]]
    s_qn, s_rro, s_wv, s_ah = sems["s_qn"], sems["s_rro"], sems["s_wv"], sems["s_ah"]
    swslot = [sems["s_wA"], sems["s_wB"]]
    tr = st["track"]
    L0 = layer == 0
    wname = "l0" if L0 else "l1"
    chunks_fn = l0_chunks if L0 else l1_chunks
    ev_cn, ev_sx = st["marks"]["ev_cn"], st["marks"]["ev_sx"]

    ph = ExitStack()
    ee = ph.enter_context
    vts = ee(nc.sbuf_tensor(f"vts{layer}", [128, NT * 1024], BF16))
    pc = [ee(nc.psum_tensor(f"pc{layer}{i}", [128, 512], F32)) for i in range(2)]
    prp = ee(nc.psum_tensor(f"prp{layer}", [128, 512], F32))
    ptA = ee(nc.psum_tensor(f"ptA{layer}", [128, 1024], BF16))
    ptB = ee(nc.psum_tensor(f"ptB{layer}", [128, 1024], BF16))
    ppv = [ee(nc.psum_tensor(f"ppv{layer}{i}", [128, 512], F32)) for i in range(2)]
    pbc = ee(nc.psum_tensor(f"pbc{layer}", [128, 512], F32))

    bank_free = [None, None]
    pbc_busy = None
    prp_busy = None

    ev_bv = p.dma("gpsimd", out=bvrow[:], in_=dt[wname + "_bv"],
                  waits=(tr["bv_rd"],), inc=(s_wv, 16))

    with nc.sbuf_tensor(f"wvs{layer}", [128, 8 * 1024], BF16) as wvs:
        p.dma("gpsimd", out=wvs[:].rearrange("p (c n) -> p c n", n=1024),
              in_=dt[wname + "_wv"].transpose([1, 0, 2]), inc=(s_wv, 16))
        ev_wv = (s_wv, s_wv.n)
        ev_vts = None
        for tt in range(NT):
            for ng in range(2):
                bank = (tt * 2 + ng) % 2
                b = pc[bank]
                p.add("tensor", lambda t, b=b, ng=ng: t.matmul(
                    b[:, 0:512], ones_rb[0:1, :],
                    bvrow[0:1, ng * 512:(ng + 1) * 512], start=True, stop=False),
                    waits=(ev_wv, ev_cn, bank_free[bank]) + tuple(ev_in),
                    inc=(s_mm, 1))
                ev_mv = None
                for kc in range(8):
                    ev_mv = p.add("tensor", lambda t, b=b, kc=kc, tt=tt, ng=ng: t.matmul(
                        b[:, 0:512],
                        xb[:, kc * S + tt * 128: kc * S + tt * 128 + 128],
                        wvs[:, kc * 1024 + ng * 512: kc * 1024 + (ng + 1) * 512],
                        start=False, stop=(kc == 7)), inc=(s_mm, 1))
                ev_vts = p.add("vector", lambda v, b=b, tt=tt, ng=ng: v.tensor_copy(
                    out=vts[:, tt * 1024 + ng * 512: tt * 1024 + (ng + 1) * 512],
                    in_=b[:, 0:512]), waits=(ev_mv,), inc=(s_ve, 1))
                bank_free[bank] = ev_vts
        tr["bv_rd"] = (s_mm, s_mm.n)
        ev_vdone = (s_mm, s_mm.n)

    sub = ExitStack()
    se = sub.enter_context
    wqs = [se(nc.sbuf_tensor(f"wqs{layer}{i}", [128, 2048], BF16)) for i in range(2)]
    qat = se(nc.sbuf_tensor(f"qat{layer}", [128, S], BF16))
    kat = se(nc.sbuf_tensor(f"kat{layer}", [128, S], BF16))
    sq2 = se(nc.sbuf_tensor(f"sq2{layer}", [128, S], BF16))
    P = [se(nc.sbuf_tensor(f"P{layer}{i}", [128, S], BF16)) for i in range(3)]
    PTs = se(nc.sbuf_tensor(f"PTs{layer}", [128, 8192], BF16))
    sums = se(nc.sbuf_tensor(f"sums{layer}", [128, 128], F32))
    ssml = se(nc.sbuf_tensor(f"ssml{layer}", [128, 64], F32))
    qn16 = se(nc.sbuf_tensor(f"qn16{layer}", [128, 16], F32))
    qrow = se(nc.sbuf_tensor(f"qrow{layer}", [1, 512], F32))
    Pf = se(nc.sbuf_tensor(f"Pf{layer}", [128, 2 * S], F32))
    i8 = se(nc.sbuf_tensor(f"i8{layer}", [128, 48], F32))
    if L0:
        wr2m = se(nc.sbuf_tensor("wr2m", [128, 257], BF16))
        wrf32 = se(nc.sbuf_tensor("wrf32", [64, 257], F32))
        rpe = [se(nc.sbuf_tensor(f"rpe{i}", [128, 513], F32)) for i in range(2)]
        bb0 = se(nc.sbuf_tensor("bb0", [128, 384], F32))
        bb = [bb0, bb0]
        sband = [pbc, pbc]
    else:
        Pd = se(nc.sbuf_tensor("Pd", [128, 128], F32))
        attn1 = se(nc.sbuf_tensor("attn1", [128, S], BF16))

    if L0:
        s_trx = sems["s_tr"]
        p.dma("gpsimd", out=wr2m[0:64, :], in_=dt["l0_wrT"], waits=(ev_vdone,),
              inc=(s_trx, 16))
        p.dma("gpsimd", out=wr2m[64:128, :], in_=dt["l0_wrT"], inc=(s_trx, 16))
        p.dma("gpsimd", out=wrf32[0:64, :], in_=dt["l0_wrT"], inc=(s_trx, 16))
        ev_wrl = (s_trx, s_trx.n)
        ev_q = p.add("scalar", lambda s: s.activation(
            out=wrf32[0:64, :], in_=wrf32[0:64, :], func=AF.Square),
            waits=(ev_wrl,), inc=(s_se, 1))
        ev_m2 = p.add("tensor", lambda t: t.matmul(
            pc[0][0:1, 0:257], ones_cf[0:64, 0:1], wrf32[0:64, :],
            start=True, stop=True), waits=(ev_q, ev_cn, bank_free[0]), inc=(s_mm, 1))
        ev_wm = p.add("vector", lambda v: v.reduce_max(
            out=scal[0:1, 0:1], in_=pc[0][0:1, 0:257], axis=AX.X),
            waits=(ev_m2,), inc=(s_ve, 1))
        bank_free[0] = ev_wm
        ev_ws = p.add("scalar", lambda s: s.activation(
            out=scal[0:1, 1:2], in_=scal[0:1, 0:1], func=AF.Sqrt),
            waits=(ev_wm,), inc=(s_se, 1))
        ev_nb = p.add("vector", lambda v: v.reduce_max(
            out=scal[0:1, 3:4], in_=brow[0:1, :], axis=AX.X,
            apply_absolute_value=True, negate=True), waits=(ev_sx,),
            inc=(s_ve, 1))
        ev_b1 = p.add("tensor", lambda t: t.matmul(
            pbc[:, 300:301], halfs_r[0:1, :], scal[0:1, 3:4], start=True, stop=True),
            waits=(ev_nb,), inc=(s_mm, 1))
        ev_c3 = p.add("vector", lambda v: v.tensor_copy(
            out=smalls[:, SM["NBR"]:SM["NBR"] + 1], in_=pbc[:, 300:301]),
            waits=(ev_b1,), inc=(s_ve, 1))
        pbc_busy = ev_c3
        ev_consts = ev_c3
        scal_sqrt_rd = None
    else:
        ev_wrl = ev_ws = ev_consts = None
        scal_sqrt_rd = None

    ptA_free = None
    ptB_free = None
    p_free = [None, None, None]
    sums_free = None
    sq2_free = None
    rpe_hw = [None, None]
    bb_free = [None, None]
    sband_free = [None, None]
    hb_rd = [None, None]
    pt_region_pv = [None] * 4
    ppv_free = [None, None]
    pf_free = [None, None]
    i8_free = [None, None, None]
    qk_free = None
    qrow_rd = None
    scal_rd = None
    ev_last = None

    wslot_ev = [None, None]
    wslot_ev[0] = p.dma("gpsimd",
                        out=wqs[0][:].rearrange("p (a c m) -> p a c m", a=2, m=128),
                        in_=dt[wname + "_wqk"][0].transpose([2, 0, 1, 3]),
                        waits=(tr["wslot_free"][0], ev_vdone), inc=(swslot[0], 16))

    for pr in range(8):
        slot = pr % 2
        if pr + 1 < 8:
            ns = (pr + 1) % 2
            wslot_ev[ns] = p.dma(
                "gpsimd",
                out=wqs[ns][:].rearrange("p (a c m) -> p a c m", a=2, m=128),
                in_=dt[wname + "_wqk"][pr + 1].transpose([2, 0, 1, 3]),
                waits=(tr["wslot_free"][ns], ev_vdone), inc=(swslot[ns], 16))

        first = True
        ev_mv = None
        ev_ev = None
        for qk in range(2):
            dst = qat if qk == 0 else kat
            bcol = (SM["L0BQ" if L0 else "L1BQ"] if qk == 0
                    else SM["L0BK" if L0 else "L1BK"]) + pr
            for ng in range(4):
                bank = (qk * 4 + ng) % 2
                b = pc[bank]
                for kc in range(8):
                    wts = [bank_free[bank]] if kc == 0 else []
                    if first:
                        wts += [wslot_ev[slot], qk_free,
                                (s_se, s_se.n), (s_ve, s_ve.n)] + list(ev_in)
                        first = False
                    ev_mv = p.add("tensor", lambda t, b=b, slot=slot, qk=qk, kc=kc, ng=ng: t.matmul(
                        b[:, 0:512],
                        wqs[slot][:, qk * 1024 + kc * 128: qk * 1024 + (kc + 1) * 128],
                        xb[:, kc * S + ng * 512: kc * S + (ng + 1) * 512],
                        start=(kc == 0), stop=(kc == 7)),
                        waits=wts, inc=(s_mm, 1))
                ev_ev = p.add("scalar", lambda s, b=b, dst=dst, ng=ng, bcol=bcol: s.activation(
                    out=dst[:, ng * 512:(ng + 1) * 512], in_=b[:, 0:512],
                    func=AF.Identity, bias=smalls[:, bcol:bcol + 1], scale=1.0),
                    waits=(ev_sx, ev_mv), inc=(s_se, 1))
                bank_free[bank] = ev_ev
        ev_qk = ev_ev
        tr["wslot_free"][slot] = (s_mm, s_mm.n)

        ev_kvar = ev_qk

        for hh in range(2):
            h = 2 * pr + hh
            hb = hh * 64
            hsl = slice(hb, hb + 64)

            ev_k2 = p.add("scalar", lambda s, hsl=hsl: s.activation(
                out=sq2[hsl, :], in_=kat[hsl, :], func=AF.Square),
                waits=(ev_qk, sq2_free), inc=(s_se, 1))
            kn_ev = None
            for g in range(4):
                ev_mk = p.add("tensor", lambda t, g=g, hsl=hsl: t.matmul(
                    pc[0][0:1, 0:512], ones_c[hsl, 0:1],
                    sq2[hsl, g * 512:(g + 1) * 512], start=True, stop=True),
                    waits=(ev_k2, bank_free[0], kn_ev), inc=(s_mm, 1))
                kn_ev = p.add("vector", lambda v, g=g: v.reduce_max(
                    out=scal[0:1, 8 + g:9 + g], in_=pc[0][0:1, 0:512], axis=AX.X),
                    waits=(ev_mk, scal_rd), inc=(s_ve, 1))
            bank_free[0] = kn_ev
            ev_kb = p.add("gpsimd", lambda g: g.tensor_copy(
                out=scal[0:1, 16:20], in_=scal[0:1, 8:12]),
                waits=(kn_ev,), inc=(s_ge, 1))
            ev_km = p.add("vector", lambda v: v.reduce_max(
                out=scal[0:1, 12:13], in_=scal[0:1, 16:20], axis=AX.X),
                waits=(scal_sqrt_rd, ev_kb), inc=(s_ve, 1))
            ev_ks = p.add("scalar", lambda s: s.activation(
                out=scal[0:1, 13:14], in_=scal[0:1, 12:13], func=AF.Sqrt),
                waits=(ev_km,), inc=(s_se, 1))
            scal_sqrt_rd = ev_ks
            if L0:
                ev_n = p.add("vector", lambda v: v.scalar_tensor_tensor(
                    out=scal[0:1, 15:16], in0=scal[0:1, 13:14], scalar=-0.125,
                    in1=scal[0:1, 1:2], op0=ALU.mult, op1=ALU.subtract),
                    waits=(ev_ks, ev_ws), inc=(s_ve, 1))
            else:
                ev_n = p.add("vector", lambda v: v.tensor_scalar(
                    out=scal[0:1, 15:16], in0=scal[0:1, 13:14], scalar1=-0.125,
                    scalar2=None, op0=ALU.mult), waits=(ev_ks,), inc=(s_ve, 1))
            ev_bcm = p.add("tensor", lambda t: t.matmul(
                pbc[:, 0:1], halfs_r[0:1, :], scal[0:1, 15:16], start=True, stop=True),
                waits=(ev_n, pbc_busy), inc=(s_mm, 1))
            scal_rd = ev_bcm
            ev_nc1 = p.add("vector", lambda v: v.tensor_copy(
                out=smalls[:, SM["NC1"]:SM["NC1"] + 1], in_=pbc[:, 0:1]),
                waits=(ev_bcm,), inc=(s_ve, 1))
            pbc_busy = ev_nc1

            ev_q2 = p.add("scalar", lambda s, hsl=hsl: s.activation(
                out=sq2[hsl, :], in_=qat[hsl, :], func=AF.Square),
                waits=(kn_ev,), inc=(s_se, 1))
            qn_dma = None
            for g in range(4):
                ev_mq = p.add("tensor", lambda t, g=g, hsl=hsl: t.matmul(
                    pc[0][0:1, 0:512], ones_c[hsl, 0:1],
                    sq2[hsl, g * 512:(g + 1) * 512], start=True, stop=True),
                    waits=(ev_q2, bank_free[0]), inc=(s_mm, 1))
                ev_qc = p.add("vector", lambda v: v.tensor_copy(
                    out=qrow[0:1, :], in_=pc[0][0:1, 0:512]),
                    waits=(ev_mq, qrow_rd), inc=(s_ve, 1))
                bank_free[0] = ev_qc
                p.dma("sync", out=st["hbm_q"].ap(), in_=qrow[0:1, :],
                      waits=(ev_qc, tr["qn_chain"]), inc=(s_qn, 16))
                ev_qw = (s_qn, s_qn.n)
                hqap = st["hbm_q"].ap()
                qn_dma = p.dma("sync", out=qn16[:, 4 * g:4 * g + 4],
                               in_=AP(hqap.tensor, 0, [[1, 128], [128, 4]]),
                               waits=(ev_qw,), inc=(s_qn, 16), slow=True)
                tr["qn_chain"] = qn_dma
                qrow_rd = ev_qw
            sq2_free = (s_mm, s_mm.n)
            ev_sq16 = p.add("scalar", lambda s: s.activation(
                out=ssml[:, 48:64], in_=qn16[:, 0:16], func=AF.Sqrt),
                waits=(qn_dma,), inc=(s_se, 1))
            tr["qn_chain"] = ev_sq16
            ev_M = p.add("vector", lambda v: v.tensor_scalar(
                out=ssml[:, 0:16], in0=ssml[:, 48:64],
                scalar1=smalls[:, SM["NC1"]:SM["NC1"] + 1],
                scalar2=(smalls[:, SM["NBR"]:SM["NBR"] + 1] if L0 else 0.0),
                op0=ALU.mult, op1=ALU.add),
                waits=(ev_sq16, ev_nc1, ev_consts), inc=(s_ve, 1))
            ev_prep = ev_M
            ev_sz = p.add("gpsimd", lambda g: g.memset(sums[:, :], 0.0),
                          waits=(sums_free,), inc=(s_ge, 1))

            exp_last = {}
            ptev = {}
            pd_free = None
            ev_pv = None
            ev_am = None
            cyc = 0
            for t in range(NT + 2):
                if t < NT:
                    chl = chunks_fn(t)
                    tb = t % 2
                    pslot = t % 3
                    if L0:
                        p.add("tensor", lambda tn: tn.matmul(
                            prp[:, 0:257], ones_r[0:1, :], brow[0:1, 0:257],
                            start=True, stop=False),
                            waits=(ev_sx, ev_cn, prp_busy), inc=(s_mm, 1))
                        ev_rp = p.add("tensor", lambda tn, t=t, hsl=hsl: tn.matmul(
                            prp[:, 0:257], qat[hsl, t * 128:(t + 1) * 128],
                            wr2m[hsl, :], start=False, stop=True),
                            waits=(ev_prep, ev_wrl), inc=(s_mm, 1))
                        ev_rpe = p.add("scalar", lambda s, tb=tb: s.activation(
                            out=rpe[tb][:, 128:385], in_=prp[:, 0:257],
                            func=AF.Copy),
                            waits=(ev_rp, rpe_hw[tb]), inc=(s_se, 1))
                        prp_busy = ev_rpe
                        p.add("gpsimd", lambda g, tb=tb: g.tensor_copy(
                            out=rpe[tb][:, 0:128],
                            in_=rpe[tb][:, 128:129].broadcast_to([128, 128])),
                            waits=(ev_rpe,), inc=(s_ge, 1))
                        ev_fill = p.add("gpsimd", lambda g, tb=tb: g.tensor_copy(
                            out=rpe[tb][:, 385:513],
                            in_=rpe[tb][:, 384:385].broadcast_to([128, 128])),
                            inc=(s_ge, 1))
                        p.dma("sync", out=st["hbm_h"][tb].ap(), in_=rpe[tb][:],
                              waits=(ev_fill, hb_rd[tb]), inc=(s_hw[tb], 16))
                        ev_hwr = (s_hw[tb], s_hw[tb].n)
                        rpe_hw[tb] = ev_hwr
                        i0 = 128 * t
                        lo = max(0, i0 - 128)
                        wB = min(S, i0 + 256) - lo
                        off = lo - i0 + 256
                        hap = st["hbm_h"][tb].ap()
                        shear = AP(hap.tensor, off, [[512, 128], [1, wB]])
                        p.dma("sync", out=bb[tb][:, 0:wB], in_=shear,
                              waits=(ev_hwr, bb_free[tb]), inc=(s_brd[tb], 16))
                        hb_rd[tb] = (s_brd[tb], s_brd[tb].n)
                        p.add("gpsimd", lambda g, tb=tb, t=t: g.tensor_scalar(
                            out=ssml[:, 16 + t:17 + t], in0=rpe[tb][:, 128:129],
                            scalar1=0.5, scalar2=ssml[:, t:t + 1],
                            op0=ALU.mult, op1=ALU.add),
                            waits=(ev_rpe, ev_prep), inc=(s_ge, 1))
                        ev_bfar = p.add("gpsimd", lambda g, tb=tb, t=t: g.tensor_scalar(
                            out=ssml[:, 32 + t:33 + t], in0=rpe[tb][:, 384:385],
                            scalar1=0.5, scalar2=ssml[:, t:t + 1],
                            op0=ALU.mult, op1=ALU.add),
                            inc=(s_ge, 1))
                    ev_ex = None
                    pfo = (t % 2) * S
                    if L0:
                        chl = ([c for c in chl if c[0] != "B"]
                               + [c for c in chl if c[0] == "B"])
                    for ci, (kind, j0, w) in enumerate(chl):
                        bank = cyc % 2
                        cyc += 1
                        b = pc[bank]
                        ev_cm = p.add("tensor", lambda tn, b=b, j0=j0, w=w, t=t, hsl=hsl: tn.matmul(
                            b[:, 0:w], qat[hsl, t * 128:(t + 1) * 128],
                            kat[hsl, j0:j0 + w], start=True, stop=True),
                            waits=(bank_free[bank], ev_kvar, ev_prep), inc=(s_mm, 1))
                        slot_ap = sums[:, t * 8 + ci: t * 8 + ci + 1]
                        if L0 and kind == "B":
                            ev_bd = p.add("vector", lambda v, b=b, w=w, tb=tb: v.scalar_tensor_tensor(
                                out=sband[tb][:, 0:w], in0=b[:, 0:w],
                                scalar=0.125, in1=bb[tb][:, 0:w],
                                op0=ALU.mult, op1=ALU.add),
                                waits=(ev_cm, hb_rd[tb], sband_free[0],
                                       sband_free[1]),
                                inc=(s_ve, 1))
                            bank_free[bank] = ev_bd
                            bb_free[0] = ev_bd
                            bb_free[1] = ev_bd
                            ev_ex = p.add("scalar", lambda s, w=w, j0=j0, t=t, tb=tb, pfo=pfo, slot_ap=slot_ap: s.activation(
                                out=Pf[:, pfo + j0:pfo + j0 + w], in_=sband[tb][:, 0:w],
                                func=AF.Exp, bias=ssml[:, t:t + 1], scale=0.5,
                                accum_out=slot_ap),
                                waits=(ev_bd, ev_sz, pf_free[t % 2]), inc=(s_se, 1))
                            sband_free[tb] = ev_ex
                        elif (not L0) and kind == "D":
                            ev_e0 = p.add("scalar", lambda s, b=b, t=t: s.activation(
                                out=Pd[:, :], in_=b[:, 0:128], func=AF.Exp,
                                bias=ssml[:, t:t + 1], scale=0.0625),
                                waits=(ev_cm, ev_sz, pd_free, pf_free[t % 2]), inc=(s_se, 1))
                            bank_free[bank] = ev_e0
                            ev_ex = p.add("vector", lambda v, j0=j0, pfo=pfo, slot_ap=slot_ap: v.scalar_tensor_tensor(
                                out=Pf[:, pfo + j0:pfo + j0 + 128], in0=Pd[:, :],
                                scalar=1.0, in1=tri01[:, :],
                                op0=ALU.mult, op1=ALU.mult, accum_out=slot_ap),
                                waits=(ev_e0, ev_sz, ev_cn),
                                inc=(s_ve, 1))
                            pd_free = ev_ex
                        else:
                            bcoli = {"L": 16, "B": 0, "R": 32, "F": 0}[kind] + t
                            wts_x = (ev_cm, ev_sz, pf_free[t % 2])
                            if L0 and kind in ("L", "R"):
                                wts_x += (ev_bfar,)
                            ev_ex = p.add("scalar", lambda s, b=b, w=w, j0=j0, pfo=pfo, bcoli=bcoli, slot_ap=slot_ap: s.activation(
                                out=Pf[:, pfo + j0:pfo + j0 + w], in_=b[:, 0:w],
                                func=AF.Exp, bias=ssml[:, bcoli:bcoli + 1],
                                scale=0.0625, accum_out=slot_ap),
                                waits=wts_x, inc=(s_se, 1))
                            bank_free[bank] = ev_ex
                    exp_last[t] = ev_ex

                    o8 = (t % 3) * 16
                    ext = S if L0 else 128 * (t + 1)
                    if L0:
                        ev_a = p.add("vector", lambda v, t=t, o8=o8: v.reduce_sum(
                            out=i8[:, o8:o8 + 1], in_=sums[:, t * 8:(t + 1) * 8],
                            axis=AX.X), waits=(ev_ex, i8_free[t % 3]), inc=(s_ve, 1))
                    else:
                        ev_a0 = p.add("gpsimd", lambda g, t=t, o8=o8: g.tensor_copy(
                            out=i8[:, o8 + 8:o8 + 16],
                            in_=sums[:, t * 8:(t + 1) * 8]),
                            waits=(ev_ex, i8_free[t % 3]), inc=(s_ge, 1))
                        ev_a = p.add("vector", lambda v, o8=o8: v.reduce_sum(
                            out=i8[:, o8:o8 + 1], in_=i8[:, o8 + 8:o8 + 16],
                            axis=AX.X), waits=(ev_a0,), inc=(s_ve, 1))
                    ev_b = p.add("gpsimd", lambda g, o8=o8: g.tensor_scalar(
                        out=i8[:, o8 + 1:o8 + 2], in0=i8[:, o8:o8 + 1],
                        scalar1=1e-30, scalar2=None, op0=ALU.add),
                        waits=(ev_a,), inc=(s_ge, 1))
                    ev_c = p.add("vector", lambda v, o8=o8: v.reciprocal(
                        out=i8[:, o8 + 2:o8 + 3], in_=i8[:, o8 + 1:o8 + 2]),
                        waits=(ev_b,), inc=(s_ve, 1))
                    ev_c2 = p.add("gpsimd", lambda g, o8=o8: g.tensor_scalar(
                        out=i8[:, o8 + 3:o8 + 4], in0=i8[:, o8 + 2:o8 + 3],
                        scalar1=i8[:, o8 + 2:o8 + 3], scalar2=1e37,
                        op0=ALU.mult, op1=ALU.min),
                        waits=(ev_c,), inc=(s_ge, 1))
                    ev_e = p.add("vector", lambda v, o8=o8, ext=ext, pfo=pfo, pslot=pslot: v.scalar_tensor_tensor(
                        out=P[pslot][:, 0:ext], in0=Pf[:, pfo:pfo + ext],
                        scalar=i8[:, o8 + 3:o8 + 4],
                        in1=Pf[:, pfo:pfo + ext],
                        op0=ALU.mult, op1=ALU.mult,
                        accum_out=i8[:, o8 + 4:o8 + 5]),
                        waits=(ev_c2, p_free[pslot]), inc=(s_ve, 1))
                    pf_free[t % 2] = ev_e
                    ev_f = p.add("gpsimd", lambda g, o8=o8: g.tensor_scalar(
                        out=i8[:, o8 + 5:o8 + 6], in0=i8[:, o8 + 4:o8 + 5],
                        scalar1=1e-30, scalar2=None, op0=ALU.add),
                        waits=(ev_e,), inc=(s_ge, 1))
                    ev_g = p.add("vector", lambda v, o8=o8: v.reciprocal(
                        out=i8[:, o8 + 6:o8 + 7], in_=i8[:, o8 + 5:o8 + 6]),
                        waits=(ev_f,), inc=(s_ve, 1))
                    ev_h2 = p.add("gpsimd", lambda g, o8=o8: g.tensor_copy(
                        out=i8[:, o8 + 7:o8 + 8], in_=i8[:, o8 + 6:o8 + 7]),
                        waits=(ev_g,), inc=(s_ge, 1))
                    ev_i = p.add("vector", lambda v, o8=o8, ext=ext, pslot=pslot: v.tensor_scalar(
                        out=P[pslot][:, 0:ext], in0=P[pslot][:, 0:ext],
                        scalar1=i8[:, o8 + 7:o8 + 8], scalar2=None,
                        op0=ALU.mult), waits=(ev_h2, ev_e), inc=(s_ve, 1))
                    i8_free[t % 3] = ev_i
                    exp_last[t] = ev_i

                tp = t - 2
                if tp >= 0:
                    nkc = NT if L0 else (tp + 1)
                    pslot2 = tp % 3
                    evA = evB = None
                    for a in range(nkc):
                        dst2 = ptA if a < 8 else ptB
                        dsl = slice((a % 8) * 128, (a % 8) * 128 + 128)
                        wts = []
                        if a == 0:
                            wts = [exp_last[tp], ptA_free]
                        if a == 8:
                            wts = [ptB_free]
                        ev_t2 = p.add("tensor", lambda tn, dst2=dst2, dsl=dsl, a=a, pslot2=pslot2: tn.transpose(
                            dst2[:, dsl], P[pslot2][:, a * 128:(a + 1) * 128],
                            ident[:]), waits=wts, inc=(s_mm, 1))
                        if a == min(nkc, 8) - 1:
                            evA = ev_t2
                        if a == nkc - 1:
                            evB = ev_t2
                    p_free[pslot2] = evB
                    region = tp % 4
                    nA = min(nkc, 8)
                    ev_eA = p.add("vector", lambda v, region=region, nA=nA: v.tensor_copy(
                        out=PTs[:, region * 2048: region * 2048 + nA * 128],
                        in_=ptA[:, 0:nA * 128]),
                        waits=(evA, pt_region_pv[region]), inc=(s_ve, 1))
                    ptA_free = ev_eA
                    ev_eB = None
                    if nkc > 8:
                        nB = nkc - 8
                        ev_eB = p.add("vector", lambda v, region=region, nB=nB: v.tensor_copy(
                            out=PTs[:, region * 2048 + 1024: region * 2048 + 1024 + nB * 128],
                            in_=ptB[:, 0:nB * 128]),
                            waits=(evB, pt_region_pv[region]), inc=(s_ve, 1))
                        ptB_free = ev_eB
                    ptev[tp] = (ev_eA, ev_eB)

                if tp >= 3 and tp % 4 == 3:
                    g = tp // 4
                    tpos = None if hb == 0 else (0, 64)
                    pvb = ppv[hh]
                    nk = NT if L0 else (4 * g + 4)
                    ptsap = PTs[:]
                    for a in range(nk):
                        if L0 or a <= 4 * g:
                            qoff, qn = 0, 512
                        else:
                            qoff = (a - 4 * g) * 128
                            qn = 512 - qoff
                        rhs = AP(ptsap.tensor,
                                 ptsap.offset + a * 128 + (qoff // 128) * 2048,
                                 [[8192, 128], [2048, qn // 128], [1, 128]])
                        wts = []
                        if a == 0:
                            wts = [x for tpi in range(4 * g, 4 * g + 4)
                                   for x in ptev[tpi] if x]
                            wts += [ev_vts, ppv_free[hh]]
                        ev_pv = p.add("tensor", lambda tn, pvb=pvb, a=a, h=h, rhs=rhs, qoff=qoff, qn=qn, hsl=hsl, tpos=tpos, nk=nk: tn.matmul(
                            pvb[hsl, qoff:qoff + qn],
                            vts[:, a * 1024 + 64 * h: a * 1024 + 64 * h + 64],
                            rhs, start=(a == 0), stop=(a == nk - 1),
                            tile_position=tpos),
                            waits=wts, inc=(s_mm, 1))
                    for r4 in range(4):
                        pt_region_pv[r4] = ev_pv
                    if L0:
                        ev_am = p.add("vector", lambda v, g=g, hsl=hsl, pr=pr, pvb=pvb: v.scalar_tensor_tensor(
                            out=xs[:, pr * S + g * 512: pr * S + (g + 1) * 512][hsl, :],
                            in0=pvb[hsl, 0:512], scalar=0.0,
                            in1=xs[:, pr * S + g * 512: pr * S + (g + 1) * 512][hsl, :],
                            op0=ALU.add, op1=ALU.add),
                            waits=(ev_sx, ev_pv), inc=(s_ve, 1))
                    else:
                        ev_am = p.add("vector", lambda v, g=g, hsl=hsl, pvb=pvb: v.tensor_copy(
                            out=attn1[hsl, g * 512:(g + 1) * 512],
                            in_=pvb[hsl, 0:512]),
                            waits=(ev_pv,), inc=(s_ve, 1))
                    ppv_free[hh] = ev_am
                    ev_last = ev_am
                    if st["dbg"] and pr == 0 and hh == 0 and (not L0) and g == 0 \
                            and st.get("stop_") == "a1res":
                        ev_all2 = [(s_mm, s_mm.n), (s_ve, s_ve.n),
                                   (s_se, s_se.n), (s_ge, s_ge.n)]
                        for nm2, t2 in [("e_PTs", PTs), ("e_P0", P[0]),
                                        ("e_P1", P[1]), ("e_sums", sums),
                                        ("e_at1", attn1), ("e_Pf", Pf),
                                        ("e_vts", vts), ("e_i8", i8)]:
                            ap2 = t2[:]
                            od2 = nc.dram_tensor(nm2, list(ap2.shape), ap2.dtype,
                                                 kind="ExternalOutput").ap()
                            dump_ev2 = p.dma("gpsimd", out=od2, in_=ap2,
                                             waits=tuple(ev_all2),
                                             inc=(sems["s_tr"], 16))
                        for k2 in range(3):
                            p_free[k2] = dump_ev2
                            i8_free[k2] = dump_ev2
                        pf_free = [dump_ev2, dump_ev2]
                        for r42 in range(4):
                            pt_region_pv[r42] = dump_ev2
                        ppv_free[hh] = dump_ev2
            if st["dbg"] and pr == 0 and hh == 0 and (L0 == (st.get("stop_", "a0res") == "a0res")):
                ev_all = [(s_mm, s_mm.n), (s_ve, s_ve.n), (s_se, s_se.n),
                          (s_ge, s_ge.n)]
                s_dbg = sems["s_tr"]
                dump_list = [("d_qat", qat), ("d_kat", kat),
                             ("d_P0", P[0]), ("d_P1", P[1]),
                             ("d_sums", sums), ("d_ssml", ssml),
                             ("d_qn16", qn16), ("d_scal", scal),
                             ("d_smalls", smalls), ("d_PTs", PTs),
                             ("d_vts", vts), ("d_Pf", Pf),
                             ("d_i8", i8), ("d_xs0", xs[:, 0:S])]
                if L0:
                    dump_list += [("d_rpe0", rpe[0]), ("d_rpe1", rpe[1]),
                                  ("d_bb0", bb[0]), ("d_bb1", bb[1])]
                else:
                    dump_list += [("d_Pd", Pd), ("d_at1", attn1)]
                dump_ev = None
                for nm, tsr in dump_list:
                    ap = tsr if isinstance(tsr, bass.AP) else tsr[:]
                    od = nc.dram_tensor(nm, list(ap.shape), ap.dtype,
                                        kind="ExternalOutput").ap()
                    st["dbgout"][nm] = True
                    dump_ev = p.dma("gpsimd", out=od, in_=ap,
                                    waits=tuple(ev_all), inc=(s_dbg, 16))
                for k in range(3):
                    p_free[k] = dump_ev
                for k in range(2):
                    rpe_hw[k] = dump_ev
                    bb_free[k] = dump_ev
                    sband_free[k] = dump_ev
                sums_free = dump_ev
                sq2_free = dump_ev
            sums_free = (s_ve, s_ve.n)
            qk_free = (s_mm, s_mm.n)

        if not L0:
            p.dma("sync", out=st["hbm_at"][pr], in_=attn1[:, :],
                  waits=(ev_last, tr["ah_wr"][pr]), inc=(s_ah, 16))
            tr["ah_wr"][pr] = (s_ah, s_ah.n)
            ppv_free = [tr["ah_wr"][pr], tr["ah_wr"][pr]]

    if L0:
        res = [(s_ve, s_ve.n)]
        sub.close()
        ph.close()
        return res

    ev_attn_done = [(s_mm, s_mm.n), (s_ve, s_ve.n), (s_se, s_se.n)]
    sub.close()
    ob = ExitStack()
    oe = ob.enter_context
    wob = oe(nc.sbuf_tensor("wob", [128, 8 * 8 * 128], BF16))
    ats = oe(nc.sbuf_tensor("ats", [128, NCH * S], BF16))
    s_wo = sems["s_wo"]
    p.dma("gpsimd",
          out=wob[:].rearrange("p (a c m) -> p a c m", a=8, m=128),
          in_=dt["l1_wo"].transpose([2, 0, 1, 3]),
          waits=tuple(ev_attn_done), inc=(s_wo, 16))
    for kc in range(8):
        p.dma("sync", out=ats[:, kc * S:(kc + 1) * S], in_=st["hbm_at"][kc],
              waits=[(s_ah, s_ah.n)] + list(ev_attn_done), inc=(s_wo, 16))
    ev_wo = (s_wo, s_wo.n)
    bank_free = [None, None]
    ev_o = None
    for oc in range(8):
        for tg in range(4):
            bank = (oc * 4 + tg) % 2
            b = pc[bank]
            ev_m3 = None
            for kc in range(8):
                wts = [bank_free[bank], ev_wo] if kc == 0 else []
                ev_m3 = p.add("tensor", lambda t, b=b, oc=oc, tg=tg, kc=kc: t.matmul(
                    b[:, 0:512],
                    wob[:, (oc * 8 + kc) * 128:(oc * 8 + kc) * 128 + 128],
                    ats[:, kc * S + tg * 512: kc * S + (tg + 1) * 512],
                    start=(kc == 0), stop=(kc == 7)), waits=wts, inc=(s_mm, 1))
            ev_o = p.add("vector", lambda v, b=b, oc=oc, tg=tg: v.scalar_tensor_tensor(
                out=xs[:, oc * S + tg * 512: oc * S + (tg + 1) * 512],
                in0=b[:, 0:512],
                scalar=smalls[:, SM["L1BO"] + oc:SM["L1BO"] + oc + 1],
                in1=xs[:, oc * S + tg * 512: oc * S + (tg + 1) * 512],
                op0=ALU.add, op1=ALU.add),
                waits=(ev_m3, ev_sx), inc=(s_ve, 1))
            bank_free[bank] = ev_o
    res = [(s_ve, s_ve.n)]
    ob.close()
    ph.close()
    return res


def _layernorm(st, ev_in):
    nc, p = st["nc"], st["p"]
    li = _LN_N[0]
    _LN_N[0] += 1
    xs, xb = st["xs"], st["xb"]
    ones_r, ones_c = st["ones_r"], st["ones_c"]
    sems = st["sems"]
    s_mm, s_ve, s_se = sems["s_mm"], sems["s_ve"], sems["s_se"]
    s_ge = sems["s_ge"]
    ev_cn = st["marks"]["ev_cn"]

    ph = ExitStack()
    ee = ph.enter_context
    lnsq = ee(nc.sbuf_tensor(f"lnsq{li}", [128, 8 * 512], BF16))
    lrow = ee(nc.sbuf_tensor(f"lrow{li}", [1, 6 * 512], F32))
    tmp = ee(nc.sbuf_tensor(f"lntmp{li}", [128, 512], F32))
    pS = [ee(nc.psum_tensor(f"lnS{li}_{i}", [1, 512], F32)) for i in range(2)]
    pQ = [ee(nc.psum_tensor(f"lnQ{li}_{i}", [1, 512], F32)) for i in range(2)]
    pA = [ee(nc.psum_tensor(f"lnA{li}_{i}", [128, 512], F32)) for i in range(2)]
    pB = [ee(nc.psum_tensor(f"lnB{li}_{i}", [128, 512], F32)) for i in range(2)]
    M0, T0, T1, T2, CA, CB = 0, 512, 1024, 1536, 2048, 2560

    ev_cast = []
    for c in range(NCH):
        sl = slice(c * S, (c + 1) * S)
        ev_cast.append(p.add("vector",
                             lambda v, sl=sl: v.tensor_copy(out=xb[:, sl], in_=xs[:, sl]),
                             waits=tuple(ev_in), inc=(s_ve, 1)))
    stat_free = [None, None]
    ab_free = [None, None]
    lrow_rd = None
    evs = [None, None]
    for tg in range(4):
        bk = tg % 2
        ev_sq = []
        for c in range(NCH):
            ev_sq.append(p.add("scalar", lambda s, c=c, tg=tg: s.activation(
                out=lnsq[:, c * 512:(c + 1) * 512],
                in_=xs[:, c * S + tg * 512: c * S + (tg + 1) * 512],
                func=AF.Square), waits=tuple(ev_in), inc=(s_se, 1)))
        ev_ms = ev_mq = None
        for c in range(NCH):
            ev_ms = p.add("tensor", lambda t, c=c, tg=tg, bk=bk: t.matmul(
                pS[bk][0:1, :], ones_c[:, 0:1],
                xb[:, c * S + tg * 512: c * S + (tg + 1) * 512],
                start=(c == 0), stop=(c == 7)),
                waits=[ev_cast[c]] + ([stat_free[bk], ev_cn] if c == 0 else []),
                inc=(s_mm, 1))
        for c in range(NCH):
            ev_mq = p.add("tensor", lambda t, c=c, bk=bk: t.matmul(
                pQ[bk][0:1, :], ones_c[:, 0:1], lnsq[:, c * 512:(c + 1) * 512],
                start=(c == 0), stop=(c == 7)),
                waits=[ev_sq[c]] + ([stat_free[bk]] if c == 0 else []),
                inc=(s_mm, 1))
        p.add("vector", lambda v, bk=bk: v.tensor_scalar(
            out=lrow[0:1, M0:M0 + 512], in0=pS[bk][0:1, :], scalar1=1.0 / D,
            scalar2=None, op0=ALU.mult), waits=(ev_ms, lrow_rd), inc=(s_ve, 1))
        p.add("vector", lambda v, bk=bk: v.tensor_scalar(
            out=lrow[0:1, T0:T0 + 512], in0=pQ[bk][0:1, :], scalar1=1.0 / D,
            scalar2=EPS, op0=ALU.mult, op1=ALU.add), waits=(ev_mq,), inc=(s_ve, 1))
        ev3 = p.add("vector", lambda v: v.tensor_mul(
            out=lrow[0:1, T1:T1 + 512], in0=lrow[0:1, M0:M0 + 512],
            in1=lrow[0:1, M0:M0 + 512]), inc=(s_ve, 1))
        ev3b = p.add("gpsimd", lambda g: g.tensor_copy(
            out=lrow[0:1, T2:T2 + 512], in_=lrow[0:1, T1:T1 + 512]),
            waits=(ev3,), inc=(s_ge, 1))
        ev4 = p.add("vector", lambda v: v.scalar_tensor_tensor(
            out=lrow[0:1, T1:T1 + 512], in0=lrow[0:1, T2:T2 + 512], scalar=-1.0,
            in1=lrow[0:1, T0:T0 + 512], op0=ALU.mult, op1=ALU.add),
            waits=(ev3b,), inc=(s_ve, 1))
        ev5 = p.add("scalar", lambda s: s.activation(
            out=lrow[0:1, T0:T0 + 512], in_=lrow[0:1, T1:T1 + 512], func=AF.Sqrt),
            waits=(ev4,), inc=(s_se, 1))
        ev6 = p.add("vector", lambda v: v.reciprocal(
            out=lrow[0:1, CA:CA + 512], in_=lrow[0:1, T0:T0 + 512]),
            waits=(ev5,), inc=(s_ve, 1))
        ev6b = p.add("gpsimd", lambda g: g.tensor_copy(
            out=lrow[0:1, T2:T2 + 512], in_=lrow[0:1, CA:CA + 512]),
            waits=(ev6,), inc=(s_ge, 1))
        ev7 = p.add("vector", lambda v: v.scalar_tensor_tensor(
            out=lrow[0:1, CB:CB + 512], in0=lrow[0:1, M0:M0 + 512], scalar=-1.0,
            in1=lrow[0:1, T2:T2 + 512], op0=ALU.mult, op1=ALU.mult),
            waits=(ev6b,), inc=(s_ve, 1))
        stat_free[bk] = ev7
        ev_mA = p.add("tensor", lambda t, bk=bk: t.matmul(
            pA[bk][:, :], ones_r[0:1, :], lrow[0:1, CA:CA + 512],
            start=True, stop=True), waits=(ev6, ev7, ab_free[bk]), inc=(s_mm, 1))
        ev_mB = p.add("tensor", lambda t, bk=bk: t.matmul(
            pB[bk][:, :], ones_r[0:1, :], lrow[0:1, CB:CB + 512],
            start=True, stop=True), inc=(s_mm, 1))
        lrow_rd = ev_mB
        ev_ap = ev_cb = None
        for c in range(NCH):
            xsl = slice(c * S + tg * 512, c * S + (tg + 1) * 512)
            p.add("vector", lambda v, xsl=xsl, bk=bk: v.tensor_mul(
                out=tmp[:, :], in0=xs[:, xsl], in1=pA[bk][:, :]),
                waits=(ev_mA,), inc=(s_ve, 1))
            ev_ap = p.add("vector", lambda v, xsl=xsl, bk=bk: v.tensor_add(
                out=xs[:, xsl], in0=tmp[:, :], in1=pB[bk][:, :]),
                waits=(ev_mB,), inc=(s_ve, 1))
            ev_cb = p.add("scalar", lambda s, xsl=xsl: s.activation(
                out=xb[:, xsl], in_=xs[:, xsl], func=AF.Copy),
                waits=(ev_ap,), inc=(s_se, 1))
        ab_free[bk] = ev_ap
        evs = [ev_ap, ev_cb]
    ph.close()
    return evs


def _ffn(st, ev_in, layer):
    nc, p = st["nc"], st["p"]
    dt, SM = st["dt"], st["SM"]
    xs, xb = st["xs"], st["xb"]
    smalls, scal, ones_r = st["smalls"], st["scal"], st["ones_r"]
    sems = st["sems"]
    s_mm, s_ve, s_se, s_ge = sems["s_mm"], sems["s_ve"], sems["s_se"], sems["s_ge"]
    sfs = [sems["s_fA"], sems["s_fB"]]
    sgs = [sems["s_gA"], sems["s_gB"]]
    s_qs, s_qc, s_tr = sems["s_qs"], sems["s_qc"], sems["s_tr"]
    L0 = layer == 0
    QAT = not L0

    ph = ExitStack()
    ee = ph.enter_context

    if QAT:
        with ExitStack() as am:
            ae = am.enter_context
            qsl = [ae(nc.sbuf_tensor(f"qsl{i}", [128, 2048], F32)) for i in range(2)]
            amax = ae(nc.sbuf_tensor("amax", [128, 32], F32))
            arow = ae(nc.sbuf_tensor("arow", [1, 128], F32))
            pk67 = [ae(nc.psum_tensor(f"pam{i}", [128, 512], F32)) for i in range(2)]
            slot_rd = [None, None]
            for wi, wname in enumerate(["l1_wfc", "l1_wproj"]):
                wap = dt[wname]
                flat = AP(wap.tensor, 0, [[2048, 128], [1, 2048]])
                red_evs = []
                for sl in range(16):
                    ss = sl % 2
                    src = AP(wap.tensor, sl * 128 * 2048, [[2048, 128], [1, 2048]])
                    ev_l = p.dma("sync", out=qsl[ss][:], in_=src,
                                 waits=(slot_rd[ss],) + (tuple(ev_in) if sl < 2 and wi == 0 else ()),
                                 inc=(s_qs, 16))
                    ev_r = p.add("vector", lambda v, ss=ss, sl=sl: v.reduce_max(
                        out=amax[:, sl:sl + 1], in_=qsl[ss][:], axis=AX.X,
                        apply_absolute_value=True), waits=(ev_l,), inc=(s_ve, 1))
                    slot_rd[ss] = ev_r
                    red_evs.append(ev_r)
                ev_cb0 = p.add("gpsimd", lambda g: g.tensor_copy(
                    out=amax[:, 16:32], in_=amax[:, 0:16]),
                    waits=(red_evs[-1],), inc=(s_ge, 1))
                ev_c = p.add("vector", lambda v: v.reduce_max(
                    out=amax[:, 0:1], in_=amax[:, 16:32], axis=AX.X),
                    waits=(ev_cb0,), inc=(s_ve, 1))
                ev_t = p.dma("sync", out=arow[0:1, :], in_=amax[:, 0:1],
                             waits=(ev_c, st["track"]["tr_chain"]),
                             inc=(s_tr, 16), slow=True)
                st["track"]["tr_chain"] = ev_t
                ev_m = p.add("vector", lambda v, wi=wi: v.reduce_max(
                    out=scal[0:1, 20 + 4 * wi:21 + 4 * wi], in_=arow[0:1, :],
                    axis=AX.X), waits=(ev_t,), inc=(s_ve, 1))
                ev_s = p.add("gpsimd", lambda g, wi=wi: g.tensor_scalar(
                    out=scal[0:1, 21 + 4 * wi:22 + 4 * wi],
                    in0=scal[0:1, 20 + 4 * wi:21 + 4 * wi],
                    scalar1=1.0 / 127.0, scalar2=None, op0=ALU.mult),
                    waits=(ev_m,), inc=(s_ge, 1))
                ev_rs = p.add("vector", lambda v, wi=wi: v.reciprocal(
                    out=scal[0:1, 22 + 4 * wi:23 + 4 * wi],
                    in_=scal[0:1, 21 + 4 * wi:22 + 4 * wi]),
                    waits=(ev_s,), inc=(s_ve, 1))
                cS = SM["SFC"] if wi == 0 else SM["SPJ"]
                cR = SM["RSFC"] if wi == 0 else SM["RSPJ"]
                ev_b1 = p.add("tensor", lambda t, wi=wi: t.matmul(
                    pk67[0][:, 0:1], ones_r[0:1, :],
                    scal[0:1, 21 + 4 * wi:22 + 4 * wi], start=True, stop=True),
                    waits=(ev_s, ev_rs), inc=(s_mm, 1))
                ev_b2 = p.add("tensor", lambda t, wi=wi: t.matmul(
                    pk67[1][:, 0:1], ones_r[0:1, :],
                    scal[0:1, 22 + 4 * wi:23 + 4 * wi], start=True, stop=True),
                    inc=(s_mm, 1))
                p.add("vector", lambda v, cS=cS: v.tensor_copy(
                    out=smalls[:, cS:cS + 1], in_=pk67[0][:, 0:1]),
                    waits=(ev_b1,), inc=(s_ve, 1))
                ev_cp = p.add("vector", lambda v, cR=cR: v.tensor_copy(
                    out=smalls[:, cR:cR + 1], in_=pk67[1][:, 0:1]),
                    waits=(ev_b2,), inc=(s_ve, 1))
            cal = ae(nc.sbuf_tensor("cal", [1, 8], F32))
            cali = ae(nc.sbuf_tensor("cali", [1, 2], I32))
            ev_z0 = p.add("gpsimd", lambda g: g.memset(cal[0:1, 0:1], 1.5),
                          inc=(s_ge, 1))
            ev_z1 = p.add("vector", lambda v: v.tensor_copy(
                out=cali[0:1, 0:1], in_=cal[0:1, 0:1]), waits=(ev_z0,),
                inc=(s_ve, 1))
            ev_z2 = p.add("gpsimd", lambda g: g.tensor_copy(
                out=cali[0:1, 1:2], in_=cali[0:1, 0:1]), waits=(ev_z1,),
                inc=(s_ge, 1))
            ev_z3 = p.add("vector", lambda v: v.tensor_scalar(
                out=cal[0:1, 2:3], in0=cali[0:1, 1:2], scalar1=-0.5,
                scalar2=257.0, op0=ALU.mult, op1=ALU.add), waits=(ev_z2,),
                inc=(s_ve, 1))
            ev_z4 = p.add("tensor", lambda t: t.matmul(
                pk67[0][:, 4:5], ones_r[0:1, :], cal[0:1, 2:3],
                start=True, stop=True), waits=(ev_z3,), inc=(s_mm, 1))
            ev_hb = p.add("vector", lambda v: v.tensor_copy(
                out=smalls[:, SM["HB"]:SM["HB"] + 1], in_=pk67[0][:, 4:5]),
                waits=(ev_z4,), inc=(s_ve, 1))
            ev_scales = ev_hb
    else:
        ev_scales = None

    hbuf = ee(nc.sbuf_tensor(f"hbuf{layer}", [128, 32 * 1024], BF16))
    if QAT:
        w1s = [ee(nc.sbuf_tensor(f"w1f{i}", [128, 1024], F32)) for i in range(2)]
        wqs1 = [ee(nc.sbuf_tensor(f"wq1{i}", [128, 1024], BF16)) for i in range(2)]
        qtmp = ee(nc.sbuf_tensor("qtmp", [128, 1024], F32))
        qint = ee(nc.sbuf_tensor("qint", [128, 1024], I32))
        blo_all = ee(nc.sbuf_tensor("blo_all", [16, 32 * 128], BF16))
        afc = ee(nc.sbuf_tensor("afc", [128, 8 * 16], BF16))
        apj = ee(nc.sbuf_tensor("apj", [128, 32 * 16], BF16))
        bpl = ee(nc.sbuf_tensor("bpl", [16, 8 * 128], BF16))
        lrx = ee(nc.sbuf_tensor("lrx", [16, 1024], BF16))
        lr2x = ee(nc.sbuf_tensor("lr2x", [16, 1024], BF16))
        w2q = [ee(nc.sbuf_tensor(f"w2q{i}", [128, 512], F32)) for i in range(2)]
    else:
        w1s = [ee(nc.sbuf_tensor(f"w1b{i}", [128, 1024], BF16)) for i in range(2)]
    w2s = [ee(nc.sbuf_tensor(f"w2s{layer}_{i}", [128, 512], BF16)) for i in range(2)]
    pk = [ee(nc.psum_tensor(f"pk{layer}{i}", [128, 512], F32)) for i in range(8)]

    bank_free = [None] * 8
    w1_rd = [None, None]
    w2_rd = [None, None]
    w1_cw = [None, None]
    w2_cw = [None, None]
    hb_free = None
    lrx_rd = None
    lr2x_rd = None
    ev_out = list(ev_in)

    b1c = SM["L0B1"] if L0 else SM["L1BFC"]
    b2c = SM["L0B2"] if L0 else SM["L1BPJ"]

    if QAT:
        s_lo = sems["s_rro"]
        p.dma("gpsimd", out=afc[:].rearrange("p (c n) -> p c n", n=16),
              in_=dt["l1_afc"].transpose([1, 0, 2]),
              waits=tuple(ev_in), inc=(s_lo, 16))
        p.dma("gpsimd", out=apj[:].rearrange("p (c n) -> p c n", n=16),
              in_=dt["l1_aproj"].transpose([1, 0, 2]), inc=(s_lo, 16))
        p.dma("gpsimd", out=bpl[:].rearrange("p (c n) -> p c n", n=128),
              in_=dt["l1_bproj_lo"].transpose([1, 0, 2]), inc=(s_lo, 16))
        p.dma("gpsimd", out=blo_all[:].rearrange("p (c n) -> p c n", n=128),
              in_=dt["l1_bfc_lo"].transpose([1, 0, 2]), inc=(s_lo, 16))
        ev_lores = (s_lo, sems["s_rro"].n)

    for th in range(2):
        tok0 = th * 1024
        if QAT:
            for ng in range(2):
                ev_ml = None
                for kc in range(8):
                    wts = []
                    if kc == 0:
                        wts = [ev_lores, bank_free[2], lrx_rd] + list(ev_in)
                    ev_ml = p.add("tensor", lambda t, kc=kc, ng=ng, tok0=tok0: t.matmul(
                        pk[2][0:16, 0:512], afc[:, kc * 16:(kc + 1) * 16],
                        xb[:, kc * S + tok0 + ng * 512: kc * S + tok0 + (ng + 1) * 512],
                        start=(kc == 0), stop=(kc == 7)), waits=wts, inc=(s_mm, 1))
                ev_le = p.add("scalar", lambda sc, ng=ng: sc.activation(
                    out=lrx[0:16, ng * 512:(ng + 1) * 512], in_=pk[2][0:16, 0:512],
                    func=AF.Identity, scale=2.0, bias=0.0),
                    waits=(ev_ml,), inc=(s_se, 1))
                bank_free[2] = ev_le
            ev_lrx = ev_le

        for ic in range(32):
            sl = ic % 2
            if QAT and th == 0:
                ev_w = p.dma("gpsimd", out=w1s[sl][:].rearrange("p (c m) -> p c m", m=128),
                             in_=dt["l1_wfc"][ic].transpose([1, 0, 2]),
                             waits=(w1_rd[sl], ev_scales), inc=(sfs[sl], 16))
                ev_q1 = p.add("vector", lambda v, sl=sl: v.tensor_scalar(
                    out=qtmp[:], in0=w1s[sl][:],
                    scalar1=smalls[:, SM["RSFC"]:SM["RSFC"] + 1],
                    scalar2=smalls[:, SM["HB"]:SM["HB"] + 1],
                    op0=ALU.mult, op1=ALU.add), waits=(ev_w,), inc=(s_ve, 1))
                w1_rd[sl] = ev_q1
                p.add("vector", lambda v: v.tensor_copy(
                    out=qint[:], in_=qtmp[:]), inc=(s_ve, 1))
                ev_q4 = p.add("vector", lambda v, sl=sl: v.tensor_scalar(
                    out=wqs1[sl][:], in0=qint[:], scalar1=-256.0,
                    scalar2=smalls[:, SM["SFC"]:SM["SFC"] + 1],
                    op0=ALU.add, op1=ALU.mult),
                    waits=(w1_cw[sl],), inc=(s_ve, 1))
                ev_cw = p.dma("sync", out=st["hbm_wfq"].ap()[ic], in_=wqs1[sl][:],
                              waits=(ev_q4,), inc=(s_qc, 16))
                w1_cw[sl] = (sems["s_qc"], sems["s_qc"].n)
                wsrc = wqs1[sl]
                ev_wrdy = ev_q4
            elif QAT:
                ev_w = p.dma("sync", out=wqs1[sl][:], in_=st["hbm_wfq"].ap()[ic],
                             waits=(w1_rd[sl], (s_qc, sems["s_qc"].n)),
                             inc=(sfs[sl], 16))
                ev_wrdy = (sfs[sl], sfs[sl].n)
                wsrc = wqs1[sl]
            else:
                p.dma("gpsimd", out=w1s[sl][:].rearrange("p (c m) -> p c m", m=128),
                      in_=dt["l0_w1"][ic].transpose([1, 0, 2]),
                      waits=(w1_rd[sl],) + (tuple(ev_in) if ic < 2 and th == 0 else ()),
                      inc=(sfs[sl], 16))
                ev_wrdy = (sfs[sl], sfs[sl].n)
                wsrc = w1s[sl]
            for ng in range(2):
                bank = ng
                ev_m1 = None
                first = True
                if QAT:
                    ev_m1 = p.add("tensor", lambda t, ic=ic, ng=ng: t.matmul(
                        pk[ng][:, 0:512],
                        blo_all[0:16, ic * 128:(ic + 1) * 128],
                        lrx[0:16, ng * 512:(ng + 1) * 512],
                        start=True, stop=False),
                        waits=(ev_wrdy, ev_lrx, bank_free[ng], hb_free, ev_lores),
                        inc=(s_mm, 1))
                    first = False
                for kc in range(8):
                    wts = []
                    if first and kc == 0:
                        wts = [ev_wrdy, bank_free[ng], hb_free]
                    ev_m1 = p.add("tensor", lambda t, wsrc=wsrc, kc=kc, ng=ng, tok0=tok0: t.matmul(
                        pk[ng][:, 0:512], wsrc[:, kc * 128:(kc + 1) * 128],
                        xb[:, kc * S + tok0 + ng * 512: kc * S + tok0 + (ng + 1) * 512],
                        start=(kc == 0 and not QAT), stop=(kc == 7)),
                        waits=wts, inc=(s_mm, 1))
                w1_rd[sl] = ev_m1
                ev_h = p.add("scalar", lambda sc, ic=ic, ng=ng, b1c=b1c: sc.activation(
                    out=hbuf[:, ic * 1024 + ng * 512: ic * 1024 + (ng + 1) * 512],
                    in_=pk[ng][:, 0:512], func=(AF.Relu if L0 else AF.Gelu),
                    bias=smalls[:, b1c + ic:b1c + ic + 1], scale=1.0),
                    waits=(ev_m1,), inc=(s_se, 1))
                bank_free[ng] = ev_h
        ev_h_done = ev_h

        if QAT:
            for sg in range(2):
                ev_ml2 = None
                for kc in range(32):
                    wts = []
                    if kc == 0:
                        wts = [ev_h_done, bank_free[3], lr2x_rd, ev_lores]
                    ev_ml2 = p.add("tensor", lambda t, kc=kc, sg=sg: t.matmul(
                        pk[3][0:16, 0:512], apj[:, kc * 16:(kc + 1) * 16],
                        hbuf[:, kc * 1024 + sg * 512: kc * 1024 + (sg + 1) * 512],
                        start=(kc == 0), stop=(kc == 31)), waits=wts, inc=(s_mm, 1))
                ev_l2 = p.add("scalar", lambda sc, sg=sg: sc.activation(
                    out=lr2x[0:16, sg * 512:(sg + 1) * 512], in_=pk[3][0:16, 0:512],
                    func=AF.Identity, scale=2.0, bias=0.0),
                    waits=(ev_ml2,), inc=(s_se, 1))
                bank_free[3] = ev_l2
            ev_lr2 = ev_l2

        for half in range(2):
            if QAT:
                for oc in range(4):
                    for sg in range(2):
                        bank = oc * 2 + sg
                        p.add("tensor", lambda t, half=half, oc=oc, sg=sg, bank=bank: t.matmul(
                            pk[bank][:, 0:512],
                            bpl[0:16, (half * 4 + oc) * 128:(half * 4 + oc) * 128 + 128],
                            lr2x[0:16, sg * 512:(sg + 1) * 512],
                            start=True, stop=False),
                            waits=(ev_lr2, ev_lores, bank_free[bank]),
                            inc=(s_mm, 1))
            for kc in range(32):
                sl = kc % 2
                if QAT and th == 0:
                    ev_w2 = p.dma("sync", out=w2q[sl][:].rearrange("p (a m) -> p a m", m=128),
                                  in_=dt["l1_wproj"][half][kc].rearrange("a p m -> p a m"),
                                  waits=(w2_rd[sl], ev_scales), inc=(sgs[sl], 16))
                    ev_q1 = p.add("vector", lambda v, sl=sl: v.tensor_scalar(
                        out=qtmp[:, 0:512], in0=w2q[sl][:],
                        scalar1=smalls[:, SM["RSPJ"]:SM["RSPJ"] + 1],
                        scalar2=smalls[:, SM["HB"]:SM["HB"] + 1],
                        op0=ALU.mult, op1=ALU.add), waits=(ev_w2,), inc=(s_ve, 1))
                    w2_rd[sl] = ev_q1
                    p.add("vector", lambda v: v.tensor_copy(
                        out=qint[:, 0:512], in_=qtmp[:, 0:512]), inc=(s_ve, 1))
                    ev_q4 = p.add("vector", lambda v, sl=sl: v.tensor_scalar(
                        out=w2s[sl][:], in0=qint[:, 0:512], scalar1=-256.0,
                        scalar2=smalls[:, SM["SPJ"]:SM["SPJ"] + 1],
                        op0=ALU.add, op1=ALU.mult),
                        waits=(w2_cw[sl],), inc=(s_ve, 1))
                    p.dma("sync", out=st["hbm_wpq"].ap()[half][kc], in_=w2s[sl][:],
                          waits=(ev_q4,), inc=(s_qc, 16))
                    w2_cw[sl] = (sems["s_qc"], sems["s_qc"].n)
                    ev_w2r = ev_q4
                elif QAT:
                    ev_w2 = p.dma("sync", out=w2s[sl][:],
                                  in_=st["hbm_wpq"].ap()[half][kc],
                                  waits=(w2_rd[sl], (s_qc, sems["s_qc"].n)),
                                  inc=(sgs[sl], 16))
                    ev_w2r = (sgs[sl], sgs[sl].n)
                else:
                    p.dma("gpsimd", out=w2s[sl][:].rearrange("p (a m) -> p a m", m=128),
                          in_=dt["l0_w2"][half][kc].transpose([1, 0, 2]),
                          waits=(w2_rd[sl],), inc=(sgs[sl], 16))
                    ev_w2r = (sgs[sl], sgs[sl].n)
                ev_m2 = None
                for oc in range(4):
                    for sg in range(2):
                        bank = oc * 2 + sg
                        wts = []
                        if oc == 0 and sg == 0:
                            wts = [ev_w2r, ev_h_done]
                            if kc == 0 and not QAT:
                                wts += [bank_free[b] for b in range(8)]
                        ev_m2 = p.add("tensor", lambda t, oc=oc, sg=sg, kc=kc, sl=sl, bank=bank: t.matmul(
                            pk[bank][:, 0:512], w2s[sl][:, oc * 128:(oc + 1) * 128],
                            hbuf[:, kc * 1024 + sg * 512: kc * 1024 + (sg + 1) * 512],
                            start=(kc == 0 and not QAT), stop=(kc == 31)),
                            waits=wts, inc=(s_mm, 1))
                w2_rd[sl] = ev_m2
            for oc in range(4):
                for sg in range(2):
                    bank = oc * 2 + sg
                    ocabs = half * 4 + oc
                    xsl = slice(ocabs * S + tok0 + sg * 512,
                                ocabs * S + tok0 + (sg + 1) * 512)
                    ev_e2 = p.add("vector", lambda v, bank=bank, ocabs=ocabs, xsl=xsl, b2c=b2c: v.scalar_tensor_tensor(
                        out=xs[:, xsl], in0=pk[bank][:, 0:512],
                        scalar=smalls[:, b2c + ocabs:b2c + ocabs + 1],
                        in1=xs[:, xsl], op0=ALU.add, op1=ALU.add),
                        waits=(ev_m2, st["marks"]["ev_sx"]), inc=(s_ve, 1))
                    bank_free[bank] = ev_e2
                    ev_out = [ev_e2]
        hb_free = (s_mm, s_mm.n)
        if QAT:
            lrx_rd = (s_mm, s_mm.n)
            lr2x_rd = (s_mm, s_mm.n)
        if st["dbg"] and QAT and th == 1 and st.get("stop_") == "f1res":
            ev_all3 = [(s_mm, s_mm.n), (s_ve, s_ve.n), (s_se, s_se.n),
                       (s_ge, s_ge.n)]
            dump_ev3 = None
            for nm3, t3 in [("f_lrx", lrx), ("f_lr2x", lr2x), ("f_hbuf", hbuf),
                            ("f_afc", afc), ("f_apj", apj), ("f_bpl", bpl),
                            ("f_wq1a", wqs1[0]), ("f_wq1b", wqs1[1]),
                            ("f_smalls", smalls), ("f_scal", scal),
                            ("f_xs", xs), ("f_xb", xb)]:
                ap3 = t3[:]
                od3 = nc.dram_tensor(nm3, list(ap3.shape), ap3.dtype,
                                     kind="ExternalOutput").ap()
                dump_ev3 = p.dma("gpsimd", out=od3, in_=ap3, waits=tuple(ev_all3),
                                 inc=(sems["s_tr"], 16))
            hb_free = dump_ev3
            lrx_rd = dump_ev3
            lr2x_rd = dump_ev3
            for bb2 in range(8):
                bank_free[bb2] = dump_ev3
            w1_rd = [dump_ev3, dump_ev3]
            w2_rd = [dump_ev3, dump_ev3]
    ph.close()
    return ev_out


def _layout_weights(inputs):
    f = np.ascontiguousarray
    w = {}
    Wqkv = inputs["L0_attn_Wqkv"]
    qr = np.concatenate([np.arange(192 * h, 192 * h + 64) for h in range(H)])
    kr = qr + 64
    vr = qr + 128
    WqT = Wqkv[qr].T
    WkT = Wqkv[kr].T
    WvT = Wqkv[vr].T
    l0_wqk = np.empty((8, 2, 8, 128, 128), np.float32)
    for pr in range(8):
        cols = slice(128 * pr, 128 * pr + 128)
        l0_wqk[pr, 0] = WqT[:, cols].reshape(8, 128, 128)
        l0_wqk[pr, 1] = WkT[:, cols].reshape(8, 128, 128)
    w["l0_wqk"] = f(l0_wqk)
    w["l0_wv"] = f(WvT.reshape(8, 128, D))
    bqkv = inputs["L0_attn_bqkv"]
    w["l0_bq"] = f(bqkv[qr].reshape(8, 128).T)
    w["l0_bk"] = f(bqkv[kr].reshape(8, 128).T)
    w["l0_bv"] = f(bqkv[vr].reshape(1, D))
    w["l0_wrT"] = f(inputs["L0_attn_Wr"].T)
    w["l0_br"] = f(inputs["L0_attn_br"].reshape(1, 257))
    W1 = inputs["L0_ffn_W1"]
    w["l0_w1"] = f(W1.reshape(8, 128, 32, 128).transpose(2, 0, 1, 3))
    w["l0_b1"] = f(inputs["L0_ffn_b1"].reshape(32, 128).T)
    W2 = inputs["L0_ffn_W2"]
    w["l0_w2"] = f(W2.reshape(32, 128, 2, 4, 128).transpose(2, 0, 3, 1, 4))
    w["l0_b2"] = f(inputs["L0_ffn_b2"].reshape(8, 128).T)
    l1_wqk = np.empty((8, 2, 8, 128, 128), np.float32)
    Wq, Wk = inputs["L1_attn_Wq"], inputs["L1_attn_Wk"]
    for pr in range(8):
        cols = slice(128 * pr, 128 * pr + 128)
        l1_wqk[pr, 0] = Wq[:, cols].reshape(8, 128, 128)
        l1_wqk[pr, 1] = Wk[:, cols].reshape(8, 128, 128)
    w["l1_wqk"] = f(l1_wqk)
    w["l1_wv"] = f(inputs["L1_attn_Wv"].reshape(8, 128, D))
    w["l1_bq"] = f(inputs["L1_attn_bq"].reshape(8, 128).T)
    w["l1_bk"] = f(inputs["L1_attn_bk"].reshape(8, 128).T)
    w["l1_bv"] = f(inputs["L1_attn_bv"].reshape(1, D))
    w["l1_wo"] = f(inputs["L1_attn_Wo"].reshape(8, 128, 8, 128).transpose(2, 0, 1, 3))
    w["l1_bo"] = f(inputs["L1_attn_bo"].reshape(8, 128).T)
    WfcT = inputs["L1_ffn_W_fc"].T
    w["l1_wfc"] = f(WfcT.reshape(8, 128, 32, 128).transpose(2, 0, 1, 3))
    w["l1_bfc"] = f(inputs["L1_ffn_b_fc"].reshape(32, 128).T)
    w["l1_afc"] = f(inputs["L1_ffn_A_fc"].T.reshape(8, 128, 16))
    w["l1_bfc_lo"] = f(inputs["L1_ffn_B_fc"].T.reshape(16, 32, 128).transpose(1, 0, 2))
    WprojT = inputs["L1_ffn_W_proj"].T
    w["l1_wproj"] = f(WprojT.reshape(32, 128, 2, 4, 128).transpose(2, 0, 3, 1, 4))
    w["l1_bproj"] = f(inputs["L1_ffn_b_proj"].reshape(8, 128).T)
    w["l1_aproj"] = f(inputs["L1_ffn_A_proj"].T.reshape(32, 128, 16))
    w["l1_bproj_lo"] = f(inputs["L1_ffn_B_proj"].T.reshape(16, 8, 128).transpose(1, 0, 2))
    return w


def kernel(_stop="full", _dbg=False, **inputs):
    x = inputs["L0_attn_x"]
    w = _layout_weights(inputs)
    nc = build_program(stop=_stop, dbg=_dbg)
    in_maps = []
    for b in range(NCORES):
        m = dict(w)
        m["x_fm"] = np.ascontiguousarray(x[b].T).reshape(NCH, 128, S)
        in_maps.append(m)
    res = run_bass_kernel_spmd(nc, in_maps, core_ids=list(range(NCORES)))
    out = np.empty((NCORES, S, D), np.float32)
    for b in range(NCORES):
        out[b] = res.results[b]["y"].reshape(D, S).T
    if _dbg:
        return out, res.results
    return out


# version 06
# speedup vs baseline: 1.2762x; 1.1645x over previous
import numpy as np
from contextlib import ExitStack
import concourse.bass as bass
import concourse.mybir as mybir
from concourse.bass_utils import run_bass_kernel_spmd

F32 = mybir.dt.float32
BF16 = mybir.dt.bfloat16
I32 = mybir.dt.int32
AF = mybir.ActivationFunctionType
ALU = mybir.AluOpType
AX = mybir.AxisListType
AP = bass.AP

NCORES = 8
S = 2048
D = 1024
H = 16
NCH = 8
NT = 16
EPS = 1e-5

ENGS = ("sync", "gpsimd", "vector", "scalar", "tensor")
OWN = {"tensor": "s_mm", "vector": "s_ve", "scalar": "s_se", "gpsimd": "s_ge"}


class Sem:
    __slots__ = ("h", "n", "name")

    def __init__(self, h, name):
        self.h = h
        self.n = 0
        self.name = name


class Prog:
    def __init__(self, nc, ctx):
        self.nc = nc
        self.ctx = ctx
        self.ops = {e: [] for e in ENGS}
        self.hw = {e: {} for e in ENGS}

    def sem(self, name):
        return Sem(self.ctx.enter_context(self.nc.semaphore(name)), name)

    def add(self, eng, fn, waits=(), inc=None):
        ws = []
        best = {}
        for w in waits:
            if w is None:
                continue
            s, v = w
            if s.name == OWN.get(eng):
                continue
            if best.get(s.name, (None, -1))[1] < v:
                best[s.name] = (s, v)
        for s, v in best.values():
            if self.hw[eng].get(s.name, -1) >= v:
                continue
            self.hw[eng][s.name] = v
            ws.append((s, v))
        rec = None
        if inc is not None:
            sem, k = inc
            sem.n += k
            rec = (sem, k)
        self.ops[eng].append((tuple(ws), fn, rec))
        if rec is not None:
            return (rec[0], rec[0].n)
        return None

    def dma(self, eng, out, in_, waits=(), inc=None, slow=False):
        return self.add(eng,
                        lambda e, o=out, i=in_, sl=slow: e.dma_start(
                            out=o, in_=i, allow_slow_non_contiguous=sl),
                        waits=waits, inc=inc)

    def emit(self, block):
        for ename in ENGS:
            oplist = self.ops[ename]

            def runner(eng, oplist=oplist):
                for ws, fn, rec in oplist:
                    for (s, v) in ws:
                        eng.wait_ge(s.h, v)
                    if fn is None:
                        continue
                    ins = fn(eng)
                    if rec is not None:
                        ins.then_inc(rec[0].h, rec[1])

            getattr(block, ename)(runner)


def l0_chunks(t):
    i0 = 128 * t
    lo = max(0, i0 - 128)
    hi = min(S, i0 + 256)
    out = []
    j = 0
    while j < lo:
        w = min(512, lo - j)
        out.append(("L", j, w))
        j += w
    out.append(("B", lo, hi - lo))
    j = hi
    while j < S:
        w = min(512, S - j)
        out.append(("R", j, w))
        j += w
    assert len(out) <= 8
    return out


def l1_chunks(t):
    i0 = 128 * t
    out = []
    j = 0
    while j < i0:
        w = min(512, i0 - j)
        out.append(("F", j, w))
        j += w
    out.append(("D", i0, 128))
    assert len(out) <= 8
    return out


_LN_N = [0]


def build_program(stop="full", dbg=False):
    _LN_N[0] = 0
    nc = bass.Bass("TRN2", target_bir_lowering=False,
                   detect_race_conditions=False)
    ctx = ExitStack()
    e = ctx.enter_context
    dt = {}

    def din(name, shape, dtype=F32):
        dt[name] = nc.dram_tensor(name, list(shape), dtype, kind="ExternalInput").ap()

    din("x_fm", [NCH, 128, S])
    y = nc.dram_tensor("y", [NCH, 128, S], F32, kind="ExternalOutput").ap()
    din("l0_wqk", [8, 2, 8, 128, 128]); din("l0_wv", [8, 128, D])
    din("l0_bq", [128, 8]); din("l0_bk", [128, 8]); din("l0_bv", [1, D])
    din("l0_wrT", [64, 257]); din("l0_br", [1, 257])
    din("l0_w1", [32, 8, 128, 128]); din("l0_b1", [128, 32])
    din("l0_w2", [2, 32, 4, 128, 128]); din("l0_b2", [128, 8])
    din("l1_wqk", [8, 2, 8, 128, 128]); din("l1_wv", [8, 128, D])
    din("l1_bq", [128, 8]); din("l1_bk", [128, 8]); din("l1_bv", [1, D])
    din("l1_wo", [8, 8, 128, 128]); din("l1_bo", [128, 8])
    din("l1_wfc", [32, 8, 128, 128]); din("l1_bfc", [128, 32])
    din("l1_afc", [8, 128, 16]); din("l1_bfc_lo", [32, 16, 128])
    din("l1_wproj", [2, 32, 4, 128, 128]); din("l1_bproj", [128, 8])
    din("l1_aproj", [32, 128, 16]); din("l1_bproj_lo", [8, 16, 128])

    hbm_h = [nc.dram_tensor(f"hshear{i}", [128, 513], F32, kind="Internal")
             for i in range(2)]
    hbm_wq = nc.dram_tensor("wprojq", [2, 32, 4, 128, 128], BF16, kind="Internal")
    hbm_at = nc.dram_tensor("attnhbm", [8, 128, S], BF16, kind="Internal")
    hbm_q = nc.dram_tensor("hbmq", [1, 512], F32, kind="Internal")
    hbm_wfq = nc.dram_tensor("wfcq", [32, 128, 1024], BF16, kind="Internal")
    hbm_wpq = nc.dram_tensor("wprojq2", [2, 32, 128, 512], BF16, kind="Internal")

    xs = e(nc.sbuf_tensor("xs", [128, NCH * S], F32))
    xb = e(nc.sbuf_tensor("xb", [128, NCH * S], BF16))
    ident = e(nc.sbuf_tensor("ident", [128, 128], BF16))
    tri01 = e(nc.sbuf_tensor("tri01", [128, 128], BF16))
    ones_r = e(nc.sbuf_tensor("ones_r", [1, 128], F32))
    halfs_r = e(nc.sbuf_tensor("halfs_r", [1, 128], F32))
    ones_rb = e(nc.sbuf_tensor("ones_rb", [1, 128], BF16))
    ones_c = e(nc.sbuf_tensor("ones_c", [128, 1], BF16))
    ones_cf = e(nc.sbuf_tensor("ones_cf", [128, 1], F32))
    smalls = e(nc.sbuf_tensor("smalls", [128, 136], F32))
    brow = e(nc.sbuf_tensor("brow", [1, 257], F32))
    bvrow = e(nc.sbuf_tensor("bvrow", [1, D], BF16))
    scal = e(nc.sbuf_tensor("scal", [1, 32], F32))

    SM = dict(L0BQ=0, L0BK=8, L0B1=16, L0B2=48, L1BQ=56, L1BK=64, L1BO=72,
              L1BFC=80, L1BPJ=112, WR8L=120, WR8R=121, NC1=122, NBR=123,
              BR0=124, BR256=125, RSFC=126, SFC=127, RSPJ=128, SPJ=129, HB=130)

    p = Prog(nc, ctx)
    sems = {k: p.sem(k) for k in
            ("s_x", "s_mm", "s_ve", "s_se", "s_ge",
             "s_wA", "s_wB", "s_hw0", "s_hw1", "s_br0", "s_br1",
             "s_qn", "s_rro", "s_out", "s_wv", "s_wo",
             "s_fA", "s_fB", "s_gA", "s_gB", "s_tr", "s_ah", "s_qs", "s_qc")}

    st = dict(stop_=stop, dbg=dbg, dbgout={}, nc=nc, p=p, ctx=ctx, dt=dt, y=y, xs=xs, xb=xb, ident=ident,
              tri01=tri01, ones_r=ones_r, halfs_r=halfs_r, ones_rb=ones_rb, ones_c=ones_c, ones_cf=ones_cf,
              smalls=smalls, brow=brow, bvrow=bvrow, scal=scal,
              hbm_h=hbm_h, hbm_wq=hbm_wq, hbm_at=hbm_at, hbm_q=hbm_q,
              hbm_wfq=hbm_wfq, hbm_wpq=hbm_wpq, sems=sems, SM=SM,
              track=dict(wslot_free=[None, None], qn_chain=None, rro_chain=None,
                         bv_rd=None, tr_chain=None, ah_wr=[None] * 8))

    s_x, s_ge, s_ve = sems["s_x"], sems["s_ge"], sems["s_ve"]
    for c in range(0, NCH, 2):
        p.dma("sync",
              out=xs[:].rearrange("p (c t) -> p c t", t=S)[:, c:c + 2, :],
              in_=dt["x_fm"][c:c + 2].transpose([1, 0, 2]), inc=(s_x, 16))

    def lsm(col, n, name):
        p.dma("sync", out=smalls[:, col:col + n], in_=dt[name], inc=(s_x, 16))

    lsm(SM["L0BQ"], 8, "l0_bq"); lsm(SM["L0BK"], 8, "l0_bk")
    lsm(SM["L0B1"], 32, "l0_b1"); lsm(SM["L0B2"], 8, "l0_b2")
    lsm(SM["L1BQ"], 8, "l1_bq"); lsm(SM["L1BK"], 8, "l1_bk")
    lsm(SM["L1BO"], 8, "l1_bo"); lsm(SM["L1BFC"], 32, "l1_bfc")
    lsm(SM["L1BPJ"], 8, "l1_bproj")
    p.dma("sync", out=brow[0:1, :], in_=dt["l0_br"], inc=(s_x, 16))
    ev_sx = (s_x, s_x.n)

    p.add("gpsimd", lambda g: g.memset(ident[:], 1.0))
    p.add("gpsimd", lambda g: g.affine_select(
        out=ident[:], in_=ident[:], pattern=[[-1, 128]], base=0,
        channel_multiplier=1, compare_op=ALU.is_equal, fill=0.0))
    p.add("gpsimd", lambda g: g.memset(tri01[:], 1.0))
    p.add("gpsimd", lambda g: g.affine_select(
        out=tri01[:], in_=tri01[:], pattern=[[-1, 128]], base=0,
        channel_multiplier=1, compare_op=ALU.is_ge, fill=0.0))
    p.add("gpsimd", lambda g: g.memset(ones_r[:], 1.0))
    p.add("gpsimd", lambda g: g.memset(halfs_r[:], 0.5))
    p.add("gpsimd", lambda g: g.memset(ones_rb[:], 1.0))
    p.add("gpsimd", lambda g: g.memset(ones_cf[:], 1.0))
    ev_cn = p.add("gpsimd", lambda g: g.memset(ones_c[:], 1.0), inc=(s_ge, 1))
    st["marks"] = dict(ev_sx=ev_sx, ev_cn=ev_cn)

    evc = None
    for c in range(NCH):
        sl = slice(c * S, (c + 1) * S)
        evc = p.add("vector",
                    lambda v, sl=sl: v.tensor_copy(out=xb[:, sl], in_=xs[:, sl]),
                    waits=(ev_sx,), inc=(s_ve, 1))
    ev = [evc]

    ev = _attention(st, ev, 0)
    if stop != "a0res":
        ev = _layernorm(st, ev)
        if stop != "x1":
            ev = _ffn(st, ev, 0)
            if stop != "f0res":
                ev = _layernorm(st, ev)
                if stop != "x2":
                    ev = _attention(st, ev, 1)
                    if stop != "a1res":
                        ev = _layernorm(st, ev)
                        if stop != "x3":
                            ev = _ffn(st, ev, 1)
                            if stop != "f1res":
                                ev = _layernorm(st, ev)

    s_out = sems["s_out"]
    for c in range(NCH):
        p.dma("sync", out=y[c], in_=xs[:, c * S:(c + 1) * S],
              waits=tuple(ev), inc=(s_out, 16))
    tot = s_out.n
    p.add("sync", None, waits=((s_out, tot),))
    p.add("gpsimd", lambda g: g.nop(), waits=((s_out, tot),))
    with nc.Block() as block:
        p.emit(block)
    ctx.close()
    return nc


def _attention(st, ev_in, layer):
    nc, p = st["nc"], st["p"]
    SM, dt = st["SM"], st["dt"]
    xs, xb = st["xs"], st["xb"]
    smalls, scal, brow, bvrow = st["smalls"], st["scal"], st["brow"], st["bvrow"]
    ones_r, ones_rb, ones_c = st["ones_r"], st["ones_rb"], st["ones_c"]
    ones_cf = st["ones_cf"]
    halfs_r = st["halfs_r"]
    ident, tri01 = st["ident"], st["tri01"]
    sems = st["sems"]
    s_mm, s_ve, s_se, s_ge = sems["s_mm"], sems["s_ve"], sems["s_se"], sems["s_ge"]
    s_hw = [sems["s_hw0"], sems["s_hw1"]]
    s_brd = [sems["s_br0"], sems["s_br1"]]
    s_qn, s_rro, s_wv, s_ah = sems["s_qn"], sems["s_rro"], sems["s_wv"], sems["s_ah"]
    swslot = [sems["s_wA"], sems["s_wB"]]
    tr = st["track"]
    L0 = layer == 0
    wname = "l0" if L0 else "l1"
    chunks_fn = l0_chunks if L0 else l1_chunks
    ev_cn, ev_sx = st["marks"]["ev_cn"], st["marks"]["ev_sx"]

    ph = ExitStack()
    ee = ph.enter_context
    vts = ee(nc.sbuf_tensor(f"vts{layer}", [128, NT * 1024], BF16))
    pc = [ee(nc.psum_tensor(f"pc{layer}{i}", [128, 512], F32)) for i in range(2)]
    prp = ee(nc.psum_tensor(f"prp{layer}", [128, 512], F32))
    ptA = ee(nc.psum_tensor(f"ptA{layer}", [128, 1024], BF16))
    ptB = ee(nc.psum_tensor(f"ptB{layer}", [128, 1024], BF16))
    ppv = [ee(nc.psum_tensor(f"ppv{layer}{i}", [128, 512], F32)) for i in range(2)]
    pbc = ee(nc.psum_tensor(f"pbc{layer}", [128, 512], F32))

    bank_free = [None, None]
    pbc_busy = None
    prp_busy = None

    ev_bv = p.dma("gpsimd", out=bvrow[:], in_=dt[wname + "_bv"],
                  waits=(tr["bv_rd"],), inc=(s_wv, 16))

    with nc.sbuf_tensor(f"wvs{layer}", [128, 8 * 1024], BF16) as wvs:
        p.dma("gpsimd", out=wvs[:].rearrange("p (c n) -> p c n", n=1024),
              in_=dt[wname + "_wv"].transpose([1, 0, 2]), inc=(s_wv, 16))
        ev_wv = (s_wv, s_wv.n)
        ev_vts = None
        for tt in range(NT):
            for ng in range(2):
                bank = (tt * 2 + ng) % 2
                b = pc[bank]
                p.add("tensor", lambda t, b=b, ng=ng: t.matmul(
                    b[:, 0:512], ones_rb[0:1, :],
                    bvrow[0:1, ng * 512:(ng + 1) * 512], start=True, stop=False),
                    waits=(ev_wv, ev_cn, bank_free[bank]) + tuple(ev_in),
                    inc=(s_mm, 1))
                ev_mv = None
                for kc in range(8):
                    ev_mv = p.add("tensor", lambda t, b=b, kc=kc, tt=tt, ng=ng: t.matmul(
                        b[:, 0:512],
                        xb[:, kc * S + tt * 128: kc * S + tt * 128 + 128],
                        wvs[:, kc * 1024 + ng * 512: kc * 1024 + (ng + 1) * 512],
                        start=False, stop=(kc == 7)), inc=(s_mm, 1))
                ev_vts = p.add("vector", lambda v, b=b, tt=tt, ng=ng: v.tensor_copy(
                    out=vts[:, tt * 1024 + ng * 512: tt * 1024 + (ng + 1) * 512],
                    in_=b[:, 0:512]), waits=(ev_mv,), inc=(s_ve, 1))
                bank_free[bank] = ev_vts
        tr["bv_rd"] = (s_mm, s_mm.n)
        ev_vdone = (s_mm, s_mm.n)

    sub = ExitStack()
    se = sub.enter_context
    wqs = [se(nc.sbuf_tensor(f"wqs{layer}{i}", [128, 2048], BF16)) for i in range(2)]
    qat = se(nc.sbuf_tensor(f"qat{layer}", [128, S], BF16))
    kat = se(nc.sbuf_tensor(f"kat{layer}", [128, S], BF16))
    sq2 = se(nc.sbuf_tensor(f"sq2{layer}", [128, S], BF16))
    P = [se(nc.sbuf_tensor(f"P{layer}{i}", [128, S], BF16)) for i in range(3)]
    PTs = se(nc.sbuf_tensor(f"PTs{layer}", [128, 8192], BF16))
    sums = se(nc.sbuf_tensor(f"sums{layer}", [128, 128], F32))
    ssml = se(nc.sbuf_tensor(f"ssml{layer}", [128, 64], F32))
    qn16 = se(nc.sbuf_tensor(f"qn16{layer}", [128, 16], F32))
    qrow = se(nc.sbuf_tensor(f"qrow{layer}", [1, 512], F32))
    Pf = se(nc.sbuf_tensor(f"Pf{layer}", [128, 2 * S], F32))
    i8 = se(nc.sbuf_tensor(f"i8{layer}", [128, 48], F32))
    if L0:
        wr2m = se(nc.sbuf_tensor("wr2m", [128, 257], BF16))
        wrf32 = se(nc.sbuf_tensor("wrf32", [64, 257], F32))
        rpe = [se(nc.sbuf_tensor(f"rpe{i}", [128, 513], F32)) for i in range(2)]
        bb0 = se(nc.sbuf_tensor("bb0", [128, 384], F32))
        bb = [bb0, bb0]
        sband = [pbc, pbc]
    else:
        Pd = se(nc.sbuf_tensor("Pd", [128, 128], F32))
        attn1 = se(nc.sbuf_tensor("attn1", [128, S], BF16))

    if L0:
        s_trx = sems["s_tr"]
        p.dma("gpsimd", out=wr2m[0:64, :], in_=dt["l0_wrT"], waits=(ev_vdone,),
              inc=(s_trx, 16))
        p.dma("gpsimd", out=wr2m[64:128, :], in_=dt["l0_wrT"], inc=(s_trx, 16))
        p.dma("gpsimd", out=wrf32[0:64, :], in_=dt["l0_wrT"], inc=(s_trx, 16))
        ev_wrl = (s_trx, s_trx.n)
        ev_q = p.add("scalar", lambda s: s.activation(
            out=wrf32[0:64, :], in_=wrf32[0:64, :], func=AF.Square),
            waits=(ev_wrl,), inc=(s_se, 1))
        ev_m2 = p.add("tensor", lambda t: t.matmul(
            pc[0][0:1, 0:257], ones_cf[0:64, 0:1], wrf32[0:64, :],
            start=True, stop=True), waits=(ev_q, ev_cn, bank_free[0]), inc=(s_mm, 1))
        ev_wm = p.add("vector", lambda v: v.reduce_max(
            out=scal[0:1, 0:1], in_=pc[0][0:1, 0:257], axis=AX.X),
            waits=(ev_m2,), inc=(s_ve, 1))
        bank_free[0] = ev_wm
        ev_ws = p.add("scalar", lambda s: s.activation(
            out=scal[0:1, 1:2], in_=scal[0:1, 0:1], func=AF.Sqrt),
            waits=(ev_wm,), inc=(s_se, 1))
        ev_nb = p.add("vector", lambda v: v.reduce_max(
            out=scal[0:1, 3:4], in_=brow[0:1, :], axis=AX.X,
            apply_absolute_value=True, negate=True), waits=(ev_sx,),
            inc=(s_ve, 1))
        ev_b1 = p.add("tensor", lambda t: t.matmul(
            pbc[:, 300:301], halfs_r[0:1, :], scal[0:1, 3:4], start=True, stop=True),
            waits=(ev_nb,), inc=(s_mm, 1))
        ev_c3 = p.add("vector", lambda v: v.tensor_copy(
            out=smalls[:, SM["NBR"]:SM["NBR"] + 1], in_=pbc[:, 300:301]),
            waits=(ev_b1,), inc=(s_ve, 1))
        pbc_busy = ev_c3
        ev_consts = ev_c3
        scal_sqrt_rd = None
    else:
        ev_wrl = ev_ws = ev_consts = None
        scal_sqrt_rd = None

    ptA_free = None
    ptB_free = None
    p_free = [None, None, None]
    sums_free = None
    sq2_free = None
    rpe_hw = [None, None]
    bb_free = [None, None]
    sband_free = [None, None]
    hb_rd = [None, None]
    pt_region_pv = [None] * 4
    ppv_free = [None, None]
    pf_free = [None, None]
    i8_free = [None, None, None]
    qk_free = None
    qrow_rd = None
    scal_rd = None
    ev_last = None

    wslot_ev = [None, None]
    wslot_ev[0] = p.dma("gpsimd",
                        out=wqs[0][:].rearrange("p (a c m) -> p a c m", a=2, m=128),
                        in_=dt[wname + "_wqk"][0].transpose([2, 0, 1, 3]),
                        waits=(tr["wslot_free"][0], ev_vdone), inc=(swslot[0], 16))

    for pr in range(8):
        slot = pr % 2
        if pr + 1 < 8:
            ns = (pr + 1) % 2
            wslot_ev[ns] = p.dma(
                "gpsimd",
                out=wqs[ns][:].rearrange("p (a c m) -> p a c m", a=2, m=128),
                in_=dt[wname + "_wqk"][pr + 1].transpose([2, 0, 1, 3]),
                waits=(tr["wslot_free"][ns], ev_vdone), inc=(swslot[ns], 16))

        first = True
        ev_mv = None
        ev_ev = None
        for qk in range(2):
            dst = qat if qk == 0 else kat
            bcol = (SM["L0BQ" if L0 else "L1BQ"] if qk == 0
                    else SM["L0BK" if L0 else "L1BK"]) + pr
            for ng in range(4):
                bank = (qk * 4 + ng) % 2
                b = pc[bank]
                for kc in range(8):
                    wts = [bank_free[bank]] if kc == 0 else []
                    if first:
                        wts += [wslot_ev[slot], qk_free,
                                (s_se, s_se.n), (s_ve, s_ve.n)] + list(ev_in)
                        first = False
                    ev_mv = p.add("tensor", lambda t, b=b, slot=slot, qk=qk, kc=kc, ng=ng: t.matmul(
                        b[:, 0:512],
                        wqs[slot][:, qk * 1024 + kc * 128: qk * 1024 + (kc + 1) * 128],
                        xb[:, kc * S + ng * 512: kc * S + (ng + 1) * 512],
                        start=(kc == 0), stop=(kc == 7)),
                        waits=wts, inc=(s_mm, 1))
                ev_ev = p.add("scalar", lambda s, b=b, dst=dst, ng=ng, bcol=bcol: s.activation(
                    out=dst[:, ng * 512:(ng + 1) * 512], in_=b[:, 0:512],
                    func=AF.Identity, bias=smalls[:, bcol:bcol + 1], scale=1.0),
                    waits=(ev_sx, ev_mv), inc=(s_se, 1))
                bank_free[bank] = ev_ev
        ev_qk = ev_ev
        tr["wslot_free"][slot] = (s_mm, s_mm.n)

        ev_kvar = ev_qk

        for hh in range(2):
            h = 2 * pr + hh
            hb = hh * 64
            hsl = slice(hb, hb + 64)

            ev_k2 = p.add("scalar", lambda s, hsl=hsl: s.activation(
                out=sq2[hsl, :], in_=kat[hsl, :], func=AF.Square),
                waits=(ev_qk, sq2_free), inc=(s_se, 1))
            kn_ev = None
            for g in range(4):
                ev_mk = p.add("tensor", lambda t, g=g, hsl=hsl: t.matmul(
                    pc[0][0:1, 0:512], ones_c[hsl, 0:1],
                    sq2[hsl, g * 512:(g + 1) * 512], start=True, stop=True),
                    waits=(ev_k2, bank_free[0], kn_ev), inc=(s_mm, 1))
                kn_ev = p.add("vector", lambda v, g=g: v.reduce_max(
                    out=scal[0:1, 8 + g:9 + g], in_=pc[0][0:1, 0:512], axis=AX.X),
                    waits=(ev_mk, scal_rd), inc=(s_ve, 1))
            bank_free[0] = kn_ev
            ev_kb = p.add("gpsimd", lambda g: g.tensor_copy(
                out=scal[0:1, 16:20], in_=scal[0:1, 8:12]),
                waits=(kn_ev,), inc=(s_ge, 1))
            ev_km = p.add("vector", lambda v: v.reduce_max(
                out=scal[0:1, 12:13], in_=scal[0:1, 16:20], axis=AX.X),
                waits=(scal_sqrt_rd, ev_kb), inc=(s_ve, 1))
            ev_ks = p.add("scalar", lambda s: s.activation(
                out=scal[0:1, 13:14], in_=scal[0:1, 12:13], func=AF.Sqrt),
                waits=(ev_km,), inc=(s_se, 1))
            scal_sqrt_rd = ev_ks
            if L0:
                ev_n = p.add("vector", lambda v: v.scalar_tensor_tensor(
                    out=scal[0:1, 15:16], in0=scal[0:1, 13:14], scalar=-0.125,
                    in1=scal[0:1, 1:2], op0=ALU.mult, op1=ALU.subtract),
                    waits=(ev_ks, ev_ws), inc=(s_ve, 1))
            else:
                ev_n = p.add("vector", lambda v: v.tensor_scalar(
                    out=scal[0:1, 15:16], in0=scal[0:1, 13:14], scalar1=-0.125,
                    scalar2=None, op0=ALU.mult), waits=(ev_ks,), inc=(s_ve, 1))
            ev_bcm = p.add("tensor", lambda t: t.matmul(
                pbc[:, 0:1], halfs_r[0:1, :], scal[0:1, 15:16], start=True, stop=True),
                waits=(ev_n, pbc_busy), inc=(s_mm, 1))
            scal_rd = ev_bcm
            ev_nc1 = p.add("vector", lambda v: v.tensor_copy(
                out=smalls[:, SM["NC1"]:SM["NC1"] + 1], in_=pbc[:, 0:1]),
                waits=(ev_bcm,), inc=(s_ve, 1))
            pbc_busy = ev_nc1

            ev_q2 = p.add("scalar", lambda s, hsl=hsl: s.activation(
                out=sq2[hsl, :], in_=qat[hsl, :], func=AF.Square),
                waits=(kn_ev,), inc=(s_se, 1))
            qn_dma = None
            for g in range(4):
                ev_mq = p.add("tensor", lambda t, g=g, hsl=hsl: t.matmul(
                    pc[0][0:1, 0:512], ones_c[hsl, 0:1],
                    sq2[hsl, g * 512:(g + 1) * 512], start=True, stop=True),
                    waits=(ev_q2, bank_free[0]), inc=(s_mm, 1))
                ev_qc = p.add("vector", lambda v: v.tensor_copy(
                    out=qrow[0:1, :], in_=pc[0][0:1, 0:512]),
                    waits=(ev_mq, qrow_rd), inc=(s_ve, 1))
                bank_free[0] = ev_qc
                p.dma("sync", out=st["hbm_q"].ap(), in_=qrow[0:1, :],
                      waits=(ev_qc, tr["qn_chain"]), inc=(s_qn, 16))
                ev_qw = (s_qn, s_qn.n)
                hqap = st["hbm_q"].ap()
                qn_dma = p.dma("sync", out=qn16[:, 4 * g:4 * g + 4],
                               in_=AP(hqap.tensor, 0, [[1, 128], [128, 4]]),
                               waits=(ev_qw,), inc=(s_qn, 16), slow=True)
                tr["qn_chain"] = qn_dma
                qrow_rd = ev_qw
            sq2_free = (s_mm, s_mm.n)
            ev_sq16 = p.add("scalar", lambda s: s.activation(
                out=ssml[:, 48:64], in_=qn16[:, 0:16], func=AF.Sqrt),
                waits=(qn_dma,), inc=(s_se, 1))
            tr["qn_chain"] = ev_sq16
            ev_M = p.add("vector", lambda v: v.tensor_scalar(
                out=ssml[:, 0:16], in0=ssml[:, 48:64],
                scalar1=smalls[:, SM["NC1"]:SM["NC1"] + 1],
                scalar2=(smalls[:, SM["NBR"]:SM["NBR"] + 1] if L0 else 0.0),
                op0=ALU.mult, op1=ALU.add),
                waits=(ev_sq16, ev_nc1, ev_consts), inc=(s_ve, 1))
            ev_prep = ev_M
            ev_sz = p.add("gpsimd", lambda g: g.memset(sums[:, :], 0.0),
                          waits=(sums_free,), inc=(s_ge, 1))

            exp_last = {}
            ptev = {}
            pd_free = None
            ev_pv = None
            ev_am = None
            cyc = 0
            def emit_norm(t, ev_ex):
                pslot = t % 3
                pfo = (t % 2) * S
                o8 = (t % 3) * 16
                ext = S if L0 else 128 * (t + 1)
                if L0:
                    ev_a = p.add("vector", lambda v, t=t, o8=o8: v.reduce_sum(
                        out=i8[:, o8:o8 + 1], in_=sums[:, t * 8:(t + 1) * 8],
                        axis=AX.X), waits=(ev_ex, i8_free[t % 3]), inc=(s_ve, 1))
                else:
                    ev_a0 = p.add("gpsimd", lambda g, t=t, o8=o8: g.tensor_copy(
                        out=i8[:, o8 + 8:o8 + 16],
                        in_=sums[:, t * 8:(t + 1) * 8]),
                        waits=(ev_ex, i8_free[t % 3]), inc=(s_ge, 1))
                    ev_a = p.add("vector", lambda v, o8=o8: v.reduce_sum(
                        out=i8[:, o8:o8 + 1], in_=i8[:, o8 + 8:o8 + 16],
                        axis=AX.X), waits=(ev_a0,), inc=(s_ve, 1))
                ev_b = p.add("gpsimd", lambda g, o8=o8: g.tensor_scalar(
                    out=i8[:, o8 + 1:o8 + 2], in0=i8[:, o8:o8 + 1],
                    scalar1=1e-30, scalar2=None, op0=ALU.add),
                    waits=(ev_a,), inc=(s_ge, 1))
                ev_c = p.add("vector", lambda v, o8=o8: v.reciprocal(
                    out=i8[:, o8 + 2:o8 + 3], in_=i8[:, o8 + 1:o8 + 2]),
                    waits=(ev_b,), inc=(s_ve, 1))
                ev_c2 = p.add("gpsimd", lambda g, o8=o8: g.tensor_scalar(
                    out=i8[:, o8 + 3:o8 + 4], in0=i8[:, o8 + 2:o8 + 3],
                    scalar1=i8[:, o8 + 2:o8 + 3], scalar2=1e37,
                    op0=ALU.mult, op1=ALU.min),
                    waits=(ev_c,), inc=(s_ge, 1))
                ev_e = p.add("vector", lambda v, o8=o8, ext=ext, pfo=pfo, pslot=pslot: v.scalar_tensor_tensor(
                    out=P[pslot][:, 0:ext], in0=Pf[:, pfo:pfo + ext],
                    scalar=i8[:, o8 + 3:o8 + 4],
                    in1=Pf[:, pfo:pfo + ext],
                    op0=ALU.mult, op1=ALU.mult,
                    accum_out=i8[:, o8 + 4:o8 + 5]),
                    waits=(ev_c2, p_free[pslot]), inc=(s_ve, 1))
                pf_free[t % 2] = ev_e
                ev_f = p.add("gpsimd", lambda g, o8=o8: g.tensor_scalar(
                    out=i8[:, o8 + 5:o8 + 6], in0=i8[:, o8 + 4:o8 + 5],
                    scalar1=1e-30, scalar2=None, op0=ALU.add),
                    waits=(ev_e,), inc=(s_ge, 1))
                ev_g = p.add("vector", lambda v, o8=o8: v.reciprocal(
                    out=i8[:, o8 + 6:o8 + 7], in_=i8[:, o8 + 5:o8 + 6]),
                    waits=(ev_f,), inc=(s_ve, 1))
                ev_h2 = p.add("gpsimd", lambda g, o8=o8: g.tensor_copy(
                    out=i8[:, o8 + 7:o8 + 8], in_=i8[:, o8 + 6:o8 + 7]),
                    waits=(ev_g,), inc=(s_ge, 1))
                ev_i = p.add("vector", lambda v, o8=o8, ext=ext, pslot=pslot: v.tensor_scalar(
                    out=P[pslot][:, 0:ext], in0=P[pslot][:, 0:ext],
                    scalar1=i8[:, o8 + 7:o8 + 8], scalar2=None,
                    op0=ALU.mult), waits=(ev_h2, ev_e), inc=(s_ve, 1))
                i8_free[t % 3] = ev_i
                exp_last[t] = ev_i

            for t in range(NT + 2):
                if t < NT:
                    chl = chunks_fn(t)
                    tb = t % 2
                    pslot = t % 3
                    if L0:
                        p.add("tensor", lambda tn: tn.matmul(
                            prp[:, 0:257], ones_r[0:1, :], brow[0:1, 0:257],
                            start=True, stop=False),
                            waits=(ev_sx, ev_cn, prp_busy), inc=(s_mm, 1))
                        ev_rp = p.add("tensor", lambda tn, t=t, hsl=hsl: tn.matmul(
                            prp[:, 0:257], qat[hsl, t * 128:(t + 1) * 128],
                            wr2m[hsl, :], start=False, stop=True),
                            waits=(ev_prep, ev_wrl), inc=(s_mm, 1))
                        ev_rpe = p.add("scalar", lambda s, tb=tb: s.activation(
                            out=rpe[tb][:, 128:385], in_=prp[:, 0:257],
                            func=AF.Copy),
                            waits=(ev_rp, rpe_hw[tb]), inc=(s_se, 1))
                        prp_busy = ev_rpe
                        p.add("gpsimd", lambda g, tb=tb: g.tensor_copy(
                            out=rpe[tb][:, 0:128],
                            in_=rpe[tb][:, 128:129].broadcast_to([128, 128])),
                            waits=(ev_rpe,), inc=(s_ge, 1))
                        ev_fill = p.add("gpsimd", lambda g, tb=tb: g.tensor_copy(
                            out=rpe[tb][:, 385:513],
                            in_=rpe[tb][:, 384:385].broadcast_to([128, 128])),
                            inc=(s_ge, 1))
                        p.dma("sync", out=st["hbm_h"][tb].ap(), in_=rpe[tb][:],
                              waits=(ev_fill, hb_rd[tb]), inc=(s_hw[tb], 16))
                        ev_hwr = (s_hw[tb], s_hw[tb].n)
                        rpe_hw[tb] = ev_hwr
                        i0 = 128 * t
                        lo = max(0, i0 - 128)
                        wB = min(S, i0 + 256) - lo
                        off = lo - i0 + 256
                        hap = st["hbm_h"][tb].ap()
                        shear = AP(hap.tensor, off, [[512, 128], [1, wB]])
                        p.dma("sync", out=bb[tb][:, 0:wB], in_=shear,
                              waits=(ev_hwr, bb_free[tb]), inc=(s_brd[tb], 16))
                        hb_rd[tb] = (s_brd[tb], s_brd[tb].n)
                        p.add("gpsimd", lambda g, tb=tb, t=t: g.tensor_scalar(
                            out=ssml[:, 16 + t:17 + t], in0=rpe[tb][:, 128:129],
                            scalar1=0.5, scalar2=ssml[:, t:t + 1],
                            op0=ALU.mult, op1=ALU.add),
                            waits=(ev_rpe, ev_prep), inc=(s_ge, 1))
                        ev_bfar = p.add("gpsimd", lambda g, tb=tb, t=t: g.tensor_scalar(
                            out=ssml[:, 32 + t:33 + t], in0=rpe[tb][:, 384:385],
                            scalar1=0.5, scalar2=ssml[:, t:t + 1],
                            op0=ALU.mult, op1=ALU.add),
                            inc=(s_ge, 1))
                    if t >= 1:
                        emit_norm(t - 1, exp_last[t - 1])
                    ev_ex = None
                    pfo = (t % 2) * S
                    if L0:
                        chl = ([c for c in chl if c[0] != "B"]
                               + [c for c in chl if c[0] == "B"])
                    for ci, (kind, j0, w) in enumerate(chl):
                        bank = cyc % 2
                        cyc += 1
                        b = pc[bank]
                        ev_cm = p.add("tensor", lambda tn, b=b, j0=j0, w=w, t=t, hsl=hsl: tn.matmul(
                            b[:, 0:w], qat[hsl, t * 128:(t + 1) * 128],
                            kat[hsl, j0:j0 + w], start=True, stop=True),
                            waits=(bank_free[bank], ev_kvar, ev_prep), inc=(s_mm, 1))
                        slot_ap = sums[:, t * 8 + ci: t * 8 + ci + 1]
                        if L0 and kind == "B":
                            ev_bd = p.add("vector", lambda v, b=b, w=w, tb=tb: v.scalar_tensor_tensor(
                                out=sband[tb][:, 0:w], in0=b[:, 0:w],
                                scalar=0.125, in1=bb[tb][:, 0:w],
                                op0=ALU.mult, op1=ALU.add),
                                waits=(ev_cm, hb_rd[tb], sband_free[0],
                                       sband_free[1]),
                                inc=(s_ve, 1))
                            bank_free[bank] = ev_bd
                            bb_free[0] = ev_bd
                            bb_free[1] = ev_bd
                            ev_ex = p.add("scalar", lambda s, w=w, j0=j0, t=t, tb=tb, pfo=pfo, slot_ap=slot_ap: s.activation(
                                out=Pf[:, pfo + j0:pfo + j0 + w], in_=sband[tb][:, 0:w],
                                func=AF.Exp, bias=ssml[:, t:t + 1], scale=0.5,
                                accum_out=slot_ap),
                                waits=(ev_bd, ev_sz, pf_free[t % 2]), inc=(s_se, 1))
                            sband_free[tb] = ev_ex
                        elif (not L0) and kind == "D":
                            ev_e0 = p.add("scalar", lambda s, b=b, t=t: s.activation(
                                out=Pd[:, :], in_=b[:, 0:128], func=AF.Exp,
                                bias=ssml[:, t:t + 1], scale=0.0625),
                                waits=(ev_cm, ev_sz, pd_free, pf_free[t % 2]), inc=(s_se, 1))
                            bank_free[bank] = ev_e0
                            ev_ex = p.add("vector", lambda v, j0=j0, pfo=pfo, slot_ap=slot_ap: v.scalar_tensor_tensor(
                                out=Pf[:, pfo + j0:pfo + j0 + 128], in0=Pd[:, :],
                                scalar=1.0, in1=tri01[:, :],
                                op0=ALU.mult, op1=ALU.mult, accum_out=slot_ap),
                                waits=(ev_e0, ev_sz, ev_cn),
                                inc=(s_ve, 1))
                            pd_free = ev_ex
                        else:
                            bcoli = {"L": 16, "B": 0, "R": 32, "F": 0}[kind] + t
                            wts_x = (ev_cm, ev_sz, pf_free[t % 2])
                            if L0 and kind in ("L", "R"):
                                wts_x += (ev_bfar,)
                            ev_ex = p.add("scalar", lambda s, b=b, w=w, j0=j0, pfo=pfo, bcoli=bcoli, slot_ap=slot_ap: s.activation(
                                out=Pf[:, pfo + j0:pfo + j0 + w], in_=b[:, 0:w],
                                func=AF.Exp, bias=ssml[:, bcoli:bcoli + 1],
                                scale=0.0625, accum_out=slot_ap),
                                waits=wts_x, inc=(s_se, 1))
                            bank_free[bank] = ev_ex
                    exp_last[t] = ev_ex


                if t == NT:
                    emit_norm(NT - 1, exp_last[NT - 1])
                tp = t - 2
                if tp >= 0:
                    nkc = NT if L0 else (tp + 1)
                    pslot2 = tp % 3
                    evA = evB = None
                    for a in range(nkc):
                        dst2 = ptA if a < 8 else ptB
                        dsl = slice((a % 8) * 128, (a % 8) * 128 + 128)
                        wts = []
                        if a == 0:
                            wts = [exp_last[tp], ptA_free]
                        if a == 8:
                            wts = [ptB_free]
                        ev_t2 = p.add("tensor", lambda tn, dst2=dst2, dsl=dsl, a=a, pslot2=pslot2: tn.transpose(
                            dst2[:, dsl], P[pslot2][:, a * 128:(a + 1) * 128],
                            ident[:]), waits=wts, inc=(s_mm, 1))
                        if a == min(nkc, 8) - 1:
                            evA = ev_t2
                        if a == nkc - 1:
                            evB = ev_t2
                    p_free[pslot2] = evB
                    region = tp % 4
                    nA = min(nkc, 8)
                    ev_eA = p.add("vector", lambda v, region=region, nA=nA: v.tensor_copy(
                        out=PTs[:, region * 2048: region * 2048 + nA * 128],
                        in_=ptA[:, 0:nA * 128]),
                        waits=(evA, pt_region_pv[region]), inc=(s_ve, 1))
                    ptA_free = ev_eA
                    ev_eB = None
                    if nkc > 8:
                        nB = nkc - 8
                        ev_eB = p.add("vector", lambda v, region=region, nB=nB: v.tensor_copy(
                            out=PTs[:, region * 2048 + 1024: region * 2048 + 1024 + nB * 128],
                            in_=ptB[:, 0:nB * 128]),
                            waits=(evB, pt_region_pv[region]), inc=(s_ve, 1))
                        ptB_free = ev_eB
                    ptev[tp] = (ev_eA, ev_eB)

                if tp >= 3 and tp % 4 == 3:
                    g = tp // 4
                    tpos = None if hb == 0 else (0, 64)
                    pvb = ppv[hh]
                    nk = NT if L0 else (4 * g + 4)
                    ptsap = PTs[:]
                    for a in range(nk):
                        if L0 or a <= 4 * g:
                            qoff, qn = 0, 512
                        else:
                            qoff = (a - 4 * g) * 128
                            qn = 512 - qoff
                        rhs = AP(ptsap.tensor,
                                 ptsap.offset + a * 128 + (qoff // 128) * 2048,
                                 [[8192, 128], [2048, qn // 128], [1, 128]])
                        wts = []
                        if a == 0:
                            wts = [x for tpi in range(4 * g, 4 * g + 4)
                                   for x in ptev[tpi] if x]
                            wts += [ev_vts, ppv_free[hh]]
                        ev_pv = p.add("tensor", lambda tn, pvb=pvb, a=a, h=h, rhs=rhs, qoff=qoff, qn=qn, hsl=hsl, tpos=tpos, nk=nk: tn.matmul(
                            pvb[hsl, qoff:qoff + qn],
                            vts[:, a * 1024 + 64 * h: a * 1024 + 64 * h + 64],
                            rhs, start=(a == 0), stop=(a == nk - 1),
                            tile_position=tpos),
                            waits=wts, inc=(s_mm, 1))
                    for r4 in range(4):
                        pt_region_pv[r4] = ev_pv
                    if L0:
                        ev_am = p.add("vector", lambda v, g=g, hsl=hsl, pr=pr, pvb=pvb: v.scalar_tensor_tensor(
                            out=xs[:, pr * S + g * 512: pr * S + (g + 1) * 512][hsl, :],
                            in0=pvb[hsl, 0:512], scalar=0.0,
                            in1=xs[:, pr * S + g * 512: pr * S + (g + 1) * 512][hsl, :],
                            op0=ALU.add, op1=ALU.add),
                            waits=(ev_sx, ev_pv), inc=(s_ve, 1))
                    else:
                        ev_am = p.add("vector", lambda v, g=g, hsl=hsl, pvb=pvb: v.tensor_copy(
                            out=attn1[hsl, g * 512:(g + 1) * 512],
                            in_=pvb[hsl, 0:512]),
                            waits=(ev_pv,), inc=(s_ve, 1))
                    ppv_free[hh] = ev_am
                    ev_last = ev_am
                    if st["dbg"] and pr == 0 and hh == 0 and (not L0) and g == 0 \
                            and st.get("stop_") == "a1res":
                        ev_all2 = [(s_mm, s_mm.n), (s_ve, s_ve.n),
                                   (s_se, s_se.n), (s_ge, s_ge.n)]
                        for nm2, t2 in [("e_PTs", PTs), ("e_P0", P[0]),
                                        ("e_P1", P[1]), ("e_sums", sums),
                                        ("e_at1", attn1), ("e_Pf", Pf),
                                        ("e_vts", vts), ("e_i8", i8)]:
                            ap2 = t2[:]
                            od2 = nc.dram_tensor(nm2, list(ap2.shape), ap2.dtype,
                                                 kind="ExternalOutput").ap()
                            dump_ev2 = p.dma("gpsimd", out=od2, in_=ap2,
                                             waits=tuple(ev_all2),
                                             inc=(sems["s_tr"], 16))
                        for k2 in range(3):
                            p_free[k2] = dump_ev2
                            i8_free[k2] = dump_ev2
                        pf_free = [dump_ev2, dump_ev2]
                        for r42 in range(4):
                            pt_region_pv[r42] = dump_ev2
                        ppv_free[hh] = dump_ev2
            if st["dbg"] and pr == 0 and hh == 0 and (L0 == (st.get("stop_", "a0res") == "a0res")):
                ev_all = [(s_mm, s_mm.n), (s_ve, s_ve.n), (s_se, s_se.n),
                          (s_ge, s_ge.n)]
                s_dbg = sems["s_tr"]
                dump_list = [("d_qat", qat), ("d_kat", kat),
                             ("d_P0", P[0]), ("d_P1", P[1]),
                             ("d_sums", sums), ("d_ssml", ssml),
                             ("d_qn16", qn16), ("d_scal", scal),
                             ("d_smalls", smalls), ("d_PTs", PTs),
                             ("d_vts", vts), ("d_Pf", Pf),
                             ("d_i8", i8), ("d_xs0", xs[:, 0:S])]
                if L0:
                    dump_list += [("d_rpe0", rpe[0]), ("d_rpe1", rpe[1]),
                                  ("d_bb0", bb[0]), ("d_bb1", bb[1])]
                else:
                    dump_list += [("d_Pd", Pd), ("d_at1", attn1)]
                dump_ev = None
                for nm, tsr in dump_list:
                    ap = tsr if isinstance(tsr, bass.AP) else tsr[:]
                    od = nc.dram_tensor(nm, list(ap.shape), ap.dtype,
                                        kind="ExternalOutput").ap()
                    st["dbgout"][nm] = True
                    dump_ev = p.dma("gpsimd", out=od, in_=ap,
                                    waits=tuple(ev_all), inc=(s_dbg, 16))
                for k in range(3):
                    p_free[k] = dump_ev
                for k in range(2):
                    rpe_hw[k] = dump_ev
                    bb_free[k] = dump_ev
                    sband_free[k] = dump_ev
                sums_free = dump_ev
                sq2_free = dump_ev
            sums_free = (s_ve, s_ve.n)
            qk_free = (s_mm, s_mm.n)

        if not L0:
            p.dma("sync", out=st["hbm_at"][pr], in_=attn1[:, :],
                  waits=(ev_last, tr["ah_wr"][pr]), inc=(s_ah, 16))
            tr["ah_wr"][pr] = (s_ah, s_ah.n)
            ppv_free = [tr["ah_wr"][pr], tr["ah_wr"][pr]]

    if L0:
        res = [(s_ve, s_ve.n)]
        sub.close()
        ph.close()
        return res

    ev_attn_done = [(s_mm, s_mm.n), (s_ve, s_ve.n), (s_se, s_se.n)]
    sub.close()
    ob = ExitStack()
    oe = ob.enter_context
    wob = oe(nc.sbuf_tensor("wob", [128, 8 * 8 * 128], BF16))
    ats = oe(nc.sbuf_tensor("ats", [128, NCH * S], BF16))
    s_wo = sems["s_wo"]
    p.dma("gpsimd",
          out=wob[:].rearrange("p (a c m) -> p a c m", a=8, m=128),
          in_=dt["l1_wo"].transpose([2, 0, 1, 3]),
          waits=tuple(ev_attn_done), inc=(s_wo, 16))
    for kc in range(8):
        p.dma("sync", out=ats[:, kc * S:(kc + 1) * S], in_=st["hbm_at"][kc],
              waits=[(s_ah, s_ah.n)] + list(ev_attn_done), inc=(s_wo, 16))
    ev_wo = (s_wo, s_wo.n)
    bank_free = [None, None]
    ev_o = None
    for oc in range(8):
        for tg in range(4):
            bank = (oc * 4 + tg) % 2
            b = pc[bank]
            ev_m3 = None
            for kc in range(8):
                wts = [bank_free[bank], ev_wo] if kc == 0 else []
                ev_m3 = p.add("tensor", lambda t, b=b, oc=oc, tg=tg, kc=kc: t.matmul(
                    b[:, 0:512],
                    wob[:, (oc * 8 + kc) * 128:(oc * 8 + kc) * 128 + 128],
                    ats[:, kc * S + tg * 512: kc * S + (tg + 1) * 512],
                    start=(kc == 0), stop=(kc == 7)), waits=wts, inc=(s_mm, 1))
            ev_o = p.add("vector", lambda v, b=b, oc=oc, tg=tg: v.scalar_tensor_tensor(
                out=xs[:, oc * S + tg * 512: oc * S + (tg + 1) * 512],
                in0=b[:, 0:512],
                scalar=smalls[:, SM["L1BO"] + oc:SM["L1BO"] + oc + 1],
                in1=xs[:, oc * S + tg * 512: oc * S + (tg + 1) * 512],
                op0=ALU.add, op1=ALU.add),
                waits=(ev_m3, ev_sx), inc=(s_ve, 1))
            bank_free[bank] = ev_o
    res = [(s_ve, s_ve.n)]
    ob.close()
    ph.close()
    return res


def _layernorm(st, ev_in):
    nc, p = st["nc"], st["p"]
    li = _LN_N[0]
    _LN_N[0] += 1
    xs, xb = st["xs"], st["xb"]
    ones_r, ones_c = st["ones_r"], st["ones_c"]
    sems = st["sems"]
    s_mm, s_ve, s_se = sems["s_mm"], sems["s_ve"], sems["s_se"]
    s_ge = sems["s_ge"]
    ev_cn = st["marks"]["ev_cn"]

    ph = ExitStack()
    ee = ph.enter_context
    lnsq = ee(nc.sbuf_tensor(f"lnsq{li}", [128, 8 * 512], BF16))
    lrow = ee(nc.sbuf_tensor(f"lrow{li}", [1, 6 * 512], F32))
    tmp = ee(nc.sbuf_tensor(f"lntmp{li}", [128, 512], F32))
    pS = [ee(nc.psum_tensor(f"lnS{li}_{i}", [1, 512], F32)) for i in range(2)]
    pQ = [ee(nc.psum_tensor(f"lnQ{li}_{i}", [1, 512], F32)) for i in range(2)]
    pA = [ee(nc.psum_tensor(f"lnA{li}_{i}", [128, 512], F32)) for i in range(2)]
    pB = [ee(nc.psum_tensor(f"lnB{li}_{i}", [128, 512], F32)) for i in range(2)]
    M0, T0, T1, T2, CA, CB = 0, 512, 1024, 1536, 2048, 2560

    ev_cast = []
    for c in range(NCH):
        sl = slice(c * S, (c + 1) * S)
        ev_cast.append(p.add("vector",
                             lambda v, sl=sl: v.tensor_copy(out=xb[:, sl], in_=xs[:, sl]),
                             waits=tuple(ev_in), inc=(s_ve, 1)))
    stat_free = [None, None]
    ab_free = [None, None]
    lrow_rd = None
    evs = [None, None]
    for tg in range(4):
        bk = tg % 2
        ev_sq = []
        for c in range(NCH):
            ev_sq.append(p.add("scalar", lambda s, c=c, tg=tg: s.activation(
                out=lnsq[:, c * 512:(c + 1) * 512],
                in_=xs[:, c * S + tg * 512: c * S + (tg + 1) * 512],
                func=AF.Square), waits=tuple(ev_in), inc=(s_se, 1)))
        ev_ms = ev_mq = None
        for c in range(NCH):
            ev_ms = p.add("tensor", lambda t, c=c, tg=tg, bk=bk: t.matmul(
                pS[bk][0:1, :], ones_c[:, 0:1],
                xb[:, c * S + tg * 512: c * S + (tg + 1) * 512],
                start=(c == 0), stop=(c == 7)),
                waits=[ev_cast[c]] + ([stat_free[bk], ev_cn] if c == 0 else []),
                inc=(s_mm, 1))
        for c in range(NCH):
            ev_mq = p.add("tensor", lambda t, c=c, bk=bk: t.matmul(
                pQ[bk][0:1, :], ones_c[:, 0:1], lnsq[:, c * 512:(c + 1) * 512],
                start=(c == 0), stop=(c == 7)),
                waits=[ev_sq[c]] + ([stat_free[bk]] if c == 0 else []),
                inc=(s_mm, 1))
        p.add("vector", lambda v, bk=bk: v.tensor_scalar(
            out=lrow[0:1, M0:M0 + 512], in0=pS[bk][0:1, :], scalar1=1.0 / D,
            scalar2=None, op0=ALU.mult), waits=(ev_ms, lrow_rd), inc=(s_ve, 1))
        p.add("vector", lambda v, bk=bk: v.tensor_scalar(
            out=lrow[0:1, T0:T0 + 512], in0=pQ[bk][0:1, :], scalar1=1.0 / D,
            scalar2=EPS, op0=ALU.mult, op1=ALU.add), waits=(ev_mq,), inc=(s_ve, 1))
        ev3 = p.add("vector", lambda v: v.tensor_mul(
            out=lrow[0:1, T1:T1 + 512], in0=lrow[0:1, M0:M0 + 512],
            in1=lrow[0:1, M0:M0 + 512]), inc=(s_ve, 1))
        ev3b = p.add("gpsimd", lambda g: g.tensor_copy(
            out=lrow[0:1, T2:T2 + 512], in_=lrow[0:1, T1:T1 + 512]),
            waits=(ev3,), inc=(s_ge, 1))
        ev4 = p.add("vector", lambda v: v.scalar_tensor_tensor(
            out=lrow[0:1, T1:T1 + 512], in0=lrow[0:1, T2:T2 + 512], scalar=-1.0,
            in1=lrow[0:1, T0:T0 + 512], op0=ALU.mult, op1=ALU.add),
            waits=(ev3b,), inc=(s_ve, 1))
        ev5 = p.add("scalar", lambda s: s.activation(
            out=lrow[0:1, T0:T0 + 512], in_=lrow[0:1, T1:T1 + 512], func=AF.Sqrt),
            waits=(ev4,), inc=(s_se, 1))
        ev6 = p.add("vector", lambda v: v.reciprocal(
            out=lrow[0:1, CA:CA + 512], in_=lrow[0:1, T0:T0 + 512]),
            waits=(ev5,), inc=(s_ve, 1))
        ev6b = p.add("gpsimd", lambda g: g.tensor_copy(
            out=lrow[0:1, T2:T2 + 512], in_=lrow[0:1, CA:CA + 512]),
            waits=(ev6,), inc=(s_ge, 1))
        ev7 = p.add("vector", lambda v: v.scalar_tensor_tensor(
            out=lrow[0:1, CB:CB + 512], in0=lrow[0:1, M0:M0 + 512], scalar=-1.0,
            in1=lrow[0:1, T2:T2 + 512], op0=ALU.mult, op1=ALU.mult),
            waits=(ev6b,), inc=(s_ve, 1))
        stat_free[bk] = ev7
        ev_mA = p.add("tensor", lambda t, bk=bk: t.matmul(
            pA[bk][:, :], ones_r[0:1, :], lrow[0:1, CA:CA + 512],
            start=True, stop=True), waits=(ev6, ev7, ab_free[bk]), inc=(s_mm, 1))
        ev_mB = p.add("tensor", lambda t, bk=bk: t.matmul(
            pB[bk][:, :], ones_r[0:1, :], lrow[0:1, CB:CB + 512],
            start=True, stop=True), inc=(s_mm, 1))
        lrow_rd = ev_mB
        ev_ap = ev_cb = None
        for c in range(NCH):
            xsl = slice(c * S + tg * 512, c * S + (tg + 1) * 512)
            p.add("vector", lambda v, xsl=xsl, bk=bk: v.tensor_mul(
                out=tmp[:, :], in0=xs[:, xsl], in1=pA[bk][:, :]),
                waits=(ev_mA,), inc=(s_ve, 1))
            ev_ap = p.add("vector", lambda v, xsl=xsl, bk=bk: v.tensor_add(
                out=xs[:, xsl], in0=tmp[:, :], in1=pB[bk][:, :]),
                waits=(ev_mB,), inc=(s_ve, 1))
            ev_cb = p.add("scalar", lambda s, xsl=xsl: s.activation(
                out=xb[:, xsl], in_=xs[:, xsl], func=AF.Copy),
                waits=(ev_ap,), inc=(s_se, 1))
        ab_free[bk] = ev_ap
        evs = [ev_ap, ev_cb]
    ph.close()
    return evs


def _ffn(st, ev_in, layer):
    nc, p = st["nc"], st["p"]
    dt, SM = st["dt"], st["SM"]
    xs, xb = st["xs"], st["xb"]
    smalls, scal, ones_r = st["smalls"], st["scal"], st["ones_r"]
    sems = st["sems"]
    s_mm, s_ve, s_se, s_ge = sems["s_mm"], sems["s_ve"], sems["s_se"], sems["s_ge"]
    sfs = [sems["s_fA"], sems["s_fB"]]
    sgs = [sems["s_gA"], sems["s_gB"]]
    s_qs, s_qc, s_tr = sems["s_qs"], sems["s_qc"], sems["s_tr"]
    L0 = layer == 0
    QAT = not L0

    ph = ExitStack()
    ee = ph.enter_context

    if QAT:
        with ExitStack() as am:
            ae = am.enter_context
            qsl = [ae(nc.sbuf_tensor(f"qsl{i}", [128, 2048], F32)) for i in range(2)]
            amax = ae(nc.sbuf_tensor("amax", [128, 32], F32))
            arow = ae(nc.sbuf_tensor("arow", [1, 128], F32))
            pk67 = [ae(nc.psum_tensor(f"pam{i}", [128, 512], F32)) for i in range(2)]
            slot_rd = [None, None]
            for wi, wname in enumerate(["l1_wfc", "l1_wproj"]):
                wap = dt[wname]
                flat = AP(wap.tensor, 0, [[2048, 128], [1, 2048]])
                red_evs = []
                for sl in range(16):
                    ss = sl % 2
                    src = AP(wap.tensor, sl * 128 * 2048, [[2048, 128], [1, 2048]])
                    ev_l = p.dma("sync", out=qsl[ss][:], in_=src,
                                 waits=(slot_rd[ss],) + (tuple(ev_in) if sl < 2 and wi == 0 else ()),
                                 inc=(s_qs, 16))
                    ev_r = p.add("vector", lambda v, ss=ss, sl=sl: v.reduce_max(
                        out=amax[:, sl:sl + 1], in_=qsl[ss][:], axis=AX.X,
                        apply_absolute_value=True), waits=(ev_l,), inc=(s_ve, 1))
                    slot_rd[ss] = ev_r
                    red_evs.append(ev_r)
                ev_cb0 = p.add("gpsimd", lambda g: g.tensor_copy(
                    out=amax[:, 16:32], in_=amax[:, 0:16]),
                    waits=(red_evs[-1],), inc=(s_ge, 1))
                ev_c = p.add("vector", lambda v: v.reduce_max(
                    out=amax[:, 0:1], in_=amax[:, 16:32], axis=AX.X),
                    waits=(ev_cb0,), inc=(s_ve, 1))
                ev_t = p.dma("sync", out=arow[0:1, :], in_=amax[:, 0:1],
                             waits=(ev_c, st["track"]["tr_chain"]),
                             inc=(s_tr, 16), slow=True)
                st["track"]["tr_chain"] = ev_t
                ev_m = p.add("vector", lambda v, wi=wi: v.reduce_max(
                    out=scal[0:1, 20 + 4 * wi:21 + 4 * wi], in_=arow[0:1, :],
                    axis=AX.X), waits=(ev_t,), inc=(s_ve, 1))
                ev_s = p.add("gpsimd", lambda g, wi=wi: g.tensor_scalar(
                    out=scal[0:1, 21 + 4 * wi:22 + 4 * wi],
                    in0=scal[0:1, 20 + 4 * wi:21 + 4 * wi],
                    scalar1=1.0 / 127.0, scalar2=None, op0=ALU.mult),
                    waits=(ev_m,), inc=(s_ge, 1))
                ev_rs = p.add("vector", lambda v, wi=wi: v.reciprocal(
                    out=scal[0:1, 22 + 4 * wi:23 + 4 * wi],
                    in_=scal[0:1, 21 + 4 * wi:22 + 4 * wi]),
                    waits=(ev_s,), inc=(s_ve, 1))
                cS = SM["SFC"] if wi == 0 else SM["SPJ"]
                cR = SM["RSFC"] if wi == 0 else SM["RSPJ"]
                ev_b1 = p.add("tensor", lambda t, wi=wi: t.matmul(
                    pk67[0][:, 0:1], ones_r[0:1, :],
                    scal[0:1, 21 + 4 * wi:22 + 4 * wi], start=True, stop=True),
                    waits=(ev_s, ev_rs), inc=(s_mm, 1))
                ev_b2 = p.add("tensor", lambda t, wi=wi: t.matmul(
                    pk67[1][:, 0:1], ones_r[0:1, :],
                    scal[0:1, 22 + 4 * wi:23 + 4 * wi], start=True, stop=True),
                    inc=(s_mm, 1))
                p.add("vector", lambda v, cS=cS: v.tensor_copy(
                    out=smalls[:, cS:cS + 1], in_=pk67[0][:, 0:1]),
                    waits=(ev_b1,), inc=(s_ve, 1))
                ev_cp = p.add("vector", lambda v, cR=cR: v.tensor_copy(
                    out=smalls[:, cR:cR + 1], in_=pk67[1][:, 0:1]),
                    waits=(ev_b2,), inc=(s_ve, 1))
            cal = ae(nc.sbuf_tensor("cal", [1, 8], F32))
            cali = ae(nc.sbuf_tensor("cali", [1, 2], I32))
            ev_z0 = p.add("gpsimd", lambda g: g.memset(cal[0:1, 0:1], 1.5),
                          inc=(s_ge, 1))
            ev_z1 = p.add("vector", lambda v: v.tensor_copy(
                out=cali[0:1, 0:1], in_=cal[0:1, 0:1]), waits=(ev_z0,),
                inc=(s_ve, 1))
            ev_z2 = p.add("gpsimd", lambda g: g.tensor_copy(
                out=cali[0:1, 1:2], in_=cali[0:1, 0:1]), waits=(ev_z1,),
                inc=(s_ge, 1))
            ev_z3 = p.add("vector", lambda v: v.tensor_scalar(
                out=cal[0:1, 2:3], in0=cali[0:1, 1:2], scalar1=-0.5,
                scalar2=257.0, op0=ALU.mult, op1=ALU.add), waits=(ev_z2,),
                inc=(s_ve, 1))
            ev_z4 = p.add("tensor", lambda t: t.matmul(
                pk67[0][:, 4:5], ones_r[0:1, :], cal[0:1, 2:3],
                start=True, stop=True), waits=(ev_z3,), inc=(s_mm, 1))
            ev_hb = p.add("vector", lambda v: v.tensor_copy(
                out=smalls[:, SM["HB"]:SM["HB"] + 1], in_=pk67[0][:, 4:5]),
                waits=(ev_z4,), inc=(s_ve, 1))
            ev_scales = ev_hb
    else:
        ev_scales = None

    hbuf = ee(nc.sbuf_tensor(f"hbuf{layer}", [128, 32 * 1024], BF16))
    if QAT:
        w1s = [ee(nc.sbuf_tensor(f"w1f{i}", [128, 1024], F32)) for i in range(2)]
        wqs1 = [ee(nc.sbuf_tensor(f"wq1{i}", [128, 1024], BF16)) for i in range(2)]
        qtmp = ee(nc.sbuf_tensor("qtmp", [128, 1024], F32))
        qint = ee(nc.sbuf_tensor("qint", [128, 1024], I32))
        blo_all = ee(nc.sbuf_tensor("blo_all", [16, 32 * 128], BF16))
        afc = ee(nc.sbuf_tensor("afc", [128, 8 * 16], BF16))
        apj = ee(nc.sbuf_tensor("apj", [128, 32 * 16], BF16))
        bpl = ee(nc.sbuf_tensor("bpl", [16, 8 * 128], BF16))
        lrx = ee(nc.sbuf_tensor("lrx", [16, 1024], BF16))
        lr2x = ee(nc.sbuf_tensor("lr2x", [16, 1024], BF16))
        w2q = [ee(nc.sbuf_tensor(f"w2q{i}", [128, 512], F32)) for i in range(2)]
    else:
        w1s = [ee(nc.sbuf_tensor(f"w1b{i}", [128, 1024], BF16)) for i in range(2)]
    w2s = [ee(nc.sbuf_tensor(f"w2s{layer}_{i}", [128, 512], BF16)) for i in range(2)]
    pk = [ee(nc.psum_tensor(f"pk{layer}{i}", [128, 512], F32)) for i in range(8)]

    bank_free = [None] * 8
    w1_rd = [None, None]
    w2_rd = [None, None]
    w1_cw = [None, None]
    w2_cw = [None, None]
    hb_free = None
    lrx_rd = None
    lr2x_rd = None
    ev_out = list(ev_in)

    b1c = SM["L0B1"] if L0 else SM["L1BFC"]
    b2c = SM["L0B2"] if L0 else SM["L1BPJ"]

    if QAT:
        s_lo = sems["s_rro"]
        p.dma("gpsimd", out=afc[:].rearrange("p (c n) -> p c n", n=16),
              in_=dt["l1_afc"].transpose([1, 0, 2]),
              waits=tuple(ev_in), inc=(s_lo, 16))
        p.dma("gpsimd", out=apj[:].rearrange("p (c n) -> p c n", n=16),
              in_=dt["l1_aproj"].transpose([1, 0, 2]), inc=(s_lo, 16))
        p.dma("gpsimd", out=bpl[:].rearrange("p (c n) -> p c n", n=128),
              in_=dt["l1_bproj_lo"].transpose([1, 0, 2]), inc=(s_lo, 16))
        p.dma("gpsimd", out=blo_all[:].rearrange("p (c n) -> p c n", n=128),
              in_=dt["l1_bfc_lo"].transpose([1, 0, 2]), inc=(s_lo, 16))
        ev_lores = (s_lo, sems["s_rro"].n)

    for th in range(2):
        tok0 = th * 1024
        if QAT:
            for ng in range(2):
                ev_ml = None
                for kc in range(8):
                    wts = []
                    if kc == 0:
                        wts = [ev_lores, bank_free[2], lrx_rd] + list(ev_in)
                    ev_ml = p.add("tensor", lambda t, kc=kc, ng=ng, tok0=tok0: t.matmul(
                        pk[2][0:16, 0:512], afc[:, kc * 16:(kc + 1) * 16],
                        xb[:, kc * S + tok0 + ng * 512: kc * S + tok0 + (ng + 1) * 512],
                        start=(kc == 0), stop=(kc == 7)), waits=wts, inc=(s_mm, 1))
                ev_le = p.add("scalar", lambda sc, ng=ng: sc.activation(
                    out=lrx[0:16, ng * 512:(ng + 1) * 512], in_=pk[2][0:16, 0:512],
                    func=AF.Identity, scale=2.0, bias=0.0),
                    waits=(ev_ml,), inc=(s_se, 1))
                bank_free[2] = ev_le
            ev_lrx = ev_le

        for ic in range(32):
            sl = ic % 2
            if QAT and th == 0:
                ev_w = p.dma("gpsimd", out=w1s[sl][:].rearrange("p (c m) -> p c m", m=128),
                             in_=dt["l1_wfc"][ic].transpose([1, 0, 2]),
                             waits=(w1_rd[sl], ev_scales), inc=(sfs[sl], 16))
                ev_q1 = p.add("vector", lambda v, sl=sl: v.tensor_scalar(
                    out=qtmp[:], in0=w1s[sl][:],
                    scalar1=smalls[:, SM["RSFC"]:SM["RSFC"] + 1],
                    scalar2=smalls[:, SM["HB"]:SM["HB"] + 1],
                    op0=ALU.mult, op1=ALU.add), waits=(ev_w,), inc=(s_ve, 1))
                w1_rd[sl] = ev_q1
                p.add("vector", lambda v: v.tensor_copy(
                    out=qint[:], in_=qtmp[:]), inc=(s_ve, 1))
                ev_q4 = p.add("vector", lambda v, sl=sl: v.tensor_scalar(
                    out=wqs1[sl][:], in0=qint[:], scalar1=-256.0,
                    scalar2=smalls[:, SM["SFC"]:SM["SFC"] + 1],
                    op0=ALU.add, op1=ALU.mult),
                    waits=(w1_cw[sl],), inc=(s_ve, 1))
                ev_cw = p.dma("sync", out=st["hbm_wfq"].ap()[ic], in_=wqs1[sl][:],
                              waits=(ev_q4,), inc=(s_qc, 16))
                w1_cw[sl] = (sems["s_qc"], sems["s_qc"].n)
                wsrc = wqs1[sl]
                ev_wrdy = ev_q4
            elif QAT:
                ev_w = p.dma("sync", out=wqs1[sl][:], in_=st["hbm_wfq"].ap()[ic],
                             waits=(w1_rd[sl], (s_qc, sems["s_qc"].n)),
                             inc=(sfs[sl], 16))
                ev_wrdy = (sfs[sl], sfs[sl].n)
                wsrc = wqs1[sl]
            else:
                p.dma("gpsimd", out=w1s[sl][:].rearrange("p (c m) -> p c m", m=128),
                      in_=dt["l0_w1"][ic].transpose([1, 0, 2]),
                      waits=(w1_rd[sl],) + (tuple(ev_in) if ic < 2 and th == 0 else ()),
                      inc=(sfs[sl], 16))
                ev_wrdy = (sfs[sl], sfs[sl].n)
                wsrc = w1s[sl]
            for ng in range(2):
                bank = ng
                ev_m1 = None
                first = True
                if QAT:
                    ev_m1 = p.add("tensor", lambda t, ic=ic, ng=ng: t.matmul(
                        pk[ng][:, 0:512],
                        blo_all[0:16, ic * 128:(ic + 1) * 128],
                        lrx[0:16, ng * 512:(ng + 1) * 512],
                        start=True, stop=False),
                        waits=(ev_wrdy, ev_lrx, bank_free[ng], hb_free, ev_lores),
                        inc=(s_mm, 1))
                    first = False
                for kc in range(8):
                    wts = []
                    if first and kc == 0:
                        wts = [ev_wrdy, bank_free[ng], hb_free]
                    ev_m1 = p.add("tensor", lambda t, wsrc=wsrc, kc=kc, ng=ng, tok0=tok0: t.matmul(
                        pk[ng][:, 0:512], wsrc[:, kc * 128:(kc + 1) * 128],
                        xb[:, kc * S + tok0 + ng * 512: kc * S + tok0 + (ng + 1) * 512],
                        start=(kc == 0 and not QAT), stop=(kc == 7)),
                        waits=wts, inc=(s_mm, 1))
                w1_rd[sl] = ev_m1
                ev_h = p.add("scalar", lambda sc, ic=ic, ng=ng, b1c=b1c: sc.activation(
                    out=hbuf[:, ic * 1024 + ng * 512: ic * 1024 + (ng + 1) * 512],
                    in_=pk[ng][:, 0:512], func=(AF.Relu if L0 else AF.Gelu),
                    bias=smalls[:, b1c + ic:b1c + ic + 1], scale=1.0),
                    waits=(ev_m1,), inc=(s_se, 1))
                bank_free[ng] = ev_h
        ev_h_done = ev_h

        if QAT:
            for sg in range(2):
                ev_ml2 = None
                for kc in range(32):
                    wts = []
                    if kc == 0:
                        wts = [ev_h_done, bank_free[3], lr2x_rd, ev_lores]
                    ev_ml2 = p.add("tensor", lambda t, kc=kc, sg=sg: t.matmul(
                        pk[3][0:16, 0:512], apj[:, kc * 16:(kc + 1) * 16],
                        hbuf[:, kc * 1024 + sg * 512: kc * 1024 + (sg + 1) * 512],
                        start=(kc == 0), stop=(kc == 31)), waits=wts, inc=(s_mm, 1))
                ev_l2 = p.add("scalar", lambda sc, sg=sg: sc.activation(
                    out=lr2x[0:16, sg * 512:(sg + 1) * 512], in_=pk[3][0:16, 0:512],
                    func=AF.Identity, scale=2.0, bias=0.0),
                    waits=(ev_ml2,), inc=(s_se, 1))
                bank_free[3] = ev_l2
            ev_lr2 = ev_l2

        for half in range(2):
            if QAT:
                for oc in range(4):
                    for sg in range(2):
                        bank = oc * 2 + sg
                        p.add("tensor", lambda t, half=half, oc=oc, sg=sg, bank=bank: t.matmul(
                            pk[bank][:, 0:512],
                            bpl[0:16, (half * 4 + oc) * 128:(half * 4 + oc) * 128 + 128],
                            lr2x[0:16, sg * 512:(sg + 1) * 512],
                            start=True, stop=False),
                            waits=(ev_lr2, ev_lores, bank_free[bank]),
                            inc=(s_mm, 1))
            for kc in range(32):
                sl = kc % 2
                if QAT and th == 0:
                    ev_w2 = p.dma("sync", out=w2q[sl][:].rearrange("p (a m) -> p a m", m=128),
                                  in_=dt["l1_wproj"][half][kc].rearrange("a p m -> p a m"),
                                  waits=(w2_rd[sl], ev_scales), inc=(sgs[sl], 16))
                    ev_q1 = p.add("vector", lambda v, sl=sl: v.tensor_scalar(
                        out=qtmp[:, 0:512], in0=w2q[sl][:],
                        scalar1=smalls[:, SM["RSPJ"]:SM["RSPJ"] + 1],
                        scalar2=smalls[:, SM["HB"]:SM["HB"] + 1],
                        op0=ALU.mult, op1=ALU.add), waits=(ev_w2,), inc=(s_ve, 1))
                    w2_rd[sl] = ev_q1
                    p.add("vector", lambda v: v.tensor_copy(
                        out=qint[:, 0:512], in_=qtmp[:, 0:512]), inc=(s_ve, 1))
                    ev_q4 = p.add("vector", lambda v, sl=sl: v.tensor_scalar(
                        out=w2s[sl][:], in0=qint[:, 0:512], scalar1=-256.0,
                        scalar2=smalls[:, SM["SPJ"]:SM["SPJ"] + 1],
                        op0=ALU.add, op1=ALU.mult),
                        waits=(w2_cw[sl],), inc=(s_ve, 1))
                    p.dma("sync", out=st["hbm_wpq"].ap()[half][kc], in_=w2s[sl][:],
                          waits=(ev_q4,), inc=(s_qc, 16))
                    w2_cw[sl] = (sems["s_qc"], sems["s_qc"].n)
                    ev_w2r = ev_q4
                elif QAT:
                    ev_w2 = p.dma("sync", out=w2s[sl][:],
                                  in_=st["hbm_wpq"].ap()[half][kc],
                                  waits=(w2_rd[sl], (s_qc, sems["s_qc"].n)),
                                  inc=(sgs[sl], 16))
                    ev_w2r = (sgs[sl], sgs[sl].n)
                else:
                    p.dma("gpsimd", out=w2s[sl][:].rearrange("p (a m) -> p a m", m=128),
                          in_=dt["l0_w2"][half][kc].transpose([1, 0, 2]),
                          waits=(w2_rd[sl],), inc=(sgs[sl], 16))
                    ev_w2r = (sgs[sl], sgs[sl].n)
                ev_m2 = None
                for oc in range(4):
                    for sg in range(2):
                        bank = oc * 2 + sg
                        wts = []
                        if oc == 0 and sg == 0:
                            wts = [ev_w2r, ev_h_done]
                            if kc == 0 and not QAT:
                                wts += [bank_free[b] for b in range(8)]
                        ev_m2 = p.add("tensor", lambda t, oc=oc, sg=sg, kc=kc, sl=sl, bank=bank: t.matmul(
                            pk[bank][:, 0:512], w2s[sl][:, oc * 128:(oc + 1) * 128],
                            hbuf[:, kc * 1024 + sg * 512: kc * 1024 + (sg + 1) * 512],
                            start=(kc == 0 and not QAT), stop=(kc == 31)),
                            waits=wts, inc=(s_mm, 1))
                w2_rd[sl] = ev_m2
            for oc in range(4):
                for sg in range(2):
                    bank = oc * 2 + sg
                    ocabs = half * 4 + oc
                    xsl = slice(ocabs * S + tok0 + sg * 512,
                                ocabs * S + tok0 + (sg + 1) * 512)
                    ev_e2 = p.add("vector", lambda v, bank=bank, ocabs=ocabs, xsl=xsl, b2c=b2c: v.scalar_tensor_tensor(
                        out=xs[:, xsl], in0=pk[bank][:, 0:512],
                        scalar=smalls[:, b2c + ocabs:b2c + ocabs + 1],
                        in1=xs[:, xsl], op0=ALU.add, op1=ALU.add),
                        waits=(ev_m2, st["marks"]["ev_sx"]), inc=(s_ve, 1))
                    bank_free[bank] = ev_e2
                    ev_out = [ev_e2]
        hb_free = (s_mm, s_mm.n)
        if QAT:
            lrx_rd = (s_mm, s_mm.n)
            lr2x_rd = (s_mm, s_mm.n)
        if st["dbg"] and QAT and th == 1 and st.get("stop_") == "f1res":
            ev_all3 = [(s_mm, s_mm.n), (s_ve, s_ve.n), (s_se, s_se.n),
                       (s_ge, s_ge.n)]
            dump_ev3 = None
            for nm3, t3 in [("f_lrx", lrx), ("f_lr2x", lr2x), ("f_hbuf", hbuf),
                            ("f_afc", afc), ("f_apj", apj), ("f_bpl", bpl),
                            ("f_wq1a", wqs1[0]), ("f_wq1b", wqs1[1]),
                            ("f_smalls", smalls), ("f_scal", scal),
                            ("f_xs", xs), ("f_xb", xb)]:
                ap3 = t3[:]
                od3 = nc.dram_tensor(nm3, list(ap3.shape), ap3.dtype,
                                     kind="ExternalOutput").ap()
                dump_ev3 = p.dma("gpsimd", out=od3, in_=ap3, waits=tuple(ev_all3),
                                 inc=(sems["s_tr"], 16))
            hb_free = dump_ev3
            lrx_rd = dump_ev3
            lr2x_rd = dump_ev3
            for bb2 in range(8):
                bank_free[bb2] = dump_ev3
            w1_rd = [dump_ev3, dump_ev3]
            w2_rd = [dump_ev3, dump_ev3]
    ph.close()
    return ev_out


def _layout_weights(inputs):
    f = np.ascontiguousarray
    w = {}
    Wqkv = inputs["L0_attn_Wqkv"]
    qr = np.concatenate([np.arange(192 * h, 192 * h + 64) for h in range(H)])
    kr = qr + 64
    vr = qr + 128
    WqT = Wqkv[qr].T
    WkT = Wqkv[kr].T
    WvT = Wqkv[vr].T
    l0_wqk = np.empty((8, 2, 8, 128, 128), np.float32)
    for pr in range(8):
        cols = slice(128 * pr, 128 * pr + 128)
        l0_wqk[pr, 0] = WqT[:, cols].reshape(8, 128, 128)
        l0_wqk[pr, 1] = WkT[:, cols].reshape(8, 128, 128)
    w["l0_wqk"] = f(l0_wqk)
    w["l0_wv"] = f(WvT.reshape(8, 128, D))
    bqkv = inputs["L0_attn_bqkv"]
    w["l0_bq"] = f(bqkv[qr].reshape(8, 128).T)
    w["l0_bk"] = f(bqkv[kr].reshape(8, 128).T)
    w["l0_bv"] = f(bqkv[vr].reshape(1, D))
    w["l0_wrT"] = f(inputs["L0_attn_Wr"].T)
    w["l0_br"] = f(inputs["L0_attn_br"].reshape(1, 257))
    W1 = inputs["L0_ffn_W1"]
    w["l0_w1"] = f(W1.reshape(8, 128, 32, 128).transpose(2, 0, 1, 3))
    w["l0_b1"] = f(inputs["L0_ffn_b1"].reshape(32, 128).T)
    W2 = inputs["L0_ffn_W2"]
    w["l0_w2"] = f(W2.reshape(32, 128, 2, 4, 128).transpose(2, 0, 3, 1, 4))
    w["l0_b2"] = f(inputs["L0_ffn_b2"].reshape(8, 128).T)
    l1_wqk = np.empty((8, 2, 8, 128, 128), np.float32)
    Wq, Wk = inputs["L1_attn_Wq"], inputs["L1_attn_Wk"]
    for pr in range(8):
        cols = slice(128 * pr, 128 * pr + 128)
        l1_wqk[pr, 0] = Wq[:, cols].reshape(8, 128, 128)
        l1_wqk[pr, 1] = Wk[:, cols].reshape(8, 128, 128)
    w["l1_wqk"] = f(l1_wqk)
    w["l1_wv"] = f(inputs["L1_attn_Wv"].reshape(8, 128, D))
    w["l1_bq"] = f(inputs["L1_attn_bq"].reshape(8, 128).T)
    w["l1_bk"] = f(inputs["L1_attn_bk"].reshape(8, 128).T)
    w["l1_bv"] = f(inputs["L1_attn_bv"].reshape(1, D))
    w["l1_wo"] = f(inputs["L1_attn_Wo"].reshape(8, 128, 8, 128).transpose(2, 0, 1, 3))
    w["l1_bo"] = f(inputs["L1_attn_bo"].reshape(8, 128).T)
    WfcT = inputs["L1_ffn_W_fc"].T
    w["l1_wfc"] = f(WfcT.reshape(8, 128, 32, 128).transpose(2, 0, 1, 3))
    w["l1_bfc"] = f(inputs["L1_ffn_b_fc"].reshape(32, 128).T)
    w["l1_afc"] = f(inputs["L1_ffn_A_fc"].T.reshape(8, 128, 16))
    w["l1_bfc_lo"] = f(inputs["L1_ffn_B_fc"].T.reshape(16, 32, 128).transpose(1, 0, 2))
    WprojT = inputs["L1_ffn_W_proj"].T
    w["l1_wproj"] = f(WprojT.reshape(32, 128, 2, 4, 128).transpose(2, 0, 3, 1, 4))
    w["l1_bproj"] = f(inputs["L1_ffn_b_proj"].reshape(8, 128).T)
    w["l1_aproj"] = f(inputs["L1_ffn_A_proj"].T.reshape(32, 128, 16))
    w["l1_bproj_lo"] = f(inputs["L1_ffn_B_proj"].T.reshape(16, 8, 128).transpose(1, 0, 2))
    return w


def kernel(_stop="full", _dbg=False, **inputs):
    x = inputs["L0_attn_x"]
    w = _layout_weights(inputs)
    nc = build_program(stop=_stop, dbg=_dbg)
    in_maps = []
    for b in range(NCORES):
        m = dict(w)
        m["x_fm"] = np.ascontiguousarray(x[b].T).reshape(NCH, 128, S)
        in_maps.append(m)
    res = run_bass_kernel_spmd(nc, in_maps, core_ids=list(range(NCORES)))
    out = np.empty((NCORES, S, D), np.float32)
    for b in range(NCORES):
        out[b] = res.results[b]["y"].reshape(D, S).T
    if _dbg:
        return out, res.results
    return out


# version 07
# speedup vs baseline: 1.2832x; 1.0055x over previous
import numpy as np
from contextlib import ExitStack
import concourse.bass as bass
import concourse.mybir as mybir
from concourse.bass_utils import run_bass_kernel_spmd

F32 = mybir.dt.float32
BF16 = mybir.dt.bfloat16
I32 = mybir.dt.int32
AF = mybir.ActivationFunctionType
ALU = mybir.AluOpType
AX = mybir.AxisListType
AP = bass.AP

NCORES = 8
S = 2048
D = 1024
H = 16
NCH = 8
NT = 16
EPS = 1e-5

ENGS = ("sync", "gpsimd", "vector", "scalar", "tensor")
OWN = {"tensor": "s_mm", "vector": "s_ve", "scalar": "s_se", "gpsimd": "s_ge"}


class Sem:
    __slots__ = ("h", "n", "name")

    def __init__(self, h, name):
        self.h = h
        self.n = 0
        self.name = name


class Prog:
    def __init__(self, nc, ctx):
        self.nc = nc
        self.ctx = ctx
        self.ops = {e: [] for e in ENGS}
        self.hw = {e: {} for e in ENGS}

    def sem(self, name):
        return Sem(self.ctx.enter_context(self.nc.semaphore(name)), name)

    def add(self, eng, fn, waits=(), inc=None):
        ws = []
        best = {}
        for w in waits:
            if w is None:
                continue
            s, v = w
            if s.name == OWN.get(eng):
                continue
            if best.get(s.name, (None, -1))[1] < v:
                best[s.name] = (s, v)
        for s, v in best.values():
            if self.hw[eng].get(s.name, -1) >= v:
                continue
            self.hw[eng][s.name] = v
            ws.append((s, v))
        rec = None
        if inc is not None:
            sem, k = inc
            sem.n += k
            rec = (sem, k)
        self.ops[eng].append((tuple(ws), fn, rec))
        if rec is not None:
            return (rec[0], rec[0].n)
        return None

    def dma(self, eng, out, in_, waits=(), inc=None, slow=False):
        return self.add(eng,
                        lambda e, o=out, i=in_, sl=slow: e.dma_start(
                            out=o, in_=i, allow_slow_non_contiguous=sl),
                        waits=waits, inc=inc)

    def emit(self, block):
        for ename in ENGS:
            oplist = self.ops[ename]

            def runner(eng, oplist=oplist):
                for ws, fn, rec in oplist:
                    for (s, v) in ws:
                        eng.wait_ge(s.h, v)
                    if fn is None:
                        continue
                    ins = fn(eng)
                    if rec is not None:
                        ins.then_inc(rec[0].h, rec[1])

            getattr(block, ename)(runner)


def l0_chunks(t):
    i0 = 128 * t
    lo = max(0, i0 - 128)
    hi = min(S, i0 + 256)
    out = []
    j = 0
    while j < lo:
        w = min(512, lo - j)
        out.append(("L", j, w))
        j += w
    out.append(("B", lo, hi - lo))
    j = hi
    while j < S:
        w = min(512, S - j)
        out.append(("R", j, w))
        j += w
    assert len(out) <= 8
    return out


def l1_chunks(t):
    i0 = 128 * t
    out = []
    j = 0
    while j < i0:
        w = min(512, i0 - j)
        out.append(("F", j, w))
        j += w
    out.append(("D", i0, 128))
    assert len(out) <= 8
    return out


_LN_N = [0]


def build_program(stop="full", dbg=False):
    _LN_N[0] = 0
    nc = bass.Bass("TRN2", target_bir_lowering=False,
                   detect_race_conditions=False)
    ctx = ExitStack()
    e = ctx.enter_context
    dt = {}

    def din(name, shape, dtype=F32):
        dt[name] = nc.dram_tensor(name, list(shape), dtype, kind="ExternalInput").ap()

    din("x_fm", [NCH, 128, S])
    y = nc.dram_tensor("y", [NCH, 128, S], F32, kind="ExternalOutput").ap()
    din("l0_wqk", [8, 2, 8, 128, 128]); din("l0_wv", [8, 128, D])
    din("l0_bq", [128, 8]); din("l0_bk", [128, 8]); din("l0_bv", [1, D])
    din("l0_wrT", [64, 257]); din("l0_br", [1, 257])
    din("l0_w1", [32, 8, 128, 128]); din("l0_b1", [128, 32])
    din("l0_w2", [2, 32, 4, 128, 128]); din("l0_b2", [128, 8])
    din("l1_wqk", [8, 2, 8, 128, 128]); din("l1_wv", [8, 128, D])
    din("l1_bq", [128, 8]); din("l1_bk", [128, 8]); din("l1_bv", [1, D])
    din("l1_wo", [8, 8, 128, 128]); din("l1_bo", [128, 8])
    din("l1_wfc", [32, 8, 128, 128]); din("l1_bfc", [128, 32])
    din("l1_afc", [8, 128, 16]); din("l1_bfc_lo", [32, 16, 128])
    din("l1_wproj", [2, 32, 4, 128, 128]); din("l1_bproj", [128, 8])
    din("l1_aproj", [32, 128, 16]); din("l1_bproj_lo", [8, 16, 128])

    hbm_h = [nc.dram_tensor(f"hshear{i}", [128, 513], F32, kind="Internal")
             for i in range(2)]
    hbm_wq = nc.dram_tensor("wprojq", [2, 32, 4, 128, 128], BF16, kind="Internal")
    hbm_at = nc.dram_tensor("attnhbm", [8, 128, S], BF16, kind="Internal")
    hbm_q = nc.dram_tensor("hbmq", [1, 512], F32, kind="Internal")
    hbm_wfq = nc.dram_tensor("wfcq", [32, 128, 1024], BF16, kind="Internal")
    hbm_wpq = nc.dram_tensor("wprojq2", [2, 32, 128, 512], BF16, kind="Internal")

    xs = e(nc.sbuf_tensor("xs", [128, NCH * S], F32))
    xb = e(nc.sbuf_tensor("xb", [128, NCH * S], BF16))
    ident = e(nc.sbuf_tensor("ident", [128, 128], BF16))
    tri01 = e(nc.sbuf_tensor("tri01", [128, 128], BF16))
    ones_r = e(nc.sbuf_tensor("ones_r", [1, 128], F32))
    halfs_r = e(nc.sbuf_tensor("halfs_r", [1, 128], F32))
    ones_rb = e(nc.sbuf_tensor("ones_rb", [1, 128], BF16))
    ones_c = e(nc.sbuf_tensor("ones_c", [128, 1], BF16))
    ones_cf = e(nc.sbuf_tensor("ones_cf", [128, 1], F32))
    smalls = e(nc.sbuf_tensor("smalls", [128, 136], F32))
    brow = e(nc.sbuf_tensor("brow", [1, 257], F32))
    bvrow = e(nc.sbuf_tensor("bvrow", [1, D], BF16))
    scal = e(nc.sbuf_tensor("scal", [1, 32], F32))

    SM = dict(L0BQ=0, L0BK=8, L0B1=16, L0B2=48, L1BQ=56, L1BK=64, L1BO=72,
              L1BFC=80, L1BPJ=112, WR8L=120, WR8R=121, NC1=122, NBR=123,
              BR0=124, BR256=125, RSFC=126, SFC=127, RSPJ=128, SPJ=129, HB=130)

    p = Prog(nc, ctx)
    sems = {k: p.sem(k) for k in
            ("s_x", "s_mm", "s_ve", "s_se", "s_ge",
             "s_wA", "s_wB", "s_hw0", "s_hw1", "s_br0", "s_br1",
             "s_qn", "s_rro", "s_out", "s_wv", "s_wo",
             "s_fA", "s_fB", "s_gA", "s_gB", "s_tr", "s_ah", "s_qs", "s_qc")}

    st = dict(stop_=stop, dbg=dbg, dbgout={}, nc=nc, p=p, ctx=ctx, dt=dt, y=y, xs=xs, xb=xb, ident=ident,
              tri01=tri01, ones_r=ones_r, halfs_r=halfs_r, ones_rb=ones_rb, ones_c=ones_c, ones_cf=ones_cf,
              smalls=smalls, brow=brow, bvrow=bvrow, scal=scal,
              hbm_h=hbm_h, hbm_wq=hbm_wq, hbm_at=hbm_at, hbm_q=hbm_q,
              hbm_wfq=hbm_wfq, hbm_wpq=hbm_wpq, sems=sems, SM=SM,
              track=dict(wslot_free=[None, None], qn_chain=None, rro_chain=None,
                         bv_rd=None, tr_chain=None, ah_wr=[None] * 8))

    s_x, s_ge, s_ve = sems["s_x"], sems["s_ge"], sems["s_ve"]
    for c in range(0, NCH, 2):
        p.dma("sync",
              out=xs[:].rearrange("p (c t) -> p c t", t=S)[:, c:c + 2, :],
              in_=dt["x_fm"][c:c + 2].transpose([1, 0, 2]), inc=(s_x, 16))

    def lsm(col, n, name):
        p.dma("sync", out=smalls[:, col:col + n], in_=dt[name], inc=(s_x, 16))

    lsm(SM["L0BQ"], 8, "l0_bq"); lsm(SM["L0BK"], 8, "l0_bk")
    lsm(SM["L0B1"], 32, "l0_b1"); lsm(SM["L0B2"], 8, "l0_b2")
    lsm(SM["L1BQ"], 8, "l1_bq"); lsm(SM["L1BK"], 8, "l1_bk")
    lsm(SM["L1BO"], 8, "l1_bo"); lsm(SM["L1BFC"], 32, "l1_bfc")
    lsm(SM["L1BPJ"], 8, "l1_bproj")
    p.dma("sync", out=brow[0:1, :], in_=dt["l0_br"], inc=(s_x, 16))
    ev_sx = (s_x, s_x.n)

    p.add("gpsimd", lambda g: g.memset(ident[:], 1.0))
    p.add("gpsimd", lambda g: g.affine_select(
        out=ident[:], in_=ident[:], pattern=[[-1, 128]], base=0,
        channel_multiplier=1, compare_op=ALU.is_equal, fill=0.0))
    p.add("gpsimd", lambda g: g.memset(tri01[:], 1.0))
    p.add("gpsimd", lambda g: g.affine_select(
        out=tri01[:], in_=tri01[:], pattern=[[-1, 128]], base=0,
        channel_multiplier=1, compare_op=ALU.is_ge, fill=0.0))
    p.add("gpsimd", lambda g: g.memset(ones_r[:], 1.0))
    p.add("gpsimd", lambda g: g.memset(halfs_r[:], 0.5))
    p.add("gpsimd", lambda g: g.memset(ones_rb[:], 1.0))
    p.add("gpsimd", lambda g: g.memset(ones_cf[:], 1.0))
    ev_cn = p.add("gpsimd", lambda g: g.memset(ones_c[:], 1.0), inc=(s_ge, 1))
    st["marks"] = dict(ev_sx=ev_sx, ev_cn=ev_cn)

    evc = None
    for c in range(NCH):
        sl = slice(c * S, (c + 1) * S)
        evc = p.add("vector",
                    lambda v, sl=sl: v.tensor_copy(out=xb[:, sl], in_=xs[:, sl]),
                    waits=(ev_sx,), inc=(s_ve, 1))
    ev = [evc]

    ev = _attention(st, ev, 0)
    if stop != "a0res":
        ev = _layernorm(st, ev)
        if stop != "x1":
            ev = _ffn(st, ev, 0)
            if stop != "f0res":
                ev = _layernorm(st, ev)
                if stop != "x2":
                    ev = _attention(st, ev, 1)
                    if stop != "a1res":
                        ev = _layernorm(st, ev)
                        if stop != "x3":
                            ev = _ffn(st, ev, 1)
                            if stop != "f1res":
                                ev = _layernorm(st, ev)

    s_out = sems["s_out"]
    for c in range(NCH):
        p.dma("sync", out=y[c], in_=xs[:, c * S:(c + 1) * S],
              waits=tuple(ev), inc=(s_out, 16))
    tot = s_out.n
    p.add("sync", None, waits=((s_out, tot),))
    p.add("gpsimd", lambda g: g.nop(), waits=((s_out, tot),))
    with nc.Block() as block:
        p.emit(block)
    ctx.close()
    return nc


def _attention(st, ev_in, layer):
    nc, p = st["nc"], st["p"]
    SM, dt = st["SM"], st["dt"]
    xs, xb = st["xs"], st["xb"]
    smalls, scal, brow, bvrow = st["smalls"], st["scal"], st["brow"], st["bvrow"]
    ones_r, ones_rb, ones_c = st["ones_r"], st["ones_rb"], st["ones_c"]
    ones_cf = st["ones_cf"]
    halfs_r = st["halfs_r"]
    ident, tri01 = st["ident"], st["tri01"]
    sems = st["sems"]
    s_mm, s_ve, s_se, s_ge = sems["s_mm"], sems["s_ve"], sems["s_se"], sems["s_ge"]
    s_hw = [sems["s_hw0"], sems["s_hw1"]]
    s_brd = [sems["s_br0"], sems["s_br1"]]
    s_qn, s_rro, s_wv, s_ah = sems["s_qn"], sems["s_rro"], sems["s_wv"], sems["s_ah"]
    swslot = [sems["s_wA"], sems["s_wB"]]
    tr = st["track"]
    L0 = layer == 0
    wname = "l0" if L0 else "l1"
    chunks_fn = l0_chunks if L0 else l1_chunks
    ev_cn, ev_sx = st["marks"]["ev_cn"], st["marks"]["ev_sx"]

    ph = ExitStack()
    ee = ph.enter_context
    vts = ee(nc.sbuf_tensor(f"vts{layer}", [128, NT * 1024], BF16))
    pc = [ee(nc.psum_tensor(f"pc{layer}{i}", [128, 512], F32)) for i in range(2)]
    prp = ee(nc.psum_tensor(f"prp{layer}", [128, 512], F32))
    ptA = ee(nc.psum_tensor(f"ptA{layer}", [128, 1024], BF16))
    ptB = ee(nc.psum_tensor(f"ptB{layer}", [128, 1024], BF16))
    ppv = [ee(nc.psum_tensor(f"ppv{layer}{i}", [128, 512], F32)) for i in range(2)]
    pbc = ee(nc.psum_tensor(f"pbc{layer}", [128, 512], F32))

    bank_free = [None, None]
    pbc_busy = None
    prp_busy = None

    ev_bv = p.dma("gpsimd", out=bvrow[:], in_=dt[wname + "_bv"],
                  waits=(tr["bv_rd"],), inc=(s_wv, 16))

    with nc.sbuf_tensor(f"wvs{layer}", [128, 8 * 1024], BF16) as wvs:
        p.dma("gpsimd", out=wvs[:].rearrange("p (c n) -> p c n", n=1024),
              in_=dt[wname + "_wv"].transpose([1, 0, 2]), inc=(s_wv, 16))
        ev_wv = (s_wv, s_wv.n)
        ev_vts = None
        for tt in range(NT):
            for ng in range(2):
                bank = (tt * 2 + ng) % 2
                b = pc[bank]
                p.add("tensor", lambda t, b=b, ng=ng: t.matmul(
                    b[:, 0:512], ones_rb[0:1, :],
                    bvrow[0:1, ng * 512:(ng + 1) * 512], start=True, stop=False),
                    waits=(ev_wv, ev_cn, bank_free[bank]) + tuple(ev_in),
                    inc=(s_mm, 1))
                ev_mv = None
                for kc in range(8):
                    ev_mv = p.add("tensor", lambda t, b=b, kc=kc, tt=tt, ng=ng: t.matmul(
                        b[:, 0:512],
                        xb[:, kc * S + tt * 128: kc * S + tt * 128 + 128],
                        wvs[:, kc * 1024 + ng * 512: kc * 1024 + (ng + 1) * 512],
                        start=False, stop=(kc == 7)), inc=(s_mm, 1))
                ev_vts = p.add("vector", lambda v, b=b, tt=tt, ng=ng: v.tensor_copy(
                    out=vts[:, tt * 1024 + ng * 512: tt * 1024 + (ng + 1) * 512],
                    in_=b[:, 0:512]), waits=(ev_mv,), inc=(s_ve, 1))
                bank_free[bank] = ev_vts
        tr["bv_rd"] = (s_mm, s_mm.n)
        ev_vdone = (s_mm, s_mm.n)

    sub = ExitStack()
    se = sub.enter_context
    wqs = [se(nc.sbuf_tensor(f"wqs{layer}{i}", [128, 2048], BF16)) for i in range(2)]
    qat = se(nc.sbuf_tensor(f"qat{layer}", [128, S], BF16))
    kat = se(nc.sbuf_tensor(f"kat{layer}", [128, S], BF16))
    sq2 = se(nc.sbuf_tensor(f"sq2{layer}", [128, S], BF16))
    P = [se(nc.sbuf_tensor(f"P{layer}{i}", [128, S], BF16)) for i in range(3)]
    PTs = se(nc.sbuf_tensor(f"PTs{layer}", [128, 8192], BF16))
    sums = se(nc.sbuf_tensor(f"sums{layer}", [128, 128], F32))
    ssml = se(nc.sbuf_tensor(f"ssml{layer}", [128, 64], F32))
    qn16 = se(nc.sbuf_tensor(f"qn16{layer}", [128, 16], F32))
    qrow = se(nc.sbuf_tensor(f"qrow{layer}", [1, 512], F32))
    Pf = se(nc.sbuf_tensor(f"Pf{layer}", [128, 2 * S], F32))
    i8 = se(nc.sbuf_tensor(f"i8{layer}", [128, 48], F32))
    if L0:
        wr2m = se(nc.sbuf_tensor("wr2m", [128, 257], BF16))
        wrf32 = se(nc.sbuf_tensor("wrf32", [64, 257], F32))
        rpe = [se(nc.sbuf_tensor(f"rpe{i}", [128, 513], F32)) for i in range(2)]
        bb0 = se(nc.sbuf_tensor("bb0", [128, 384], F32))
        bb = [bb0, bb0]
        sband = [pbc, pbc]
    else:
        Pd = se(nc.sbuf_tensor("Pd", [128, 128], F32))
        attn1 = se(nc.sbuf_tensor("attn1", [128, S], BF16))

    if L0:
        s_trx = sems["s_tr"]
        p.dma("gpsimd", out=wr2m[0:64, :], in_=dt["l0_wrT"], waits=(ev_vdone,),
              inc=(s_trx, 16))
        p.dma("gpsimd", out=wr2m[64:128, :], in_=dt["l0_wrT"], inc=(s_trx, 16))
        p.dma("gpsimd", out=wrf32[0:64, :], in_=dt["l0_wrT"], inc=(s_trx, 16))
        ev_wrl = (s_trx, s_trx.n)
        ev_q = p.add("scalar", lambda s: s.activation(
            out=wrf32[0:64, :], in_=wrf32[0:64, :], func=AF.Square),
            waits=(ev_wrl,), inc=(s_se, 1))
        ev_m2 = p.add("tensor", lambda t: t.matmul(
            pc[0][0:1, 0:257], ones_cf[0:64, 0:1], wrf32[0:64, :],
            start=True, stop=True), waits=(ev_q, ev_cn, bank_free[0]), inc=(s_mm, 1))
        ev_wm = p.add("vector", lambda v: v.reduce_max(
            out=scal[0:1, 0:1], in_=pc[0][0:1, 0:257], axis=AX.X),
            waits=(ev_m2,), inc=(s_ve, 1))
        bank_free[0] = ev_wm
        ev_ws = p.add("scalar", lambda s: s.activation(
            out=scal[0:1, 1:2], in_=scal[0:1, 0:1], func=AF.Sqrt),
            waits=(ev_wm,), inc=(s_se, 1))
        ev_nb = p.add("vector", lambda v: v.reduce_max(
            out=scal[0:1, 3:4], in_=brow[0:1, :], axis=AX.X,
            apply_absolute_value=True, negate=True), waits=(ev_sx,),
            inc=(s_ve, 1))
        ev_b1 = p.add("tensor", lambda t: t.matmul(
            pbc[:, 300:301], halfs_r[0:1, :], scal[0:1, 3:4], start=True, stop=True),
            waits=(ev_nb,), inc=(s_mm, 1))
        ev_c3 = p.add("vector", lambda v: v.tensor_copy(
            out=smalls[:, SM["NBR"]:SM["NBR"] + 1], in_=pbc[:, 300:301]),
            waits=(ev_b1,), inc=(s_ve, 1))
        pbc_busy = ev_c3
        ev_consts = ev_c3
        scal_sqrt_rd = None
    else:
        ev_wrl = ev_ws = ev_consts = None
        scal_sqrt_rd = None

    ptA_free = None
    ptB_free = None
    p_free = [None, None, None]
    sums_free = None
    sq2_free = None
    rpe_hw = [None, None]
    bb_free = [None, None]
    sband_free = [None, None]
    hb_rd = [None, None]
    pt_region_pv = [None] * 4
    ppv_free = [None, None]
    pf_free = [None, None]
    i8_free = [None, None, None]
    qk_free = None
    qrow_rd = None
    scal_rd = None
    ev_last = None

    wslot_ev = [None, None]
    wslot_ev[0] = p.dma("gpsimd",
                        out=wqs[0][:].rearrange("p (a c m) -> p a c m", a=2, m=128),
                        in_=dt[wname + "_wqk"][0].transpose([2, 0, 1, 3]),
                        waits=(tr["wslot_free"][0], ev_vdone), inc=(swslot[0], 16))

    for pr in range(8):
        slot = pr % 2
        if pr + 1 < 8:
            ns = (pr + 1) % 2
            wslot_ev[ns] = p.dma(
                "gpsimd",
                out=wqs[ns][:].rearrange("p (a c m) -> p a c m", a=2, m=128),
                in_=dt[wname + "_wqk"][pr + 1].transpose([2, 0, 1, 3]),
                waits=(tr["wslot_free"][ns], ev_vdone), inc=(swslot[ns], 16))

        first = True
        ev_mv = None
        ev_ev = None
        for qk in range(2):
            dst = qat if qk == 0 else kat
            bcol = (SM["L0BQ" if L0 else "L1BQ"] if qk == 0
                    else SM["L0BK" if L0 else "L1BK"]) + pr
            for ng in range(4):
                bank = (qk * 4 + ng) % 2
                b = pc[bank]
                for kc in range(8):
                    wts = [bank_free[bank]] if kc == 0 else []
                    if first:
                        wts += [wslot_ev[slot], qk_free,
                                (s_se, s_se.n), (s_ve, s_ve.n)] + list(ev_in)
                        first = False
                    ev_mv = p.add("tensor", lambda t, b=b, slot=slot, qk=qk, kc=kc, ng=ng: t.matmul(
                        b[:, 0:512],
                        wqs[slot][:, qk * 1024 + kc * 128: qk * 1024 + (kc + 1) * 128],
                        xb[:, kc * S + ng * 512: kc * S + (ng + 1) * 512],
                        start=(kc == 0), stop=(kc == 7)),
                        waits=wts, inc=(s_mm, 1))
                ev_ev = p.add("scalar", lambda s, b=b, dst=dst, ng=ng, bcol=bcol: s.activation(
                    out=dst[:, ng * 512:(ng + 1) * 512], in_=b[:, 0:512],
                    func=AF.Identity, bias=smalls[:, bcol:bcol + 1], scale=1.0),
                    waits=(ev_sx, ev_mv), inc=(s_se, 1))
                bank_free[bank] = ev_ev
        ev_qk = ev_ev
        tr["wslot_free"][slot] = (s_mm, s_mm.n)

        ev_kvar = ev_qk

        for hh in range(2):
            h = 2 * pr + hh
            hb = hh * 64
            hsl = slice(hb, hb + 64)

            ev_k2 = p.add("scalar", lambda s, hsl=hsl: s.activation(
                out=sq2[hsl, :], in_=kat[hsl, :], func=AF.Square),
                waits=(ev_qk, sq2_free), inc=(s_se, 1))
            kn_ev = None
            for g in range(4):
                ev_mk = p.add("tensor", lambda t, g=g, hsl=hsl: t.matmul(
                    pc[0][0:1, 0:512], ones_c[hsl, 0:1],
                    sq2[hsl, g * 512:(g + 1) * 512], start=True, stop=True),
                    waits=(ev_k2, bank_free[0], kn_ev), inc=(s_mm, 1))
                kn_ev = p.add("vector", lambda v, g=g: v.reduce_max(
                    out=scal[0:1, 8 + g:9 + g], in_=pc[0][0:1, 0:512], axis=AX.X),
                    waits=(ev_mk, scal_rd), inc=(s_ve, 1))
            bank_free[0] = kn_ev
            ev_kb = p.add("gpsimd", lambda g: g.tensor_copy(
                out=scal[0:1, 16:20], in_=scal[0:1, 8:12]),
                waits=(kn_ev,), inc=(s_ge, 1))
            ev_km = p.add("vector", lambda v: v.reduce_max(
                out=scal[0:1, 12:13], in_=scal[0:1, 16:20], axis=AX.X),
                waits=(scal_sqrt_rd, ev_kb), inc=(s_ve, 1))
            ev_ks = p.add("scalar", lambda s: s.activation(
                out=scal[0:1, 13:14], in_=scal[0:1, 12:13], func=AF.Sqrt),
                waits=(ev_km,), inc=(s_se, 1))
            scal_sqrt_rd = ev_ks
            if L0:
                ev_n = p.add("vector", lambda v: v.scalar_tensor_tensor(
                    out=scal[0:1, 15:16], in0=scal[0:1, 13:14], scalar=-0.125,
                    in1=scal[0:1, 1:2], op0=ALU.mult, op1=ALU.subtract),
                    waits=(ev_ks, ev_ws), inc=(s_ve, 1))
            else:
                ev_n = p.add("vector", lambda v: v.tensor_scalar(
                    out=scal[0:1, 15:16], in0=scal[0:1, 13:14], scalar1=-0.125,
                    scalar2=None, op0=ALU.mult), waits=(ev_ks,), inc=(s_ve, 1))
            ev_bcm = p.add("tensor", lambda t: t.matmul(
                pbc[:, 0:1], halfs_r[0:1, :], scal[0:1, 15:16], start=True, stop=True),
                waits=(ev_n, pbc_busy), inc=(s_mm, 1))
            scal_rd = ev_bcm
            ev_nc1 = p.add("vector", lambda v: v.tensor_copy(
                out=smalls[:, SM["NC1"]:SM["NC1"] + 1], in_=pbc[:, 0:1]),
                waits=(ev_bcm,), inc=(s_ve, 1))
            pbc_busy = ev_nc1

            ev_q2 = p.add("scalar", lambda s, hsl=hsl: s.activation(
                out=sq2[hsl, :], in_=qat[hsl, :], func=AF.Square),
                waits=(kn_ev,), inc=(s_se, 1))
            qn_dma = None
            for g in range(4):
                ev_mq = p.add("tensor", lambda t, g=g, hsl=hsl: t.matmul(
                    pc[0][0:1, 0:512], ones_c[hsl, 0:1],
                    sq2[hsl, g * 512:(g + 1) * 512], start=True, stop=True),
                    waits=(ev_q2, bank_free[0]), inc=(s_mm, 1))
                ev_qc = p.add("vector", lambda v: v.tensor_copy(
                    out=qrow[0:1, :], in_=pc[0][0:1, 0:512]),
                    waits=(ev_mq, qrow_rd), inc=(s_ve, 1))
                bank_free[0] = ev_qc
                p.dma("sync", out=st["hbm_q"].ap(), in_=qrow[0:1, :],
                      waits=(ev_qc, tr["qn_chain"]), inc=(s_qn, 16))
                ev_qw = (s_qn, s_qn.n)
                hqap = st["hbm_q"].ap()
                qn_dma = p.dma("sync", out=qn16[:, 4 * g:4 * g + 4],
                               in_=AP(hqap.tensor, 0, [[1, 128], [128, 4]]),
                               waits=(ev_qw,), inc=(s_qn, 16), slow=True)
                tr["qn_chain"] = qn_dma
                qrow_rd = ev_qw
            sq2_free = (s_mm, s_mm.n)
            ev_sq16 = p.add("scalar", lambda s: s.activation(
                out=ssml[:, 48:64], in_=qn16[:, 0:16], func=AF.Sqrt),
                waits=(qn_dma,), inc=(s_se, 1))
            tr["qn_chain"] = ev_sq16
            ev_M = p.add("vector", lambda v: v.tensor_scalar(
                out=ssml[:, 0:16], in0=ssml[:, 48:64],
                scalar1=smalls[:, SM["NC1"]:SM["NC1"] + 1],
                scalar2=(smalls[:, SM["NBR"]:SM["NBR"] + 1] if L0 else 0.0),
                op0=ALU.mult, op1=ALU.add),
                waits=(ev_sq16, ev_nc1, ev_consts), inc=(s_ve, 1))
            ev_prep = ev_M
            ev_sz = p.add("gpsimd", lambda g: g.memset(sums[:, :], 0.0),
                          waits=(sums_free,), inc=(s_ge, 1))

            exp_last = {}
            ptev = {}
            pd_free = None
            ev_pv = None
            ev_am = None
            cyc = 0
            def emit_norm(t, ev_ex):
                pslot = t % 3
                pfo = (t % 2) * S
                o8 = (t % 3) * 16
                ext = S if L0 else 128 * (t + 1)
                if L0:
                    ev_a = p.add("vector", lambda v, t=t, o8=o8: v.reduce_sum(
                        out=i8[:, o8:o8 + 1], in_=sums[:, t * 8:(t + 1) * 8],
                        axis=AX.X), waits=(ev_ex, i8_free[t % 3]), inc=(s_ve, 1))
                else:
                    ev_a0 = p.add("gpsimd", lambda g, t=t, o8=o8: g.tensor_copy(
                        out=i8[:, o8 + 8:o8 + 16],
                        in_=sums[:, t * 8:(t + 1) * 8]),
                        waits=(ev_ex, i8_free[t % 3]), inc=(s_ge, 1))
                    ev_a = p.add("vector", lambda v, o8=o8: v.reduce_sum(
                        out=i8[:, o8:o8 + 1], in_=i8[:, o8 + 8:o8 + 16],
                        axis=AX.X), waits=(ev_a0,), inc=(s_ve, 1))
                ev_b = p.add("gpsimd", lambda g, o8=o8: g.tensor_scalar(
                    out=i8[:, o8 + 1:o8 + 2], in0=i8[:, o8:o8 + 1],
                    scalar1=1e-30, scalar2=None, op0=ALU.add),
                    waits=(ev_a,), inc=(s_ge, 1))
                ev_c = p.add("vector", lambda v, o8=o8: v.reciprocal(
                    out=i8[:, o8 + 2:o8 + 3], in_=i8[:, o8 + 1:o8 + 2]),
                    waits=(ev_b,), inc=(s_ve, 1))
                ev_c2 = p.add("gpsimd", lambda g, o8=o8: g.tensor_scalar(
                    out=i8[:, o8 + 3:o8 + 4], in0=i8[:, o8 + 2:o8 + 3],
                    scalar1=i8[:, o8 + 2:o8 + 3], scalar2=1e37,
                    op0=ALU.mult, op1=ALU.min),
                    waits=(ev_c,), inc=(s_ge, 1))
                ev_e = p.add("vector", lambda v, o8=o8, ext=ext, pfo=pfo, pslot=pslot: v.scalar_tensor_tensor(
                    out=P[pslot][:, 0:ext], in0=Pf[:, pfo:pfo + ext],
                    scalar=i8[:, o8 + 3:o8 + 4],
                    in1=Pf[:, pfo:pfo + ext],
                    op0=ALU.mult, op1=ALU.mult,
                    accum_out=i8[:, o8 + 4:o8 + 5]),
                    waits=(ev_c2, p_free[pslot]), inc=(s_ve, 1))
                pf_free[t % 2] = ev_e
                ev_f = p.add("gpsimd", lambda g, o8=o8: g.tensor_scalar(
                    out=i8[:, o8 + 5:o8 + 6], in0=i8[:, o8 + 4:o8 + 5],
                    scalar1=1e-30, scalar2=None, op0=ALU.add),
                    waits=(ev_e,), inc=(s_ge, 1))
                ev_g = p.add("vector", lambda v, o8=o8: v.reciprocal(
                    out=i8[:, o8 + 6:o8 + 7], in_=i8[:, o8 + 5:o8 + 6]),
                    waits=(ev_f,), inc=(s_ve, 1))
                ev_h2 = p.add("gpsimd", lambda g, o8=o8: g.tensor_copy(
                    out=i8[:, o8 + 7:o8 + 8], in_=i8[:, o8 + 6:o8 + 7]),
                    waits=(ev_g,), inc=(s_ge, 1))
                ev_i = p.add("vector", lambda v, o8=o8, ext=ext, pslot=pslot: v.tensor_scalar(
                    out=P[pslot][:, 0:ext], in0=P[pslot][:, 0:ext],
                    scalar1=i8[:, o8 + 7:o8 + 8], scalar2=None,
                    op0=ALU.mult), waits=(ev_h2, ev_e), inc=(s_ve, 1))
                i8_free[t % 3] = ev_i
                exp_last[t] = ev_i

            for t in range(NT + 2):
                if t < NT:
                    chl = chunks_fn(t)
                    tb = t % 2
                    pslot = t % 3
                    if L0:
                        p.add("tensor", lambda tn: tn.matmul(
                            prp[:, 0:257], ones_r[0:1, :], brow[0:1, 0:257],
                            start=True, stop=False),
                            waits=(ev_sx, ev_cn, prp_busy), inc=(s_mm, 1))
                        ev_rp = p.add("tensor", lambda tn, t=t, hsl=hsl: tn.matmul(
                            prp[:, 0:257], qat[hsl, t * 128:(t + 1) * 128],
                            wr2m[hsl, :], start=False, stop=True),
                            waits=(ev_prep, ev_wrl), inc=(s_mm, 1))
                        ev_rpe = p.add("vector", lambda v, tb=tb: v.tensor_copy(
                            out=rpe[tb][:, 128:385], in_=prp[:, 0:257]),
                            waits=(ev_rp, rpe_hw[tb]), inc=(s_ve, 1))
                        prp_busy = ev_rpe
                        p.add("vector", lambda v, tb=tb: v.tensor_copy(
                            out=rpe[tb][:, 0:128],
                            in_=rpe[tb][:, 128:129].broadcast_to([128, 128])),
                            inc=(s_ve, 1))
                        ev_fill = p.add("vector", lambda v, tb=tb: v.tensor_copy(
                            out=rpe[tb][:, 385:513],
                            in_=rpe[tb][:, 384:385].broadcast_to([128, 128])),
                            inc=(s_ve, 1))
                        p.dma("sync", out=st["hbm_h"][tb].ap(), in_=rpe[tb][:],
                              waits=(ev_fill, hb_rd[tb]), inc=(s_hw[tb], 16))
                        ev_hwr = (s_hw[tb], s_hw[tb].n)
                        rpe_hw[tb] = ev_hwr
                        i0 = 128 * t
                        lo = max(0, i0 - 128)
                        wB = min(S, i0 + 256) - lo
                        off = lo - i0 + 256
                        hap = st["hbm_h"][tb].ap()
                        shear = AP(hap.tensor, off, [[512, 128], [1, wB]])
                        p.dma("sync", out=bb[tb][:, 0:wB], in_=shear,
                              waits=(ev_hwr, bb_free[tb]), inc=(s_brd[tb], 16))
                        hb_rd[tb] = (s_brd[tb], s_brd[tb].n)
                        p.add("gpsimd", lambda g, tb=tb, t=t: g.tensor_scalar(
                            out=ssml[:, 16 + t:17 + t], in0=rpe[tb][:, 128:129],
                            scalar1=0.5, scalar2=ssml[:, t:t + 1],
                            op0=ALU.mult, op1=ALU.add),
                            waits=(ev_rpe, ev_prep), inc=(s_ge, 1))
                        ev_bfar = p.add("gpsimd", lambda g, tb=tb, t=t: g.tensor_scalar(
                            out=ssml[:, 32 + t:33 + t], in0=rpe[tb][:, 384:385],
                            scalar1=0.5, scalar2=ssml[:, t:t + 1],
                            op0=ALU.mult, op1=ALU.add),
                            inc=(s_ge, 1))
                    if t >= 1:
                        emit_norm(t - 1, exp_last[t - 1])
                    ev_ex = None
                    pfo = (t % 2) * S
                    if L0:
                        chl = ([c for c in chl if c[0] != "B"]
                               + [c for c in chl if c[0] == "B"])
                    for ci, (kind, j0, w) in enumerate(chl):
                        bank = cyc % 2
                        cyc += 1
                        b = pc[bank]
                        ev_cm = p.add("tensor", lambda tn, b=b, j0=j0, w=w, t=t, hsl=hsl: tn.matmul(
                            b[:, 0:w], qat[hsl, t * 128:(t + 1) * 128],
                            kat[hsl, j0:j0 + w], start=True, stop=True),
                            waits=(bank_free[bank], ev_kvar, ev_prep), inc=(s_mm, 1))
                        slot_ap = sums[:, t * 8 + ci: t * 8 + ci + 1]
                        if L0 and kind == "B":
                            ev_bd = p.add("vector", lambda v, b=b, w=w, tb=tb: v.scalar_tensor_tensor(
                                out=sband[tb][:, 0:w], in0=b[:, 0:w],
                                scalar=0.125, in1=bb[tb][:, 0:w],
                                op0=ALU.mult, op1=ALU.add),
                                waits=(ev_cm, hb_rd[tb], sband_free[0],
                                       sband_free[1]),
                                inc=(s_ve, 1))
                            bank_free[bank] = ev_bd
                            bb_free[0] = ev_bd
                            bb_free[1] = ev_bd
                            ev_ex = p.add("scalar", lambda s, w=w, j0=j0, t=t, tb=tb, pfo=pfo, slot_ap=slot_ap: s.activation(
                                out=Pf[:, pfo + j0:pfo + j0 + w], in_=sband[tb][:, 0:w],
                                func=AF.Exp, bias=ssml[:, t:t + 1], scale=0.5,
                                accum_out=slot_ap),
                                waits=(ev_bd, ev_sz, pf_free[t % 2]), inc=(s_se, 1))
                            sband_free[tb] = ev_ex
                        elif (not L0) and kind == "D":
                            ev_e0 = p.add("scalar", lambda s, b=b, t=t: s.activation(
                                out=Pd[:, :], in_=b[:, 0:128], func=AF.Exp,
                                bias=ssml[:, t:t + 1], scale=0.0625),
                                waits=(ev_cm, ev_sz, pd_free, pf_free[t % 2]), inc=(s_se, 1))
                            bank_free[bank] = ev_e0
                            ev_ex = p.add("vector", lambda v, j0=j0, pfo=pfo, slot_ap=slot_ap: v.scalar_tensor_tensor(
                                out=Pf[:, pfo + j0:pfo + j0 + 128], in0=Pd[:, :],
                                scalar=1.0, in1=tri01[:, :],
                                op0=ALU.mult, op1=ALU.mult, accum_out=slot_ap),
                                waits=(ev_e0, ev_sz, ev_cn),
                                inc=(s_ve, 1))
                            pd_free = ev_ex
                        else:
                            bcoli = {"L": 16, "B": 0, "R": 32, "F": 0}[kind] + t
                            wts_x = (ev_cm, ev_sz, pf_free[t % 2])
                            if L0 and kind in ("L", "R"):
                                wts_x += (ev_bfar,)
                            ev_ex = p.add("scalar", lambda s, b=b, w=w, j0=j0, pfo=pfo, bcoli=bcoli, slot_ap=slot_ap: s.activation(
                                out=Pf[:, pfo + j0:pfo + j0 + w], in_=b[:, 0:w],
                                func=AF.Exp, bias=ssml[:, bcoli:bcoli + 1],
                                scale=0.0625, accum_out=slot_ap),
                                waits=wts_x, inc=(s_se, 1))
                            bank_free[bank] = ev_ex
                    exp_last[t] = ev_ex


                if t == NT:
                    emit_norm(NT - 1, exp_last[NT - 1])
                tp = t - 2
                if tp >= 0:
                    nkc = NT if L0 else (tp + 1)
                    pslot2 = tp % 3
                    evA = evB = None
                    for a in range(nkc):
                        dst2 = ptA if a < 8 else ptB
                        dsl = slice((a % 8) * 128, (a % 8) * 128 + 128)
                        wts = []
                        if a == 0:
                            wts = [exp_last[tp], ptA_free]
                        if a == 8:
                            wts = [ptB_free]
                        ev_t2 = p.add("tensor", lambda tn, dst2=dst2, dsl=dsl, a=a, pslot2=pslot2: tn.transpose(
                            dst2[:, dsl], P[pslot2][:, a * 128:(a + 1) * 128],
                            ident[:]), waits=wts, inc=(s_mm, 1))
                        if a == min(nkc, 8) - 1:
                            evA = ev_t2
                        if a == nkc - 1:
                            evB = ev_t2
                    p_free[pslot2] = evB
                    region = tp % 4
                    nA = min(nkc, 8)
                    ev_eA = p.add("vector", lambda v, region=region, nA=nA: v.tensor_copy(
                        out=PTs[:, region * 2048: region * 2048 + nA * 128],
                        in_=ptA[:, 0:nA * 128]),
                        waits=(evA, pt_region_pv[region]), inc=(s_ve, 1))
                    ptA_free = ev_eA
                    ev_eB = None
                    if nkc > 8:
                        nB = nkc - 8
                        ev_eB = p.add("vector", lambda v, region=region, nB=nB: v.tensor_copy(
                            out=PTs[:, region * 2048 + 1024: region * 2048 + 1024 + nB * 128],
                            in_=ptB[:, 0:nB * 128]),
                            waits=(evB, pt_region_pv[region]), inc=(s_ve, 1))
                        ptB_free = ev_eB
                    ptev[tp] = (ev_eA, ev_eB)

                if tp >= 3 and tp % 4 == 3:
                    g = tp // 4
                    tpos = None if hb == 0 else (0, 64)
                    pvb = ppv[hh]
                    nk = NT if L0 else (4 * g + 4)
                    ptsap = PTs[:]
                    for a in range(nk):
                        if L0 or a <= 4 * g:
                            qoff, qn = 0, 512
                        else:
                            qoff = (a - 4 * g) * 128
                            qn = 512 - qoff
                        rhs = AP(ptsap.tensor,
                                 ptsap.offset + a * 128 + (qoff // 128) * 2048,
                                 [[8192, 128], [2048, qn // 128], [1, 128]])
                        wts = []
                        if a == 0:
                            wts = [x for tpi in range(4 * g, 4 * g + 4)
                                   for x in ptev[tpi] if x]
                            wts += [ev_vts, ppv_free[hh]]
                        ev_pv = p.add("tensor", lambda tn, pvb=pvb, a=a, h=h, rhs=rhs, qoff=qoff, qn=qn, hsl=hsl, tpos=tpos, nk=nk: tn.matmul(
                            pvb[hsl, qoff:qoff + qn],
                            vts[:, a * 1024 + 64 * h: a * 1024 + 64 * h + 64],
                            rhs, start=(a == 0), stop=(a == nk - 1),
                            tile_position=tpos),
                            waits=wts, inc=(s_mm, 1))
                    for r4 in range(4):
                        pt_region_pv[r4] = ev_pv
                    if L0:
                        ev_am = p.add("vector", lambda v, g=g, hsl=hsl, pr=pr, pvb=pvb: v.scalar_tensor_tensor(
                            out=xs[:, pr * S + g * 512: pr * S + (g + 1) * 512][hsl, :],
                            in0=pvb[hsl, 0:512], scalar=0.0,
                            in1=xs[:, pr * S + g * 512: pr * S + (g + 1) * 512][hsl, :],
                            op0=ALU.add, op1=ALU.add),
                            waits=(ev_sx, ev_pv), inc=(s_ve, 1))
                    else:
                        ev_am = p.add("vector", lambda v, g=g, hsl=hsl, pvb=pvb: v.tensor_copy(
                            out=attn1[hsl, g * 512:(g + 1) * 512],
                            in_=pvb[hsl, 0:512]),
                            waits=(ev_pv,), inc=(s_ve, 1))
                    ppv_free[hh] = ev_am
                    ev_last = ev_am
                    if st["dbg"] and pr == 0 and hh == 0 and (not L0) and g == 0 \
                            and st.get("stop_") == "a1res":
                        ev_all2 = [(s_mm, s_mm.n), (s_ve, s_ve.n),
                                   (s_se, s_se.n), (s_ge, s_ge.n)]
                        for nm2, t2 in [("e_PTs", PTs), ("e_P0", P[0]),
                                        ("e_P1", P[1]), ("e_sums", sums),
                                        ("e_at1", attn1), ("e_Pf", Pf),
                                        ("e_vts", vts), ("e_i8", i8)]:
                            ap2 = t2[:]
                            od2 = nc.dram_tensor(nm2, list(ap2.shape), ap2.dtype,
                                                 kind="ExternalOutput").ap()
                            dump_ev2 = p.dma("gpsimd", out=od2, in_=ap2,
                                             waits=tuple(ev_all2),
                                             inc=(sems["s_tr"], 16))
                        for k2 in range(3):
                            p_free[k2] = dump_ev2
                            i8_free[k2] = dump_ev2
                        pf_free = [dump_ev2, dump_ev2]
                        for r42 in range(4):
                            pt_region_pv[r42] = dump_ev2
                        ppv_free[hh] = dump_ev2
            if st["dbg"] and pr == 0 and hh == 0 and (L0 == (st.get("stop_", "a0res") == "a0res")):
                ev_all = [(s_mm, s_mm.n), (s_ve, s_ve.n), (s_se, s_se.n),
                          (s_ge, s_ge.n)]
                s_dbg = sems["s_tr"]
                dump_list = [("d_qat", qat), ("d_kat", kat),
                             ("d_P0", P[0]), ("d_P1", P[1]),
                             ("d_sums", sums), ("d_ssml", ssml),
                             ("d_qn16", qn16), ("d_scal", scal),
                             ("d_smalls", smalls), ("d_PTs", PTs),
                             ("d_vts", vts), ("d_Pf", Pf),
                             ("d_i8", i8), ("d_xs0", xs[:, 0:S])]
                if L0:
                    dump_list += [("d_rpe0", rpe[0]), ("d_rpe1", rpe[1]),
                                  ("d_bb0", bb[0]), ("d_bb1", bb[1])]
                else:
                    dump_list += [("d_Pd", Pd), ("d_at1", attn1)]
                dump_ev = None
                for nm, tsr in dump_list:
                    ap = tsr if isinstance(tsr, bass.AP) else tsr[:]
                    od = nc.dram_tensor(nm, list(ap.shape), ap.dtype,
                                        kind="ExternalOutput").ap()
                    st["dbgout"][nm] = True
                    dump_ev = p.dma("gpsimd", out=od, in_=ap,
                                    waits=tuple(ev_all), inc=(s_dbg, 16))
                for k in range(3):
                    p_free[k] = dump_ev
                for k in range(2):
                    rpe_hw[k] = dump_ev
                    bb_free[k] = dump_ev
                    sband_free[k] = dump_ev
                sums_free = dump_ev
                sq2_free = dump_ev
            sums_free = (s_ve, s_ve.n)
            qk_free = (s_mm, s_mm.n)

        if not L0:
            p.dma("sync", out=st["hbm_at"][pr], in_=attn1[:, :],
                  waits=(ev_last, tr["ah_wr"][pr]), inc=(s_ah, 16))
            tr["ah_wr"][pr] = (s_ah, s_ah.n)
            ppv_free = [tr["ah_wr"][pr], tr["ah_wr"][pr]]

    if L0:
        res = [(s_ve, s_ve.n)]
        sub.close()
        ph.close()
        return res

    ev_attn_done = [(s_mm, s_mm.n), (s_ve, s_ve.n), (s_se, s_se.n)]
    sub.close()
    ob = ExitStack()
    oe = ob.enter_context
    wob = oe(nc.sbuf_tensor("wob", [128, 8 * 8 * 128], BF16))
    ats = oe(nc.sbuf_tensor("ats", [128, NCH * S], BF16))
    s_wo = sems["s_wo"]
    p.dma("gpsimd",
          out=wob[:].rearrange("p (a c m) -> p a c m", a=8, m=128),
          in_=dt["l1_wo"].transpose([2, 0, 1, 3]),
          waits=tuple(ev_attn_done), inc=(s_wo, 16))
    for kc in range(8):
        p.dma("sync", out=ats[:, kc * S:(kc + 1) * S], in_=st["hbm_at"][kc],
              waits=[(s_ah, s_ah.n)] + list(ev_attn_done), inc=(s_wo, 16))
    ev_wo = (s_wo, s_wo.n)
    bank_free = [None, None]
    ev_o = None
    for oc in range(8):
        for tg in range(4):
            bank = (oc * 4 + tg) % 2
            b = pc[bank]
            ev_m3 = None
            for kc in range(8):
                wts = [bank_free[bank], ev_wo] if kc == 0 else []
                ev_m3 = p.add("tensor", lambda t, b=b, oc=oc, tg=tg, kc=kc: t.matmul(
                    b[:, 0:512],
                    wob[:, (oc * 8 + kc) * 128:(oc * 8 + kc) * 128 + 128],
                    ats[:, kc * S + tg * 512: kc * S + (tg + 1) * 512],
                    start=(kc == 0), stop=(kc == 7)), waits=wts, inc=(s_mm, 1))
            ev_o = p.add("vector", lambda v, b=b, oc=oc, tg=tg: v.scalar_tensor_tensor(
                out=xs[:, oc * S + tg * 512: oc * S + (tg + 1) * 512],
                in0=b[:, 0:512],
                scalar=smalls[:, SM["L1BO"] + oc:SM["L1BO"] + oc + 1],
                in1=xs[:, oc * S + tg * 512: oc * S + (tg + 1) * 512],
                op0=ALU.add, op1=ALU.add),
                waits=(ev_m3, ev_sx), inc=(s_ve, 1))
            bank_free[bank] = ev_o
    res = [(s_ve, s_ve.n)]
    ob.close()
    ph.close()
    return res


def _layernorm(st, ev_in):
    nc, p = st["nc"], st["p"]
    li = _LN_N[0]
    _LN_N[0] += 1
    xs, xb = st["xs"], st["xb"]
    ones_r, ones_c = st["ones_r"], st["ones_c"]
    sems = st["sems"]
    s_mm, s_ve, s_se = sems["s_mm"], sems["s_ve"], sems["s_se"]
    s_ge = sems["s_ge"]
    ev_cn = st["marks"]["ev_cn"]

    ph = ExitStack()
    ee = ph.enter_context
    lnsq = ee(nc.sbuf_tensor(f"lnsq{li}", [128, 8 * 512], BF16))
    lrow = ee(nc.sbuf_tensor(f"lrow{li}", [1, 6 * 512], F32))
    tmp = ee(nc.sbuf_tensor(f"lntmp{li}", [128, 512], F32))
    pS = [ee(nc.psum_tensor(f"lnS{li}_{i}", [1, 512], F32)) for i in range(2)]
    pQ = [ee(nc.psum_tensor(f"lnQ{li}_{i}", [1, 512], F32)) for i in range(2)]
    pA = [ee(nc.psum_tensor(f"lnA{li}_{i}", [128, 512], F32)) for i in range(2)]
    pB = [ee(nc.psum_tensor(f"lnB{li}_{i}", [128, 512], F32)) for i in range(2)]
    M0, T0, T1, T2, CA, CB = 0, 512, 1024, 1536, 2048, 2560

    ev_cast = []
    for c in range(NCH):
        sl = slice(c * S, (c + 1) * S)
        ev_cast.append(p.add("vector",
                             lambda v, sl=sl: v.tensor_copy(out=xb[:, sl], in_=xs[:, sl]),
                             waits=tuple(ev_in), inc=(s_ve, 1)))
    stat_free = [None, None]
    ab_free = [None, None]
    lrow_rd = None
    evs = [None, None]
    for tg in range(4):
        bk = tg % 2
        ev_sq = []
        for c in range(NCH):
            ev_sq.append(p.add("scalar", lambda s, c=c, tg=tg: s.activation(
                out=lnsq[:, c * 512:(c + 1) * 512],
                in_=xs[:, c * S + tg * 512: c * S + (tg + 1) * 512],
                func=AF.Square), waits=tuple(ev_in), inc=(s_se, 1)))
        ev_ms = ev_mq = None
        for c in range(NCH):
            ev_ms = p.add("tensor", lambda t, c=c, tg=tg, bk=bk: t.matmul(
                pS[bk][0:1, :], ones_c[:, 0:1],
                xb[:, c * S + tg * 512: c * S + (tg + 1) * 512],
                start=(c == 0), stop=(c == 7)),
                waits=[ev_cast[c]] + ([stat_free[bk], ev_cn] if c == 0 else []),
                inc=(s_mm, 1))
        for c in range(NCH):
            ev_mq = p.add("tensor", lambda t, c=c, bk=bk: t.matmul(
                pQ[bk][0:1, :], ones_c[:, 0:1], lnsq[:, c * 512:(c + 1) * 512],
                start=(c == 0), stop=(c == 7)),
                waits=[ev_sq[c]] + ([stat_free[bk]] if c == 0 else []),
                inc=(s_mm, 1))
        p.add("vector", lambda v, bk=bk: v.tensor_scalar(
            out=lrow[0:1, M0:M0 + 512], in0=pS[bk][0:1, :], scalar1=1.0 / D,
            scalar2=None, op0=ALU.mult), waits=(ev_ms, lrow_rd), inc=(s_ve, 1))
        p.add("vector", lambda v, bk=bk: v.tensor_scalar(
            out=lrow[0:1, T0:T0 + 512], in0=pQ[bk][0:1, :], scalar1=1.0 / D,
            scalar2=EPS, op0=ALU.mult, op1=ALU.add), waits=(ev_mq,), inc=(s_ve, 1))
        ev3 = p.add("vector", lambda v: v.tensor_mul(
            out=lrow[0:1, T1:T1 + 512], in0=lrow[0:1, M0:M0 + 512],
            in1=lrow[0:1, M0:M0 + 512]), inc=(s_ve, 1))
        ev3b = p.add("gpsimd", lambda g: g.tensor_copy(
            out=lrow[0:1, T2:T2 + 512], in_=lrow[0:1, T1:T1 + 512]),
            waits=(ev3,), inc=(s_ge, 1))
        ev4 = p.add("vector", lambda v: v.scalar_tensor_tensor(
            out=lrow[0:1, T1:T1 + 512], in0=lrow[0:1, T2:T2 + 512], scalar=-1.0,
            in1=lrow[0:1, T0:T0 + 512], op0=ALU.mult, op1=ALU.add),
            waits=(ev3b,), inc=(s_ve, 1))
        ev5 = p.add("scalar", lambda s: s.activation(
            out=lrow[0:1, T0:T0 + 512], in_=lrow[0:1, T1:T1 + 512], func=AF.Sqrt),
            waits=(ev4,), inc=(s_se, 1))
        ev6 = p.add("vector", lambda v: v.reciprocal(
            out=lrow[0:1, CA:CA + 512], in_=lrow[0:1, T0:T0 + 512]),
            waits=(ev5,), inc=(s_ve, 1))
        ev6b = p.add("gpsimd", lambda g: g.tensor_copy(
            out=lrow[0:1, T2:T2 + 512], in_=lrow[0:1, CA:CA + 512]),
            waits=(ev6,), inc=(s_ge, 1))
        ev7 = p.add("vector", lambda v: v.scalar_tensor_tensor(
            out=lrow[0:1, CB:CB + 512], in0=lrow[0:1, M0:M0 + 512], scalar=-1.0,
            in1=lrow[0:1, T2:T2 + 512], op0=ALU.mult, op1=ALU.mult),
            waits=(ev6b,), inc=(s_ve, 1))
        stat_free[bk] = ev7
        ev_mA = p.add("tensor", lambda t, bk=bk: t.matmul(
            pA[bk][:, :], ones_r[0:1, :], lrow[0:1, CA:CA + 512],
            start=True, stop=True), waits=(ev6, ev7, ab_free[bk]), inc=(s_mm, 1))
        ev_mB = p.add("tensor", lambda t, bk=bk: t.matmul(
            pB[bk][:, :], ones_r[0:1, :], lrow[0:1, CB:CB + 512],
            start=True, stop=True), inc=(s_mm, 1))
        lrow_rd = ev_mB
        ev_ap = ev_cb = None
        for c in range(NCH):
            xsl = slice(c * S + tg * 512, c * S + (tg + 1) * 512)
            p.add("vector", lambda v, xsl=xsl, bk=bk: v.tensor_mul(
                out=tmp[:, :], in0=xs[:, xsl], in1=pA[bk][:, :]),
                waits=(ev_mA,), inc=(s_ve, 1))
            ev_ap = p.add("vector", lambda v, xsl=xsl, bk=bk: v.tensor_add(
                out=xs[:, xsl], in0=tmp[:, :], in1=pB[bk][:, :]),
                waits=(ev_mB,), inc=(s_ve, 1))
            ev_cb = p.add("scalar", lambda s, xsl=xsl: s.activation(
                out=xb[:, xsl], in_=xs[:, xsl], func=AF.Copy),
                waits=(ev_ap,), inc=(s_se, 1))
        ab_free[bk] = ev_ap
        evs = [ev_ap, ev_cb]
    ph.close()
    return evs


def _ffn(st, ev_in, layer):
    nc, p = st["nc"], st["p"]
    dt, SM = st["dt"], st["SM"]
    xs, xb = st["xs"], st["xb"]
    smalls, scal, ones_r = st["smalls"], st["scal"], st["ones_r"]
    sems = st["sems"]
    s_mm, s_ve, s_se, s_ge = sems["s_mm"], sems["s_ve"], sems["s_se"], sems["s_ge"]
    sfs = [sems["s_fA"], sems["s_fB"]]
    sgs = [sems["s_gA"], sems["s_gB"]]
    s_qs, s_qc, s_tr = sems["s_qs"], sems["s_qc"], sems["s_tr"]
    L0 = layer == 0
    QAT = not L0

    ph = ExitStack()
    ee = ph.enter_context

    if QAT:
        with ExitStack() as am:
            ae = am.enter_context
            qsl = [ae(nc.sbuf_tensor(f"qsl{i}", [128, 2048], F32)) for i in range(2)]
            amax = ae(nc.sbuf_tensor("amax", [128, 32], F32))
            arow = ae(nc.sbuf_tensor("arow", [1, 128], F32))
            pk67 = [ae(nc.psum_tensor(f"pam{i}", [128, 512], F32)) for i in range(2)]
            slot_rd = [None, None]
            for wi, wname in enumerate(["l1_wfc", "l1_wproj"]):
                wap = dt[wname]
                flat = AP(wap.tensor, 0, [[2048, 128], [1, 2048]])
                red_evs = []
                for sl in range(16):
                    ss = sl % 2
                    src = AP(wap.tensor, sl * 128 * 2048, [[2048, 128], [1, 2048]])
                    ev_l = p.dma("sync", out=qsl[ss][:], in_=src,
                                 waits=(slot_rd[ss],) + (tuple(ev_in) if sl < 2 and wi == 0 else ()),
                                 inc=(s_qs, 16))
                    ev_r = p.add("vector", lambda v, ss=ss, sl=sl: v.reduce_max(
                        out=amax[:, sl:sl + 1], in_=qsl[ss][:], axis=AX.X,
                        apply_absolute_value=True), waits=(ev_l,), inc=(s_ve, 1))
                    slot_rd[ss] = ev_r
                    red_evs.append(ev_r)
                ev_cb0 = p.add("gpsimd", lambda g: g.tensor_copy(
                    out=amax[:, 16:32], in_=amax[:, 0:16]),
                    waits=(red_evs[-1],), inc=(s_ge, 1))
                ev_c = p.add("vector", lambda v: v.reduce_max(
                    out=amax[:, 0:1], in_=amax[:, 16:32], axis=AX.X),
                    waits=(ev_cb0,), inc=(s_ve, 1))
                ev_t = p.dma("sync", out=arow[0:1, :], in_=amax[:, 0:1],
                             waits=(ev_c, st["track"]["tr_chain"]),
                             inc=(s_tr, 16), slow=True)
                st["track"]["tr_chain"] = ev_t
                ev_m = p.add("vector", lambda v, wi=wi: v.reduce_max(
                    out=scal[0:1, 20 + 4 * wi:21 + 4 * wi], in_=arow[0:1, :],
                    axis=AX.X), waits=(ev_t,), inc=(s_ve, 1))
                ev_s = p.add("gpsimd", lambda g, wi=wi: g.tensor_scalar(
                    out=scal[0:1, 21 + 4 * wi:22 + 4 * wi],
                    in0=scal[0:1, 20 + 4 * wi:21 + 4 * wi],
                    scalar1=1.0 / 127.0, scalar2=None, op0=ALU.mult),
                    waits=(ev_m,), inc=(s_ge, 1))
                ev_rs = p.add("vector", lambda v, wi=wi: v.reciprocal(
                    out=scal[0:1, 22 + 4 * wi:23 + 4 * wi],
                    in_=scal[0:1, 21 + 4 * wi:22 + 4 * wi]),
                    waits=(ev_s,), inc=(s_ve, 1))
                cS = SM["SFC"] if wi == 0 else SM["SPJ"]
                cR = SM["RSFC"] if wi == 0 else SM["RSPJ"]
                ev_b1 = p.add("tensor", lambda t, wi=wi: t.matmul(
                    pk67[0][:, 0:1], ones_r[0:1, :],
                    scal[0:1, 21 + 4 * wi:22 + 4 * wi], start=True, stop=True),
                    waits=(ev_s, ev_rs), inc=(s_mm, 1))
                ev_b2 = p.add("tensor", lambda t, wi=wi: t.matmul(
                    pk67[1][:, 0:1], ones_r[0:1, :],
                    scal[0:1, 22 + 4 * wi:23 + 4 * wi], start=True, stop=True),
                    inc=(s_mm, 1))
                p.add("vector", lambda v, cS=cS: v.tensor_copy(
                    out=smalls[:, cS:cS + 1], in_=pk67[0][:, 0:1]),
                    waits=(ev_b1,), inc=(s_ve, 1))
                ev_cp = p.add("vector", lambda v, cR=cR: v.tensor_copy(
                    out=smalls[:, cR:cR + 1], in_=pk67[1][:, 0:1]),
                    waits=(ev_b2,), inc=(s_ve, 1))
            cal = ae(nc.sbuf_tensor("cal", [1, 8], F32))
            cali = ae(nc.sbuf_tensor("cali", [1, 2], I32))
            ev_z0 = p.add("gpsimd", lambda g: g.memset(cal[0:1, 0:1], 1.5),
                          inc=(s_ge, 1))
            ev_z1 = p.add("vector", lambda v: v.tensor_copy(
                out=cali[0:1, 0:1], in_=cal[0:1, 0:1]), waits=(ev_z0,),
                inc=(s_ve, 1))
            ev_z2 = p.add("gpsimd", lambda g: g.tensor_copy(
                out=cali[0:1, 1:2], in_=cali[0:1, 0:1]), waits=(ev_z1,),
                inc=(s_ge, 1))
            ev_z3 = p.add("vector", lambda v: v.tensor_scalar(
                out=cal[0:1, 2:3], in0=cali[0:1, 1:2], scalar1=-0.5,
                scalar2=257.0, op0=ALU.mult, op1=ALU.add), waits=(ev_z2,),
                inc=(s_ve, 1))
            ev_z4 = p.add("tensor", lambda t: t.matmul(
                pk67[0][:, 4:5], ones_r[0:1, :], cal[0:1, 2:3],
                start=True, stop=True), waits=(ev_z3,), inc=(s_mm, 1))
            ev_hb = p.add("vector", lambda v: v.tensor_copy(
                out=smalls[:, SM["HB"]:SM["HB"] + 1], in_=pk67[0][:, 4:5]),
                waits=(ev_z4,), inc=(s_ve, 1))
            ev_scales = ev_hb
    else:
        ev_scales = None

    hbuf = ee(nc.sbuf_tensor(f"hbuf{layer}", [128, 32 * 1024], BF16))
    if QAT:
        w1s = [ee(nc.sbuf_tensor(f"w1f{i}", [128, 1024], F32)) for i in range(2)]
        wqs1 = [ee(nc.sbuf_tensor(f"wq1{i}", [128, 1024], BF16)) for i in range(2)]
        qtmp = ee(nc.sbuf_tensor("qtmp", [128, 1024], F32))
        qint = ee(nc.sbuf_tensor("qint", [128, 1024], I32))
        blo_all = ee(nc.sbuf_tensor("blo_all", [16, 32 * 128], BF16))
        afc = ee(nc.sbuf_tensor("afc", [128, 8 * 16], BF16))
        apj = ee(nc.sbuf_tensor("apj", [128, 32 * 16], BF16))
        bpl = ee(nc.sbuf_tensor("bpl", [16, 8 * 128], BF16))
        lrx = ee(nc.sbuf_tensor("lrx", [16, 1024], BF16))
        lr2x = ee(nc.sbuf_tensor("lr2x", [16, 1024], BF16))
        w2q = [ee(nc.sbuf_tensor(f"w2q{i}", [128, 512], F32)) for i in range(2)]
    else:
        w1s = [ee(nc.sbuf_tensor(f"w1b{i}", [128, 1024], BF16)) for i in range(2)]
    w2s = [ee(nc.sbuf_tensor(f"w2s{layer}_{i}", [128, 512], BF16)) for i in range(2)]
    pk = [ee(nc.psum_tensor(f"pk{layer}{i}", [128, 512], F32)) for i in range(8)]

    bank_free = [None] * 8
    w1_rd = [None, None]
    w2_rd = [None, None]
    w1_cw = [None, None]
    w2_cw = [None, None]
    hb_free = None
    lrx_rd = None
    lr2x_rd = None
    ev_out = list(ev_in)

    b1c = SM["L0B1"] if L0 else SM["L1BFC"]
    b2c = SM["L0B2"] if L0 else SM["L1BPJ"]

    if QAT:
        s_lo = sems["s_rro"]
        p.dma("gpsimd", out=afc[:].rearrange("p (c n) -> p c n", n=16),
              in_=dt["l1_afc"].transpose([1, 0, 2]),
              waits=tuple(ev_in), inc=(s_lo, 16))
        p.dma("gpsimd", out=apj[:].rearrange("p (c n) -> p c n", n=16),
              in_=dt["l1_aproj"].transpose([1, 0, 2]), inc=(s_lo, 16))
        p.dma("gpsimd", out=bpl[:].rearrange("p (c n) -> p c n", n=128),
              in_=dt["l1_bproj_lo"].transpose([1, 0, 2]), inc=(s_lo, 16))
        p.dma("gpsimd", out=blo_all[:].rearrange("p (c n) -> p c n", n=128),
              in_=dt["l1_bfc_lo"].transpose([1, 0, 2]), inc=(s_lo, 16))
        ev_lores = (s_lo, sems["s_rro"].n)

    for th in range(2):
        tok0 = th * 1024
        if QAT:
            for ng in range(2):
                ev_ml = None
                for kc in range(8):
                    wts = []
                    if kc == 0:
                        wts = [ev_lores, bank_free[2], lrx_rd] + list(ev_in)
                    ev_ml = p.add("tensor", lambda t, kc=kc, ng=ng, tok0=tok0: t.matmul(
                        pk[2][0:16, 0:512], afc[:, kc * 16:(kc + 1) * 16],
                        xb[:, kc * S + tok0 + ng * 512: kc * S + tok0 + (ng + 1) * 512],
                        start=(kc == 0), stop=(kc == 7)), waits=wts, inc=(s_mm, 1))
                ev_le = p.add("scalar", lambda sc, ng=ng: sc.activation(
                    out=lrx[0:16, ng * 512:(ng + 1) * 512], in_=pk[2][0:16, 0:512],
                    func=AF.Identity, scale=2.0, bias=0.0),
                    waits=(ev_ml,), inc=(s_se, 1))
                bank_free[2] = ev_le
            ev_lrx = ev_le

        for ic in range(32):
            sl = ic % 2
            if QAT and th == 0:
                ev_w = p.dma("gpsimd", out=w1s[sl][:].rearrange("p (c m) -> p c m", m=128),
                             in_=dt["l1_wfc"][ic].transpose([1, 0, 2]),
                             waits=(w1_rd[sl], ev_scales), inc=(sfs[sl], 16))
                ev_q1 = p.add("vector", lambda v, sl=sl: v.tensor_scalar(
                    out=qtmp[:], in0=w1s[sl][:],
                    scalar1=smalls[:, SM["RSFC"]:SM["RSFC"] + 1],
                    scalar2=smalls[:, SM["HB"]:SM["HB"] + 1],
                    op0=ALU.mult, op1=ALU.add), waits=(ev_w,), inc=(s_ve, 1))
                w1_rd[sl] = ev_q1
                p.add("vector", lambda v: v.tensor_copy(
                    out=qint[:], in_=qtmp[:]), inc=(s_ve, 1))
                ev_q4 = p.add("vector", lambda v, sl=sl: v.tensor_scalar(
                    out=wqs1[sl][:], in0=qint[:], scalar1=-256.0,
                    scalar2=smalls[:, SM["SFC"]:SM["SFC"] + 1],
                    op0=ALU.add, op1=ALU.mult),
                    waits=(w1_cw[sl],), inc=(s_ve, 1))
                ev_cw = p.dma("sync", out=st["hbm_wfq"].ap()[ic], in_=wqs1[sl][:],
                              waits=(ev_q4,), inc=(s_qc, 16))
                w1_cw[sl] = (sems["s_qc"], sems["s_qc"].n)
                wsrc = wqs1[sl]
                ev_wrdy = ev_q4
            elif QAT:
                ev_w = p.dma("sync", out=wqs1[sl][:], in_=st["hbm_wfq"].ap()[ic],
                             waits=(w1_rd[sl], (s_qc, sems["s_qc"].n)),
                             inc=(sfs[sl], 16))
                ev_wrdy = (sfs[sl], sfs[sl].n)
                wsrc = wqs1[sl]
            else:
                p.dma("gpsimd", out=w1s[sl][:].rearrange("p (c m) -> p c m", m=128),
                      in_=dt["l0_w1"][ic].transpose([1, 0, 2]),
                      waits=(w1_rd[sl],) + (tuple(ev_in) if ic < 2 and th == 0 else ()),
                      inc=(sfs[sl], 16))
                ev_wrdy = (sfs[sl], sfs[sl].n)
                wsrc = w1s[sl]
            for ng in range(2):
                bank = ng
                ev_m1 = None
                first = True
                if QAT:
                    ev_m1 = p.add("tensor", lambda t, ic=ic, ng=ng: t.matmul(
                        pk[ng][:, 0:512],
                        blo_all[0:16, ic * 128:(ic + 1) * 128],
                        lrx[0:16, ng * 512:(ng + 1) * 512],
                        start=True, stop=False),
                        waits=(ev_wrdy, ev_lrx, bank_free[ng], hb_free, ev_lores),
                        inc=(s_mm, 1))
                    first = False
                for kc in range(8):
                    wts = []
                    if first and kc == 0:
                        wts = [ev_wrdy, bank_free[ng], hb_free]
                    ev_m1 = p.add("tensor", lambda t, wsrc=wsrc, kc=kc, ng=ng, tok0=tok0: t.matmul(
                        pk[ng][:, 0:512], wsrc[:, kc * 128:(kc + 1) * 128],
                        xb[:, kc * S + tok0 + ng * 512: kc * S + tok0 + (ng + 1) * 512],
                        start=(kc == 0 and not QAT), stop=(kc == 7)),
                        waits=wts, inc=(s_mm, 1))
                w1_rd[sl] = ev_m1
                ev_h = p.add("scalar", lambda sc, ic=ic, ng=ng, b1c=b1c: sc.activation(
                    out=hbuf[:, ic * 1024 + ng * 512: ic * 1024 + (ng + 1) * 512],
                    in_=pk[ng][:, 0:512], func=(AF.Relu if L0 else AF.Gelu),
                    bias=smalls[:, b1c + ic:b1c + ic + 1], scale=1.0),
                    waits=(ev_m1,), inc=(s_se, 1))
                bank_free[ng] = ev_h
        ev_h_done = ev_h

        if QAT:
            for sg in range(2):
                ev_ml2 = None
                for kc in range(32):
                    wts = []
                    if kc == 0:
                        wts = [ev_h_done, bank_free[3], lr2x_rd, ev_lores]
                    ev_ml2 = p.add("tensor", lambda t, kc=kc, sg=sg: t.matmul(
                        pk[3][0:16, 0:512], apj[:, kc * 16:(kc + 1) * 16],
                        hbuf[:, kc * 1024 + sg * 512: kc * 1024 + (sg + 1) * 512],
                        start=(kc == 0), stop=(kc == 31)), waits=wts, inc=(s_mm, 1))
                ev_l2 = p.add("scalar", lambda sc, sg=sg: sc.activation(
                    out=lr2x[0:16, sg * 512:(sg + 1) * 512], in_=pk[3][0:16, 0:512],
                    func=AF.Identity, scale=2.0, bias=0.0),
                    waits=(ev_ml2,), inc=(s_se, 1))
                bank_free[3] = ev_l2
            ev_lr2 = ev_l2

        for half in range(2):
            if QAT:
                for oc in range(4):
                    for sg in range(2):
                        bank = oc * 2 + sg
                        p.add("tensor", lambda t, half=half, oc=oc, sg=sg, bank=bank: t.matmul(
                            pk[bank][:, 0:512],
                            bpl[0:16, (half * 4 + oc) * 128:(half * 4 + oc) * 128 + 128],
                            lr2x[0:16, sg * 512:(sg + 1) * 512],
                            start=True, stop=False),
                            waits=(ev_lr2, ev_lores, bank_free[bank]),
                            inc=(s_mm, 1))
            for kc in range(32):
                sl = kc % 2
                if QAT and th == 0:
                    ev_w2 = p.dma("sync", out=w2q[sl][:].rearrange("p (a m) -> p a m", m=128),
                                  in_=dt["l1_wproj"][half][kc].rearrange("a p m -> p a m"),
                                  waits=(w2_rd[sl], ev_scales), inc=(sgs[sl], 16))
                    ev_q1 = p.add("vector", lambda v, sl=sl: v.tensor_scalar(
                        out=qtmp[:, 0:512], in0=w2q[sl][:],
                        scalar1=smalls[:, SM["RSPJ"]:SM["RSPJ"] + 1],
                        scalar2=smalls[:, SM["HB"]:SM["HB"] + 1],
                        op0=ALU.mult, op1=ALU.add), waits=(ev_w2,), inc=(s_ve, 1))
                    w2_rd[sl] = ev_q1
                    p.add("vector", lambda v: v.tensor_copy(
                        out=qint[:, 0:512], in_=qtmp[:, 0:512]), inc=(s_ve, 1))
                    ev_q4 = p.add("vector", lambda v, sl=sl: v.tensor_scalar(
                        out=w2s[sl][:], in0=qint[:, 0:512], scalar1=-256.0,
                        scalar2=smalls[:, SM["SPJ"]:SM["SPJ"] + 1],
                        op0=ALU.add, op1=ALU.mult),
                        waits=(w2_cw[sl],), inc=(s_ve, 1))
                    p.dma("sync", out=st["hbm_wpq"].ap()[half][kc], in_=w2s[sl][:],
                          waits=(ev_q4,), inc=(s_qc, 16))
                    w2_cw[sl] = (sems["s_qc"], sems["s_qc"].n)
                    ev_w2r = ev_q4
                elif QAT:
                    ev_w2 = p.dma("sync", out=w2s[sl][:],
                                  in_=st["hbm_wpq"].ap()[half][kc],
                                  waits=(w2_rd[sl], (s_qc, sems["s_qc"].n)),
                                  inc=(sgs[sl], 16))
                    ev_w2r = (sgs[sl], sgs[sl].n)
                else:
                    p.dma("gpsimd", out=w2s[sl][:].rearrange("p (a m) -> p a m", m=128),
                          in_=dt["l0_w2"][half][kc].transpose([1, 0, 2]),
                          waits=(w2_rd[sl],), inc=(sgs[sl], 16))
                    ev_w2r = (sgs[sl], sgs[sl].n)
                ev_m2 = None
                for oc in range(4):
                    for sg in range(2):
                        bank = oc * 2 + sg
                        wts = []
                        if oc == 0 and sg == 0:
                            wts = [ev_w2r, ev_h_done]
                            if kc == 0 and not QAT:
                                wts += [bank_free[b] for b in range(8)]
                        ev_m2 = p.add("tensor", lambda t, oc=oc, sg=sg, kc=kc, sl=sl, bank=bank: t.matmul(
                            pk[bank][:, 0:512], w2s[sl][:, oc * 128:(oc + 1) * 128],
                            hbuf[:, kc * 1024 + sg * 512: kc * 1024 + (sg + 1) * 512],
                            start=(kc == 0 and not QAT), stop=(kc == 31)),
                            waits=wts, inc=(s_mm, 1))
                w2_rd[sl] = ev_m2
            for oc in range(4):
                for sg in range(2):
                    bank = oc * 2 + sg
                    ocabs = half * 4 + oc
                    xsl = slice(ocabs * S + tok0 + sg * 512,
                                ocabs * S + tok0 + (sg + 1) * 512)
                    ev_e2 = p.add("vector", lambda v, bank=bank, ocabs=ocabs, xsl=xsl, b2c=b2c: v.scalar_tensor_tensor(
                        out=xs[:, xsl], in0=pk[bank][:, 0:512],
                        scalar=smalls[:, b2c + ocabs:b2c + ocabs + 1],
                        in1=xs[:, xsl], op0=ALU.add, op1=ALU.add),
                        waits=(ev_m2, st["marks"]["ev_sx"]), inc=(s_ve, 1))
                    bank_free[bank] = ev_e2
                    ev_out = [ev_e2]
        hb_free = (s_mm, s_mm.n)
        if QAT:
            lrx_rd = (s_mm, s_mm.n)
            lr2x_rd = (s_mm, s_mm.n)
        if st["dbg"] and QAT and th == 1 and st.get("stop_") == "f1res":
            ev_all3 = [(s_mm, s_mm.n), (s_ve, s_ve.n), (s_se, s_se.n),
                       (s_ge, s_ge.n)]
            dump_ev3 = None
            for nm3, t3 in [("f_lrx", lrx), ("f_lr2x", lr2x), ("f_hbuf", hbuf),
                            ("f_afc", afc), ("f_apj", apj), ("f_bpl", bpl),
                            ("f_wq1a", wqs1[0]), ("f_wq1b", wqs1[1]),
                            ("f_smalls", smalls), ("f_scal", scal),
                            ("f_xs", xs), ("f_xb", xb)]:
                ap3 = t3[:]
                od3 = nc.dram_tensor(nm3, list(ap3.shape), ap3.dtype,
                                     kind="ExternalOutput").ap()
                dump_ev3 = p.dma("gpsimd", out=od3, in_=ap3, waits=tuple(ev_all3),
                                 inc=(sems["s_tr"], 16))
            hb_free = dump_ev3
            lrx_rd = dump_ev3
            lr2x_rd = dump_ev3
            for bb2 in range(8):
                bank_free[bb2] = dump_ev3
            w1_rd = [dump_ev3, dump_ev3]
            w2_rd = [dump_ev3, dump_ev3]
    ph.close()
    return ev_out


def _layout_weights(inputs):
    f = np.ascontiguousarray
    w = {}
    Wqkv = inputs["L0_attn_Wqkv"]
    qr = np.concatenate([np.arange(192 * h, 192 * h + 64) for h in range(H)])
    kr = qr + 64
    vr = qr + 128
    WqT = Wqkv[qr].T
    WkT = Wqkv[kr].T
    WvT = Wqkv[vr].T
    l0_wqk = np.empty((8, 2, 8, 128, 128), np.float32)
    for pr in range(8):
        cols = slice(128 * pr, 128 * pr + 128)
        l0_wqk[pr, 0] = WqT[:, cols].reshape(8, 128, 128)
        l0_wqk[pr, 1] = WkT[:, cols].reshape(8, 128, 128)
    w["l0_wqk"] = f(l0_wqk)
    w["l0_wv"] = f(WvT.reshape(8, 128, D))
    bqkv = inputs["L0_attn_bqkv"]
    w["l0_bq"] = f(bqkv[qr].reshape(8, 128).T)
    w["l0_bk"] = f(bqkv[kr].reshape(8, 128).T)
    w["l0_bv"] = f(bqkv[vr].reshape(1, D))
    w["l0_wrT"] = f(inputs["L0_attn_Wr"].T)
    w["l0_br"] = f(inputs["L0_attn_br"].reshape(1, 257))
    W1 = inputs["L0_ffn_W1"]
    w["l0_w1"] = f(W1.reshape(8, 128, 32, 128).transpose(2, 0, 1, 3))
    w["l0_b1"] = f(inputs["L0_ffn_b1"].reshape(32, 128).T)
    W2 = inputs["L0_ffn_W2"]
    w["l0_w2"] = f(W2.reshape(32, 128, 2, 4, 128).transpose(2, 0, 3, 1, 4))
    w["l0_b2"] = f(inputs["L0_ffn_b2"].reshape(8, 128).T)
    l1_wqk = np.empty((8, 2, 8, 128, 128), np.float32)
    Wq, Wk = inputs["L1_attn_Wq"], inputs["L1_attn_Wk"]
    for pr in range(8):
        cols = slice(128 * pr, 128 * pr + 128)
        l1_wqk[pr, 0] = Wq[:, cols].reshape(8, 128, 128)
        l1_wqk[pr, 1] = Wk[:, cols].reshape(8, 128, 128)
    w["l1_wqk"] = f(l1_wqk)
    w["l1_wv"] = f(inputs["L1_attn_Wv"].reshape(8, 128, D))
    w["l1_bq"] = f(inputs["L1_attn_bq"].reshape(8, 128).T)
    w["l1_bk"] = f(inputs["L1_attn_bk"].reshape(8, 128).T)
    w["l1_bv"] = f(inputs["L1_attn_bv"].reshape(1, D))
    w["l1_wo"] = f(inputs["L1_attn_Wo"].reshape(8, 128, 8, 128).transpose(2, 0, 1, 3))
    w["l1_bo"] = f(inputs["L1_attn_bo"].reshape(8, 128).T)
    WfcT = inputs["L1_ffn_W_fc"].T
    w["l1_wfc"] = f(WfcT.reshape(8, 128, 32, 128).transpose(2, 0, 1, 3))
    w["l1_bfc"] = f(inputs["L1_ffn_b_fc"].reshape(32, 128).T)
    w["l1_afc"] = f(inputs["L1_ffn_A_fc"].T.reshape(8, 128, 16))
    w["l1_bfc_lo"] = f(inputs["L1_ffn_B_fc"].T.reshape(16, 32, 128).transpose(1, 0, 2))
    WprojT = inputs["L1_ffn_W_proj"].T
    w["l1_wproj"] = f(WprojT.reshape(32, 128, 2, 4, 128).transpose(2, 0, 3, 1, 4))
    w["l1_bproj"] = f(inputs["L1_ffn_b_proj"].reshape(8, 128).T)
    w["l1_aproj"] = f(inputs["L1_ffn_A_proj"].T.reshape(32, 128, 16))
    w["l1_bproj_lo"] = f(inputs["L1_ffn_B_proj"].T.reshape(16, 8, 128).transpose(1, 0, 2))
    return w


def kernel(_stop="full", _dbg=False, **inputs):
    x = inputs["L0_attn_x"]
    w = _layout_weights(inputs)
    nc = build_program(stop=_stop, dbg=_dbg)
    in_maps = []
    for b in range(NCORES):
        m = dict(w)
        m["x_fm"] = np.ascontiguousarray(x[b].T).reshape(NCH, 128, S)
        in_maps.append(m)
    res = run_bass_kernel_spmd(nc, in_maps, core_ids=list(range(NCORES)))
    out = np.empty((NCORES, S, D), np.float32)
    for b in range(NCORES):
        out[b] = res.results[b]["y"].reshape(D, S).T
    if _dbg:
        return out, res.results
    return out
